# version 03
# baseline (speedup 1.0000x reference)
import contextlib
import numpy as np
import ml_dtypes
import concourse.bass as bass
import concourse.mybir as mybir
from concourse.bass_utils import run_bass_kernel_spmd

F32 = mybir.dt.float32
BF16 = mybir.dt.bfloat16
I32 = mybir.dt.int32
ALU = mybir.AluOpType
AF = mybir.ActivationFunctionType
AX = mybir.AxisListType.X

NCORES = 8
T = 1024
D = 2048
HEADS = 16
HD = 128
CH = 128
NT = T // 128
HU = 64
N_HU = T // HU
NW_ELEMS = float(8192 * 2048)
PG_GU = 128 * 2048
PG_D = 128 * 1024
SL_ELEMS = 8 * PG_GU * 2 + 16 * PG_D

PARTS = ("ffn1", "attn", "ffn2")
ATTN_STAGES = None


class _StopEmit(Exception):
    def __init__(self, ev):
        self.ev = ev
DEBUG_EPRIME = False


def _sylvester(n):
    h = np.array([[1.0]])
    while h.shape[0] < n:
        h = np.block([[h, h], [h, -h]])
    return h.astype(np.float32)


def _ll():
    return -(2.0 ** (-8.0 * (np.arange(HEADS, dtype=np.float64) + 1.0) / HEADS))


def _lamtab(core):
    ll = _ll()
    t = np.zeros((128, 128), np.float32)
    for cp in range(NCORES):
        for h in range(HEADS):
            if cp < core:
                t[:, cp * 16 + h] = np.float32(np.exp(ll[h] * CH * 8.0 * (core - 1 - cp)))
    return t


def host_tables(core):
    H64B = np.kron(np.eye(2, dtype=np.float32), _sylvester(64)).astype(np.float32)
    tg = np.arange(core * T, (core + 1) * T, dtype=np.float64)
    j = np.arange(64, dtype=np.float64)
    theta = 10000.0 ** (-2.0 * j / 128.0)
    ang = tg[:, None] * theta[None, :]
    ll = _ll()
    s = np.arange(CH, dtype=np.float64)
    dmask = np.zeros((HEADS * CH, CH), dtype=np.float32)
    for h in range(HEADS):
        col = np.exp(-ll[h] * (s + 1.0 - 64.0))
        m = (np.arange(CH)[None, :] >= s[:, None])
        dmask[h * CH:(h + 1) * CH, :] = (col[:, None] * m).astype(np.float32)
    dqcol = np.exp(ll[None, :] * (s[:, None] + 1.0 - 64.0)).astype(np.float32)
    dkcol = np.exp(ll[None, :] * (CH - 1.0 - s[:, None])).astype(np.float32)
    P4 = np.zeros((32, 512), dtype=np.float32)
    for usub in range(2):
        for tlo in range(2):
            for th in range(32):
                P4[th, (2 * usub + tlo) * 128 + usub * 64 + 2 * th + tlo] = 1.0
    cos_rep = np.tile(np.cos(ang).astype(np.float32), (1, 16))
    sin_rep = np.tile(np.sin(ang).astype(np.float32), (1, 16))
    dkrep = np.zeros((128, D), np.float32)
    for h in range(HEADS):
        dkrep[:, h * 128:(h + 1) * 128] = dkcol[:, h:h + 1]
    return {
        "H128": _sylvester(128) * np.float32(1.0 / np.sqrt(8192)), "H64B": H64B,
        "I128": np.asarray(np.eye(128, dtype=np.float32).astype(ml_dtypes.bfloat16)),
        "cos_rep": cos_rep, "sin_rep": sin_rep, "dkrep": dkrep,
        "dmask": dmask, "dqcol": dqcol, "dkcol": dkcol, "P4": P4,
        "lamtab": _lamtab(core),
        "SEL2": np.concatenate([np.kron(np.eye(2, dtype=np.float32), np.ones((1, 64), np.float32)), np.zeros((30, 128), np.float32)], 0),
    }


class Sem:
    def __init__(self, h):
        self.h, self.n = h, 0

    def inc(self, instr, k=1):
        instr.then_inc(self.h, k)
        self.n += k
        return instr

    def incd(self, instr):
        return self.inc(instr, 16)


def _sched_B(base_sc, base_ev, base_mm, base_gp):
    M = {}
    sc, ev, mm, gp = base_sc, base_ev, base_mm, base_gp
    for u in range(N_HU):
        sc += 1; M["reord", u] = sc
        for b in range(8):
            mm += 1; M["r1", u, b] = mm
            if b >= 2:
                mm += 1; M["r2", u, b - 2] = mm
        for b in [6, 7]:
            mm += 1; M["r2", u, b] = mm
        for b in range(8):
            ev += 1; M["r1ev", u, b] = ev
        for b in range(8):
            sc += 1; M["yev", u, b] = sc
        ev += 1; M["mx1", u] = ev
        mm += 1; M["mx1T", u] = mm
        ev += 1; M["mx1Tev", u] = ev
        ev += 1; M["chain", u] = ev
        mm += 1; M["sT", u] = mm
        ev += 1; M["sTev", u] = ev
        mm += 1; M["ssm", u] = mm
        ev += 1; M["ssmev", u] = ev
        ev += 1; M["vmult", u] = ev
        ev += 1; M["vp", u, 0] = ev
        ev += 1; M["c", u, 0] = ev
        ev += 1; M["vp", u, 1] = ev
        ev += 1; M["c", u, 1] = ev
        gp += 1; M["d", u, 0] = gp
        gp += 1; M["ge", u, 0] = gp
        gp += 1; M["r", u, 0] = gp
        gp += 1; M["d", u, 1] = gp
        gp += 1; M["ge", u, 1] = gp
        gp += 1; M["r", u, 1] = gp
        ev += 1; M["q", u, 0] = ev
        ev += 1; M["q", u, 1] = ev
        M["vdone", u] = ev
        for b in range(8):
            mm += 1; M["r3", u, b] = mm
        for b in range(8):
            sc += 1; M["r3ev", u, b] = sc
        if u % 4 == 3:
            g4 = u // 4
            mm += 1; M["dqc", g4, 0] = mm
            mm += 1; M["dqc", g4, 1] = mm
            ev += 2; M["dqcev", g4] = ev
            for o in range(16):
                mm += 1; M["dn", g4, o] = mm
                sc += 1; M["dnev", g4, o] = sc
                for tb in range(2):
                    mm += 1; M["dntm", g4, o, tb] = mm
                    ev += 1; M["resid", g4, o, tb] = ev
    return M, sc, ev, mm, gp


def build_program(parts=PARTS):
    nc = bass.Bass("TRN2", target_bir_lowering=False, debug=False)
    ll = _ll()
    lamC = [float(np.float32(np.exp(ll[h] * CH))) for h in range(HEADS)]
    sScl = [float(np.float32(np.exp(ll[h] * 64.0))) for h in range(HEADS)]

    x_in = nc.dram_tensor("x_in", [T, D], F32, kind="ExternalInput").ap()
    wq_d = nc.dram_tensor("wq", [D, D], F32, kind="ExternalInput").ap()
    wk_d = nc.dram_tensor("wk", [D, D], F32, kind="ExternalInput").ap()
    wv_d = nc.dram_tensor("wv", [D, D], F32, kind="ExternalInput").ap()
    wo_d = nc.dram_tensor("wo", [D, D], F32, kind="ExternalInput").ap()
    wg1_d = nc.dram_tensor("wg1", [D, HD], F32, kind="ExternalInput").ap()
    wg2_d = nc.dram_tensor("wg2", [HD, D], F32, kind="ExternalInput").ap()
    nrmw_d = nc.dram_tensor("nrmw", [1, D], F32, kind="ExternalInput").ap()
    wg_sl = nc.dram_tensor("wg_sl", [T, D], F32, kind="ExternalInput").ap()
    wu_sl = nc.dram_tensor("wu_sl", [T, D], F32, kind="ExternalInput").ap()
    wd_sl = nc.dram_tensor("wd_sl", [D, T], F32, kind="ExternalInput").ap()
    y_out = nc.dram_tensor("y_out", [T, D], F32, kind="ExternalOutput").ap()
    dbg_out = (nc.dram_tensor("dbg_out", [N_HU * 128, 64 * HU], F32, kind="ExternalOutput").ap()
               if DEBUG_EPRIME else None)
    dbg_sb = (nc.dram_tensor("dbg_sb", [128, 10312], F32, kind="ExternalOutput").ap()
              if DEBUG_EPRIME else None)
    dbg_bf = (nc.dram_tensor("dbg_bf", [128, 16384], BF16, kind="ExternalOutput").ap()
              if DEBUG_EPRIME else None)

    tH128 = nc.dram_tensor("H128", [128, 128], F32, kind="ExternalInput").ap()
    tH64B = nc.dram_tensor("H64B", [128, 128], F32, kind="ExternalInput").ap()
    tI128 = nc.dram_tensor("I128", [128, 128], BF16, kind="ExternalInput").ap()
    tcos = nc.dram_tensor("cos_rep", [T, 1024], F32, kind="ExternalInput").ap()
    tsin = nc.dram_tensor("sin_rep", [T, 1024], F32, kind="ExternalInput").ap()
    tdkrep = nc.dram_tensor("dkrep", [128, D], F32, kind="ExternalInput").ap()
    tnwrep = nc.dram_tensor("nrmw_rep", [128, D], F32, kind="ExternalInput").ap()
    tdmask = nc.dram_tensor("dmask", [HEADS * CH, CH], F32, kind="ExternalInput").ap()
    tdqcol = nc.dram_tensor("dqcol", [CH, HEADS], F32, kind="ExternalInput").ap()
    tdkcol = nc.dram_tensor("dkcol", [CH, HEADS], F32, kind="ExternalInput").ap()
    tP4 = nc.dram_tensor("P4", [32, 512], F32, kind="ExternalInput").ap()
    tlam = nc.dram_tensor("lamtab", [128, 128], F32, kind="ExternalInput").ap()
    tSEL2 = nc.dram_tensor("SEL2", [32, 128], F32, kind="ExternalInput").ap()

    ag_w_in = nc.dram_tensor("ag_w_in", [SL_ELEMS], BF16)
    GU_SL = 16 * PG_GU
    D_SL = 16 * PG_D
    ag_gu_out = nc.dram_tensor("ag_gu_out", [NCORES * GU_SL], BF16)
    ag_d_out = nc.dram_tensor("ag_d_out", [NCORES * D_SL], BF16)
    ar_s_in = nc.dram_tensor("ar_s_in", [128, 4], F32)
    ar_s_out = nc.dram_tensor("ar_s_out", [128, 4], F32)
    ag_st_in = nc.dram_tensor("ag_st_in", [HEADS * HD, HD], F32)
    ag_st_out = nc.dram_tensor("ag_st_out", [NCORES * HEADS * HD, HD], F32)
    eprime = nc.dram_tensor("eprime", [N_HU, 128, 64, HU], F32)
    kcache = nc.dram_tensor("kcache", [T, D], BF16)
    kdcache = nc.dram_tensor("kdcache", [T, D], BF16)
    vcache = nc.dram_tensor("vcache", [T, D], BF16)
    qcache = nc.dram_tensor("qcache", [T, D], BF16)
    gcache = nc.dram_tensor("gcache", [T, D], F32)
    oncache = nc.dram_tensor("oncache", [T, D], BF16)
    replica = [list(range(NCORES))]

    with contextlib.ExitStack() as _stack:
        _e = _stack.enter_context
        xres = _e(nc.sbuf_tensor("xres", [128, NT * D], F32))
        h128 = _e(nc.sbuf_tensor("h128", [128, 128], F32))
        h64b = _e(nc.sbuf_tensor("h64b", [128, 128], F32))
        i128 = _e(nc.sbuf_tensor("i128", [128, 128], BF16))
        i128f = _e(nc.sbuf_tensor("i128f", [128, 128], F32))
        ones = _e(nc.sbuf_tensor("ones", [128, 128], F32))
        sel2 = _e(nc.sbuf_tensor("sel2", [32, 128], F32))
        srow = _e(nc.sbuf_tensor("srow", [32, 64], F32))
        p4 = _e(nc.sbuf_tensor("p4", [32, 512], F32))
        colp = _e(nc.sbuf_tensor("colp", [128, 64], F32))
        adeq = _e(nc.sbuf_tensor("adeq", [128, NT], F32))
        dqc = _e(nc.sbuf_tensor("dqc", [128, HEADS], F32))
        dkc = _e(nc.sbuf_tensor("dkc", [128, HEADS], F32))
        nwrep = _e(nc.sbuf_tensor("nwrep", [128, D], F32))
        lamt = _e(nc.sbuf_tensor("lamt", [128, 128], F32))
        bfBIG = _e(nc.sbuf_tensor("bfBIG", [128, 16384], BF16))
        bfSM = _e(nc.sbuf_tensor("bfSM", [128, 4096], BF16))
        f32P1 = _e(nc.sbuf_tensor("f32P1", [128, 8192], F32))
        scr = _e(nc.sbuf_tensor("scr", [128, 4096], F32))
        scr2 = _e(nc.sbuf_tensor("scr2", [128, 2048], F32))
        scrC = _e(nc.sbuf_tensor("scrC", [128, 2048], F32))
        bfW = _e(nc.sbuf_tensor("bfW", [128, 8192], BF16))
        iscr = _e(nc.sbuf_tensor("iscr", [128, 2048], I32))
        _s1 = _e(nc.semaphore("s_ld"))
        _s2 = _e(nc.semaphore("s_w"))
        _s3 = _e(nc.semaphore("s_mm"))
        _s4 = _e(nc.semaphore("s_ev"))
        _s5 = _e(nc.semaphore("s_sc"))
        _s6 = _e(nc.semaphore("s_gp"))
        _s7 = _e(nc.semaphore("s_st"))
        _s8 = _e(nc.semaphore("s_cc"))
        _s9 = _e(nc.semaphore("s_fb"))
        _s10 = _e(nc.semaphore("s_gst"))
        _s11 = _e(nc.semaphore("s_vs"))
        _s12 = _e(nc.semaphore("s_ss"))
        pb0 = _e(nc.psum_tensor("pb0", [128, 512], F32))
        pb1 = _e(nc.psum_tensor("pb1", [128, 512], F32))
        pb2 = _e(nc.psum_tensor("pb2", [128, 512], F32))
        pb3 = _e(nc.psum_tensor("pb3", [128, 512], F32))
        pb4 = _e(nc.psum_tensor("pb4", [128, 512], F32))
        pb5 = _e(nc.psum_tensor("pb5", [128, 512], F32))
        pb6 = _e(nc.psum_tensor("pb6", [128, 512], F32))
        pb7 = _e(nc.psum_tensor("pb7", [128, 512], F32))
        block = _e(nc.Block())
        sld, sw, smm, sev, ssc, sgp, sst, scc, sfb = (
            Sem(_s1), Sem(_s2), Sem(_s3), Sem(_s4), Sem(_s5), Sem(_s6), Sem(_s7), Sem(_s8), Sem(_s9))
        sgst = Sem(_s10)
        svs, sss = Sem(_s11), Sem(_s12)

        def VS(v, instr):
            svs.inc(instr)
            v.wait_ge(svs.h, svs.n)
            return instr

        def SS(s, instr):
            sss.inc(instr)
            s.wait_ge(sss.h, sss.n)
            return instr
        banks = [pb0, pb1, pb2, pb3, pb4, pb5, pb6, pb7]

        @block.sync
        def _(sync):
            for t in range(NT):
                sld.incd(sync.dma_start(out=xres[:, t * D:(t + 1) * D],
                                        in_=x_in.rearrange("(a p) d -> a p d", p=128)[t, :, :]))
            sld.incd(sync.dma_start(out=h128[:], in_=tH128[:, :]))
            sld.incd(sync.dma_start(out=h64b[:], in_=tH64B[:, :]))
            sld.incd(sync.dma_start(out=i128[:], in_=tI128[:, :]))
            sld.incd(sync.dma_start(out=p4[:, :], in_=tP4[:, :]))
            sld.incd(sync.dma_start(out=dqc[:, :], in_=tdqcol[:, :]))
            sld.incd(sync.dma_start(out=dkc[:, :], in_=tdkcol[:, :]))
            sld.incd(sync.dma_start(out=sel2[:, :], in_=tSEL2[:, :]))
            sld.incd(sync.dma_start(out=lamt[:, :], in_=tlam[:, :]))
        P0_LD = sld.n

        @block.vector
        def _(v):
            v.wait_ge(sld.h, P0_LD)
            v.tensor_copy(i128f[:], i128[:])
            v.memset(srow[:, :], 0.0)
            sev.inc(v.memset(ones[:], 1.0))
        P0_EV = sev.n

        tiles = [(0, wg_sl, i, 2048) for i in range(8)] + \
                [(1, wu_sl, i, 2048) for i in range(8)] + \
                [(2, wd_sl, i, 1024) for i in range(16)]
        B1_ev = sev.n

        @block.sync
        def _(sync):
            sync.wait_ge(sld.h, P0_LD)
            for idx, (m, ap_, i, w_) in enumerate(tiles):
                if idx >= 2:
                    sync.wait_ge(sev.h, B1_ev + idx - 1)
                if idx >= 1:
                    sync.wait_ge(sld.h, P0_LD + 16 * idx)
                buf = scr if idx % 2 == 0 else scr2
                src = ap_.rearrange("(a p) d -> a p d", p=128)[i, :, :]
                sld.incd(sync.dma_start(out=buf[:, 0:w_], in_=src))

        @block.vector
        def _(v):
            v.wait_ge(sev.h, P0_EV)
            v.memset(colp[:, 0:4], 0.0)
            for idx, (m, ap_, i, w_) in enumerate(tiles):
                v.wait_ge(sld.h, P0_LD + 16 * (idx + 1))
                buf = scr if idx % 2 == 0 else scr2
                VS(v, v.tensor_reduce(colp[:, 8:9], buf[:, 0:w_], AX, ALU.add, apply_absolute_value=True))
                sev.inc(v.tensor_tensor(colp[:, m:m + 1], colp[:, m:m + 1], colp[:, 8:9], ALU.add))
                v.wait_ge(sev.h, sev.n)
        P1A_EV = sev.n

        @block.tensor
        def _(tn):
            tn.wait_ge(sev.h, P1A_EV)
            smm.inc(tn.matmul(pb0[0:128, 0:4], ones[:, 0:128], colp[:, 0:4], start=True, stop=True))
        P1A_MM = smm.n

        @block.vector
        def _(v):
            v.wait_ge(smm.h, P1A_MM)
            sev.inc(v.tensor_copy(colp[:, 16:20], pb0[:, 0:4]))
        P1B_EV = sev.n

        @block.sync
        def _(sync):
            sync.wait_ge(sev.h, P1B_EV)
            sst.incd(sync.dma_start(out=ar_s_in[:, :], in_=colp[:, 16:20]))
        P1_ST1 = sst.n

        @block.gpsimd
        def _(g):
            g.wait_ge(sst.h, P1_ST1)
            scc.inc(g.collective_compute("AllReduce", ALU.add, replica_groups=replica,
                                         ins=[ar_s_in.ap().opt()], outs=[ar_s_out.ap().opt()]))
        P1_CC1 = scc.n

        @block.sync
        def _(sync):
            sync.wait_ge(scc.h, P1_CC1)
            sync.wait_ge(sld.h, sld.n)
            sld.incd(sync.dma_start(out=colp[:, 20:24], in_=ar_s_out[:, :]))
        P1_LD2 = sld.n

        @block.vector
        def _(v):
            v.wait_ge(sld.h, P1_LD2)
            VS(v, v.tensor_scalar(colp[:, 24:28], colp[:, 20:24], 1.0 / NW_ELEMS, 1e-5, ALU.mult, ALU.max))
            VS(v, v.reciprocal(colp[:, 40:44], colp[:, 24:28]))
            sev.inc(v.reciprocal(colp[:, 44:48], colp[:, 40:44]))
            v.wait_ge(sev.h, sev.n)
        P1D_EV = sev.n

        PAGE_OFF = [0, 8 * PG_GU, 16 * PG_GU]
        B2_ld, B2_ev, B2_mm, B2_sc, B2_st = sw.n, sev.n, smm.n, ssc.n, sst.n
        nbat = [w_ // 512 for (_, _, _, w_) in tiles]
        cumb = np.cumsum([0] + nbat).tolist()

        @block.sync
        def _(sync):
            for idx, (m, ap_, i, w_) in enumerate(tiles):
                if idx >= 2:
                    sync.wait_ge(sev.h, B2_ev + idx - 1)
                if idx >= 1:
                    sync.wait_ge(sw.h, B2_ld + 16 * idx)
                buf = scr if idx % 2 == 0 else scr2
                src = ap_.rearrange("(a p) d -> a p d", p=128)[i, :, :]
                sw.incd(sync.dma_start(out=buf[:, 0:w_], in_=src))

        @block.vector
        def _(v):
            v.wait_ge(sev.h, P1D_EV)
            for idx, (m, ap_, i, w_) in enumerate(tiles):
                v.wait_ge(sw.h, B2_ld + 16 * (idx + 1))
                if idx >= 2:
                    v.wait_ge(smm.h, B2_mm + cumb[idx - 1])
                buf = scr if idx % 2 == 0 else scr2
                tb = bfSM[:, 0:2048] if idx % 2 == 0 else bfSM[:, 2048:4096]
                sc_ = colp[:, 40 + m:41 + m]
                v.tensor_scalar(scrC[:, 0:w_], buf[:, 0:w_], sc_, 0.5, ALU.mult, ALU.is_ge)
                v.tensor_scalar(buf[:, 0:w_], buf[:, 0:w_], sc_, -0.5, ALU.mult, ALU.is_le)
                sev.inc(v.tensor_tensor(tb[:, 0:w_], scrC[:, 0:w_], buf[:, 0:w_], ALU.subtract))

        @block.tensor
        def _(tn):
            gb = 0
            for idx, (m, ap_, i, w_) in enumerate(tiles):
                tn.wait_ge(sev.h, B2_ev + idx + 1)
                tb = bfSM[:, 0:2048] if idx % 2 == 0 else bfSM[:, 2048:4096]
                for h in range(w_ // 128):
                    bank = banks[4 + (gb % 2)]
                    if h % 4 == 0 and gb >= 2:
                        tn.wait_ge(ssc.h, B2_sc + gb - 1)
                    mm = tn.matmul(bank[:, (h % 4) * 128:(h % 4) * 128 + 128],
                                   tb[:, h * 128:(h + 1) * 128], i128[:, :],
                                   start=True, stop=True)
                    if h % 4 == 3:
                        smm.inc(mm)
                        gb += 1

        @block.scalar
        def _(s):
            gb = 0
            for idx, (m, ap_, i, w_) in enumerate(tiles):
                slot = bfW[:, 0:4096] if idx % 2 == 0 else bfW[:, 4096:8192]
                if idx >= 2:
                    s.wait_ge(sgst.h, 16 * (idx - 1))
                for b in range(w_ // 512):
                    s.wait_ge(smm.h, B2_mm + gb + 1)
                    bank = banks[4 + (gb % 2)]
                    ssc.inc(s.activation(slot[:, b * 512:(b + 1) * 512], bank[:, :], AF.Identity))
                    gb += 1

        @block.gpsimd
        def _(g):
            for idx, (m, ap_, i, w_) in enumerate(tiles):
                g.wait_ge(ssc.h, B2_sc + cumb[idx + 1])
                if idx >= 1:
                    g.wait_ge(sgst.h, 16 * idx)
                slot = bfW[:, 0:4096] if idx % 2 == 0 else bfW[:, 4096:8192]
                off = PAGE_OFF[m] + i * (PG_GU if m < 2 else PG_D)
                dst = ag_w_in.ap()[off:off + 128 * w_].rearrange("(p c) -> p c", c=w_)
                sgst.incd(g.dma_start(out=dst, in_=slot[:, 0:w_]))
                if idx == 15:
                    g.wait_ge(sgst.h, 16 * 16)
                    scc.inc(g.collective_compute(
                        "AllGather", ALU.bypass, replica_groups=replica,
                        ins=[ag_w_in.ap()[0:GU_SL].opt()], outs=[ag_gu_out.ap().opt()]))
            g.wait_ge(sgst.h, sgst.n)
            scc.inc(g.collective_compute(
                "AllGather", ALU.bypass, replica_groups=replica,
                ins=[ag_w_in.ap()[GU_SL:SL_ELEMS].opt()], outs=[ag_d_out.ap().opt()]))
        P1E_ST = sst.n
        P1_CC_GU = scc.n - 1
        P1_CC2 = scc.n

        def gu_page(m, i1):
            r, iloc = i1 >> 3, i1 & 7
            return r * GU_SL + PAGE_OFF[m] + iloc * PG_GU

        def d_page(o, r):
            return r * D_SL + o * PG_D

        xq = bfBIG.rearrange("p (hh c) -> p hh c", c=T)

        def ffn_call(cid, wait_ev):
            CALL_mm0 = smm.n
            L_ev, L_sc, L_mm = sev.n, ssc.n, smm.n

            @block.scalar
            def _(s):
                s.wait_ge(sev.h, wait_ev)
                for t in range(NT):
                    xt = xres[:, t * D:(t + 1) * D]
                    ssc.inc(s.activation(scrC[:, 0:2048], xt, AF.Square))
                    s.wait_ge(sev.h, L_ev + 3 * t + 1)
                    ssc.inc(s.activation(colp[:, 8:9], colp[:, 6:7], AF.Sqrt))
                    s.wait_ge(ssc.h, ssc.n)
                    s.wait_ge(sev.h, L_ev + 3 * t + 2)
                    s.activation(scrC[:, 0:2048], xt, AF.Identity, bias=colp[:, 7:8])
                    ssc.inc(s.activation(scr[:, 0:2048], scrC[:, 0:2048], AF.Identity, scale=colp[:, 15:16]))
                    for b in range(4):
                        gb = 4 * t + b
                        s.wait_ge(smm.h, L_mm + gb + 1)
                        bank = banks[4 + (gb % 2)]
                        dst = xq[:, 4 * b:4 * b + 4, t * 128:t * 128 + 128]
                        ssc.inc(s.activation(dst, bank.rearrange("p (h c) -> p h c", c=128), AF.Identity))

            @block.vector
            def _(v):
                for t in range(NT):
                    xt = xres[:, t * D:(t + 1) * D]
                    v.wait_ge(ssc.h, L_sc + 7 * t + 1)
                    VS(v, v.tensor_reduce(colp[:, 0:1], xt, AX, ALU.add))
                    VS(v, v.tensor_reduce(colp[:, 1:2], scrC[:, 0:2048], AX, ALU.add))
                    VS(v, v.tensor_scalar_mul(colp[:, 2:3], colp[:, 0:1], 1.0 / D))
                    VS(v, v.tensor_scalar_mul(colp[:, 3:4], colp[:, 1:2], 1.0 / D))
                    VS(v, v.tensor_tensor(colp[:, 4:5], colp[:, 2:3], colp[:, 2:3], ALU.mult))
                    VS(v, v.scalar_tensor_tensor(colp[:, 5:6], colp[:, 3:4], 1e-5, colp[:, 4:5],
                                                 ALU.add, ALU.subtract))
                    VS(v, v.reciprocal(colp[:, 6:7], colp[:, 5:6]))
                    sev.inc(v.tensor_scalar_mul(colp[:, 7:8], colp[:, 2:3], -1.0))
                    v.wait_ge(sev.h, sev.n)
                    v.wait_ge(ssc.h, L_sc + 7 * t + 2)
                    VS(v, v.tensor_tensor(colp[:, 14:15], colp[:, 8:9], colp[:, 8:9], ALU.mult))
                    VS(v, v.tensor_tensor(colp[:, 14:15], colp[:, 14:15], colp[:, 5:6], ALU.mult))
                    VS(v, v.tensor_scalar(colp[:, 14:15], colp[:, 14:15], -0.5, 1.5, ALU.mult, ALU.add))
                    sev.inc(v.tensor_tensor(colp[:, 15:16], colp[:, 8:9], colp[:, 14:15], ALU.mult))
                    v.wait_ge(sev.h, sev.n)
                    v.wait_ge(ssc.h, L_sc + 7 * t + 3)
                    if t >= 2:
                        v.wait_ge(smm.h, L_mm + 4 * (t - 1))
                    VS(v, v.tensor_reduce(colp[:, 10:11], scr[:, 0:2048], AX, ALU.max, apply_absolute_value=True))
                    VS(v, v.tensor_scalar_max(colp[:, 11:12], colp[:, 10:11], 1e-5))
                    VS(v, v.reciprocal(colp[:, 12:13], colp[:, 11:12]))
                    VS(v, v.tensor_scalar_mul(colp[:, 13:14], colp[:, 12:13], 128.0))
                    VS(v, v.reciprocal(adeq[:, t:t + 1], colp[:, 13:14]))
                    sl = bfSM[:, (t % 2) * 2048:(t % 2) * 2048 + 2048]
                    v.tensor_scalar_mul(scr2[:, 0:2048], scr[:, 0:2048], colp[:, 13:14])
                    v.tensor_scalar_add(iscr[:, 0:2048], scr2[:, 0:2048], 256.0)
                    v.scalar_tensor_tensor(scr[:, 0:2048], scr2[:, 0:2048], 256.0, iscr[:, 0:2048],
                                           ALU.add, ALU.subtract)
                    v.scalar_tensor_tensor(scr2[:, 0:2048], scr[:, 0:2048], 0.5, iscr[:, 0:2048],
                                           ALU.is_ge, ALU.add)
                    sev.inc(v.tensor_scalar(sl, scr2[:, 0:2048], -256.0, 127.0, ALU.add, ALU.min))

            @block.tensor
            def _(tn):
                for t in range(NT):
                    tn.wait_ge(sev.h, L_ev + 3 * t + 3)
                    sl = bfSM[:, (t % 2) * 2048:(t % 2) * 2048 + 2048]
                    for h in range(16):
                        gb = 4 * t + h // 4
                        bank = banks[4 + (gb % 2)]
                        if h % 4 == 0 and gb >= 2:
                            pgb = gb - 2
                            tn.wait_ge(ssc.h, L_sc + 7 * (pgb // 4) + 3 + (pgb % 4) + 1)
                        mm = tn.matmul(bank[:, (h % 4) * 128:(h % 4) * 128 + 128],
                                       sl[:, h * 128:(h + 1) * 128], i128[:, :],
                                       start=True, stop=True)
                        if h % 4 == 3:
                            smm.inc(mm)
            L_done_ev = sev.n

            A_w, A_mm, A_ev, A_sc, A_gp, A_st = sw.n, smm.n, sev.n, ssc.n, sgp.n, sst.n

            def f_mm_deq(g, ts_):
                return A_mm + 68 * g + ts_ + 1

            def f_mm_gi(gi):
                return A_mm + 68 * (gi // 64) + 4 + (gi % 64) + 1

            def f_sc_b(g, ts_):
                return A_sc + 68 * g + ts_ + 1

            def f_sc_silu(gi):
                return A_sc + 68 * (gi // 64) + 4 + (gi % 64) + 1

            def f_ev_deq(g, ts_):
                return A_ev + 132 * g + ts_ + 1

            def f_ev_t0(gi):
                return A_ev + 132 * (gi // 64) + 4 + 2 * (gi % 64) + 1

            def f_ev_u1(gi):
                return A_ev + 132 * (gi // 64) + 4 + 2 * (gi % 64) + 2

            def f_gp(gi):
                return A_gp + gi + 1

            @block.sync
            def _(sync):
                sync.wait_ge(scc.h, P1_CC_GU)
                sync.wait_ge(smm.h, CALL_mm0)
                for gi in range(128):
                    i1 = gi % 64
                    if gi >= 2:
                        sync.wait_ge(smm.h, f_mm_gi(gi - 2))
                    if gi >= 1:
                        sync.wait_ge(sw.h, A_w + 32 * gi)
                    wslot = bfW[:, (gi % 2) * 4096:(gi % 2) * 4096 + 4096]
                    go, uo = gu_page(0, i1), gu_page(1, i1)
                    sw.incd(sync.dma_start(out=wslot[:, 0:2048],
                                           in_=ag_gu_out.ap()[go:go + PG_GU].rearrange("(p c) -> p c", c=2048)))
                    sw.incd(sync.dma_start(out=wslot[:, 2048:4096],
                                           in_=ag_gu_out.ap()[uo:uo + PG_GU].rearrange("(p c) -> p c", c=2048)))

            @block.scalar
            def _(s):
                for g in range(2):
                    for ts_ in range(4):
                        tt = 4 * g + ts_
                        if g == 0 and ts_ == 0:
                            s.wait_ge(sev.h, L_done_ev)
                        if g == 1 and ts_ == 0:
                            s.wait_ge(smm.h, f_mm_deq(0, 3))
                        ssc.inc(s.activation(scrC[:, 0:128], ones[:, 0:128], AF.Identity,
                                             scale=adeq[:, tt:tt + 1]))
                        if ts_ < 3:
                            s.wait_ge(smm.h, f_mm_deq(g, ts_))
                    for i1 in range(64):
                        gi = g * 64 + i1
                        s.wait_ge(sev.h, f_ev_t0(gi))
                        if gi >= 2:
                            s.wait_ge(sgp.h, f_gp(gi - 2))
                        t0slot = scrC[:, 512 + (gi % 2) * 768:512 + (gi % 2) * 768 + 512]
                        oslot = scr[:, (gi % 2) * 2048:(gi % 2) * 2048 + 512]
                        ssc.inc(s.activation(oslot, t0slot, AF.Silu))
                        if gi % 8 == 7:
                            b = gi // 8
                            s.wait_ge(sgp.h, f_gp(gi))
                            if b >= 1:
                                s.wait_ge(sst.h, A_st + 128 * b)
                            for uloc in range(8):
                                u = g * 8 + uloc
                                fsrc = f32P1[:, 0:4096].rearrange(
                                    "p (sl c) -> p sl c", c=512)[:, :, uloc * 64:(uloc + 1) * 64]
                                sst.incd(s.dma_start(out=eprime[u, :, (b % 8) * 8:(b % 8) * 8 + 8, :],
                                                     in_=fsrc))

            @block.gpsimd
            def _(g_):
                for gi in range(128):
                    g_.wait_ge(ssc.h, f_sc_silu(gi))
                    g_.wait_ge(sev.h, f_ev_u1(gi))
                    if gi >= 8:
                        g_.wait_ge(sst.h, A_st + 128 * (gi // 8))
                    oslot = scr[:, (gi % 2) * 2048:(gi % 2) * 2048 + 512]
                    u1slot = scr[:, (gi % 2) * 2048 + 512:(gi % 2) * 2048 + 1024]
                    slot = f32P1[:, (gi % 8) * 512:(gi % 8) * 512 + 512]
                    sgp.inc(g_.tensor_tensor(slot, oslot, u1slot, ALU.mult))

            @block.vector
            def _(v):
                for g in range(2):
                    for ts_ in range(4):
                        v.wait_ge(smm.h, f_mm_deq(g, ts_))
                        sev.inc(v.tensor_copy(scr2[:, ts_ * 128:(ts_ + 1) * 128],
                                              pb6[:, ts_ * 128:(ts_ + 1) * 128]))
                        v.wait_ge(sev.h, sev.n)
                    for i1 in range(64):
                        gi = g * 64 + i1
                        v.wait_ge(smm.h, f_mm_gi(gi))
                        gB = banks[gi % 2]
                        uB = banks[2 + (gi % 2)]
                        t0slot = scrC[:, 512 + (gi % 2) * 768:512 + (gi % 2) * 768 + 512]
                        oslot = scr[:, (gi % 2) * 2048:(gi % 2) * 2048 + 512]
                        u1slot = scr[:, (gi % 2) * 2048 + 512:(gi % 2) * 2048 + 1024]
                        if gi >= 2:
                            v.wait_ge(ssc.h, f_sc_silu(gi - 2))
                        sev.inc(v.scalar_tensor_tensor(t0slot, gB[:, :], colp[:, 44:45],
                                                       scr2[:, 0:512], ALU.mult, ALU.mult))
                        if gi >= 2:
                            v.wait_ge(sgp.h, f_gp(gi - 2))
                        sev.inc(v.scalar_tensor_tensor(u1slot, uB[:, :], colp[:, 45:46],
                                                       scr2[:, 0:512], ALU.mult, ALU.mult))

            @block.tensor
            def _(tn):
                tn.wait_ge(scc.h, P1_CC_GU)
                for g in range(2):
                    for ts_ in range(4):
                        tn.wait_ge(ssc.h, f_sc_b(g, ts_))
                        if g == 1:
                            tn.wait_ge(sev.h, f_ev_deq(0, ts_))
                        smm.inc(tn.matmul(pb6[:, ts_ * 128:(ts_ + 1) * 128], scrC[:, 0:128],
                                          i128f[:, :], start=True, stop=True))
                    for i1 in range(64):
                        gi = g * 64 + i1
                        tn.wait_ge(sw.h, A_w + 32 * (gi + 1))
                        if gi >= 2:
                            tn.wait_ge(sev.h, f_ev_u1(gi - 2))
                        wslot = bfW[:, (gi % 2) * 4096:(gi % 2) * 4096 + 4096]
                        gB = banks[gi % 2]
                        uB = banks[2 + (gi % 2)]
                        for h in range(16):
                            tn.matmul(gB[:, :], wslot[:, h * 128:(h + 1) * 128],
                                      xq[:, h, g * 512:(g + 1) * 512],
                                      start=(h == 0), stop=(h == 15))
                        for h in range(16):
                            mm = tn.matmul(uB[:, :], wslot[:, 2048 + h * 128:2048 + (h + 1) * 128],
                                           xq[:, h, g * 512:(g + 1) * 512],
                                           start=(h == 0), stop=(h == 15))
                        smm.inc(mm)


            A_st_end = A_st + 128 * 16

            B_fb, B_sc, B_ev, B_mm, B_gp, B_w2 = sfb.n, ssc.n, sev.n, smm.n, sgp.n, sw.n
            M, _, _, _, _ = _sched_B(B_sc, B_ev, B_mm, B_gp)
            eq = f32P1[:, 0:4096]
            ewin = f32P1[:, 4096:8192]
            zr = scrC

            @block.sync
            def _(sync):
                sync.wait_ge(sst.h, A_st_end)
                for u in range(N_HU):
                    if u >= 1:
                        sync.wait_ge(sev.h, M["vdone", u - 1])
                        sync.wait_ge(sfb.h, B_fb + 16 * u)
                    src = eprime[u].rearrange("p a b -> p (a b)")
                    sfb.incd(sync.dma_start(out=eq, in_=src))


            @block.scalar
            def _(s):
                for u in range(N_HU):
                    u4 = u % 4
                    s.wait_ge(sfb.h, B_fb + 16 * (u + 1))
                    if u >= 1:
                        s.wait_ge(smm.h, M["r1", u - 1, 7])
                        s.wait_ge(sev.h, M["q", u - 1, 1])
                    ssc.inc(s.copy(
                        ewin.rearrange("p (t i) -> p t i", i=64),
                        eq.rearrange("p (i t) -> p i t", t=64).transpose([0, 2, 1])))
                    for b in range(8):
                        s.wait_ge(smm.h, M["r2", u, b])
                        yB = banks[2 + (b % 2)]
                        ssc.inc(s.activation(eq[:, b * 512:(b + 1) * 512], yB[:, :], AF.Identity))
                    for b in range(8):
                        s.wait_ge(smm.h, M["r3", u, b])
                        if u // 4 >= 1 and b == 0:
                            s.wait_ge(smm.h, M["dn", u // 4 - 1, 15])
                        bank = banks[4 + (b % 2)]
                        dst = bfBIG[:, u4 * 4096 + b * 512: u4 * 4096 + (b + 1) * 512]
                        ssc.inc(s.activation(dst, bank[:, :], AF.Identity))
                    if u % 4 == 3:
                        g4 = u // 4
                        if g4 == 0:
                            s.wait_ge(smm.h, f_mm_gi(127))
                        for o in range(16):
                            if not (g4 == 0 and o == 0):
                                po = (g4, o - 1) if o >= 1 else (g4 - 1, 15)
                                s.wait_ge(smm.h, M["dn", po[0], po[1]])
                                s.wait_ge(sw.h, B_w2 + 128 * (16 * g4 + o))
                            if g4 == 0 and o == 0:
                                s.wait_ge(scc.h, P1_CC2)
                            for r in range(NCORES):
                                off = d_page(o, r)
                                sw.incd(s.dma_start(
                                    out=bfW[:, r * 1024:(r + 1) * 1024],
                                    in_=ag_d_out.ap()[off:off + PG_D].rearrange("(p c) -> p c", c=1024)))
                            if o >= 1:
                                s.wait_ge(smm.h, M["dn", g4, o - 1])
                                if o >= 3:
                                    s.wait_ge(smm.h, M["dntm", g4, o - 3, 1])
                                dB = banks[6 + ((o - 1) % 2)]
                                ssc.inc(s.activation(scr2[:, ((o - 1) % 2) * 256:((o - 1) % 2) * 256 + 256],
                                                     dB[:, 0:256], AF.Identity))
                        s.wait_ge(smm.h, M["dn", g4, 15])
                        s.wait_ge(smm.h, M["dntm", g4, 13, 1])
                        dB = banks[6 + (15 % 2)]
                        ssc.inc(s.activation(scr2[:, (15 % 2) * 256:(15 % 2) * 256 + 256],
                                             dB[:, 0:256], AF.Identity))

            @block.tensor
            def _(tn):
                for u in range(N_HU):
                    u4 = u % 4
                    tn.wait_ge(ssc.h, M["reord", u])
                    if u % 4 == 0 and u >= 4:
                        tn.wait_ge(sev.h, M["resid", u // 4 - 1, 15, 1])
                    def emit_r2(b):
                        tn.wait_ge(sev.h, M["r1ev", u, b])
                        if b >= 2:
                            tn.wait_ge(ssc.h, M["yev", u, b - 2])
                        yB = banks[2 + (b % 2)]
                        smm.inc(tn.matmul(yB[:, :], h64b[:, :], zr[:, (b % 4) * 512:(b % 4) * 512 + 512],
                                          start=True, stop=True))

                    for b in range(8):
                        if b >= 2:
                            tn.wait_ge(sev.h, M["r1ev", u, b - 2])
                        bank = banks[b % 2]
                        for wsub in range(4):
                            w = 4 * b + wsub
                            mm = tn.matmul(bank[0:128, wsub * 128:(wsub + 1) * 128],
                                           ewin[:, w * 128:(w + 1) * 128], h128[:, :],
                                           start=True, stop=True)
                        smm.inc(mm)
                        if b >= 2:
                            emit_r2(b - 2)
                    for b in [6, 7]:
                        emit_r2(b)
                    tn.wait_ge(sev.h, M["mx1", u])
                    smm.inc(tn.matmul(pb6[0:32, 0:128], colp[:, 0:32], i128f[:, :], start=True, stop=True))
                    tn.wait_ge(sev.h, M["chain", u])
                    smm.inc(tn.matmul(pb6[0:2, 128:160], scr2[0:32, 646:648], i128f[0:32, 0:32],
                                      start=True, stop=True))
                    tn.wait_ge(sev.h, M["sTev", u])
                    smm.inc(tn.matmul(pb6[0:128, 160:192], sel2[:, :], srow[:, 0:32],
                                      start=True, stop=True))
                    for b in range(8):
                        tn.wait_ge(sev.h, M["q", u, 1 if b >= 4 else 0])
                        if b >= 2:
                            tn.wait_ge(ssc.h, M["r3ev", u, b - 2])
                        bank = banks[4 + (b % 2)]
                        for ths in range(4):
                            th = 4 * b + ths
                            mm = tn.matmul(bank[0:128, ths * 128:(ths + 1) * 128],
                                           bfSM[:, th * 128:(th + 1) * 128], i128[:, :],
                                           start=True, stop=True)
                        smm.inc(mm)
                    if u % 4 == 3:
                        g4 = u // 4
                        tn.wait_ge(sev.h, M["chain", u])
                        tn.wait_ge(ssc.h, M["r3ev", u, 7])
                        for tb in range(2):
                            for k in range(4):
                                usub, tlo = k // 2, k % 2
                                uu = 4 * g4 + 2 * tb + usub
                                mm = tn.matmul(pb4[0:128, 500 + tb:501 + tb],
                                               p4[:, (2 * usub + tlo) * 128:(2 * usub + tlo) * 128 + 128],
                                               scr2[0:32, 700 + 2 * uu + tlo:701 + 2 * uu + tlo],
                                               start=(k == 0), stop=(k == 3))
                            smm.inc(mm)
                        for o in range(16):
                            tn.wait_ge(sw.h, B_w2 + 128 * (16 * g4 + o + 1))
                            dB = banks[6 + (o % 2)]
                            if o >= 2:
                                tn.wait_ge(ssc.h, M["dnev", g4, o - 2])
                            for i1p in range(64):
                                r, iloc = i1p >> 3, i1p & 7
                                lt = bfW[:, r * 1024 + iloc * 128:r * 1024 + iloc * 128 + 128]
                                rhs = bfBIG.rearrange("p (uu c) -> p uu c", c=4096).rearrange(
                                    "p uu (tt i) -> p uu tt i", i=64)[:, :, :, i1p]
                                mm = tn.matmul(dB[:, 0:256], lt, rhs,
                                               start=(i1p == 0), stop=(i1p == 63))
                            smm.inc(mm)
                            tn.wait_ge(ssc.h, M["dnev", g4, o])
                            if (g4, o) > (0, 1):
                                po = (g4, o - 2) if o >= 2 else (g4 - 1, 14 + o)
                                tn.wait_ge(sev.h, M["resid", po[0], po[1], 1])
                            for tb in range(2):
                                mm = tn.matmul(banks[2 * tb + (o % 2)][:, 0:128],
                                               scr2[:, (o % 2) * 256 + tb * 128:(o % 2) * 256 + tb * 128 + 128],
                                               i128f[:, :], start=True, stop=True)
                                smm.inc(mm)

            @block.vector
            def _(v):
                for u in range(N_HU):
                    for b in range(8):
                        v.wait_ge(smm.h, M["r1", u, b])
                        if b >= 4:
                            v.wait_ge(smm.h, M["r2", u, b - 4])
                        bank = banks[b % 2]
                        sev.inc(v.tensor_copy(zr[:, (b % 4) * 512:(b % 4) * 512 + 512], bank[:, :]))
                    v.wait_ge(ssc.h, M["yev", u, 7])
                    sev.inc(v.tensor_reduce(colp[:, 0:32].rearrange("p (a o) -> p a o", o=1),
                                            eq.rearrange("p (a c) -> p a c", c=128), AX,
                                            ALU.max, apply_absolute_value=True))
                    v.wait_ge(smm.h, M["mx1T", u])
                    sev.inc(v.tensor_copy(scr2[0:32, 512:640], pb6[0:32, 0:128]))
                    v.wait_ge(sev.h, sev.n)
                    VS(v, v.tensor_reduce(scr2[0:32, 640:642].rearrange("p (a o) -> p a o", o=1),
                                          scr2[0:32, 512:640].rearrange("p (a c) -> p a c", c=64), AX,
                                          ALU.max, apply_absolute_value=False))
                    VS(v, v.tensor_scalar_max(scr2[0:32, 642:644], scr2[0:32, 640:642], 1e-5))
                    VS(v, v.reciprocal(scr2[0:32, 644:646], scr2[0:32, 642:644]))
                    VS(v, v.tensor_scalar_mul(scr2[0:32, 646:648], scr2[0:32, 644:646], 128.0))
                    VS(v, v.reciprocal(scr2[0:32, 648:650], scr2[0:32, 646:648]))
                    u4 = u % 4
                    sev.inc(v.tensor_copy(scr2[0:32, 700 + 2 * u:702 + 2 * u], scr2[0:32, 648:650]))
                    v.wait_ge(sev.h, sev.n)
                    v.wait_ge(smm.h, M["sT", u])
                    sev.inc(v.tensor_copy(srow[0:2, 0:32], pb6[0:2, 128:160]))
                    v.wait_ge(sev.h, sev.n)
                    v.wait_ge(smm.h, M["ssm", u])
                    sev.inc(v.tensor_copy(scr2[:, 1024:1056], pb6[:, 160:192]))
                    v.wait_ge(sev.h, sev.n)
                    for th in range(32):
                        sl = ewin[:, th * 128:(th + 1) * 128]
                        mmv = v.tensor_scalar_mul(sl, eq[:, th * 128:(th + 1) * 128],
                                                  scr2[:, 1024 + th:1025 + th])
                    sev.inc(mmv)
                    sev.inc(v.tensor_scalar_add(scr[:, 0:2048], ewin[:, 0:2048], 256.0))
                    sev.inc(v.tensor_copy(iscr[:, 0:2048], scr[:, 0:2048]))
                    v.wait_ge(sgp.h, M["d", u, 0])
                    sev.inc(v.tensor_scalar_add(scr[:, 0:2048], ewin[:, 2048:4096], 256.0))
                    v.wait_ge(sgp.h, M["r", u, 0])
                    sev.inc(v.tensor_copy(iscr[:, 0:2048], scr[:, 0:2048]))
                    sev.inc(v.tensor_scalar(bfSM[:, 0:2048], ewin[:, 0:2048], -256.0, 127.0,
                                            ALU.add, ALU.min))
                    v.wait_ge(sgp.h, M["r", u, 1])
                    sev.inc(v.tensor_scalar(bfSM[:, 2048:4096], ewin[:, 2048:4096], -256.0, 127.0,
                                            ALU.add, ALU.min))
                    if u % 4 == 3:
                        g4 = u // 4
                        for tb in range(2):
                            v.wait_ge(smm.h, M["dqc", g4, tb])
                            sev.inc(v.tensor_copy(colp[:, 24 + tb:25 + tb], pb4[:, 500 + tb:501 + tb]))
                            v.wait_ge(sev.h, sev.n)
                        for o in range(16):
                            for tb in range(2):
                                v.wait_ge(smm.h, M["dntm", g4, o, tb])
                                tmp = scr2[:, 1088:1216]
                                v.tensor_scalar(tmp, banks[2 * tb + (o % 2)][:, 0:128],
                                                colp[:, 24 + tb:25 + tb],
                                                colp[:, 46:47], ALU.mult, ALU.mult)
                                xslice = xres[:, (2 * g4 + tb) * D + o * 128:(2 * g4 + tb) * D + o * 128 + 128]
                                sev.inc(v.scalar_tensor_tensor(xslice, tmp, 0.5, xslice,
                                                               ALU.mult, ALU.add))

            @block.gpsimd
            def _(g_):
                for u in range(N_HU):
                    for half in range(2):
                        g_.wait_ge(sev.h, M["c", u, half])
                        hs = slice(half * 2048, (half + 1) * 2048)
                        sgp.inc(g_.tensor_tensor(scr[:, 2048:4096], scr[:, 0:2048],
                                                 iscr[:, 0:2048], ALU.subtract))
                        sgp.inc(g_.tensor_single_scalar(ewin[:, hs], scr[:, 2048:4096], 0.5, ALU.is_ge))
                        sgp.inc(g_.tensor_tensor(ewin[:, hs], ewin[:, hs], iscr[:, 0:2048], ALU.add))

            M_end, B_sc_end, B_ev_end, B_mm_end, B_gp_end = _sched_B(B_sc, B_ev, B_mm, B_gp)
            assert B_sc_end == ssc.n and B_ev_end == sev.n and B_mm_end == smm.n and B_gp_end == sgp.n, (
                "stage B schedule mismatch",
                (B_sc_end, ssc.n), (B_ev_end, sev.n), (B_mm_end, smm.n), (B_gp_end, sgp.n))
            return M["resid", 3, 15, 1]

        def attn_call(wait_ev):
            CALL_mm0 = smm.n
            llv = _ll()
            lamC_f = [float(np.float32(np.exp(llv[h] * CH))) for h in range(HEADS)]
            sScl_f = [float(np.float32(np.exp(llv[h] * 64.0))) for h in range(HEADS)]
            grp = bfBIG.rearrange("p (a c) -> p a c", c=512)
            wsv = bfW.rearrange("p (hh c) -> p hh c", c=512)
            wfv = bfW.rearrange("p (hh c) -> p hh c", c=128)
            xlv = bfBIG.rearrange("p (hh c) -> p hh c", c=512)
            smv = bfSM.rearrange("p (a c) -> p a c", c=512)
            wsrc = {"k": wk_d, "v": wv_d, "q": wq_d}
            cach = {"k": kcache, "v": vcache, "q": qcache}
            B = {}

            def bar(tag):
                B[tag] = (smm.n, sev.n, ssc.n, sgp.n, sw.n, sld.n, sst.n, sfb.n)

            _ckn = [0]

            def ck():
                _ckn[0] += 1
                if ATTN_STAGES is not None and _ckn[0] >= ATTN_STAGES:
                    raise _StopEmit(sev.n)

            @block.sync
            def _(sync):
                sync.wait_ge(sld.h, sld.n)
                sld.incd(sync.dma_start(out=nwrep[:, :], in_=tdkrep[:, :]))
            A0_ld = sld.n

            for g in range(2):
                L_ev, L_sc, L_mm = sev.n, ssc.n, smm.n

                @block.scalar
                def _(s, g=g, L_ev=L_ev, L_mm=L_mm):
                    for t4 in range(4):
                        t = g * 4 + t4
                        xt = xres[:, t * D:(t + 1) * D]
                        if g == 0 and t4 == 0:
                            s.wait_ge(sev.h, wait_ev)
                        if g == 1 and t4 == 0:
                            s.wait_ge(smm.h, B["gate", 0][0])
                            s.wait_ge(sev.h, B[("state", 0)][1])
                        ssc.inc(s.activation(scrC[:, 0:2048], xt, AF.Square))
                        s.wait_ge(sev.h, L_ev + 2 * t4 + 1)
                        ssc.inc(s.activation(colp[:, 8:9], colp[:, 6:7], AF.Sqrt))
                        s.wait_ge(ssc.h, ssc.n)
                        s.wait_ge(sev.h, L_ev + 2 * t4 + 2)
                        s.activation(scrC[:, 0:2048], xt, AF.Identity, bias=colp[:, 7:8])
                        ssc.inc(s.activation(bfSM[:, 0:2048], scrC[:, 0:2048], AF.Identity,
                                             scale=colp[:, 15:16]))
                        for b in range(4):
                            gb = 4 * t4 + b
                            s.wait_ge(smm.h, L_mm + gb + 1)
                            bank = banks[4 + (gb % 2)]
                            dst = xlv[:, 4 * b:4 * b + 4, t4 * 128:t4 * 128 + 128]
                            ssc.inc(s.activation(dst, bank.rearrange("p (h c) -> p h c", c=128),
                                                 AF.Identity))

                @block.vector
                def _(v, g=g, L_sc=L_sc):
                    for t4 in range(4):
                        t = g * 4 + t4
                        xt = xres[:, t * D:(t + 1) * D]
                        v.wait_ge(ssc.h, L_sc + 7 * t4 + 1)
                        VS(v, v.tensor_reduce(colp[:, 0:1], xt, AX, ALU.add))
                        VS(v, v.tensor_reduce(colp[:, 1:2], scrC[:, 0:2048], AX, ALU.add))
                        VS(v, v.tensor_scalar_mul(colp[:, 2:3], colp[:, 0:1], 1.0 / D))
                        VS(v, v.tensor_scalar_mul(colp[:, 3:4], colp[:, 1:2], 1.0 / D))
                        VS(v, v.tensor_tensor(colp[:, 4:5], colp[:, 2:3], colp[:, 2:3], ALU.mult))
                        VS(v, v.scalar_tensor_tensor(colp[:, 5:6], colp[:, 3:4], 1e-5, colp[:, 4:5],
                                                     ALU.add, ALU.subtract))
                        VS(v, v.reciprocal(colp[:, 6:7], colp[:, 5:6]))
                        sev.inc(v.tensor_scalar_mul(colp[:, 7:8], colp[:, 2:3], -1.0))
                        v.wait_ge(sev.h, sev.n)
                        v.wait_ge(ssc.h, L_sc + 7 * t4 + 2)
                        VS(v, v.tensor_tensor(colp[:, 14:15], colp[:, 8:9], colp[:, 8:9], ALU.mult))
                        VS(v, v.tensor_tensor(colp[:, 14:15], colp[:, 14:15], colp[:, 5:6], ALU.mult))
                        VS(v, v.tensor_scalar(colp[:, 14:15], colp[:, 14:15], -0.5, 1.5, ALU.mult, ALU.add))
                        sev.inc(v.tensor_tensor(colp[:, 15:16], colp[:, 8:9], colp[:, 14:15], ALU.mult))
                        v.wait_ge(sev.h, sev.n)

                @block.tensor
                def _(tn, g=g, L_sc=L_sc):
                    for t4 in range(4):
                        tn.wait_ge(ssc.h, L_sc + 7 * t4 + 3)
                        for h in range(16):
                            gb = 4 * t4 + h // 4
                            bank = banks[4 + (gb % 2)]
                            if h % 4 == 0 and gb >= 2:
                                pgb = gb - 2
                                tn.wait_ge(ssc.h, L_sc + 7 * (pgb // 4) + 3 + (pgb % 4) + 1)
                            mm = tn.matmul(bank[:, (h % 4) * 128:(h % 4) * 128 + 128],
                                           bfSM[:, h * 128:(h + 1) * 128], i128[:, :],
                                           start=True, stop=True)
                            if h % 4 == 3:
                                smm.inc(mm)
                bar(("ln", g))
                ck()

                def proj(pj, act, extra_st_wait=None, g=g):
                    P_g, P_mm, P_sc = sgst.n, smm.n, ssc.n

                    @block.gpsimd
                    def _(g_, pj=pj, P_mm=P_mm):
                        g_.wait_ge(smm.h, P_mm)
                        for ob in range(4):
                            if ob >= 1:
                                g_.wait_ge(smm.h, P_mm + 4 * ob)
                            src = wsrc[pj].rearrange("(hh p) o -> p hh o", p=128)[:, :, ob * 512:(ob + 1) * 512]
                            sgst.incd(g_.dma_start(out=wsv[:, 0:16, :], in_=src))

                    @block.tensor
                    def _(tn, P_g=P_g, P_sc=P_sc, extra_st_wait=extra_st_wait):
                        tn.wait_ge(ssc.h, P_sc)
                        if extra_st_wait is not None:
                            tn.wait_ge(sst.h, extra_st_wait)
                        for ob in range(4):
                            tn.wait_ge(sgst.h, P_g + 16 * (ob + 1))
                            for t4 in range(4):
                                idx = 4 * ob + t4
                                bank = banks[4 + (idx % 2)]
                                if idx >= 2:
                                    tn.wait_ge(ssc.h, P_sc + idx - 1)
                                for h in range(16):
                                    mm = tn.matmul(bank[:, :], xlv[:, h, t4 * 128:t4 * 128 + 128],
                                                   wsv[:, h, :], start=(h == 0), stop=(h == 15))
                                smm.inc(mm)

                    @block.scalar
                    def _(s, P_mm=P_mm, act=act):
                        for ob in range(4):
                            for t4 in range(4):
                                idx = 4 * ob + t4
                                s.wait_ge(smm.h, P_mm + idx + 1)
                                bank = banks[4 + (idx % 2)]
                                ssc.inc(s.activation(grp[:, 16 + t4 * 4 + ob, :], bank[:, :], act))
                    bar(("proj", g, pj))

                def rot(pj, write_kd, g=g):
                    R_ld, R_ev, R_st = sld.n, sev.n, sst.n

                    def stores(sync, t4, pj, write_kd, g):
                        t = g * 4 + t4
                        kt = bfBIG[:, 8192 + t4 * 2048:8192 + (t4 + 1) * 2048]
                        sst.incd(sync.dma_start(
                            out=cach[pj].rearrange("(a p) d -> a p d", p=128)[t], in_=kt))
                        if write_kd:
                            kd = bfSM[:, (t4 % 2) * 2048:(t4 % 2) * 2048 + 2048]
                            sst.incd(sync.dma_start(
                                out=kdcache.rearrange("(a p) d -> a p d", p=128)[t], in_=kd))

                    @block.sync
                    def _(sync, pj=pj, R_ev=R_ev, g=g, write_kd=write_kd):
                        sync.wait_ge(ssc.h, B[("proj", g, pj)][2])
                        for t4 in range(4):
                            t = g * 4 + t4
                            if t4 >= 1:
                                sync.wait_ge(sev.h, R_ev + 4 * t4)
                                stores(sync, t4 - 1, pj, write_kd, g)
                            sld.incd(sync.dma_start(out=scr[:, 0:1024],
                                                    in_=tcos.rearrange("(a p) j -> a p j", p=128)[t]))
                            sld.incd(sync.dma_start(out=scr[:, 1024:2048],
                                                    in_=tsin.rearrange("(a p) j -> a p j", p=128)[t]))
                        sync.wait_ge(sev.h, R_ev + 16)
                        stores(sync, 3, pj, write_kd, g)

                    @block.vector
                    def _(v, R_ld=R_ld, R_st=R_st, g=g, write_kd=write_kd):
                        v.wait_ge(smm.h, B[("ln", g)][0])
                        for t4 in range(4):
                            v.wait_ge(sld.h, R_ld + 32 * (t4 + 1))
                            kt = bfBIG[:, 8192 + t4 * 2048:8192 + (t4 + 1) * 2048]
                            ktv = kt.rearrange("p (h two j) -> p h two j", two=2, j=64)
                            x1, x2 = ktv[:, :, 0, :], ktv[:, :, 1, :]
                            cosr = scr[:, 0:1024].rearrange("p (h j) -> p h j", j=64)
                            sinr = scr[:, 1024:2048].rearrange("p (h j) -> p h j", j=64)
                            ra = scr2[:, 0:1024].rearrange("p (h j) -> p h j", j=64)
                            rb = scr2[:, 1024:2048].rearrange("p (h j) -> p h j", j=64)
                            rc = scrC[:, 0:1024].rearrange("p (h j) -> p h j", j=64)
                            rd = scrC[:, 1024:2048].rearrange("p (h j) -> p h j", j=64)
                            v.tensor_tensor(ra, x1, cosr, ALU.mult)
                            v.tensor_tensor(rb, x2, sinr, ALU.mult)
                            v.tensor_tensor(rc, x1, sinr, ALU.mult)
                            sev.inc(v.tensor_tensor(rd, x2, cosr, ALU.mult))
                            v.wait_ge(sev.h, sev.n)
                            sev.inc(v.tensor_tensor(x1, ra, rb, ALU.subtract))
                            v.wait_ge(sev.h, sev.n)
                            sev.inc(v.tensor_tensor(x2, rc, rd, ALU.add))
                            v.wait_ge(sev.h, sev.n)
                            if write_kd:
                                if t4 >= 2:
                                    v.wait_ge(sst.h, R_st + 2 * (t4 - 1))
                                kdd = bfSM[:, (t4 % 2) * 2048:(t4 % 2) * 2048 + 2048]
                                sev.inc(v.tensor_tensor(kdd, kt, nwrep[:, :], ALU.mult))
                                v.wait_ge(sev.h, sev.n)
                            else:
                                sev.inc(v.memset(colp[:, 33:34], 0.0))
                    bar(("rot", g, pj))

                proj("k", AF.Silu)
                ck()
                rot("k", True)
                ck()
                proj("v", AF.Identity, extra_st_wait=B[("rot", g, "k")][6])
                ck()

                F_fb, S_mm, S_ev = sfb.n, smm.n, sev.n

                @block.sync
                def _(sync, g=g, F_fb=F_fb, S_mm=S_mm):
                    sync.wait_ge(sst.h, B[("rot", g, "k")][6])
                    for t4 in range(4):
                        t = g * 4 + t4
                        if t4 >= 2:
                            sync.wait_ge(smm.h, S_mm + 16 * (t4 - 1))
                        dst = bfSM[:, (t4 % 2) * 2048:(t4 % 2) * 2048 + 2048]
                        sfb.incd(sync.dma_start(
                            out=dst, in_=kdcache.rearrange("(a p) d -> a p d", p=128)[t]))

                S_sc = ssc.n

                @block.tensor
                def _(tn, g=g, F_fb=F_fb, S_sc=S_sc):
                    for t4 in range(4):
                        tn.wait_ge(sfb.h, F_fb + 16 * (t4 + 1))
                        for h in range(16):
                            s_ = 16 * t4 + h
                            if s_ >= 4:
                                tn.wait_ge(ssc.h, S_sc + s_ - 3)
                            kdd = bfSM[:, (t4 % 2) * 2048 + h * 128:(t4 % 2) * 2048 + h * 128 + 128]
                            vt = bfBIG[:, 8192 + t4 * 2048 + h * 128:8192 + t4 * 2048 + h * 128 + 128]
                            smm.inc(tn.matmul(banks[h % 4][:, (h // 4) * 128:(h // 4) * 128 + 128],
                                              kdd, vt, start=True, stop=True))

                @block.scalar
                def _(s, g=g, S_mm=S_mm, S_ev=S_ev):
                    for t4 in range(4):
                        for h in range(16):
                            s.wait_ge(smm.h, S_mm + 16 * t4 + h + 1)
                            if t4 >= 1:
                                s.wait_ge(sev.h, S_ev + 16 * (t4 - 1) + h + 1)
                            bk = banks[h % 4][:, (h // 4) * 128:(h // 4) * 128 + 128]
                            ssc.inc(s.activation(scrC[:, h * 128:(h + 1) * 128], bk, AF.Identity))

                @block.vector
                def _(v, g=g, S_sc=S_sc):
                    v.wait_ge(ssc.h, B[("proj", g, "v")][2])
                    for t4 in range(4):
                        c = g * 4 + t4
                        for h in range(16):
                            v.wait_ge(ssc.h, S_sc + 16 * t4 + h + 1)
                            dst = f32P1[:, 2048 + h * 128:2048 + (h + 1) * 128]
                            sa = scrC[:, h * 128:(h + 1) * 128]
                            if c == 0:
                                sev.inc(v.tensor_copy(dst, sa))
                            else:
                                sev.inc(v.scalar_tensor_tensor(dst, dst, lamC_f[h], sa,
                                                               ALU.mult, ALU.add))
                            v.wait_ge(sev.h, sev.n)
                bar(("state", g))
                ck()

                @block.sync
                def _(sync, g=g):
                    sync.wait_ge(smm.h, B[("state", g)][0])
                    for t4 in range(4):
                        t = g * 4 + t4
                        vt = bfBIG[:, 8192 + t4 * 2048:8192 + (t4 + 1) * 2048]
                        sst.incd(sync.dma_start(
                            out=vcache.rearrange("(a p) d -> a p d", p=128)[t], in_=vt))
                bar(("vst", g))
                ck()

                proj("q", AF.Silu, extra_st_wait=B[("vst", g)][6])
                ck()
                rot("q", False)
                ck()

                G_g, G_mm, G_sc = sgst.n, smm.n, ssc.n

                @block.gpsimd
                def _(g_, g=g):
                    g_.wait_ge(smm.h, B[("proj", g, "q")][0])
                    sgst.incd(g_.dma_start(out=wfv[:, 0:16, :],
                                           in_=wg1_d.rearrange("(hh p) o -> p hh o", p=128)))
                    sgst.incd(g_.dma_start(out=bfSM.rearrange("p (a c) -> p a c", c=2048)[:, 1, :],
                                           in_=wg2_d[:, :]))

                @block.tensor
                def _(tn, G_g=G_g, G_sc=G_sc):
                    tn.wait_ge(sgst.h, G_g + 16)
                    tn.wait_ge(ssc.h, G_sc)
                    for h in range(16):
                        mm = tn.matmul(pb6[:, 0:512], wfv[:, h, :],
                                       xlv[:, h, 0:512], start=(h == 0), stop=(h == 15))
                    smm.inc(mm)
                    tn.wait_ge(sgst.h, G_g + 32)
                    tn.wait_ge(ssc.h, G_sc + 1)
                    for t4 in range(4):
                        for ob in range(4):
                            idx = 4 * t4 + ob
                            bank = banks[4 + (idx % 2)]
                            if idx >= 2:
                                tn.wait_ge(ssc.h, G_sc + idx)
                            smm.inc(tn.matmul(bank[:, :], bfSM[:, t4 * 128:t4 * 128 + 128],
                                              smv[:, 4 + ob, :], start=True, stop=True))

                @block.scalar
                def _(s, g=g, G_mm=G_mm):
                    s.wait_ge(sev.h, B[("rot", g, "q")][1])
                    s.wait_ge(smm.h, G_mm + 1)
                    ssc.inc(s.activation(bfSM[:, 0:512], pb6[:, 0:512], AF.Identity))
                    for t4 in range(4):
                        t = g * 4 + t4
                        for ob in range(4):
                            idx = 4 * t4 + ob
                            s.wait_ge(smm.h, G_mm + 1 + idx + 1)
                            bank = banks[4 + (idx % 2)]
                            ssc.inc(s.activation(scr2[:, ob * 512:(ob + 1) * 512], bank[:, :], AF.Sigmoid))
                        s.wait_ge(sst.h, sst.n)
                        sst.incd(s.dma_start(out=gcache.rearrange("(a p) d -> a p d", p=128)[t],
                                             in_=scr2[:, 0:2048]))
                        s.wait_ge(sst.h, sst.n)
                bar(("gate", g))
                ck()

            @block.sync
            def _(sync):
                sync.wait_ge(sev.h, B[("state", 1)][1])
                sync.wait_ge(sst.h, sst.n)
                for b in range(4):
                    dst = ag_st_in.rearrange("(hh pp) c -> pp hh c", pp=128)[:, 4 * b:4 * b + 4, :]
                    srcv = f32P1.rearrange("p (a c) -> p a c", c=128)[:, 16 + 4 * b:16 + 4 * b + 4, :]
                    sst.incd(sync.dma_start(out=dst, in_=srcv))
            E_st2 = sst.n

            @block.gpsimd
            def _(g_):
                g_.wait_ge(sst.h, E_st2)
                scc.inc(g_.collective_compute("AllGather", ALU.bypass, replica_groups=replica,
                                              ins=[ag_st_in.ap().opt()], outs=[ag_st_out.ap().opt()]))
            AT_CC = scc.n

            C_ld, C_ev = sld.n, sev.n

            @block.sync
            def _(sync):
                sync.wait_ge(scc.h, AT_CC)
                for cp in range(NCORES):
                    if cp >= 1:
                        sync.wait_ge(sev.h, C_ev + 1 + 16 * cp)
                    src = ag_st_out.rearrange("(cc hh pp) c -> cc pp hh c", pp=128, hh=HEADS)[cp]
                    sld.incd(sync.dma_start(out=scrC.rearrange("p (hh c) -> p hh c", c=128), in_=src))

            @block.vector
            def _(v):
                v.wait_ge(scc.h, AT_CC)
                sev.inc(v.memset(f32P1[:, 2048:4096], 0.0))
                for cp in range(NCORES):
                    v.wait_ge(sld.h, C_ld + 16 * (cp + 1))
                    for h in range(16):
                        sev.inc(v.scalar_tensor_tensor(
                            f32P1[:, 2048 + h * 128:2048 + h * 128 + 128],
                            scrC[:, h * 128:(h + 1) * 128], lamt[:, cp * 16 + h:cp * 16 + h + 1],
                            f32P1[:, 2048 + h * 128:2048 + h * 128 + 128], ALU.mult, ALU.add))
                    v.wait_ge(sev.h, sev.n)
            bar("sinit")
            ck()

            @block.sync
            def _(sync):
                sync.wait_ge(sld.h, sld.n)
                sld.incd(sync.dma_start(
                    out=f32P1.rearrange("p (a c) -> p a c", c=128)[:, 32:48, :],
                    in_=tdmask.rearrange("(hh s) t -> s hh t", s=128)))
                sync.wait_ge(sld.h, sld.n)
                sld.incd(sync.dma_start(out=nwrep[:, :], in_=tnwrep[:, :]))
            P2_ld0 = sld.n

            prev_ev = [None]
            for c in range(8):
                CL_ld = sfb.n

                @block.sync
                def _(sync, c=c, CL_ld=CL_ld):
                    if c == 0:
                        sync.wait_ge(smm.h, B[("gate", 1)][0])
                        sync.wait_ge(sld.h, P2_ld0)
                    else:
                        sync.wait_ge(smm.h, B[("scan", c - 1)][0])
                    for j, nm in enumerate(["q", "k", "kd", "v"]):
                        srcd = kdcache if nm == "kd" else cach[nm]
                        sfb.incd(sync.dma_start(out=bfBIG[:, j * 2048:(j + 1) * 2048],
                                                in_=srcd.rearrange("(a p) d -> a p d", p=128)[c]))

                H_mm, H_sc, H_ev = smm.n, ssc.n, sev.n
                P_EV = prev_ev[0]

                @block.tensor
                def _(tn, c=c, CL_ld=CL_ld, H_mm=H_mm, H_sc=H_sc, H_ev=H_ev):
                    tn.wait_ge(sfb.h, CL_ld + 64)
                    if c != 0:
                        tn.wait_ge(sev.h, B[("scan", c - 1)][1])
                        tn.wait_ge(ssc.h, B[("scan", c - 1)][2])
                    for h in range(16):
                        pq = pb4 if h % 2 == 0 else pb5
                        po = pb2 if h % 2 == 0 else pb3
                        scb = pb6 if h % 2 == 0 else pb7
                        if h >= 2:
                            tn.wait_ge(ssc.h, H_sc + 3 * (h - 2) + 1)
                        qs = bfBIG[:, h * 128:h * 128 + 128]
                        ks = bfBIG[:, 2048 + h * 128:2048 + h * 128 + 128]
                        tn.matmul(pq[:, 0:128], qs, i128[:, :], start=True, stop=True)
                        smm.inc(tn.matmul(pq[:, 128:256], ks, i128[:, :], start=True, stop=True))
                        tn.wait_ge(ssc.h, H_sc + 3 * h + 1)
                        if h >= 2:
                            tn.wait_ge(sev.h, H_ev + 2 * (h - 2) + 1)
                        smm.inc(tn.matmul(scb[:, 0:128],
                                          bfSM[:, 256:384], bfSM[:, 128:256], start=True, stop=True))
                        tn.wait_ge(ssc.h, H_sc + 3 * h + 2)
                        if h >= 2:
                            tn.wait_ge(ssc.h, H_sc + 3 * (h - 2) + 3)
                        tn.matmul(po[:, 0:128],
                                  bfSM[:, 128:256], bfSM[:, 384:512], start=True, stop=False)
                        tn.wait_ge(sev.h, H_ev + 2 * h + 1)
                        vt = bfBIG[:, 6144 + h * 128:6144 + h * 128 + 128]
                        smm.inc(tn.matmul(po[:, 0:128],
                                          bfSM[:, 0:128], vt, start=False, stop=True))
                        kdd = bfBIG[:, 4096 + h * 128:4096 + h * 128 + 128]
                        if h >= 2:
                            tn.wait_ge(sev.h, H_ev + 2 * (h - 2) + 2)
                        smm.inc(tn.matmul(banks[h % 2][:, 0:128], kdd, vt, start=True, stop=True))

                @block.scalar
                def _(s, c=c, H_mm=H_mm, P_EV=P_EV):
                    for h in range(16):
                        pq = pb4 if h % 2 == 0 else pb5
                        po = pb2 if h % 2 == 0 else pb3
                        s.wait_ge(smm.h, H_mm + 4 * h + 1)
                        if h >= 1:
                            s.wait_ge(smm.h, H_mm + 4 * (h - 1) + 3)
                        s.activation(bfSM[:, 128:256], pq[:, 0:128], AF.Identity)
                        ssc.inc(s.activation(bfSM[:, 256:384], pq[:, 128:256], AF.Identity))
                        if c == 0:
                            s.wait_ge(sev.h, B["sinit"][1])
                        else:
                            s.wait_ge(sev.h, P_EV + 2 * h + 2)
                        ssc.inc(s.activation(bfSM[:, 384:512],
                                             f32P1[:, 2048 + h * 128:2048 + (h + 1) * 128],
                                             AF.Identity, scale=sScl_f[h]))
                        s.wait_ge(smm.h, H_mm + 4 * h + 3)
                        ssc.inc(s.activation(f32P1[:, h * 128:(h + 1) * 128],
                                             po[:, 0:128],
                                             AF.Identity, scale=dqc[:, h:h + 1]))

                @block.vector
                def _(v, c=c, H_mm=H_mm, H_sc=H_sc):
                    if c == 0:
                        v.wait_ge(sld.h, P2_ld0)
                    for h in range(16):
                        v.wait_ge(smm.h, H_mm + 4 * h + 2)
                        dm = f32P1[:, 4096 + h * 128:4096 + h * 128 + 128]
                        scb = pb6 if h % 2 == 0 else pb7
                        sev.inc(v.tensor_tensor(bfSM[:, 0:128], scb[:, 0:128], dm, ALU.mult))
                        v.wait_ge(sev.h, sev.n)
                        v.wait_ge(smm.h, H_mm + 4 * h + 4)
                        v.wait_ge(ssc.h, H_sc + 3 * h + 2)
                        dst = f32P1[:, 2048 + h * 128:2048 + (h + 1) * 128]
                        sev.inc(v.scalar_tensor_tensor(
                            dst, dst, lamC_f[h], banks[h % 2][:, 0:128],
                            ALU.mult, ALU.add))
                        v.wait_ge(sev.h, sev.n)
                bar(("scan", c))
                prev_ev[0] = H_ev

                GA_ld, GA_ev, GA_sc = sld.n, sev.n, ssc.n

                @block.sync
                def _(sync, c=c):
                    sync.wait_ge(sld.h, sld.n)
                    sld.incd(sync.dma_start(out=scr2[:, 0:2048],
                                            in_=gcache.rearrange("(a p) d -> a p d", p=128)[c]))

                @block.vector
                def _(v, c=c, GA_ld=GA_ld, GA_sc=GA_sc):
                    v.wait_ge(sld.h, GA_ld + 16)
                    v.wait_ge(smm.h, B[("scan", c)][0])
                    v.wait_ge(ssc.h, B[("scan", c)][2])
                    sev.inc(v.tensor_tensor(scr[:, 0:2048], f32P1[:, 0:2048], scr2[:, 0:2048], ALU.mult))
                    v.wait_ge(sev.h, sev.n)
                    v.wait_ge(ssc.h, GA_sc + 1)
                    VS(v, v.tensor_reduce(colp[:, 0:16].rearrange("p (a o) -> p a o", o=1),
                                          scrC.rearrange("p (a c) -> p a c", c=128), AX, ALU.add))
                    VS(v, v.tensor_scalar(colp[:, 16:32], colp[:, 0:16], 1.0 / 128.0, 1e-6,
                                          ALU.mult, ALU.add))
                    sev.inc(v.reciprocal(colp[:, 48:64], colp[:, 16:32]))
                    v.wait_ge(sev.h, sev.n)
                    v.wait_ge(ssc.h, GA_sc + 2)
                    VS(v, v.tensor_tensor(colp[:, 0:16], colp[:, 48:64], colp[:, 48:64], ALU.mult))
                    VS(v, v.tensor_tensor(colp[:, 0:16], colp[:, 0:16], colp[:, 16:32], ALU.mult))
                    VS(v, v.tensor_scalar(colp[:, 0:16], colp[:, 0:16], -0.5, 1.5, ALU.mult, ALU.add))
                    VS(v, v.tensor_tensor(colp[:, 48:64], colp[:, 48:64], colp[:, 0:16], ALU.mult))
                    for h in range(16):
                        v.tensor_scalar_mul(scr.rearrange("p (a cc) -> p a cc", cc=128)[:, h, :],
                                            scr.rearrange("p (a cc) -> p a cc", cc=128)[:, h, :],
                                            colp[:, 48 + h:49 + h])
                    v.wait_ge(svs.h, svs.n)
                    sev.inc(v.tensor_tensor(bfSM[:, 0:2048], scr[:, 0:2048], nwrep[:, :], ALU.mult))
                    v.wait_ge(sev.h, sev.n)

                @block.scalar
                def _(s, GA_ev=GA_ev):
                    s.wait_ge(sev.h, GA_ev + 1)
                    ssc.inc(s.activation(scrC[:, 0:2048], scr[:, 0:2048], AF.Square))
                    s.wait_ge(sev.h, GA_ev + 2)
                    ssc.inc(s.activation(colp[:, 48:64], colp[:, 48:64], AF.Sqrt))
                    s.wait_ge(ssc.h, ssc.n)
                bar(("ga", c))

                @block.sync
                def _(sync, c=c):
                    sync.wait_ge(sev.h, B[("ga", c)][1])
                    sync.wait_ge(sst.h, sst.n)
                    sst.incd(sync.dma_start(out=oncache.rearrange("(a p) d -> a p d", p=128)[c],
                                            in_=bfSM[:, 0:2048]))
                bar(("sc2", c))
                ck()

            A11_mm0, A11_sc0, A11_ev0, A11_g0, A11_fb0 = smm.n, ssc.n, sev.n, sgst.n, sfb.n

            @block.gpsimd
            def _(g_):
                g_.wait_ge(smm.h, A11_mm0)
                for ob in range(4):
                    if ob >= 1:
                        g_.wait_ge(smm.h, A11_mm0 + 40 * ob)
                    src = wo_d.rearrange("(hh p) o -> p hh o", p=128)[:, :, ob * 512:(ob + 1) * 512]
                    sgst.incd(g_.dma_start(out=wsv[:, 0:16, :], in_=src))

            @block.sync
            def _(sync):
                sync.wait_ge(sst.h, B[("sc2", 7)][6])
                for ob in range(4):
                    for t in range(8):
                        it = 8 * ob + t
                        if it >= 2:
                            sync.wait_ge(smm.h, A11_mm0 + 5 * (it - 1))
                        dst = bfSM[:, (t % 2) * 2048:(t % 2) * 2048 + 2048]
                        sfb.incd(sync.dma_start(out=dst,
                                                in_=oncache.rearrange("(a p) d -> a p d", p=128)[t]))

            @block.tensor
            def _(tn):
                tn.wait_ge(ssc.h, A11_sc0)
                tn.wait_ge(sev.h, B[("sc2", 7)][1])
                for ob in range(4):
                    tn.wait_ge(sgst.h, A11_g0 + 16 * (ob + 1))
                    for t in range(8):
                        it = 8 * ob + t
                        tn.wait_ge(sfb.h, A11_fb0 + 16 * (it + 1))
                        ot = bfSM[:, (t % 2) * 2048:(t % 2) * 2048 + 2048]
                        if it >= 1:
                            tn.wait_ge(ssc.h, A11_sc0 + 4 * it)
                        for ec in range(16):
                            gb = ec // 4
                            bank = banks[4 + (gb % 2)]
                            if ec % 4 == 0 and gb >= 2:
                                tn.wait_ge(ssc.h, A11_sc0 + 4 * it + gb - 1)
                            mm = tn.matmul(bank[:, (ec % 4) * 128:(ec % 4) * 128 + 128],
                                           ot[:, ec * 128:ec * 128 + 128], i128[:, :],
                                           start=True, stop=True)
                            if ec % 4 == 3:
                                smm.inc(mm)
                        tn.wait_ge(ssc.h, A11_sc0 + 4 * it + 4)
                        if it >= 2:
                            tn.wait_ge(sev.h, A11_ev0 + it - 1)
                        accb = banks[6 + (it % 2)]
                        for ec in range(16):
                            mm = tn.matmul(accb[:, 0:512], bfBIG[:, ec * 128:ec * 128 + 128],
                                           wsv[:, ec, :], start=(ec == 0), stop=(ec == 15))
                        smm.inc(mm)

            @block.scalar
            def _(s):
                for ob in range(4):
                    for t in range(8):
                        it = 8 * ob + t
                        for gb in range(4):
                            s.wait_ge(smm.h, A11_mm0 + 5 * it + gb + 1)
                            if it >= 1 and gb == 0:
                                s.wait_ge(smm.h, A11_mm0 + 5 * it)
                            bank = banks[4 + (gb % 2)]
                            ssc.inc(s.activation(bfBIG[:, gb * 512:(gb + 1) * 512], bank[:, :],
                                                 AF.Identity))

            @block.vector
            def _(v):
                for ob in range(4):
                    for t in range(8):
                        it = 8 * ob + t
                        v.wait_ge(smm.h, A11_mm0 + 5 * it + 5)
                        accb = banks[6 + (it % 2)]
                        xsl = xres[:, t * D + ob * 512:t * D + ob * 512 + 512]
                        sev.inc(v.tensor_tensor(xsl, accb[:, 0:512], xsl, ALU.add))
            return A11_ev0 + 32

        last_ev = P0_EV
        cid = 0
        for p in parts:
            if p in ("ffn1", "ffn2"):
                last_ev = ffn_call(cid, last_ev)
                cid += 1
            elif p == "attn":
                try:
                    last_ev = attn_call(last_ev)
                except _StopEmit as e:
                    last_ev = e.ev
            else:
                raise ValueError(p)

        @block.sync
        def _(sync):
            sync.wait_ge(sev.h, last_ev)
            for t in range(NT):
                sst.incd(sync.dma_start(out=y_out.rearrange("(a p) d -> a p d", p=128)[t, :, :],
                                        in_=xres[:, t * D:(t + 1) * D]))
            if DEBUG_EPRIME:
                dbv = dbg_out.rearrange("(u p) c -> u p c", p=128)
                for u in range(N_HU):
                    sst.incd(sync.dma_start(out=dbv[u], in_=eprime[u].rearrange("p a b -> p (a b)")))
                sst.incd(sync.dma_start(out=dbg_sb[:, 0:8192], in_=f32P1[:, :]))
                sst.incd(sync.dma_start(out=dbg_sb[:, 8192:10240], in_=scr2[:, :]))
                sst.incd(sync.dma_start(out=dbg_sb[:, 10240:10304], in_=colp[:, :]))
                sst.incd(sync.dma_start(out=dbg_sb[:, 10304:10312], in_=adeq[:, :]))
                sst.incd(sync.dma_start(out=dbg_bf[:, :], in_=bfBIG[:, :]))
            sync.wait_ge(sst.h, sst.n)

    return nc


def kernel(**inputs):
    xs = np.split(np.ascontiguousarray(inputs["L0_attn_x"][0]), NCORES, axis=0)
    gs = np.split(np.ascontiguousarray(inputs["L0_ffn_w_gate"]), NCORES, axis=0)
    us = np.split(np.ascontiguousarray(inputs["L0_ffn_w_up"]), NCORES, axis=0)
    ds = np.split(np.ascontiguousarray(inputs["L0_ffn_w_down"]), NCORES, axis=1)
    nc = build_program(PARTS)
    in_maps = []
    for c in range(NCORES):
        tb = host_tables(c)
        in_maps.append({
            "x_in": xs[c],
            "wq": np.ascontiguousarray(inputs["L0_attn_Wq"]),
            "wk": np.ascontiguousarray(inputs["L0_attn_Wk"]),
            "wv": np.ascontiguousarray(inputs["L0_attn_Wv"]),
            "wo": np.ascontiguousarray(inputs["L0_attn_Wo"]),
            "wg1": np.ascontiguousarray(inputs["L0_attn_Wg1"]),
            "wg2": np.ascontiguousarray(inputs["L0_attn_Wg2"]),
            "nrmw": np.ascontiguousarray(inputs["L0_attn_norm_weight"]).reshape(1, D),
            "nrmw_rep": np.tile(np.ascontiguousarray(inputs["L0_attn_norm_weight"]).reshape(1, D), (128, 1)),
            "wg_sl": gs[c], "wu_sl": us[c], "wd_sl": np.ascontiguousarray(ds[c]),
            **tb,
        })
    res = run_bass_kernel_spmd(nc, in_maps, core_ids=list(range(NCORES)))
    y = np.concatenate([res.results[c]["y_out"] for c in range(NCORES)], axis=0)
    return y.reshape(1, T * NCORES, D).astype(np.float32)


# version 04
# speedup vs baseline: 1.1600x; 1.1600x over previous
import contextlib
import numpy as np
import ml_dtypes
import concourse.bass as bass
import concourse.mybir as mybir
from concourse.bass_utils import run_bass_kernel_spmd

F32 = mybir.dt.float32
BF16 = mybir.dt.bfloat16
I32 = mybir.dt.int32
ALU = mybir.AluOpType
AF = mybir.ActivationFunctionType
AX = mybir.AxisListType.X

NCORES = 8
T = 1024
D = 2048
HEADS = 16
HD = 128
CH = 128
NT = T // 128
HU = 64
N_HU = T // HU
NW_ELEMS = float(8192 * 2048)
PG_GU = 128 * 2048
PG_D = 128 * 1024
SL_ELEMS = 8 * PG_GU * 2 + 16 * PG_D

PARTS = ("ffn1", "attn", "ffn2")
ATTN_STAGES = None


class _StopEmit(Exception):
    def __init__(self, ev):
        self.ev = ev
DEBUG_EPRIME = False


def _sylvester(n):
    h = np.array([[1.0]])
    while h.shape[0] < n:
        h = np.block([[h, h], [h, -h]])
    return h.astype(np.float32)


def _ll():
    return -(2.0 ** (-8.0 * (np.arange(HEADS, dtype=np.float64) + 1.0) / HEADS))


def _lamtab(core):
    ll = _ll()
    t = np.zeros((128, 128), np.float32)
    for cp in range(NCORES):
        for h in range(HEADS):
            if cp < core:
                t[:, cp * 16 + h] = np.float32(np.exp(ll[h] * CH * 8.0 * (core - 1 - cp)))
    return t


def host_tables(core):
    H64B = np.kron(np.eye(2, dtype=np.float32), _sylvester(64)).astype(np.float32)
    tg = np.arange(core * T, (core + 1) * T, dtype=np.float64)
    j = np.arange(64, dtype=np.float64)
    theta = 10000.0 ** (-2.0 * j / 128.0)
    ang = tg[:, None] * theta[None, :]
    ll = _ll()
    s = np.arange(CH, dtype=np.float64)
    dmask = np.zeros((HEADS * CH, CH), dtype=np.float32)
    for h in range(HEADS):
        col = np.exp(-ll[h] * (s + 1.0 - 64.0))
        m = (np.arange(CH)[None, :] >= s[:, None])
        dmask[h * CH:(h + 1) * CH, :] = (col[:, None] * m).astype(np.float32)
    dqcol = np.exp(ll[None, :] * (s[:, None] + 1.0 - 64.0)).astype(np.float32)
    dkcol = np.exp(ll[None, :] * (CH - 1.0 - s[:, None])).astype(np.float32)
    P4 = np.zeros((32, 512), dtype=np.float32)
    for usub in range(2):
        for tlo in range(2):
            for th in range(32):
                P4[th, (2 * usub + tlo) * 128 + usub * 64 + 2 * th + tlo] = 1.0
    cos_rep = np.tile(np.cos(ang).astype(np.float32), (1, 16))
    sin_rep = np.tile(np.sin(ang).astype(np.float32), (1, 16))
    dkrep = np.zeros((128, D), np.float32)
    for h in range(HEADS):
        dkrep[:, h * 128:(h + 1) * 128] = dkcol[:, h:h + 1]
    return {
        "H128": _sylvester(128) * np.float32(1.0 / np.sqrt(8192)), "H64B": H64B,
        "I128": np.asarray(np.eye(128, dtype=np.float32).astype(ml_dtypes.bfloat16)),
        "cos_rep": cos_rep, "sin_rep": sin_rep, "dkrep": dkrep,
        "dmask": dmask, "dqcol": dqcol, "dkcol": dkcol, "P4": P4,
        "lamtab": _lamtab(core),
        "SEL2": np.concatenate([np.kron(np.eye(2, dtype=np.float32), np.ones((1, 64), np.float32)), np.zeros((30, 128), np.float32)], 0),
    }


class Sem:
    def __init__(self, h):
        self.h, self.n = h, 0

    def inc(self, instr, k=1):
        instr.then_inc(self.h, k)
        self.n += k
        return instr

    def incd(self, instr):
        return self.inc(instr, 16)


def _sched_B(base_sc, base_ev, base_mm, base_gp):
    M = {}
    sc, ev, mm, gp = base_sc, base_ev, base_mm, base_gp
    for u in range(N_HU):
        sc += 1; M["reord", u] = sc
        for b in range(8):
            mm += 1; M["r1", u, b] = mm
            if b >= 2:
                mm += 1; M["r2", u, b - 2] = mm
        for b in [6, 7]:
            mm += 1; M["r2", u, b] = mm
        for b in range(8):
            ev += 1; M["r1ev", u, b] = ev
        for b in range(8):
            sc += 1; M["yev", u, b] = sc
        ev += 1; M["mx1", u] = ev
        mm += 1; M["mx1T", u] = mm
        ev += 1; M["mx1Tev", u] = ev
        ev += 1; M["chain", u] = ev
        mm += 1; M["sT", u] = mm
        ev += 1; M["sTev", u] = ev
        mm += 1; M["ssm", u] = mm
        ev += 1; M["ssmev", u] = ev
        ev += 1; M["vmult", u] = ev
        for half in range(2):
            ev += 1; M["vp", u, half] = ev
            ev += 1; M["c", u, half] = ev
            ev += 1; M["dd", u, half] = ev
            ev += 1; M["ge", u, half] = ev
            ev += 1; M["r", u, half] = ev
            ev += 1; M["q", u, half] = ev
        M["vdone", u] = ev
        for b in range(8):
            mm += 1; M["r3", u, b] = mm
        for b in range(8):
            sc += 1; M["r3ev", u, b] = sc
        if u % 4 == 3:
            g4 = u // 4
            mm += 1; M["dqc", g4, 0] = mm
            mm += 1; M["dqc", g4, 1] = mm
            ev += 2; M["dqcev", g4] = ev
            for o in range(16):
                mm += 1; M["dn", g4, o] = mm
                sc += 1; M["dnev", g4, o] = sc
                for tb in range(2):
                    mm += 1; M["dntm", g4, o, tb] = mm
                    ev += 1; M["resid", g4, o, tb] = ev
    return M, sc, ev, mm, gp


def build_program(parts=PARTS):
    nc = bass.Bass("TRN2", target_bir_lowering=False, debug=False)
    ll = _ll()
    lamC = [float(np.float32(np.exp(ll[h] * CH))) for h in range(HEADS)]
    sScl = [float(np.float32(np.exp(ll[h] * 64.0))) for h in range(HEADS)]

    x_in = nc.dram_tensor("x_in", [T, D], F32, kind="ExternalInput").ap()
    wq_d = nc.dram_tensor("wq", [D, D], F32, kind="ExternalInput").ap()
    wk_d = nc.dram_tensor("wk", [D, D], F32, kind="ExternalInput").ap()
    wv_d = nc.dram_tensor("wv", [D, D], F32, kind="ExternalInput").ap()
    wo_d = nc.dram_tensor("wo", [D, D], F32, kind="ExternalInput").ap()
    wg1_d = nc.dram_tensor("wg1", [D, HD], F32, kind="ExternalInput").ap()
    wg2_d = nc.dram_tensor("wg2", [HD, D], F32, kind="ExternalInput").ap()
    nrmw_d = nc.dram_tensor("nrmw", [1, D], F32, kind="ExternalInput").ap()
    wg_sl = nc.dram_tensor("wg_sl", [T, D], F32, kind="ExternalInput").ap()
    wu_sl = nc.dram_tensor("wu_sl", [T, D], F32, kind="ExternalInput").ap()
    wd_sl = nc.dram_tensor("wd_sl", [D, T], F32, kind="ExternalInput").ap()
    y_out = nc.dram_tensor("y_out", [T, D], F32, kind="ExternalOutput").ap()
    dbg_out = (nc.dram_tensor("dbg_out", [N_HU * 128, 64 * HU], F32, kind="ExternalOutput").ap()
               if DEBUG_EPRIME else None)
    dbg_sb = (nc.dram_tensor("dbg_sb", [128, 10312], F32, kind="ExternalOutput").ap()
              if DEBUG_EPRIME else None)
    dbg_bf = (nc.dram_tensor("dbg_bf", [128, 16384], BF16, kind="ExternalOutput").ap()
              if DEBUG_EPRIME else None)

    tH128 = nc.dram_tensor("H128", [128, 128], F32, kind="ExternalInput").ap()
    tH64B = nc.dram_tensor("H64B", [128, 128], F32, kind="ExternalInput").ap()
    tI128 = nc.dram_tensor("I128", [128, 128], BF16, kind="ExternalInput").ap()
    tcos = nc.dram_tensor("cos_rep", [T, 1024], F32, kind="ExternalInput").ap()
    tsin = nc.dram_tensor("sin_rep", [T, 1024], F32, kind="ExternalInput").ap()
    tdkrep = nc.dram_tensor("dkrep", [128, D], F32, kind="ExternalInput").ap()
    tnwrep = nc.dram_tensor("nrmw_rep", [128, D], F32, kind="ExternalInput").ap()
    tdmask = nc.dram_tensor("dmask", [HEADS * CH, CH], F32, kind="ExternalInput").ap()
    tdqcol = nc.dram_tensor("dqcol", [CH, HEADS], F32, kind="ExternalInput").ap()
    tdkcol = nc.dram_tensor("dkcol", [CH, HEADS], F32, kind="ExternalInput").ap()
    tP4 = nc.dram_tensor("P4", [32, 512], F32, kind="ExternalInput").ap()
    tlam = nc.dram_tensor("lamtab", [128, 128], F32, kind="ExternalInput").ap()
    tSEL2 = nc.dram_tensor("SEL2", [32, 128], F32, kind="ExternalInput").ap()

    ag_w_in = nc.dram_tensor("ag_w_in", [SL_ELEMS], BF16)
    GU_SL = 16 * PG_GU
    D_SL = 16 * PG_D
    ag_gu_out = nc.dram_tensor("ag_gu_out", [NCORES * GU_SL], BF16)
    ag_d_out = nc.dram_tensor("ag_d_out", [NCORES * D_SL], BF16)
    ar_s_in = nc.dram_tensor("ar_s_in", [128, 4], F32)
    ar_s_out = nc.dram_tensor("ar_s_out", [128, 4], F32)
    ag_st_in = nc.dram_tensor("ag_st_in", [HEADS * HD, HD], F32)
    ag_st_out = nc.dram_tensor("ag_st_out", [NCORES * HEADS * HD, HD], F32)
    eprime = nc.dram_tensor("eprime", [N_HU, 128, 64, HU], F32)
    kcache = nc.dram_tensor("kcache", [T, D], BF16)
    kdcache = nc.dram_tensor("kdcache", [T, D], BF16)
    vcache = nc.dram_tensor("vcache", [T, D], BF16)
    qcache = nc.dram_tensor("qcache", [T, D], BF16)
    gcache = nc.dram_tensor("gcache", [T, D], F32)
    oncache = nc.dram_tensor("oncache", [T, D], BF16)
    replica = [list(range(NCORES))]

    with contextlib.ExitStack() as _stack:
        _e = _stack.enter_context
        xres = _e(nc.sbuf_tensor("xres", [128, NT * D], F32))
        h128 = _e(nc.sbuf_tensor("h128", [128, 128], F32))
        h64b = _e(nc.sbuf_tensor("h64b", [128, 128], F32))
        i128 = _e(nc.sbuf_tensor("i128", [128, 128], BF16))
        i128f = _e(nc.sbuf_tensor("i128f", [128, 128], F32))
        ones = _e(nc.sbuf_tensor("ones", [128, 128], F32))
        sel2 = _e(nc.sbuf_tensor("sel2", [32, 128], F32))
        srow = _e(nc.sbuf_tensor("srow", [32, 64], F32))
        p4 = _e(nc.sbuf_tensor("p4", [32, 512], F32))
        colp = _e(nc.sbuf_tensor("colp", [128, 64], F32))
        adeq = _e(nc.sbuf_tensor("adeq", [128, NT], F32))
        dqc = _e(nc.sbuf_tensor("dqc", [128, HEADS], F32))
        dkc = _e(nc.sbuf_tensor("dkc", [128, HEADS], F32))
        nwrep = _e(nc.sbuf_tensor("nwrep", [128, D], F32))
        lamt = _e(nc.sbuf_tensor("lamt", [128, 128], F32))
        bfBIG = _e(nc.sbuf_tensor("bfBIG", [128, 16384], BF16))
        bfSM = _e(nc.sbuf_tensor("bfSM", [128, 4096], BF16))
        f32P1 = _e(nc.sbuf_tensor("f32P1", [128, 8192], F32))
        scr = _e(nc.sbuf_tensor("scr", [128, 4096], F32))
        scr2 = _e(nc.sbuf_tensor("scr2", [128, 2048], F32))
        scrC = _e(nc.sbuf_tensor("scrC", [128, 2048], F32))
        bfW = _e(nc.sbuf_tensor("bfW", [128, 8192], BF16))
        iscr = _e(nc.sbuf_tensor("iscr", [128, 2048], I32))
        _s1 = _e(nc.semaphore("s_ld"))
        _s2 = _e(nc.semaphore("s_w"))
        _s3 = _e(nc.semaphore("s_mm"))
        _s4 = _e(nc.semaphore("s_ev"))
        _s5 = _e(nc.semaphore("s_sc"))
        _s6 = _e(nc.semaphore("s_gp"))
        _s7 = _e(nc.semaphore("s_st"))
        _s8 = _e(nc.semaphore("s_cc"))
        _s9 = _e(nc.semaphore("s_fb"))
        _s10 = _e(nc.semaphore("s_gst"))
        _s11 = _e(nc.semaphore("s_vs"))
        _s12 = _e(nc.semaphore("s_ss"))
        pb0 = _e(nc.psum_tensor("pb0", [128, 512], F32))
        pb1 = _e(nc.psum_tensor("pb1", [128, 512], F32))
        pb2 = _e(nc.psum_tensor("pb2", [128, 512], F32))
        pb3 = _e(nc.psum_tensor("pb3", [128, 512], F32))
        pb4 = _e(nc.psum_tensor("pb4", [128, 512], F32))
        pb5 = _e(nc.psum_tensor("pb5", [128, 512], F32))
        pb6 = _e(nc.psum_tensor("pb6", [128, 512], F32))
        pb7 = _e(nc.psum_tensor("pb7", [128, 512], F32))
        block = _e(nc.Block())
        sld, sw, smm, sev, ssc, sgp, sst, scc, sfb = (
            Sem(_s1), Sem(_s2), Sem(_s3), Sem(_s4), Sem(_s5), Sem(_s6), Sem(_s7), Sem(_s8), Sem(_s9))
        sgst = Sem(_s10)
        svs, sss = Sem(_s11), Sem(_s12)

        def VS(v, instr):
            svs.inc(instr)
            v.wait_ge(svs.h, svs.n)
            return instr

        def SS(s, instr):
            sss.inc(instr)
            s.wait_ge(sss.h, sss.n)
            return instr
        banks = [pb0, pb1, pb2, pb3, pb4, pb5, pb6, pb7]

        @block.sync
        def _(sync):
            for t in range(NT):
                sld.incd(sync.dma_start(out=xres[:, t * D:(t + 1) * D],
                                        in_=x_in.rearrange("(a p) d -> a p d", p=128)[t, :, :]))
            sld.incd(sync.dma_start(out=h128[:], in_=tH128[:, :]))
            sld.incd(sync.dma_start(out=h64b[:], in_=tH64B[:, :]))
            sld.incd(sync.dma_start(out=i128[:], in_=tI128[:, :]))
            sld.incd(sync.dma_start(out=p4[:, :], in_=tP4[:, :]))
            sld.incd(sync.dma_start(out=dqc[:, :], in_=tdqcol[:, :]))
            sld.incd(sync.dma_start(out=dkc[:, :], in_=tdkcol[:, :]))
            sld.incd(sync.dma_start(out=sel2[:, :], in_=tSEL2[:, :]))
            sld.incd(sync.dma_start(out=lamt[:, :], in_=tlam[:, :]))
        P0_LD = sld.n

        @block.vector
        def _(v):
            v.wait_ge(sld.h, P0_LD)
            v.tensor_copy(i128f[:], i128[:])
            v.memset(srow[:, :], 0.0)
            sev.inc(v.memset(ones[:], 1.0))
        P0_EV = sev.n

        tiles = [(0, wg_sl, i, 2048) for i in range(8)] + \
                [(1, wu_sl, i, 2048) for i in range(8)] + \
                [(2, wd_sl, i, 1024) for i in range(16)]
        B1_ev = sev.n

        @block.sync
        def _(sync):
            sync.wait_ge(sld.h, P0_LD)
            for idx, (m, ap_, i, w_) in enumerate(tiles):
                if idx >= 2:
                    sync.wait_ge(sev.h, B1_ev + idx - 1)
                if idx >= 1:
                    sync.wait_ge(sld.h, P0_LD + 16 * idx)
                buf = scr if idx % 2 == 0 else scr2
                src = ap_.rearrange("(a p) d -> a p d", p=128)[i, :, :]
                sld.incd(sync.dma_start(out=buf[:, 0:w_], in_=src))

        @block.vector
        def _(v):
            v.wait_ge(sev.h, P0_EV)
            v.memset(colp[:, 0:4], 0.0)
            for idx, (m, ap_, i, w_) in enumerate(tiles):
                v.wait_ge(sld.h, P0_LD + 16 * (idx + 1))
                buf = scr if idx % 2 == 0 else scr2
                VS(v, v.tensor_reduce(colp[:, 8:9], buf[:, 0:w_], AX, ALU.add, apply_absolute_value=True))
                sev.inc(v.tensor_tensor(colp[:, m:m + 1], colp[:, m:m + 1], colp[:, 8:9], ALU.add))
                v.wait_ge(sev.h, sev.n)
        P1A_EV = sev.n

        @block.tensor
        def _(tn):
            tn.wait_ge(sev.h, P1A_EV)
            smm.inc(tn.matmul(pb0[0:128, 0:4], ones[:, 0:128], colp[:, 0:4], start=True, stop=True))
        P1A_MM = smm.n

        @block.vector
        def _(v):
            v.wait_ge(smm.h, P1A_MM)
            sev.inc(v.tensor_copy(colp[:, 16:20], pb0[:, 0:4]))
        P1B_EV = sev.n

        @block.sync
        def _(sync):
            sync.wait_ge(sev.h, P1B_EV)
            sst.incd(sync.dma_start(out=ar_s_in[:, :], in_=colp[:, 16:20]))
        P1_ST1 = sst.n

        @block.gpsimd
        def _(g):
            g.wait_ge(sst.h, P1_ST1)
            scc.inc(g.collective_compute("AllReduce", ALU.add, replica_groups=replica,
                                         ins=[ar_s_in.ap().opt()], outs=[ar_s_out.ap().opt()]))
        P1_CC1 = scc.n

        @block.sync
        def _(sync):
            sync.wait_ge(scc.h, P1_CC1)
            sync.wait_ge(sld.h, sld.n)
            sld.incd(sync.dma_start(out=colp[:, 20:24], in_=ar_s_out[:, :]))
        P1_LD2 = sld.n

        @block.vector
        def _(v):
            v.wait_ge(sld.h, P1_LD2)
            VS(v, v.tensor_scalar(colp[:, 24:28], colp[:, 20:24], 1.0 / NW_ELEMS, 1e-5, ALU.mult, ALU.max))
            VS(v, v.reciprocal(colp[:, 40:44], colp[:, 24:28]))
            sev.inc(v.reciprocal(colp[:, 44:48], colp[:, 40:44]))
            v.wait_ge(sev.h, sev.n)
        P1D_EV = sev.n

        PAGE_OFF = [0, 8 * PG_GU, 16 * PG_GU]
        B2_ld, B2_ev, B2_mm, B2_sc, B2_st = sw.n, sev.n, smm.n, ssc.n, sst.n
        nbat = [w_ // 512 for (_, _, _, w_) in tiles]
        cumb = np.cumsum([0] + nbat).tolist()

        @block.sync
        def _(sync):
            for idx, (m, ap_, i, w_) in enumerate(tiles):
                if idx >= 2:
                    sync.wait_ge(sev.h, B2_ev + idx - 1)
                if idx >= 1:
                    sync.wait_ge(sw.h, B2_ld + 16 * idx)
                buf = scr if idx % 2 == 0 else scr2
                src = ap_.rearrange("(a p) d -> a p d", p=128)[i, :, :]
                sw.incd(sync.dma_start(out=buf[:, 0:w_], in_=src))

        @block.vector
        def _(v):
            v.wait_ge(sev.h, P1D_EV)
            for idx, (m, ap_, i, w_) in enumerate(tiles):
                v.wait_ge(sw.h, B2_ld + 16 * (idx + 1))
                if idx >= 2:
                    v.wait_ge(smm.h, B2_mm + cumb[idx - 1])
                buf = scr if idx % 2 == 0 else scr2
                tb = bfSM[:, 0:2048] if idx % 2 == 0 else bfSM[:, 2048:4096]
                sc_ = colp[:, 40 + m:41 + m]
                v.tensor_scalar(scrC[:, 0:w_], buf[:, 0:w_], sc_, 0.5, ALU.mult, ALU.is_ge)
                v.tensor_scalar(buf[:, 0:w_], buf[:, 0:w_], sc_, -0.5, ALU.mult, ALU.is_le)
                sev.inc(v.tensor_tensor(tb[:, 0:w_], scrC[:, 0:w_], buf[:, 0:w_], ALU.subtract))

        @block.tensor
        def _(tn):
            gb = 0
            for idx, (m, ap_, i, w_) in enumerate(tiles):
                tn.wait_ge(sev.h, B2_ev + idx + 1)
                tb = bfSM[:, 0:2048] if idx % 2 == 0 else bfSM[:, 2048:4096]
                for h in range(w_ // 128):
                    bank = banks[4 + (gb % 2)]
                    if h % 4 == 0 and gb >= 2:
                        tn.wait_ge(ssc.h, B2_sc + gb - 1)
                    mm = tn.matmul(bank[:, (h % 4) * 128:(h % 4) * 128 + 128],
                                   tb[:, h * 128:(h + 1) * 128], i128[:, :],
                                   start=True, stop=True)
                    if h % 4 == 3:
                        smm.inc(mm)
                        gb += 1

        @block.scalar
        def _(s):
            gb = 0
            for idx, (m, ap_, i, w_) in enumerate(tiles):
                slot = bfW[:, 0:4096] if idx % 2 == 0 else bfW[:, 4096:8192]
                if idx >= 2:
                    s.wait_ge(sgst.h, 16 * (idx - 1))
                for b in range(w_ // 512):
                    s.wait_ge(smm.h, B2_mm + gb + 1)
                    bank = banks[4 + (gb % 2)]
                    ssc.inc(s.activation(slot[:, b * 512:(b + 1) * 512], bank[:, :], AF.Identity))
                    gb += 1

        @block.gpsimd
        def _(g):
            for idx, (m, ap_, i, w_) in enumerate(tiles):
                g.wait_ge(ssc.h, B2_sc + cumb[idx + 1])
                if idx >= 1:
                    g.wait_ge(sgst.h, 16 * idx)
                slot = bfW[:, 0:4096] if idx % 2 == 0 else bfW[:, 4096:8192]
                off = PAGE_OFF[m] + i * (PG_GU if m < 2 else PG_D)
                dst = ag_w_in.ap()[off:off + 128 * w_].rearrange("(p c) -> p c", c=w_)
                sgst.incd(g.dma_start(out=dst, in_=slot[:, 0:w_]))
                if idx == 15:
                    g.wait_ge(sgst.h, 16 * 16)
                    scc.inc(g.collective_compute(
                        "AllGather", ALU.bypass, replica_groups=replica,
                        ins=[ag_w_in.ap()[0:GU_SL].opt()], outs=[ag_gu_out.ap().opt()]))
            g.wait_ge(sgst.h, sgst.n)
            scc.inc(g.collective_compute(
                "AllGather", ALU.bypass, replica_groups=replica,
                ins=[ag_w_in.ap()[GU_SL:SL_ELEMS].opt()], outs=[ag_d_out.ap().opt()]))
        P1E_ST = sst.n
        P1_CC_GU = scc.n - 1
        P1_CC2 = scc.n

        def gu_page(m, i1):
            r, iloc = i1 >> 3, i1 & 7
            return r * GU_SL + PAGE_OFF[m] + iloc * PG_GU

        def d_page(o, r):
            return r * D_SL + o * PG_D

        xq = bfBIG.rearrange("p (hh c) -> p hh c", c=T)

        def ffn_call(cid, wait_ev):
            CALL_mm0 = smm.n
            L_ev, L_sc, L_mm = sev.n, ssc.n, smm.n

            @block.scalar
            def _(s):
                s.wait_ge(sev.h, wait_ev)
                for t in range(NT):
                    xt = xres[:, t * D:(t + 1) * D]
                    ssc.inc(s.activation(scrC[:, 0:2048], xt, AF.Square))
                    s.wait_ge(sev.h, L_ev + 3 * t + 1)
                    ssc.inc(s.activation(colp[:, 8:9], colp[:, 6:7], AF.Sqrt))
                    s.wait_ge(ssc.h, ssc.n)
                    s.wait_ge(sev.h, L_ev + 3 * t + 2)
                    s.activation(scrC[:, 0:2048], xt, AF.Identity, bias=colp[:, 7:8])
                    ssc.inc(s.activation(scr[:, 0:2048], scrC[:, 0:2048], AF.Identity, scale=colp[:, 15:16]))
                    for b in range(4):
                        gb = 4 * t + b
                        s.wait_ge(smm.h, L_mm + gb + 1)
                        bank = banks[4 + (gb % 2)]
                        dst = xq[:, 4 * b:4 * b + 4, t * 128:t * 128 + 128]
                        ssc.inc(s.activation(dst, bank.rearrange("p (h c) -> p h c", c=128), AF.Identity))

            @block.vector
            def _(v):
                for t in range(NT):
                    xt = xres[:, t * D:(t + 1) * D]
                    v.wait_ge(ssc.h, L_sc + 7 * t + 1)
                    VS(v, v.tensor_reduce(colp[:, 0:1], xt, AX, ALU.add))
                    VS(v, v.tensor_reduce(colp[:, 1:2], scrC[:, 0:2048], AX, ALU.add))
                    VS(v, v.tensor_scalar_mul(colp[:, 2:3], colp[:, 0:1], 1.0 / D))
                    VS(v, v.tensor_scalar_mul(colp[:, 3:4], colp[:, 1:2], 1.0 / D))
                    VS(v, v.tensor_tensor(colp[:, 4:5], colp[:, 2:3], colp[:, 2:3], ALU.mult))
                    VS(v, v.scalar_tensor_tensor(colp[:, 5:6], colp[:, 3:4], 1e-5, colp[:, 4:5],
                                                 ALU.add, ALU.subtract))
                    VS(v, v.reciprocal(colp[:, 6:7], colp[:, 5:6]))
                    sev.inc(v.tensor_scalar_mul(colp[:, 7:8], colp[:, 2:3], -1.0))
                    v.wait_ge(sev.h, sev.n)
                    v.wait_ge(ssc.h, L_sc + 7 * t + 2)
                    VS(v, v.tensor_tensor(colp[:, 14:15], colp[:, 8:9], colp[:, 8:9], ALU.mult))
                    VS(v, v.tensor_tensor(colp[:, 14:15], colp[:, 14:15], colp[:, 5:6], ALU.mult))
                    VS(v, v.tensor_scalar(colp[:, 14:15], colp[:, 14:15], -0.5, 1.5, ALU.mult, ALU.add))
                    sev.inc(v.tensor_tensor(colp[:, 15:16], colp[:, 8:9], colp[:, 14:15], ALU.mult))
                    v.wait_ge(sev.h, sev.n)
                    v.wait_ge(ssc.h, L_sc + 7 * t + 3)
                    if t >= 2:
                        v.wait_ge(smm.h, L_mm + 4 * (t - 1))
                    VS(v, v.tensor_reduce(colp[:, 10:11], scr[:, 0:2048], AX, ALU.max, apply_absolute_value=True))
                    VS(v, v.tensor_scalar_max(colp[:, 11:12], colp[:, 10:11], 1e-5))
                    VS(v, v.reciprocal(colp[:, 12:13], colp[:, 11:12]))
                    VS(v, v.tensor_scalar_mul(colp[:, 13:14], colp[:, 12:13], 128.0))
                    VS(v, v.reciprocal(adeq[:, t:t + 1], colp[:, 13:14]))
                    sl = bfSM[:, (t % 2) * 2048:(t % 2) * 2048 + 2048]
                    v.tensor_scalar_mul(scr2[:, 0:2048], scr[:, 0:2048], colp[:, 13:14])
                    v.tensor_scalar_add(iscr[:, 0:2048], scr2[:, 0:2048], 256.0)
                    v.scalar_tensor_tensor(scr[:, 0:2048], scr2[:, 0:2048], 256.0, iscr[:, 0:2048],
                                           ALU.add, ALU.subtract)
                    v.scalar_tensor_tensor(scr2[:, 0:2048], scr[:, 0:2048], 0.5, iscr[:, 0:2048],
                                           ALU.is_ge, ALU.add)
                    sev.inc(v.tensor_scalar(sl, scr2[:, 0:2048], -256.0, 127.0, ALU.add, ALU.min))

            @block.tensor
            def _(tn):
                for t in range(NT):
                    tn.wait_ge(sev.h, L_ev + 3 * t + 3)
                    sl = bfSM[:, (t % 2) * 2048:(t % 2) * 2048 + 2048]
                    for h in range(16):
                        gb = 4 * t + h // 4
                        bank = banks[4 + (gb % 2)]
                        if h % 4 == 0 and gb >= 2:
                            pgb = gb - 2
                            tn.wait_ge(ssc.h, L_sc + 7 * (pgb // 4) + 3 + (pgb % 4) + 1)
                        mm = tn.matmul(bank[:, (h % 4) * 128:(h % 4) * 128 + 128],
                                       sl[:, h * 128:(h + 1) * 128], i128[:, :],
                                       start=True, stop=True)
                        if h % 4 == 3:
                            smm.inc(mm)
            L_done_ev = sev.n

            A_w, A_mm, A_ev, A_sc, A_gp, A_st = sw.n, smm.n, sev.n, ssc.n, sgp.n, sst.n

            def f_mm_deq(g, ts_):
                return A_mm + 68 * g + ts_ + 1

            def f_mm_gi(gi):
                return A_mm + 68 * (gi // 64) + 4 + (gi % 64) + 1

            def f_sc_b(g, ts_):
                return A_sc + 68 * g + ts_ + 1

            def f_sc_silu(gi):
                return A_sc + 68 * (gi // 64) + 4 + (gi % 64) + 1

            def f_ev_deq(g, ts_):
                return A_ev + 132 * g + ts_ + 1

            def f_ev_t0(gi):
                return A_ev + 132 * (gi // 64) + 4 + 2 * (gi % 64) + 1

            def f_ev_u1(gi):
                return A_ev + 132 * (gi // 64) + 4 + 2 * (gi % 64) + 2

            def f_gp(gi):
                return A_gp + gi + 1

            @block.sync
            def _(sync):
                sync.wait_ge(scc.h, P1_CC_GU)
                sync.wait_ge(smm.h, CALL_mm0)
                for gi in range(128):
                    i1 = gi % 64
                    if gi >= 2:
                        sync.wait_ge(smm.h, f_mm_gi(gi - 2))
                    if gi >= 1:
                        sync.wait_ge(sw.h, A_w + 32 * gi)
                    wslot = bfW[:, (gi % 2) * 4096:(gi % 2) * 4096 + 4096]
                    go, uo = gu_page(0, i1), gu_page(1, i1)
                    sw.incd(sync.dma_start(out=wslot[:, 0:2048],
                                           in_=ag_gu_out.ap()[go:go + PG_GU].rearrange("(p c) -> p c", c=2048)))
                    sw.incd(sync.dma_start(out=wslot[:, 2048:4096],
                                           in_=ag_gu_out.ap()[uo:uo + PG_GU].rearrange("(p c) -> p c", c=2048)))

            @block.scalar
            def _(s):
                for g in range(2):
                    for ts_ in range(4):
                        tt = 4 * g + ts_
                        if g == 0 and ts_ == 0:
                            s.wait_ge(sev.h, L_done_ev)
                        if g == 1 and ts_ == 0:
                            s.wait_ge(smm.h, f_mm_deq(0, 3))
                        ssc.inc(s.activation(scrC[:, 0:128], ones[:, 0:128], AF.Identity,
                                             scale=adeq[:, tt:tt + 1]))
                        if ts_ < 3:
                            s.wait_ge(smm.h, f_mm_deq(g, ts_))
                    for i1 in range(64):
                        gi = g * 64 + i1
                        s.wait_ge(sev.h, f_ev_t0(gi))
                        if gi >= 2:
                            s.wait_ge(sgp.h, f_gp(gi - 2))
                        t0slot = scrC[:, 512 + (gi % 2) * 768:512 + (gi % 2) * 768 + 512]
                        oslot = scr[:, (gi % 2) * 2048:(gi % 2) * 2048 + 512]
                        ssc.inc(s.activation(oslot, t0slot, AF.Silu))
                        if gi % 8 == 7:
                            b = gi // 8
                            s.wait_ge(sgp.h, f_gp(gi))
                            if b >= 1:
                                s.wait_ge(sst.h, A_st + 128 * b)
                            for uloc in range(8):
                                u = g * 8 + uloc
                                fsrc = f32P1[:, 0:4096].rearrange(
                                    "p (sl c) -> p sl c", c=512)[:, :, uloc * 64:(uloc + 1) * 64]
                                sst.incd(s.dma_start(out=eprime[u, :, (b % 8) * 8:(b % 8) * 8 + 8, :],
                                                     in_=fsrc))

            @block.gpsimd
            def _(g_):
                for gi in range(128):
                    g_.wait_ge(ssc.h, f_sc_silu(gi))
                    g_.wait_ge(sev.h, f_ev_u1(gi))
                    if gi >= 8:
                        g_.wait_ge(sst.h, A_st + 128 * (gi // 8))
                    oslot = scr[:, (gi % 2) * 2048:(gi % 2) * 2048 + 512]
                    u1slot = scr[:, (gi % 2) * 2048 + 512:(gi % 2) * 2048 + 1024]
                    slot = f32P1[:, (gi % 8) * 512:(gi % 8) * 512 + 512]
                    sgp.inc(g_.tensor_tensor(slot, oslot, u1slot, ALU.mult))

            @block.vector
            def _(v):
                for g in range(2):
                    for ts_ in range(4):
                        v.wait_ge(smm.h, f_mm_deq(g, ts_))
                        sev.inc(v.tensor_copy(scr2[:, ts_ * 128:(ts_ + 1) * 128],
                                              pb6[:, ts_ * 128:(ts_ + 1) * 128]))
                        v.wait_ge(sev.h, sev.n)
                    for i1 in range(64):
                        gi = g * 64 + i1
                        v.wait_ge(smm.h, f_mm_gi(gi))
                        gB = banks[gi % 2]
                        uB = banks[2 + (gi % 2)]
                        t0slot = scrC[:, 512 + (gi % 2) * 768:512 + (gi % 2) * 768 + 512]
                        oslot = scr[:, (gi % 2) * 2048:(gi % 2) * 2048 + 512]
                        u1slot = scr[:, (gi % 2) * 2048 + 512:(gi % 2) * 2048 + 1024]
                        if gi >= 2:
                            v.wait_ge(ssc.h, f_sc_silu(gi - 2))
                        sev.inc(v.scalar_tensor_tensor(t0slot, gB[:, :], colp[:, 44:45],
                                                       scr2[:, 0:512], ALU.mult, ALU.mult))
                        if gi >= 2:
                            v.wait_ge(sgp.h, f_gp(gi - 2))
                        sev.inc(v.scalar_tensor_tensor(u1slot, uB[:, :], colp[:, 45:46],
                                                       scr2[:, 0:512], ALU.mult, ALU.mult))

            @block.tensor
            def _(tn):
                tn.wait_ge(scc.h, P1_CC_GU)
                for g in range(2):
                    for ts_ in range(4):
                        tn.wait_ge(ssc.h, f_sc_b(g, ts_))
                        if g == 1:
                            tn.wait_ge(sev.h, f_ev_deq(0, ts_))
                        smm.inc(tn.matmul(pb6[:, ts_ * 128:(ts_ + 1) * 128], scrC[:, 0:128],
                                          i128f[:, :], start=True, stop=True))
                    for i1 in range(64):
                        gi = g * 64 + i1
                        tn.wait_ge(sw.h, A_w + 32 * (gi + 1))
                        if gi >= 2:
                            tn.wait_ge(sev.h, f_ev_u1(gi - 2))
                        wslot = bfW[:, (gi % 2) * 4096:(gi % 2) * 4096 + 4096]
                        gB = banks[gi % 2]
                        uB = banks[2 + (gi % 2)]
                        for h in range(16):
                            tn.matmul(gB[:, :], wslot[:, h * 128:(h + 1) * 128],
                                      xq[:, h, g * 512:(g + 1) * 512],
                                      start=(h == 0), stop=(h == 15))
                        for h in range(16):
                            mm = tn.matmul(uB[:, :], wslot[:, 2048 + h * 128:2048 + (h + 1) * 128],
                                           xq[:, h, g * 512:(g + 1) * 512],
                                           start=(h == 0), stop=(h == 15))
                        smm.inc(mm)


            A_st_end = A_st + 128 * 16

            B_fb, B_sc, B_ev, B_mm, B_gp, B_w2 = sfb.n, ssc.n, sev.n, smm.n, sgp.n, sw.n
            M, _, _, _, _ = _sched_B(B_sc, B_ev, B_mm, B_gp)
            eq = f32P1[:, 0:4096]
            ewin = f32P1[:, 4096:8192]
            zr = scrC

            @block.sync
            def _(sync):
                sync.wait_ge(sst.h, A_st_end)
                for u in range(N_HU):
                    if u >= 1:
                        sync.wait_ge(sev.h, M["vdone", u - 1])
                        sync.wait_ge(sfb.h, B_fb + 16 * u)
                    src = eprime[u].rearrange("p a b -> p (a b)")
                    sfb.incd(sync.dma_start(out=eq, in_=src))


            @block.scalar
            def _(s):
                for u in range(N_HU):
                    u4 = u % 4
                    s.wait_ge(sfb.h, B_fb + 16 * (u + 1))
                    if u >= 1:
                        s.wait_ge(smm.h, M["r1", u - 1, 7])
                        s.wait_ge(sev.h, M["q", u - 1, 1])
                    ssc.inc(s.copy(
                        ewin.rearrange("p (t i) -> p t i", i=64),
                        eq.rearrange("p (i t) -> p i t", t=64).transpose([0, 2, 1])))
                    for b in range(8):
                        s.wait_ge(smm.h, M["r2", u, b])
                        yB = banks[2 + (b % 2)]
                        ssc.inc(s.activation(eq[:, b * 512:(b + 1) * 512], yB[:, :], AF.Identity))
                    for b in range(8):
                        s.wait_ge(smm.h, M["r3", u, b])
                        if u // 4 >= 1 and b == 0:
                            s.wait_ge(smm.h, M["dn", u // 4 - 1, 15])
                        bank = banks[4 + (b % 2)]
                        dst = bfBIG[:, u4 * 4096 + b * 512: u4 * 4096 + (b + 1) * 512]
                        ssc.inc(s.activation(dst, bank[:, :], AF.Identity))
                    if u % 4 == 3:
                        g4 = u // 4
                        if g4 == 0:
                            s.wait_ge(smm.h, f_mm_gi(127))
                        for o in range(16):
                            if not (g4 == 0 and o == 0):
                                po = (g4, o - 1) if o >= 1 else (g4 - 1, 15)
                                s.wait_ge(smm.h, M["dn", po[0], po[1]])
                                s.wait_ge(sw.h, B_w2 + 128 * (16 * g4 + o))
                            if g4 == 0 and o == 0:
                                s.wait_ge(scc.h, P1_CC2)
                            for r in range(NCORES):
                                off = d_page(o, r)
                                sw.incd(s.dma_start(
                                    out=bfW[:, r * 1024:(r + 1) * 1024],
                                    in_=ag_d_out.ap()[off:off + PG_D].rearrange("(p c) -> p c", c=1024)))
                            if o >= 1:
                                s.wait_ge(smm.h, M["dn", g4, o - 1])
                                if o >= 3:
                                    s.wait_ge(smm.h, M["dntm", g4, o - 3, 1])
                                dB = banks[6 + ((o - 1) % 2)]
                                ssc.inc(s.activation(scr2[:, ((o - 1) % 2) * 256:((o - 1) % 2) * 256 + 256],
                                                     dB[:, 0:256], AF.Identity))
                        s.wait_ge(smm.h, M["dn", g4, 15])
                        s.wait_ge(smm.h, M["dntm", g4, 13, 1])
                        dB = banks[6 + (15 % 2)]
                        ssc.inc(s.activation(scr2[:, (15 % 2) * 256:(15 % 2) * 256 + 256],
                                             dB[:, 0:256], AF.Identity))

            @block.tensor
            def _(tn):
                for u in range(N_HU):
                    u4 = u % 4
                    tn.wait_ge(ssc.h, M["reord", u])
                    if u % 4 == 0 and u >= 4:
                        tn.wait_ge(sev.h, M["resid", u // 4 - 1, 15, 1])
                    def emit_r2(b):
                        tn.wait_ge(sev.h, M["r1ev", u, b])
                        if b >= 2:
                            tn.wait_ge(ssc.h, M["yev", u, b - 2])
                        yB = banks[2 + (b % 2)]
                        smm.inc(tn.matmul(yB[:, :], h64b[:, :], zr[:, (b % 4) * 512:(b % 4) * 512 + 512],
                                          start=True, stop=True))

                    for b in range(8):
                        if b >= 2:
                            tn.wait_ge(sev.h, M["r1ev", u, b - 2])
                        bank = banks[b % 2]
                        for wsub in range(4):
                            w = 4 * b + wsub
                            mm = tn.matmul(bank[0:128, wsub * 128:(wsub + 1) * 128],
                                           ewin[:, w * 128:(w + 1) * 128], h128[:, :],
                                           start=True, stop=True)
                        smm.inc(mm)
                        if b >= 2:
                            emit_r2(b - 2)
                    for b in [6, 7]:
                        emit_r2(b)
                    tn.wait_ge(sev.h, M["mx1", u])
                    smm.inc(tn.matmul(pb6[0:32, 0:128], colp[:, 0:32], i128f[:, :], start=True, stop=True))
                    tn.wait_ge(sev.h, M["chain", u])
                    smm.inc(tn.matmul(pb6[0:2, 128:160], scr2[0:32, 646:648], i128f[0:32, 0:32],
                                      start=True, stop=True))
                    tn.wait_ge(sev.h, M["sTev", u])
                    smm.inc(tn.matmul(pb6[0:128, 160:192], sel2[:, :], srow[:, 0:32],
                                      start=True, stop=True))
                    for b in range(8):
                        tn.wait_ge(sev.h, M["q", u, 1 if b >= 4 else 0])
                        if b >= 2:
                            tn.wait_ge(ssc.h, M["r3ev", u, b - 2])
                        bank = banks[4 + (b % 2)]
                        for ths in range(4):
                            th = 4 * b + ths
                            mm = tn.matmul(bank[0:128, ths * 128:(ths + 1) * 128],
                                           bfSM[:, th * 128:(th + 1) * 128], i128[:, :],
                                           start=True, stop=True)
                        smm.inc(mm)
                    if u % 4 == 3:
                        g4 = u // 4
                        tn.wait_ge(sev.h, M["chain", u])
                        tn.wait_ge(ssc.h, M["r3ev", u, 7])
                        for tb in range(2):
                            for k in range(4):
                                usub, tlo = k // 2, k % 2
                                uu = 4 * g4 + 2 * tb + usub
                                mm = tn.matmul(pb4[0:128, 500 + tb:501 + tb],
                                               p4[:, (2 * usub + tlo) * 128:(2 * usub + tlo) * 128 + 128],
                                               scr2[0:32, 700 + 2 * uu + tlo:701 + 2 * uu + tlo],
                                               start=(k == 0), stop=(k == 3))
                            smm.inc(mm)
                        for o in range(16):
                            tn.wait_ge(sw.h, B_w2 + 128 * (16 * g4 + o + 1))
                            dB = banks[6 + (o % 2)]
                            if o >= 2:
                                tn.wait_ge(ssc.h, M["dnev", g4, o - 2])
                            for i1p in range(64):
                                r, iloc = i1p >> 3, i1p & 7
                                lt = bfW[:, r * 1024 + iloc * 128:r * 1024 + iloc * 128 + 128]
                                rhs = bfBIG.rearrange("p (uu c) -> p uu c", c=4096).rearrange(
                                    "p uu (tt i) -> p uu tt i", i=64)[:, :, :, i1p]
                                mm = tn.matmul(dB[:, 0:256], lt, rhs,
                                               start=(i1p == 0), stop=(i1p == 63))
                            smm.inc(mm)
                            tn.wait_ge(ssc.h, M["dnev", g4, o])
                            if (g4, o) > (0, 1):
                                po = (g4, o - 2) if o >= 2 else (g4 - 1, 14 + o)
                                tn.wait_ge(sev.h, M["resid", po[0], po[1], 1])
                            for tb in range(2):
                                mm = tn.matmul(banks[2 * tb + (o % 2)][:, 0:128],
                                               scr2[:, (o % 2) * 256 + tb * 128:(o % 2) * 256 + tb * 128 + 128],
                                               i128f[:, :], start=True, stop=True)
                                smm.inc(mm)

            @block.vector
            def _(v):
                for u in range(N_HU):
                    for b in range(8):
                        v.wait_ge(smm.h, M["r1", u, b])
                        if b >= 4:
                            v.wait_ge(smm.h, M["r2", u, b - 4])
                        bank = banks[b % 2]
                        sev.inc(v.tensor_copy(zr[:, (b % 4) * 512:(b % 4) * 512 + 512], bank[:, :]))
                    v.wait_ge(ssc.h, M["yev", u, 7])
                    sev.inc(v.tensor_reduce(colp[:, 0:32].rearrange("p (a o) -> p a o", o=1),
                                            eq.rearrange("p (a c) -> p a c", c=128), AX,
                                            ALU.max, apply_absolute_value=True))
                    v.wait_ge(smm.h, M["mx1T", u])
                    sev.inc(v.tensor_copy(scr2[0:32, 512:640], pb6[0:32, 0:128]))
                    v.wait_ge(sev.h, sev.n)
                    VS(v, v.tensor_reduce(scr2[0:32, 640:642].rearrange("p (a o) -> p a o", o=1),
                                          scr2[0:32, 512:640].rearrange("p (a c) -> p a c", c=64), AX,
                                          ALU.max, apply_absolute_value=False))
                    VS(v, v.tensor_scalar_max(scr2[0:32, 642:644], scr2[0:32, 640:642], 1e-5))
                    VS(v, v.reciprocal(scr2[0:32, 644:646], scr2[0:32, 642:644]))
                    VS(v, v.tensor_scalar_mul(scr2[0:32, 646:648], scr2[0:32, 644:646], 128.0))
                    VS(v, v.reciprocal(scr2[0:32, 648:650], scr2[0:32, 646:648]))
                    u4 = u % 4
                    sev.inc(v.tensor_copy(scr2[0:32, 700 + 2 * u:702 + 2 * u], scr2[0:32, 648:650]))
                    v.wait_ge(sev.h, sev.n)
                    v.wait_ge(smm.h, M["sT", u])
                    sev.inc(v.tensor_copy(srow[0:2, 0:32], pb6[0:2, 128:160]))
                    v.wait_ge(sev.h, sev.n)
                    v.wait_ge(smm.h, M["ssm", u])
                    sev.inc(v.tensor_copy(scr2[:, 1024:1056], pb6[:, 160:192]))
                    v.wait_ge(sev.h, sev.n)
                    for th in range(32):
                        sl = ewin[:, th * 128:(th + 1) * 128]
                        mmv = v.tensor_scalar_mul(sl, eq[:, th * 128:(th + 1) * 128],
                                                  scr2[:, 1024 + th:1025 + th])
                    sev.inc(mmv)
                    for half in range(2):
                        hs = slice(half * 2048, (half + 1) * 2048)
                        sev.inc(v.tensor_scalar_add(scr[:, 0:2048], ewin[:, hs], 256.0))
                        sev.inc(v.tensor_copy(iscr[:, 0:2048], scr[:, 0:2048]))
                        sev.inc(v.tensor_tensor(scr[:, 2048:4096], scr[:, 0:2048],
                                                iscr[:, 0:2048], ALU.subtract))
                        sev.inc(v.tensor_single_scalar(ewin[:, hs], scr[:, 2048:4096], 0.5,
                                                       ALU.is_ge))
                        sev.inc(v.tensor_tensor(ewin[:, hs], ewin[:, hs], iscr[:, 0:2048],
                                                ALU.add))
                        sev.inc(v.tensor_scalar(bfSM[:, hs], ewin[:, hs], -256.0, 127.0,
                                                ALU.add, ALU.min))
                    if u % 4 == 3:
                        g4 = u // 4
                        for tb in range(2):
                            v.wait_ge(smm.h, M["dqc", g4, tb])
                            sev.inc(v.tensor_copy(colp[:, 24 + tb:25 + tb], pb4[:, 500 + tb:501 + tb]))
                            v.wait_ge(sev.h, sev.n)
                        for o in range(16):
                            for tb in range(2):
                                v.wait_ge(smm.h, M["dntm", g4, o, tb])
                                tmp = scr2[:, 1088:1216]
                                v.tensor_scalar(tmp, banks[2 * tb + (o % 2)][:, 0:128],
                                                colp[:, 24 + tb:25 + tb],
                                                colp[:, 46:47], ALU.mult, ALU.mult)
                                xslice = xres[:, (2 * g4 + tb) * D + o * 128:(2 * g4 + tb) * D + o * 128 + 128]
                                sev.inc(v.scalar_tensor_tensor(xslice, tmp, 0.5, xslice,
                                                               ALU.mult, ALU.add))

            M_end, B_sc_end, B_ev_end, B_mm_end, B_gp_end = _sched_B(B_sc, B_ev, B_mm, B_gp)
            assert B_sc_end == ssc.n and B_ev_end == sev.n and B_mm_end == smm.n and B_gp_end == sgp.n, (
                "stage B schedule mismatch",
                (B_sc_end, ssc.n), (B_ev_end, sev.n), (B_mm_end, smm.n), (B_gp_end, sgp.n))
            return M["resid", 3, 15, 1]

        def attn_call(wait_ev):
            CALL_mm0 = smm.n
            llv = _ll()
            lamC_f = [float(np.float32(np.exp(llv[h] * CH))) for h in range(HEADS)]
            sScl_f = [float(np.float32(np.exp(llv[h] * 64.0))) for h in range(HEADS)]
            grp = bfBIG.rearrange("p (a c) -> p a c", c=512)
            wsv = bfW.rearrange("p (hh c) -> p hh c", c=512)
            wfv = bfW.rearrange("p (hh c) -> p hh c", c=128)
            xlv = bfBIG.rearrange("p (hh c) -> p hh c", c=512)
            smv = bfSM.rearrange("p (a c) -> p a c", c=512)
            wsrc = {"k": wk_d, "v": wv_d, "q": wq_d}
            cach = {"k": kcache, "v": vcache, "q": qcache}
            B = {}

            def bar(tag):
                B[tag] = (smm.n, sev.n, ssc.n, sgp.n, sw.n, sld.n, sst.n, sfb.n)

            _ckn = [0]

            def ck():
                _ckn[0] += 1
                if ATTN_STAGES is not None and _ckn[0] >= ATTN_STAGES:
                    raise _StopEmit(sev.n)

            @block.sync
            def _(sync):
                sync.wait_ge(sld.h, sld.n)
                sld.incd(sync.dma_start(out=nwrep[:, :], in_=tdkrep[:, :]))
            A0_ld = sld.n

            for g in range(2):
                L_ev, L_sc, L_mm = sev.n, ssc.n, smm.n

                @block.scalar
                def _(s, g=g, L_ev=L_ev, L_mm=L_mm):
                    for t4 in range(4):
                        t = g * 4 + t4
                        xt = xres[:, t * D:(t + 1) * D]
                        if g == 0 and t4 == 0:
                            s.wait_ge(sev.h, wait_ev)
                        if g == 1 and t4 == 0:
                            s.wait_ge(smm.h, B["gate", 0][0])
                            s.wait_ge(sev.h, B[("state", 0)][1])
                        ssc.inc(s.activation(scrC[:, 0:2048], xt, AF.Square))
                        s.wait_ge(sev.h, L_ev + 2 * t4 + 1)
                        ssc.inc(s.activation(colp[:, 8:9], colp[:, 6:7], AF.Sqrt))
                        s.wait_ge(ssc.h, ssc.n)
                        s.wait_ge(sev.h, L_ev + 2 * t4 + 2)
                        s.activation(scrC[:, 0:2048], xt, AF.Identity, bias=colp[:, 7:8])
                        ssc.inc(s.activation(bfSM[:, 0:2048], scrC[:, 0:2048], AF.Identity,
                                             scale=colp[:, 15:16]))
                        for b in range(4):
                            gb = 4 * t4 + b
                            s.wait_ge(smm.h, L_mm + gb + 1)
                            bank = banks[4 + (gb % 2)]
                            dst = xlv[:, 4 * b:4 * b + 4, t4 * 128:t4 * 128 + 128]
                            ssc.inc(s.activation(dst, bank.rearrange("p (h c) -> p h c", c=128),
                                                 AF.Identity))

                @block.vector
                def _(v, g=g, L_sc=L_sc):
                    for t4 in range(4):
                        t = g * 4 + t4
                        xt = xres[:, t * D:(t + 1) * D]
                        v.wait_ge(ssc.h, L_sc + 7 * t4 + 1)
                        VS(v, v.tensor_reduce(colp[:, 0:1], xt, AX, ALU.add))
                        VS(v, v.tensor_reduce(colp[:, 1:2], scrC[:, 0:2048], AX, ALU.add))
                        VS(v, v.tensor_scalar_mul(colp[:, 2:3], colp[:, 0:1], 1.0 / D))
                        VS(v, v.tensor_scalar_mul(colp[:, 3:4], colp[:, 1:2], 1.0 / D))
                        VS(v, v.tensor_tensor(colp[:, 4:5], colp[:, 2:3], colp[:, 2:3], ALU.mult))
                        VS(v, v.scalar_tensor_tensor(colp[:, 5:6], colp[:, 3:4], 1e-5, colp[:, 4:5],
                                                     ALU.add, ALU.subtract))
                        VS(v, v.reciprocal(colp[:, 6:7], colp[:, 5:6]))
                        sev.inc(v.tensor_scalar_mul(colp[:, 7:8], colp[:, 2:3], -1.0))
                        v.wait_ge(sev.h, sev.n)
                        v.wait_ge(ssc.h, L_sc + 7 * t4 + 2)
                        VS(v, v.tensor_tensor(colp[:, 14:15], colp[:, 8:9], colp[:, 8:9], ALU.mult))
                        VS(v, v.tensor_tensor(colp[:, 14:15], colp[:, 14:15], colp[:, 5:6], ALU.mult))
                        VS(v, v.tensor_scalar(colp[:, 14:15], colp[:, 14:15], -0.5, 1.5, ALU.mult, ALU.add))
                        sev.inc(v.tensor_tensor(colp[:, 15:16], colp[:, 8:9], colp[:, 14:15], ALU.mult))
                        v.wait_ge(sev.h, sev.n)

                @block.tensor
                def _(tn, g=g, L_sc=L_sc):
                    for t4 in range(4):
                        tn.wait_ge(ssc.h, L_sc + 7 * t4 + 3)
                        for h in range(16):
                            gb = 4 * t4 + h // 4
                            bank = banks[4 + (gb % 2)]
                            if h % 4 == 0 and gb >= 2:
                                pgb = gb - 2
                                tn.wait_ge(ssc.h, L_sc + 7 * (pgb // 4) + 3 + (pgb % 4) + 1)
                            mm = tn.matmul(bank[:, (h % 4) * 128:(h % 4) * 128 + 128],
                                           bfSM[:, h * 128:(h + 1) * 128], i128[:, :],
                                           start=True, stop=True)
                            if h % 4 == 3:
                                smm.inc(mm)
                bar(("ln", g))
                ck()

                def proj(pj, act, extra_st_wait=None, g=g):
                    P_g, P_mm, P_sc = sgst.n, smm.n, ssc.n

                    @block.gpsimd
                    def _(g_, pj=pj, P_mm=P_mm):
                        g_.wait_ge(smm.h, P_mm)
                        for ob in range(4):
                            if ob >= 1:
                                g_.wait_ge(smm.h, P_mm + 4 * ob)
                            src = wsrc[pj].rearrange("(hh p) o -> p hh o", p=128)[:, :, ob * 512:(ob + 1) * 512]
                            sgst.incd(g_.dma_start(out=wsv[:, 0:16, :], in_=src))

                    @block.tensor
                    def _(tn, P_g=P_g, P_sc=P_sc, extra_st_wait=extra_st_wait):
                        tn.wait_ge(ssc.h, P_sc)
                        if extra_st_wait is not None:
                            tn.wait_ge(sst.h, extra_st_wait)
                        for ob in range(4):
                            tn.wait_ge(sgst.h, P_g + 16 * (ob + 1))
                            for t4 in range(4):
                                idx = 4 * ob + t4
                                bank = banks[4 + (idx % 2)]
                                if idx >= 2:
                                    tn.wait_ge(ssc.h, P_sc + idx - 1)
                                for h in range(16):
                                    mm = tn.matmul(bank[:, :], xlv[:, h, t4 * 128:t4 * 128 + 128],
                                                   wsv[:, h, :], start=(h == 0), stop=(h == 15))
                                smm.inc(mm)

                    @block.scalar
                    def _(s, P_mm=P_mm, act=act):
                        for ob in range(4):
                            for t4 in range(4):
                                idx = 4 * ob + t4
                                s.wait_ge(smm.h, P_mm + idx + 1)
                                bank = banks[4 + (idx % 2)]
                                ssc.inc(s.activation(grp[:, 16 + t4 * 4 + ob, :], bank[:, :], act))
                    bar(("proj", g, pj))

                def rot(pj, write_kd, g=g):
                    R_ld, R_ev, R_st = sld.n, sev.n, sst.n

                    def stores(sync, t4, pj, write_kd, g):
                        t = g * 4 + t4
                        kt = bfBIG[:, 8192 + t4 * 2048:8192 + (t4 + 1) * 2048]
                        sst.incd(sync.dma_start(
                            out=cach[pj].rearrange("(a p) d -> a p d", p=128)[t], in_=kt))
                        if write_kd:
                            kd = bfSM[:, (t4 % 2) * 2048:(t4 % 2) * 2048 + 2048]
                            sst.incd(sync.dma_start(
                                out=kdcache.rearrange("(a p) d -> a p d", p=128)[t], in_=kd))

                    @block.sync
                    def _(sync, pj=pj, R_ev=R_ev, g=g, write_kd=write_kd):
                        sync.wait_ge(ssc.h, B[("proj", g, pj)][2])
                        for t4 in range(4):
                            t = g * 4 + t4
                            if t4 >= 1:
                                sync.wait_ge(sev.h, R_ev + 4 * t4)
                                stores(sync, t4 - 1, pj, write_kd, g)
                            sld.incd(sync.dma_start(out=scr[:, 0:1024],
                                                    in_=tcos.rearrange("(a p) j -> a p j", p=128)[t]))
                            sld.incd(sync.dma_start(out=scr[:, 1024:2048],
                                                    in_=tsin.rearrange("(a p) j -> a p j", p=128)[t]))
                        sync.wait_ge(sev.h, R_ev + 16)
                        stores(sync, 3, pj, write_kd, g)

                    @block.vector
                    def _(v, R_ld=R_ld, R_st=R_st, g=g, write_kd=write_kd):
                        v.wait_ge(smm.h, B[("ln", g)][0])
                        for t4 in range(4):
                            v.wait_ge(sld.h, R_ld + 32 * (t4 + 1))
                            kt = bfBIG[:, 8192 + t4 * 2048:8192 + (t4 + 1) * 2048]
                            ktv = kt.rearrange("p (h two j) -> p h two j", two=2, j=64)
                            x1, x2 = ktv[:, :, 0, :], ktv[:, :, 1, :]
                            cosr = scr[:, 0:1024].rearrange("p (h j) -> p h j", j=64)
                            sinr = scr[:, 1024:2048].rearrange("p (h j) -> p h j", j=64)
                            ra = scr2[:, 0:1024].rearrange("p (h j) -> p h j", j=64)
                            rb = scr2[:, 1024:2048].rearrange("p (h j) -> p h j", j=64)
                            rc = scrC[:, 0:1024].rearrange("p (h j) -> p h j", j=64)
                            rd = scrC[:, 1024:2048].rearrange("p (h j) -> p h j", j=64)
                            v.tensor_tensor(ra, x1, cosr, ALU.mult)
                            v.tensor_tensor(rb, x2, sinr, ALU.mult)
                            v.tensor_tensor(rc, x1, sinr, ALU.mult)
                            sev.inc(v.tensor_tensor(rd, x2, cosr, ALU.mult))
                            v.wait_ge(sev.h, sev.n)
                            sev.inc(v.tensor_tensor(x1, ra, rb, ALU.subtract))
                            v.wait_ge(sev.h, sev.n)
                            sev.inc(v.tensor_tensor(x2, rc, rd, ALU.add))
                            v.wait_ge(sev.h, sev.n)
                            if write_kd:
                                if t4 >= 2:
                                    v.wait_ge(sst.h, R_st + 2 * (t4 - 1))
                                kdd = bfSM[:, (t4 % 2) * 2048:(t4 % 2) * 2048 + 2048]
                                sev.inc(v.tensor_tensor(kdd, kt, nwrep[:, :], ALU.mult))
                                v.wait_ge(sev.h, sev.n)
                            else:
                                sev.inc(v.memset(colp[:, 33:34], 0.0))
                    bar(("rot", g, pj))

                proj("k", AF.Silu)
                ck()
                rot("k", True)
                ck()
                proj("v", AF.Identity, extra_st_wait=B[("rot", g, "k")][6])
                ck()

                F_fb, S_mm, S_ev = sfb.n, smm.n, sev.n

                @block.sync
                def _(sync, g=g, F_fb=F_fb, S_mm=S_mm):
                    sync.wait_ge(sst.h, B[("rot", g, "k")][6])
                    for t4 in range(4):
                        t = g * 4 + t4
                        if t4 >= 2:
                            sync.wait_ge(smm.h, S_mm + 16 * (t4 - 1))
                        dst = bfSM[:, (t4 % 2) * 2048:(t4 % 2) * 2048 + 2048]
                        sfb.incd(sync.dma_start(
                            out=dst, in_=kdcache.rearrange("(a p) d -> a p d", p=128)[t]))

                S_sc = ssc.n

                @block.tensor
                def _(tn, g=g, F_fb=F_fb, S_sc=S_sc):
                    for t4 in range(4):
                        tn.wait_ge(sfb.h, F_fb + 16 * (t4 + 1))
                        for h in range(16):
                            s_ = 16 * t4 + h
                            if s_ >= 4:
                                tn.wait_ge(ssc.h, S_sc + s_ - 3)
                            kdd = bfSM[:, (t4 % 2) * 2048 + h * 128:(t4 % 2) * 2048 + h * 128 + 128]
                            vt = bfBIG[:, 8192 + t4 * 2048 + h * 128:8192 + t4 * 2048 + h * 128 + 128]
                            smm.inc(tn.matmul(banks[h % 4][:, (h // 4) * 128:(h // 4) * 128 + 128],
                                              kdd, vt, start=True, stop=True))

                @block.scalar
                def _(s, g=g, S_mm=S_mm, S_ev=S_ev):
                    for t4 in range(4):
                        for h in range(16):
                            s.wait_ge(smm.h, S_mm + 16 * t4 + h + 1)
                            if t4 >= 1:
                                s.wait_ge(sev.h, S_ev + 16 * (t4 - 1) + h + 1)
                            bk = banks[h % 4][:, (h // 4) * 128:(h // 4) * 128 + 128]
                            ssc.inc(s.activation(scrC[:, h * 128:(h + 1) * 128], bk, AF.Identity))

                @block.vector
                def _(v, g=g, S_sc=S_sc):
                    v.wait_ge(ssc.h, B[("proj", g, "v")][2])
                    for t4 in range(4):
                        c = g * 4 + t4
                        for h in range(16):
                            v.wait_ge(ssc.h, S_sc + 16 * t4 + h + 1)
                            dst = f32P1[:, 2048 + h * 128:2048 + (h + 1) * 128]
                            sa = scrC[:, h * 128:(h + 1) * 128]
                            if c == 0:
                                sev.inc(v.tensor_copy(dst, sa))
                            else:
                                sev.inc(v.scalar_tensor_tensor(dst, dst, lamC_f[h], sa,
                                                               ALU.mult, ALU.add))
                            v.wait_ge(sev.h, sev.n)
                bar(("state", g))
                ck()

                @block.sync
                def _(sync, g=g):
                    sync.wait_ge(smm.h, B[("state", g)][0])
                    for t4 in range(4):
                        t = g * 4 + t4
                        vt = bfBIG[:, 8192 + t4 * 2048:8192 + (t4 + 1) * 2048]
                        sst.incd(sync.dma_start(
                            out=vcache.rearrange("(a p) d -> a p d", p=128)[t], in_=vt))
                bar(("vst", g))
                ck()

                proj("q", AF.Silu, extra_st_wait=B[("vst", g)][6])
                ck()
                rot("q", False)
                ck()

                G_g, G_mm, G_sc = sgst.n, smm.n, ssc.n

                @block.gpsimd
                def _(g_, g=g):
                    g_.wait_ge(smm.h, B[("proj", g, "q")][0])
                    sgst.incd(g_.dma_start(out=wfv[:, 0:16, :],
                                           in_=wg1_d.rearrange("(hh p) o -> p hh o", p=128)))
                    sgst.incd(g_.dma_start(out=bfSM.rearrange("p (a c) -> p a c", c=2048)[:, 1, :],
                                           in_=wg2_d[:, :]))

                @block.tensor
                def _(tn, G_g=G_g, G_sc=G_sc):
                    tn.wait_ge(sgst.h, G_g + 16)
                    tn.wait_ge(ssc.h, G_sc)
                    for h in range(16):
                        mm = tn.matmul(pb6[:, 0:512], wfv[:, h, :],
                                       xlv[:, h, 0:512], start=(h == 0), stop=(h == 15))
                    smm.inc(mm)
                    tn.wait_ge(sgst.h, G_g + 32)
                    tn.wait_ge(ssc.h, G_sc + 1)
                    for t4 in range(4):
                        for ob in range(4):
                            idx = 4 * t4 + ob
                            bank = banks[4 + (idx % 2)]
                            if idx >= 2:
                                tn.wait_ge(ssc.h, G_sc + idx)
                            smm.inc(tn.matmul(bank[:, :], bfSM[:, t4 * 128:t4 * 128 + 128],
                                              smv[:, 4 + ob, :], start=True, stop=True))

                @block.scalar
                def _(s, g=g, G_mm=G_mm):
                    s.wait_ge(sev.h, B[("rot", g, "q")][1])
                    s.wait_ge(smm.h, G_mm + 1)
                    ssc.inc(s.activation(bfSM[:, 0:512], pb6[:, 0:512], AF.Identity))
                    for t4 in range(4):
                        t = g * 4 + t4
                        for ob in range(4):
                            idx = 4 * t4 + ob
                            s.wait_ge(smm.h, G_mm + 1 + idx + 1)
                            bank = banks[4 + (idx % 2)]
                            ssc.inc(s.activation(scr2[:, ob * 512:(ob + 1) * 512], bank[:, :], AF.Sigmoid))
                        s.wait_ge(sst.h, sst.n)
                        sst.incd(s.dma_start(out=gcache.rearrange("(a p) d -> a p d", p=128)[t],
                                             in_=scr2[:, 0:2048]))
                        s.wait_ge(sst.h, sst.n)
                bar(("gate", g))
                ck()

            @block.sync
            def _(sync):
                sync.wait_ge(sev.h, B[("state", 1)][1])
                sync.wait_ge(sst.h, sst.n)
                for b in range(4):
                    dst = ag_st_in.rearrange("(hh pp) c -> pp hh c", pp=128)[:, 4 * b:4 * b + 4, :]
                    srcv = f32P1.rearrange("p (a c) -> p a c", c=128)[:, 16 + 4 * b:16 + 4 * b + 4, :]
                    sst.incd(sync.dma_start(out=dst, in_=srcv))
            E_st2 = sst.n

            @block.gpsimd
            def _(g_):
                g_.wait_ge(sst.h, E_st2)
                scc.inc(g_.collective_compute("AllGather", ALU.bypass, replica_groups=replica,
                                              ins=[ag_st_in.ap().opt()], outs=[ag_st_out.ap().opt()]))
            AT_CC = scc.n

            C_ld, C_ev = sld.n, sev.n

            @block.sync
            def _(sync):
                sync.wait_ge(scc.h, AT_CC)
                for cp in range(NCORES):
                    if cp >= 1:
                        sync.wait_ge(sev.h, C_ev + 1 + 16 * cp)
                    src = ag_st_out.rearrange("(cc hh pp) c -> cc pp hh c", pp=128, hh=HEADS)[cp]
                    sld.incd(sync.dma_start(out=scrC.rearrange("p (hh c) -> p hh c", c=128), in_=src))

            @block.vector
            def _(v):
                v.wait_ge(scc.h, AT_CC)
                sev.inc(v.memset(f32P1[:, 2048:4096], 0.0))
                for cp in range(NCORES):
                    v.wait_ge(sld.h, C_ld + 16 * (cp + 1))
                    for h in range(16):
                        sev.inc(v.scalar_tensor_tensor(
                            f32P1[:, 2048 + h * 128:2048 + h * 128 + 128],
                            scrC[:, h * 128:(h + 1) * 128], lamt[:, cp * 16 + h:cp * 16 + h + 1],
                            f32P1[:, 2048 + h * 128:2048 + h * 128 + 128], ALU.mult, ALU.add))
                    v.wait_ge(sev.h, sev.n)
            bar("sinit")
            ck()

            @block.sync
            def _(sync):
                sync.wait_ge(sld.h, sld.n)
                sld.incd(sync.dma_start(
                    out=f32P1.rearrange("p (a c) -> p a c", c=128)[:, 32:48, :],
                    in_=tdmask.rearrange("(hh s) t -> s hh t", s=128)))
                sync.wait_ge(sld.h, sld.n)
                sld.incd(sync.dma_start(out=nwrep[:, :], in_=tnwrep[:, :]))
            P2_ld0 = sld.n

            prev_ev = [None]
            for c in range(8):
                CL_ld = sfb.n

                @block.sync
                def _(sync, c=c, CL_ld=CL_ld):
                    if c == 0:
                        sync.wait_ge(smm.h, B[("gate", 1)][0])
                        sync.wait_ge(sld.h, P2_ld0)
                    else:
                        sync.wait_ge(smm.h, B[("scan", c - 1)][0])
                    for j, nm in enumerate(["q", "k", "kd", "v"]):
                        srcd = kdcache if nm == "kd" else cach[nm]
                        sfb.incd(sync.dma_start(out=bfBIG[:, j * 2048:(j + 1) * 2048],
                                                in_=srcd.rearrange("(a p) d -> a p d", p=128)[c]))

                H_mm, H_sc, H_ev = smm.n, ssc.n, sev.n
                P_EV = prev_ev[0]

                @block.tensor
                def _(tn, c=c, CL_ld=CL_ld, H_mm=H_mm, H_sc=H_sc, H_ev=H_ev):
                    tn.wait_ge(sfb.h, CL_ld + 64)
                    if c != 0:
                        tn.wait_ge(sev.h, B[("scan", c - 1)][1])
                        tn.wait_ge(ssc.h, B[("scan", c - 1)][2])
                    for h in range(16):
                        pq = pb4 if h % 2 == 0 else pb5
                        po = pb2 if h % 2 == 0 else pb3
                        scb = pb6 if h % 2 == 0 else pb7
                        if h >= 2:
                            tn.wait_ge(ssc.h, H_sc + 3 * (h - 2) + 1)
                        qs = bfBIG[:, h * 128:h * 128 + 128]
                        ks = bfBIG[:, 2048 + h * 128:2048 + h * 128 + 128]
                        tn.matmul(pq[:, 0:128], qs, i128[:, :], start=True, stop=True)
                        smm.inc(tn.matmul(pq[:, 128:256], ks, i128[:, :], start=True, stop=True))
                        tn.wait_ge(ssc.h, H_sc + 3 * h + 1)
                        if h >= 2:
                            tn.wait_ge(sev.h, H_ev + 2 * (h - 2) + 1)
                        smm.inc(tn.matmul(scb[:, 0:128],
                                          bfSM[:, 256:384], bfSM[:, 128:256], start=True, stop=True))
                        tn.wait_ge(ssc.h, H_sc + 3 * h + 2)
                        if h >= 2:
                            tn.wait_ge(ssc.h, H_sc + 3 * (h - 2) + 3)
                        tn.matmul(po[:, 0:128],
                                  bfSM[:, 128:256], bfSM[:, 384:512], start=True, stop=False)
                        tn.wait_ge(sev.h, H_ev + 2 * h + 1)
                        vt = bfBIG[:, 6144 + h * 128:6144 + h * 128 + 128]
                        smm.inc(tn.matmul(po[:, 0:128],
                                          bfSM[:, 0:128], vt, start=False, stop=True))
                        kdd = bfBIG[:, 4096 + h * 128:4096 + h * 128 + 128]
                        if h >= 2:
                            tn.wait_ge(sev.h, H_ev + 2 * (h - 2) + 2)
                        smm.inc(tn.matmul(banks[h % 2][:, 0:128], kdd, vt, start=True, stop=True))

                @block.scalar
                def _(s, c=c, H_mm=H_mm, P_EV=P_EV):
                    for h in range(16):
                        pq = pb4 if h % 2 == 0 else pb5
                        po = pb2 if h % 2 == 0 else pb3
                        s.wait_ge(smm.h, H_mm + 4 * h + 1)
                        if h >= 1:
                            s.wait_ge(smm.h, H_mm + 4 * (h - 1) + 3)
                        s.activation(bfSM[:, 128:256], pq[:, 0:128], AF.Identity)
                        ssc.inc(s.activation(bfSM[:, 256:384], pq[:, 128:256], AF.Identity))
                        if c == 0:
                            s.wait_ge(sev.h, B["sinit"][1])
                        else:
                            s.wait_ge(sev.h, P_EV + 2 * h + 2)
                        ssc.inc(s.activation(bfSM[:, 384:512],
                                             f32P1[:, 2048 + h * 128:2048 + (h + 1) * 128],
                                             AF.Identity, scale=sScl_f[h]))
                        s.wait_ge(smm.h, H_mm + 4 * h + 3)
                        ssc.inc(s.activation(f32P1[:, h * 128:(h + 1) * 128],
                                             po[:, 0:128],
                                             AF.Identity, scale=dqc[:, h:h + 1]))

                @block.vector
                def _(v, c=c, H_mm=H_mm, H_sc=H_sc):
                    if c == 0:
                        v.wait_ge(sld.h, P2_ld0)
                    for h in range(16):
                        v.wait_ge(smm.h, H_mm + 4 * h + 2)
                        dm = f32P1[:, 4096 + h * 128:4096 + h * 128 + 128]
                        scb = pb6 if h % 2 == 0 else pb7
                        sev.inc(v.tensor_tensor(bfSM[:, 0:128], scb[:, 0:128], dm, ALU.mult))
                        v.wait_ge(sev.h, sev.n)
                        v.wait_ge(smm.h, H_mm + 4 * h + 4)
                        v.wait_ge(ssc.h, H_sc + 3 * h + 2)
                        dst = f32P1[:, 2048 + h * 128:2048 + (h + 1) * 128]
                        sev.inc(v.scalar_tensor_tensor(
                            dst, dst, lamC_f[h], banks[h % 2][:, 0:128],
                            ALU.mult, ALU.add))
                        v.wait_ge(sev.h, sev.n)
                bar(("scan", c))
                prev_ev[0] = H_ev

                GA_ld, GA_ev, GA_sc = sld.n, sev.n, ssc.n

                @block.sync
                def _(sync, c=c):
                    sync.wait_ge(sld.h, sld.n)
                    sld.incd(sync.dma_start(out=scr2[:, 0:2048],
                                            in_=gcache.rearrange("(a p) d -> a p d", p=128)[c]))

                @block.vector
                def _(v, c=c, GA_ld=GA_ld, GA_sc=GA_sc):
                    v.wait_ge(sld.h, GA_ld + 16)
                    v.wait_ge(smm.h, B[("scan", c)][0])
                    v.wait_ge(ssc.h, B[("scan", c)][2])
                    sev.inc(v.tensor_tensor(scr[:, 0:2048], f32P1[:, 0:2048], scr2[:, 0:2048], ALU.mult))
                    v.wait_ge(sev.h, sev.n)
                    v.wait_ge(ssc.h, GA_sc + 1)
                    VS(v, v.tensor_reduce(colp[:, 0:16].rearrange("p (a o) -> p a o", o=1),
                                          scrC.rearrange("p (a c) -> p a c", c=128), AX, ALU.add))
                    VS(v, v.tensor_scalar(colp[:, 16:32], colp[:, 0:16], 1.0 / 128.0, 1e-6,
                                          ALU.mult, ALU.add))
                    sev.inc(v.reciprocal(colp[:, 48:64], colp[:, 16:32]))
                    v.wait_ge(sev.h, sev.n)
                    v.wait_ge(ssc.h, GA_sc + 2)
                    VS(v, v.tensor_tensor(colp[:, 0:16], colp[:, 48:64], colp[:, 48:64], ALU.mult))
                    VS(v, v.tensor_tensor(colp[:, 0:16], colp[:, 0:16], colp[:, 16:32], ALU.mult))
                    VS(v, v.tensor_scalar(colp[:, 0:16], colp[:, 0:16], -0.5, 1.5, ALU.mult, ALU.add))
                    VS(v, v.tensor_tensor(colp[:, 48:64], colp[:, 48:64], colp[:, 0:16], ALU.mult))
                    for h in range(16):
                        v.tensor_scalar_mul(scr.rearrange("p (a cc) -> p a cc", cc=128)[:, h, :],
                                            scr.rearrange("p (a cc) -> p a cc", cc=128)[:, h, :],
                                            colp[:, 48 + h:49 + h])
                    v.wait_ge(svs.h, svs.n)
                    sev.inc(v.tensor_tensor(bfSM[:, 0:2048], scr[:, 0:2048], nwrep[:, :], ALU.mult))
                    v.wait_ge(sev.h, sev.n)

                @block.scalar
                def _(s, GA_ev=GA_ev):
                    s.wait_ge(sev.h, GA_ev + 1)
                    ssc.inc(s.activation(scrC[:, 0:2048], scr[:, 0:2048], AF.Square))
                    s.wait_ge(sev.h, GA_ev + 2)
                    ssc.inc(s.activation(colp[:, 48:64], colp[:, 48:64], AF.Sqrt))
                    s.wait_ge(ssc.h, ssc.n)
                bar(("ga", c))

                @block.sync
                def _(sync, c=c):
                    sync.wait_ge(sev.h, B[("ga", c)][1])
                    sync.wait_ge(sst.h, sst.n)
                    sst.incd(sync.dma_start(out=oncache.rearrange("(a p) d -> a p d", p=128)[c],
                                            in_=bfSM[:, 0:2048]))
                bar(("sc2", c))
                ck()

            A11_mm0, A11_sc0, A11_ev0, A11_g0, A11_fb0 = smm.n, ssc.n, sev.n, sgst.n, sfb.n

            @block.gpsimd
            def _(g_):
                g_.wait_ge(smm.h, A11_mm0)
                for ob in range(4):
                    if ob >= 1:
                        g_.wait_ge(smm.h, A11_mm0 + 40 * ob)
                    src = wo_d.rearrange("(hh p) o -> p hh o", p=128)[:, :, ob * 512:(ob + 1) * 512]
                    sgst.incd(g_.dma_start(out=wsv[:, 0:16, :], in_=src))

            @block.sync
            def _(sync):
                sync.wait_ge(sst.h, B[("sc2", 7)][6])
                for ob in range(4):
                    for t in range(8):
                        it = 8 * ob + t
                        if it >= 2:
                            sync.wait_ge(smm.h, A11_mm0 + 5 * (it - 1))
                        dst = bfSM[:, (t % 2) * 2048:(t % 2) * 2048 + 2048]
                        sfb.incd(sync.dma_start(out=dst,
                                                in_=oncache.rearrange("(a p) d -> a p d", p=128)[t]))

            @block.tensor
            def _(tn):
                tn.wait_ge(ssc.h, A11_sc0)
                tn.wait_ge(sev.h, B[("sc2", 7)][1])
                for ob in range(4):
                    tn.wait_ge(sgst.h, A11_g0 + 16 * (ob + 1))
                    for t in range(8):
                        it = 8 * ob + t
                        tn.wait_ge(sfb.h, A11_fb0 + 16 * (it + 1))
                        ot = bfSM[:, (t % 2) * 2048:(t % 2) * 2048 + 2048]
                        if it >= 1:
                            tn.wait_ge(ssc.h, A11_sc0 + 4 * it)
                        for ec in range(16):
                            gb = ec // 4
                            bank = banks[4 + (gb % 2)]
                            if ec % 4 == 0 and gb >= 2:
                                tn.wait_ge(ssc.h, A11_sc0 + 4 * it + gb - 1)
                            mm = tn.matmul(bank[:, (ec % 4) * 128:(ec % 4) * 128 + 128],
                                           ot[:, ec * 128:ec * 128 + 128], i128[:, :],
                                           start=True, stop=True)
                            if ec % 4 == 3:
                                smm.inc(mm)
                        tn.wait_ge(ssc.h, A11_sc0 + 4 * it + 4)
                        if it >= 2:
                            tn.wait_ge(sev.h, A11_ev0 + it - 1)
                        accb = banks[6 + (it % 2)]
                        for ec in range(16):
                            mm = tn.matmul(accb[:, 0:512], bfBIG[:, ec * 128:ec * 128 + 128],
                                           wsv[:, ec, :], start=(ec == 0), stop=(ec == 15))
                        smm.inc(mm)

            @block.scalar
            def _(s):
                for ob in range(4):
                    for t in range(8):
                        it = 8 * ob + t
                        for gb in range(4):
                            s.wait_ge(smm.h, A11_mm0 + 5 * it + gb + 1)
                            if it >= 1 and gb == 0:
                                s.wait_ge(smm.h, A11_mm0 + 5 * it)
                            bank = banks[4 + (gb % 2)]
                            ssc.inc(s.activation(bfBIG[:, gb * 512:(gb + 1) * 512], bank[:, :],
                                                 AF.Identity))

            @block.vector
            def _(v):
                for ob in range(4):
                    for t in range(8):
                        it = 8 * ob + t
                        v.wait_ge(smm.h, A11_mm0 + 5 * it + 5)
                        accb = banks[6 + (it % 2)]
                        xsl = xres[:, t * D + ob * 512:t * D + ob * 512 + 512]
                        sev.inc(v.tensor_tensor(xsl, accb[:, 0:512], xsl, ALU.add))
            return A11_ev0 + 32

        last_ev = P0_EV
        cid = 0
        for p in parts:
            if p in ("ffn1", "ffn2"):
                last_ev = ffn_call(cid, last_ev)
                cid += 1
            elif p == "attn":
                try:
                    last_ev = attn_call(last_ev)
                except _StopEmit as e:
                    last_ev = e.ev
            else:
                raise ValueError(p)

        @block.sync
        def _(sync):
            sync.wait_ge(sev.h, last_ev)
            for t in range(NT):
                sst.incd(sync.dma_start(out=y_out.rearrange("(a p) d -> a p d", p=128)[t, :, :],
                                        in_=xres[:, t * D:(t + 1) * D]))
            if DEBUG_EPRIME:
                dbv = dbg_out.rearrange("(u p) c -> u p c", p=128)
                for u in range(N_HU):
                    sst.incd(sync.dma_start(out=dbv[u], in_=eprime[u].rearrange("p a b -> p (a b)")))
                sst.incd(sync.dma_start(out=dbg_sb[:, 0:8192], in_=f32P1[:, :]))
                sst.incd(sync.dma_start(out=dbg_sb[:, 8192:10240], in_=scr2[:, :]))
                sst.incd(sync.dma_start(out=dbg_sb[:, 10240:10304], in_=colp[:, :]))
                sst.incd(sync.dma_start(out=dbg_sb[:, 10304:10312], in_=adeq[:, :]))
                sst.incd(sync.dma_start(out=dbg_bf[:, :], in_=bfBIG[:, :]))
            sync.wait_ge(sst.h, sst.n)

    return nc


def kernel(**inputs):
    xs = np.split(np.ascontiguousarray(inputs["L0_attn_x"][0]), NCORES, axis=0)
    gs = np.split(np.ascontiguousarray(inputs["L0_ffn_w_gate"]), NCORES, axis=0)
    us = np.split(np.ascontiguousarray(inputs["L0_ffn_w_up"]), NCORES, axis=0)
    ds = np.split(np.ascontiguousarray(inputs["L0_ffn_w_down"]), NCORES, axis=1)
    nc = build_program(PARTS)
    in_maps = []
    for c in range(NCORES):
        tb = host_tables(c)
        in_maps.append({
            "x_in": xs[c],
            "wq": np.ascontiguousarray(inputs["L0_attn_Wq"]),
            "wk": np.ascontiguousarray(inputs["L0_attn_Wk"]),
            "wv": np.ascontiguousarray(inputs["L0_attn_Wv"]),
            "wo": np.ascontiguousarray(inputs["L0_attn_Wo"]),
            "wg1": np.ascontiguousarray(inputs["L0_attn_Wg1"]),
            "wg2": np.ascontiguousarray(inputs["L0_attn_Wg2"]),
            "nrmw": np.ascontiguousarray(inputs["L0_attn_norm_weight"]).reshape(1, D),
            "nrmw_rep": np.tile(np.ascontiguousarray(inputs["L0_attn_norm_weight"]).reshape(1, D), (128, 1)),
            "wg_sl": gs[c], "wu_sl": us[c], "wd_sl": np.ascontiguousarray(ds[c]),
            **tb,
        })
    res = run_bass_kernel_spmd(nc, in_maps, core_ids=list(range(NCORES)))
    y = np.concatenate([res.results[c]["y_out"] for c in range(NCORES)], axis=0)
    return y.reshape(1, T * NCORES, D).astype(np.float32)


# version 05
# speedup vs baseline: 1.4018x; 1.2084x over previous
import contextlib
import numpy as np
import ml_dtypes
import concourse.bass as bass
import concourse.mybir as mybir
from concourse.bass_utils import run_bass_kernel_spmd

F32 = mybir.dt.float32
BF16 = mybir.dt.bfloat16
I32 = mybir.dt.int32
ALU = mybir.AluOpType
AF = mybir.ActivationFunctionType
AX = mybir.AxisListType.X

NCORES = 8
T = 1024
D = 2048
HEADS = 16
HD = 128
CH = 128
NT = T // 128
HU = 64
N_HU = T // HU
NW_ELEMS = float(8192 * 2048)
PG_GU = 128 * 2048
PG_D = 128 * 1024
SL_ELEMS = 8 * PG_GU * 2 + 16 * PG_D

PARTS = ("ffn1", "attn", "ffn2")
ATTN_STAGES = None


class _StopEmit(Exception):
    def __init__(self, ev):
        self.ev = ev
DEBUG_EPRIME = False


def _sylvester(n):
    h = np.array([[1.0]])
    while h.shape[0] < n:
        h = np.block([[h, h], [h, -h]])
    return h.astype(np.float32)


def _ll():
    return -(2.0 ** (-8.0 * (np.arange(HEADS, dtype=np.float64) + 1.0) / HEADS))


def _lamtab(core):
    ll = _ll()
    t = np.zeros((128, 128), np.float32)
    for cp in range(NCORES):
        for h in range(HEADS):
            if cp < core:
                t[:, cp * 16 + h] = np.float32(np.exp(ll[h] * CH * 8.0 * (core - 1 - cp)))
    return t


def host_tables(core):
    H64B = np.kron(np.eye(2, dtype=np.float32), _sylvester(64)).astype(np.float32)
    tg = np.arange(core * T, (core + 1) * T, dtype=np.float64)
    j = np.arange(64, dtype=np.float64)
    theta = 10000.0 ** (-2.0 * j / 128.0)
    ang = tg[:, None] * theta[None, :]
    ll = _ll()
    s = np.arange(CH, dtype=np.float64)
    dmask = np.zeros((HEADS * CH, CH), dtype=np.float32)
    for h in range(HEADS):
        col = np.exp(-ll[h] * (s + 1.0 - 64.0))
        m = (np.arange(CH)[None, :] >= s[:, None])
        dmask[h * CH:(h + 1) * CH, :] = (col[:, None] * m).astype(np.float32)
    dqcol = np.exp(ll[None, :] * (s[:, None] + 1.0 - 64.0)).astype(np.float32)
    dkcol = np.exp(ll[None, :] * (CH - 1.0 - s[:, None])).astype(np.float32)
    P4 = np.zeros((32, 512), dtype=np.float32)
    for usub in range(2):
        for tlo in range(2):
            for th in range(32):
                P4[th, (2 * usub + tlo) * 128 + usub * 64 + 2 * th + tlo] = 1.0
    cos_rep = np.tile(np.cos(ang).astype(np.float32), (1, 16))
    sin_rep = np.tile(np.sin(ang).astype(np.float32), (1, 16))
    dkrep = np.zeros((128, D), np.float32)
    for h in range(HEADS):
        dkrep[:, h * 128:(h + 1) * 128] = dkcol[:, h:h + 1]
    return {
        "H128": _sylvester(128) * np.float32(1.0 / np.sqrt(8192)), "H64B": H64B,
        "I128": np.asarray(np.eye(128, dtype=np.float32).astype(ml_dtypes.bfloat16)),
        "cos_rep": cos_rep, "sin_rep": sin_rep, "dkrep": dkrep,
        "dmask": dmask, "dqcol": dqcol, "dkcol": dkcol, "P4": P4,
        "lamtab": _lamtab(core),
        "SEL2": np.concatenate([np.kron(np.eye(2, dtype=np.float32), np.ones((1, 64), np.float32)), np.zeros((30, 128), np.float32)], 0),
    }


class Sem:
    def __init__(self, h):
        self.h, self.n = h, 0

    def inc(self, instr, k=1):
        instr.then_inc(self.h, k)
        self.n += k
        return instr

    def incd(self, instr):
        return self.inc(instr, 16)


def _sched_B(base_sc, base_ev, base_mm, base_gp):
    M = {}
    sc, ev, mm, gp = base_sc, base_ev, base_mm, base_gp
    for u in range(N_HU):
        sc += 1; M["reord", u] = sc
        for b in range(8):
            mm += 1; M["r1", u, b] = mm
            if b >= 2:
                mm += 1; M["r2", u, b - 2] = mm
        for b in [6, 7]:
            mm += 1; M["r2", u, b] = mm
        for b in range(8):
            ev += 1; M["r1ev", u, b] = ev
        for b in range(8):
            sc += 1; M["yev", u, b] = sc
        ev += 1; M["mx1", u] = ev
        mm += 1; M["mx1T", u] = mm
        ev += 1; M["mx1Tev", u] = ev
        ev += 1; M["chain", u] = ev
        mm += 1; M["sT", u] = mm
        ev += 1; M["sTev", u] = ev
        mm += 1; M["ssm", u] = mm
        ev += 1; M["ssmev", u] = ev
        ev += 1; M["vmult", u] = ev
        for half in range(2):
            ev += 1; M["vp", u, half] = ev
            ev += 1; M["c", u, half] = ev
            ev += 1; M["dd", u, half] = ev
            ev += 1; M["ge", u, half] = ev
            ev += 1; M["r", u, half] = ev
            ev += 1; M["q", u, half] = ev
        M["vdone", u] = ev
        for b in range(8):
            mm += 1; M["r3", u, b] = mm
        for b in range(8):
            sc += 1; M["r3ev", u, b] = sc
        if u % 4 == 3:
            g4 = u // 4
            mm += 1; M["dqc", g4, 0] = mm
            mm += 1; M["dqc", g4, 1] = mm
            ev += 2; M["dqcev", g4] = ev
            for o in range(16):
                mm += 1; M["dn", g4, o] = mm
                sc += 1; M["dnev", g4, o] = sc
                for tb in range(2):
                    mm += 1; M["dntm", g4, o, tb] = mm
                    ev += 1; M["resid", g4, o, tb] = ev
    return M, sc, ev, mm, gp


def build_program(parts=PARTS):
    nc = bass.Bass("TRN2", target_bir_lowering=False, debug=False)
    ll = _ll()
    lamC = [float(np.float32(np.exp(ll[h] * CH))) for h in range(HEADS)]
    sScl = [float(np.float32(np.exp(ll[h] * 64.0))) for h in range(HEADS)]

    x_in = nc.dram_tensor("x_in", [T, D], F32, kind="ExternalInput").ap()
    wq_d = nc.dram_tensor("wq", [D, D], F32, kind="ExternalInput").ap()
    wk_d = nc.dram_tensor("wk", [D, D], F32, kind="ExternalInput").ap()
    wv_d = nc.dram_tensor("wv", [D, D], F32, kind="ExternalInput").ap()
    wo_d = nc.dram_tensor("wo", [D, D], F32, kind="ExternalInput").ap()
    wg1_d = nc.dram_tensor("wg1", [D, HD], F32, kind="ExternalInput").ap()
    wg2_d = nc.dram_tensor("wg2", [HD, D], F32, kind="ExternalInput").ap()
    nrmw_d = nc.dram_tensor("nrmw", [1, D], F32, kind="ExternalInput").ap()
    wg_sl = nc.dram_tensor("wg_sl", [T, D], F32, kind="ExternalInput").ap()
    wu_sl = nc.dram_tensor("wu_sl", [T, D], F32, kind="ExternalInput").ap()
    wd_sl = nc.dram_tensor("wd_sl", [D, T], F32, kind="ExternalInput").ap()
    y_out = nc.dram_tensor("y_out", [T, D], F32, kind="ExternalOutput").ap()
    dbg_out = (nc.dram_tensor("dbg_out", [N_HU * 128, 64 * HU], F32, kind="ExternalOutput").ap()
               if DEBUG_EPRIME else None)
    dbg_sb = (nc.dram_tensor("dbg_sb", [128, 10312], F32, kind="ExternalOutput").ap()
              if DEBUG_EPRIME else None)
    dbg_bf = (nc.dram_tensor("dbg_bf", [128, 16384], BF16, kind="ExternalOutput").ap()
              if DEBUG_EPRIME else None)

    tH128 = nc.dram_tensor("H128", [128, 128], F32, kind="ExternalInput").ap()
    tH64B = nc.dram_tensor("H64B", [128, 128], F32, kind="ExternalInput").ap()
    tI128 = nc.dram_tensor("I128", [128, 128], BF16, kind="ExternalInput").ap()
    tcos = nc.dram_tensor("cos_rep", [T, 1024], F32, kind="ExternalInput").ap()
    tsin = nc.dram_tensor("sin_rep", [T, 1024], F32, kind="ExternalInput").ap()
    tdkrep = nc.dram_tensor("dkrep", [128, D], F32, kind="ExternalInput").ap()
    tnwrep = nc.dram_tensor("nrmw_rep", [128, D], F32, kind="ExternalInput").ap()
    tdmask = nc.dram_tensor("dmask", [HEADS * CH, CH], F32, kind="ExternalInput").ap()
    tdqcol = nc.dram_tensor("dqcol", [CH, HEADS], F32, kind="ExternalInput").ap()
    tdkcol = nc.dram_tensor("dkcol", [CH, HEADS], F32, kind="ExternalInput").ap()
    tP4 = nc.dram_tensor("P4", [32, 512], F32, kind="ExternalInput").ap()
    tlam = nc.dram_tensor("lamtab", [128, 128], F32, kind="ExternalInput").ap()
    tSEL2 = nc.dram_tensor("SEL2", [32, 128], F32, kind="ExternalInput").ap()

    ag_w_in = nc.dram_tensor("ag_w_in", [SL_ELEMS], BF16)
    GU_SL = 16 * PG_GU
    D_SL = 16 * PG_D
    ag_gu_out = nc.dram_tensor("ag_gu_out", [NCORES * GU_SL], BF16)
    ag_d_out = nc.dram_tensor("ag_d_out", [NCORES * D_SL], BF16)
    ar_s_in = nc.dram_tensor("ar_s_in", [128, 4], F32)
    ar_s_out = nc.dram_tensor("ar_s_out", [128, 4], F32)
    ag_st_in = nc.dram_tensor("ag_st_in", [HEADS * HD, HD], F32)
    ag_st_out = nc.dram_tensor("ag_st_out", [NCORES * HEADS * HD, HD], F32)
    eprime = nc.dram_tensor("eprime", [N_HU, 128, 64, HU], F32)
    kcache = nc.dram_tensor("kcache", [T, D], BF16)
    kdcache = nc.dram_tensor("kdcache", [T, D], BF16)
    vcache = nc.dram_tensor("vcache", [T, D], BF16)
    qcache = nc.dram_tensor("qcache", [T, D], BF16)
    gcache = nc.dram_tensor("gcache", [T, D], F32)
    oncache = nc.dram_tensor("oncache", [T, D], BF16)
    replica = [list(range(NCORES))]

    with contextlib.ExitStack() as _stack:
        _e = _stack.enter_context
        xres = _e(nc.sbuf_tensor("xres", [128, NT * D], F32))
        h128 = _e(nc.sbuf_tensor("h128", [128, 128], F32))
        h64b = _e(nc.sbuf_tensor("h64b", [128, 128], F32))
        i128 = _e(nc.sbuf_tensor("i128", [128, 128], BF16))
        i128f = _e(nc.sbuf_tensor("i128f", [128, 128], F32))
        ones = _e(nc.sbuf_tensor("ones", [128, 128], F32))
        sel2 = _e(nc.sbuf_tensor("sel2", [32, 128], F32))
        srow = _e(nc.sbuf_tensor("srow", [32, 64], F32))
        p4 = _e(nc.sbuf_tensor("p4", [32, 512], F32))
        colp = _e(nc.sbuf_tensor("colp", [128, 64], F32))
        adeq = _e(nc.sbuf_tensor("adeq", [128, NT], F32))
        dqc = _e(nc.sbuf_tensor("dqc", [128, HEADS], F32))
        dkc = _e(nc.sbuf_tensor("dkc", [128, HEADS], F32))
        nwrep = _e(nc.sbuf_tensor("nwrep", [128, D], F32))
        lamt = _e(nc.sbuf_tensor("lamt", [128, 128], F32))
        bfBIG = _e(nc.sbuf_tensor("bfBIG", [128, 16384], BF16))
        bfSM = _e(nc.sbuf_tensor("bfSM", [128, 4096], BF16))
        f32P1 = _e(nc.sbuf_tensor("f32P1", [128, 8192], F32))
        scr = _e(nc.sbuf_tensor("scr", [128, 4096], F32))
        scr2 = _e(nc.sbuf_tensor("scr2", [128, 2048], F32))
        scrC = _e(nc.sbuf_tensor("scrC", [128, 2048], F32))
        bfW = _e(nc.sbuf_tensor("bfW", [128, 8192], BF16))
        iscr = _e(nc.sbuf_tensor("iscr", [128, 2048], I32))
        _s1 = _e(nc.semaphore("s_ld"))
        _s2 = _e(nc.semaphore("s_w"))
        _s3 = _e(nc.semaphore("s_mm"))
        _s4 = _e(nc.semaphore("s_ev"))
        _s5 = _e(nc.semaphore("s_sc"))
        _s6 = _e(nc.semaphore("s_gp"))
        _s7 = _e(nc.semaphore("s_st"))
        _s8 = _e(nc.semaphore("s_cc"))
        _s9 = _e(nc.semaphore("s_fb"))
        _s10 = _e(nc.semaphore("s_gst"))
        _s11 = _e(nc.semaphore("s_vs"))
        _s12 = _e(nc.semaphore("s_ss"))
        pb0 = _e(nc.psum_tensor("pb0", [128, 512], F32))
        pb1 = _e(nc.psum_tensor("pb1", [128, 512], F32))
        pb2 = _e(nc.psum_tensor("pb2", [128, 512], F32))
        pb3 = _e(nc.psum_tensor("pb3", [128, 512], F32))
        pb4 = _e(nc.psum_tensor("pb4", [128, 512], F32))
        pb5 = _e(nc.psum_tensor("pb5", [128, 512], F32))
        pb6 = _e(nc.psum_tensor("pb6", [128, 512], F32))
        pb7 = _e(nc.psum_tensor("pb7", [128, 512], F32))
        block = _e(nc.Block())
        sld, sw, smm, sev, ssc, sgp, sst, scc, sfb = (
            Sem(_s1), Sem(_s2), Sem(_s3), Sem(_s4), Sem(_s5), Sem(_s6), Sem(_s7), Sem(_s8), Sem(_s9))
        sgst = Sem(_s10)
        svs, sss = Sem(_s11), Sem(_s12)

        def VS(v, instr):
            svs.inc(instr)
            v.wait_ge(svs.h, svs.n)
            return instr

        def SS(s, instr):
            sss.inc(instr)
            s.wait_ge(sss.h, sss.n)
            return instr
        banks = [pb0, pb1, pb2, pb3, pb4, pb5, pb6, pb7]

        @block.sync
        def _(sync):
            for t in range(NT):
                sld.incd(sync.dma_start(out=xres[:, t * D:(t + 1) * D],
                                        in_=x_in.rearrange("(a p) d -> a p d", p=128)[t, :, :]))
            sld.incd(sync.dma_start(out=h128[:], in_=tH128[:, :]))
            sld.incd(sync.dma_start(out=h64b[:], in_=tH64B[:, :]))
            sld.incd(sync.dma_start(out=i128[:], in_=tI128[:, :]))
            sld.incd(sync.dma_start(out=p4[:, :], in_=tP4[:, :]))
            sld.incd(sync.dma_start(out=dqc[:, :], in_=tdqcol[:, :]))
            sld.incd(sync.dma_start(out=dkc[:, :], in_=tdkcol[:, :]))
            sld.incd(sync.dma_start(out=sel2[:, :], in_=tSEL2[:, :]))
            sld.incd(sync.dma_start(out=lamt[:, :], in_=tlam[:, :]))
        P0_LD = sld.n

        @block.vector
        def _(v):
            v.wait_ge(sld.h, P0_LD)
            v.tensor_copy(i128f[:], i128[:])
            v.memset(srow[:, :], 0.0)
            sev.inc(v.memset(ones[:], 1.0))
        P0_EV = sev.n

        tiles = [(0, wg_sl, i, 2048) for i in range(8)] + \
                [(1, wu_sl, i, 2048) for i in range(8)] + \
                [(2, wd_sl, i, 1024) for i in range(16)]
        B1_ev = sev.n

        @block.sync
        def _(sync):
            sync.wait_ge(sld.h, P0_LD)
            for idx, (m, ap_, i, w_) in enumerate(tiles):
                if idx >= 2:
                    sync.wait_ge(sev.h, B1_ev + idx - 1)
                if idx >= 1:
                    sync.wait_ge(sld.h, P0_LD + 16 * idx)
                buf = scr if idx % 2 == 0 else scr2
                src = ap_.rearrange("(a p) d -> a p d", p=128)[i, :, :]
                sld.incd(sync.dma_start(out=buf[:, 0:w_], in_=src))

        @block.vector
        def _(v):
            v.wait_ge(sev.h, P0_EV)
            v.memset(colp[:, 0:4], 0.0)
            for idx, (m, ap_, i, w_) in enumerate(tiles):
                v.wait_ge(sld.h, P0_LD + 16 * (idx + 1))
                buf = scr if idx % 2 == 0 else scr2
                VS(v, v.tensor_reduce(colp[:, 8:9], buf[:, 0:w_], AX, ALU.add, apply_absolute_value=True))
                sev.inc(v.tensor_tensor(colp[:, m:m + 1], colp[:, m:m + 1], colp[:, 8:9], ALU.add))
                v.wait_ge(sev.h, sev.n)
        P1A_EV = sev.n

        @block.tensor
        def _(tn):
            tn.wait_ge(sev.h, P1A_EV)
            smm.inc(tn.matmul(pb0[0:128, 0:4], ones[:, 0:128], colp[:, 0:4], start=True, stop=True))
        P1A_MM = smm.n

        @block.vector
        def _(v):
            v.wait_ge(smm.h, P1A_MM)
            sev.inc(v.tensor_copy(colp[:, 16:20], pb0[:, 0:4]))
        P1B_EV = sev.n

        @block.sync
        def _(sync):
            sync.wait_ge(sev.h, P1B_EV)
            sst.incd(sync.dma_start(out=ar_s_in[:, :], in_=colp[:, 16:20]))
        P1_ST1 = sst.n

        @block.gpsimd
        def _(g):
            g.wait_ge(sst.h, P1_ST1)
            scc.inc(g.collective_compute("AllReduce", ALU.add, replica_groups=replica,
                                         ins=[ar_s_in.ap().opt()], outs=[ar_s_out.ap().opt()]))
        P1_CC1 = scc.n

        @block.sync
        def _(sync):
            sync.wait_ge(scc.h, P1_CC1)
            sync.wait_ge(sld.h, sld.n)
            sld.incd(sync.dma_start(out=colp[:, 20:24], in_=ar_s_out[:, :]))
        P1_LD2 = sld.n

        @block.vector
        def _(v):
            v.wait_ge(sld.h, P1_LD2)
            VS(v, v.tensor_scalar(colp[:, 24:28], colp[:, 20:24], 1.0 / NW_ELEMS, 1e-5, ALU.mult, ALU.max))
            VS(v, v.reciprocal(colp[:, 40:44], colp[:, 24:28]))
            sev.inc(v.reciprocal(colp[:, 44:48], colp[:, 40:44]))
            v.wait_ge(sev.h, sev.n)
        P1D_EV = sev.n

        PAGE_OFF = [0, 8 * PG_GU, 16 * PG_GU]
        B2_ld, B2_ev, B2_mm, B2_sc, B2_st = sw.n, sev.n, smm.n, ssc.n, sst.n
        nbat = [w_ // 512 for (_, _, _, w_) in tiles]
        cumb = np.cumsum([0] + nbat).tolist()

        @block.sync
        def _(sync):
            for idx, (m, ap_, i, w_) in enumerate(tiles):
                if idx >= 2:
                    sync.wait_ge(sev.h, B2_ev + idx - 1)
                if idx >= 1:
                    sync.wait_ge(sw.h, B2_ld + 16 * idx)
                buf = scr if idx % 2 == 0 else scr2
                src = ap_.rearrange("(a p) d -> a p d", p=128)[i, :, :]
                sw.incd(sync.dma_start(out=buf[:, 0:w_], in_=src))

        @block.vector
        def _(v):
            v.wait_ge(sev.h, P1D_EV)
            for idx, (m, ap_, i, w_) in enumerate(tiles):
                v.wait_ge(sw.h, B2_ld + 16 * (idx + 1))
                if idx >= 2:
                    v.wait_ge(smm.h, B2_mm + cumb[idx - 1])
                buf = scr if idx % 2 == 0 else scr2
                tb = bfSM[:, 0:2048] if idx % 2 == 0 else bfSM[:, 2048:4096]
                sc_ = colp[:, 40 + m:41 + m]
                v.tensor_scalar(scrC[:, 0:w_], buf[:, 0:w_], sc_, 0.5, ALU.mult, ALU.is_ge)
                v.tensor_scalar(buf[:, 0:w_], buf[:, 0:w_], sc_, -0.5, ALU.mult, ALU.is_le)
                sev.inc(v.tensor_tensor(tb[:, 0:w_], scrC[:, 0:w_], buf[:, 0:w_], ALU.subtract))

        @block.tensor
        def _(tn):
            gb = 0
            for idx, (m, ap_, i, w_) in enumerate(tiles):
                tn.wait_ge(sev.h, B2_ev + idx + 1)
                tb = bfSM[:, 0:2048] if idx % 2 == 0 else bfSM[:, 2048:4096]
                for h in range(w_ // 128):
                    bank = banks[4 + (gb % 2)]
                    if h % 4 == 0 and gb >= 2:
                        tn.wait_ge(ssc.h, B2_sc + gb - 1)
                    mm = tn.matmul(bank[:, (h % 4) * 128:(h % 4) * 128 + 128],
                                   tb[:, h * 128:(h + 1) * 128], i128[:, :],
                                   start=True, stop=True)
                    if h % 4 == 3:
                        smm.inc(mm)
                        gb += 1

        @block.scalar
        def _(s):
            gb = 0
            for idx, (m, ap_, i, w_) in enumerate(tiles):
                slot = bfW[:, 0:4096] if idx % 2 == 0 else bfW[:, 4096:8192]
                if idx >= 2:
                    s.wait_ge(sgst.h, 16 * (idx - 1))
                for b in range(w_ // 512):
                    s.wait_ge(smm.h, B2_mm + gb + 1)
                    bank = banks[4 + (gb % 2)]
                    ssc.inc(s.activation(slot[:, b * 512:(b + 1) * 512], bank[:, :], AF.Identity))
                    gb += 1

        @block.gpsimd
        def _(g):
            for idx, (m, ap_, i, w_) in enumerate(tiles):
                g.wait_ge(ssc.h, B2_sc + cumb[idx + 1])
                if idx >= 1:
                    g.wait_ge(sgst.h, 16 * idx)
                slot = bfW[:, 0:4096] if idx % 2 == 0 else bfW[:, 4096:8192]
                off = PAGE_OFF[m] + i * (PG_GU if m < 2 else PG_D)
                dst = ag_w_in.ap()[off:off + 128 * w_].rearrange("(p c) -> p c", c=w_)
                sgst.incd(g.dma_start(out=dst, in_=slot[:, 0:w_]))
                if idx == 15:
                    g.wait_ge(sgst.h, 16 * 16)
                    scc.inc(g.collective_compute(
                        "AllGather", ALU.bypass, replica_groups=replica,
                        ins=[ag_w_in.ap()[0:GU_SL].opt()], outs=[ag_gu_out.ap().opt()]))
            g.wait_ge(sgst.h, sgst.n)
            scc.inc(g.collective_compute(
                "AllGather", ALU.bypass, replica_groups=replica,
                ins=[ag_w_in.ap()[GU_SL:SL_ELEMS].opt()], outs=[ag_d_out.ap().opt()]))
        P1E_ST = sst.n
        P1_CC_GU = scc.n - 1
        P1_CC2 = scc.n

        def gu_page(m, i1):
            r, iloc = i1 >> 3, i1 & 7
            return r * GU_SL + PAGE_OFF[m] + iloc * PG_GU

        def d_page(o, r):
            return r * D_SL + o * PG_D

        xq = bfBIG.rearrange("p (hh c) -> p hh c", c=T)

        def ffn_call(cid, wait_ev):
            CALL_mm0 = smm.n
            L_ev, L_sc, L_mm = sev.n, ssc.n, smm.n

            @block.scalar
            def _(s):
                s.wait_ge(sev.h, wait_ev)
                for t in range(NT):
                    xt = xres[:, t * D:(t + 1) * D]
                    ssc.inc(s.activation(scrC[:, 0:2048], xt, AF.Square))
                    s.wait_ge(sev.h, L_ev + 3 * t + 1)
                    ssc.inc(s.activation(colp[:, 8:9], colp[:, 6:7], AF.Sqrt))
                    s.wait_ge(ssc.h, ssc.n)
                    s.wait_ge(sev.h, L_ev + 3 * t + 2)
                    s.activation(scrC[:, 0:2048], xt, AF.Identity, bias=colp[:, 7:8])
                    ssc.inc(s.activation(scr[:, 0:2048], scrC[:, 0:2048], AF.Identity, scale=colp[:, 15:16]))
                    for b in range(4):
                        gb = 4 * t + b
                        s.wait_ge(smm.h, L_mm + gb + 1)
                        bank = banks[4 + (gb % 2)]
                        dst = xq[:, 4 * b:4 * b + 4, t * 128:t * 128 + 128]
                        ssc.inc(s.activation(dst, bank.rearrange("p (h c) -> p h c", c=128), AF.Identity))

            @block.vector
            def _(v):
                for t in range(NT):
                    xt = xres[:, t * D:(t + 1) * D]
                    v.wait_ge(ssc.h, L_sc + 7 * t + 1)
                    VS(v, v.tensor_reduce(colp[:, 0:1], xt, AX, ALU.add))
                    VS(v, v.tensor_reduce(colp[:, 1:2], scrC[:, 0:2048], AX, ALU.add))
                    VS(v, v.tensor_scalar_mul(colp[:, 2:3], colp[:, 0:1], 1.0 / D))
                    VS(v, v.tensor_scalar_mul(colp[:, 3:4], colp[:, 1:2], 1.0 / D))
                    VS(v, v.tensor_tensor(colp[:, 4:5], colp[:, 2:3], colp[:, 2:3], ALU.mult))
                    VS(v, v.scalar_tensor_tensor(colp[:, 5:6], colp[:, 3:4], 1e-5, colp[:, 4:5],
                                                 ALU.add, ALU.subtract))
                    VS(v, v.reciprocal(colp[:, 6:7], colp[:, 5:6]))
                    sev.inc(v.tensor_scalar_mul(colp[:, 7:8], colp[:, 2:3], -1.0))
                    v.wait_ge(sev.h, sev.n)
                    v.wait_ge(ssc.h, L_sc + 7 * t + 2)
                    VS(v, v.tensor_tensor(colp[:, 14:15], colp[:, 8:9], colp[:, 8:9], ALU.mult))
                    VS(v, v.tensor_tensor(colp[:, 14:15], colp[:, 14:15], colp[:, 5:6], ALU.mult))
                    VS(v, v.tensor_scalar(colp[:, 14:15], colp[:, 14:15], -0.5, 1.5, ALU.mult, ALU.add))
                    sev.inc(v.tensor_tensor(colp[:, 15:16], colp[:, 8:9], colp[:, 14:15], ALU.mult))
                    v.wait_ge(sev.h, sev.n)
                    v.wait_ge(ssc.h, L_sc + 7 * t + 3)
                    if t >= 2:
                        v.wait_ge(smm.h, L_mm + 4 * (t - 1))
                    VS(v, v.tensor_reduce(colp[:, 10:11], scr[:, 0:2048], AX, ALU.max, apply_absolute_value=True))
                    VS(v, v.tensor_scalar_max(colp[:, 11:12], colp[:, 10:11], 1e-5))
                    VS(v, v.reciprocal(colp[:, 12:13], colp[:, 11:12]))
                    VS(v, v.tensor_scalar_mul(colp[:, 13:14], colp[:, 12:13], 128.0))
                    VS(v, v.reciprocal(adeq[:, t:t + 1], colp[:, 13:14]))
                    sl = bfSM[:, (t % 2) * 2048:(t % 2) * 2048 + 2048]
                    v.tensor_scalar_mul(scr2[:, 0:2048], scr[:, 0:2048], colp[:, 13:14])
                    v.tensor_scalar_add(iscr[:, 0:2048], scr2[:, 0:2048], 256.0)
                    v.scalar_tensor_tensor(scr[:, 0:2048], scr2[:, 0:2048], 256.0, iscr[:, 0:2048],
                                           ALU.add, ALU.subtract)
                    v.scalar_tensor_tensor(scr2[:, 0:2048], scr[:, 0:2048], 0.5, iscr[:, 0:2048],
                                           ALU.is_ge, ALU.add)
                    sev.inc(v.tensor_scalar(sl, scr2[:, 0:2048], -256.0, 127.0, ALU.add, ALU.min))

            @block.tensor
            def _(tn):
                for t in range(NT):
                    tn.wait_ge(sev.h, L_ev + 3 * t + 3)
                    sl = bfSM[:, (t % 2) * 2048:(t % 2) * 2048 + 2048]
                    for h in range(16):
                        gb = 4 * t + h // 4
                        bank = banks[4 + (gb % 2)]
                        if h % 4 == 0 and gb >= 2:
                            pgb = gb - 2
                            tn.wait_ge(ssc.h, L_sc + 7 * (pgb // 4) + 3 + (pgb % 4) + 1)
                        mm = tn.matmul(bank[:, (h % 4) * 128:(h % 4) * 128 + 128],
                                       sl[:, h * 128:(h + 1) * 128], i128[:, :],
                                       start=True, stop=True)
                        if h % 4 == 3:
                            smm.inc(mm)
            L_done_ev = sev.n

            A_w, A_mm, A_ev, A_sc, A_gp, A_st = sw.n, smm.n, sev.n, ssc.n, sgp.n, sst.n

            def f_mm_deq(g, ts_):
                return A_mm + 68 * g + ts_ + 1

            def f_mm_gi(gi):
                return A_mm + 68 * (gi // 64) + 4 + (gi % 64) + 1

            def f_sc_b(g, ts_):
                return A_sc + 68 * g + ts_ + 1

            def f_sc_silu(gi):
                return A_sc + 68 * (gi // 64) + 4 + (gi % 64) + 1

            def f_ev_deq(g, ts_):
                return A_ev + 132 * g + ts_ + 1

            def f_ev_t0(gi):
                return A_ev + 132 * (gi // 64) + 4 + 2 * (gi % 64) + 1

            def f_ev_u1(gi):
                return A_ev + 132 * (gi // 64) + 4 + 2 * (gi % 64) + 2

            def f_gp(gi):
                return A_gp + gi + 1

            @block.sync
            def _(sync):
                sync.wait_ge(scc.h, P1_CC_GU)
                sync.wait_ge(smm.h, CALL_mm0)
                for gi in range(128):
                    i1 = gi % 64
                    if gi >= 2:
                        sync.wait_ge(smm.h, f_mm_gi(gi - 2))
                    if gi >= 1:
                        sync.wait_ge(sw.h, A_w + 32 * gi)
                    wslot = bfW[:, (gi % 2) * 4096:(gi % 2) * 4096 + 4096]
                    go, uo = gu_page(0, i1), gu_page(1, i1)
                    sw.incd(sync.dma_start(out=wslot[:, 0:2048],
                                           in_=ag_gu_out.ap()[go:go + PG_GU].rearrange("(p c) -> p c", c=2048)))
                    sw.incd(sync.dma_start(out=wslot[:, 2048:4096],
                                           in_=ag_gu_out.ap()[uo:uo + PG_GU].rearrange("(p c) -> p c", c=2048)))

            @block.scalar
            def _(s):
                for g in range(2):
                    for ts_ in range(4):
                        tt = 4 * g + ts_
                        if g == 0 and ts_ == 0:
                            s.wait_ge(sev.h, L_done_ev)
                        if g == 1 and ts_ == 0:
                            s.wait_ge(smm.h, f_mm_deq(0, 3))
                        ssc.inc(s.activation(scrC[:, 0:128], ones[:, 0:128], AF.Identity,
                                             scale=adeq[:, tt:tt + 1]))
                        if ts_ < 3:
                            s.wait_ge(smm.h, f_mm_deq(g, ts_))
                    for i1 in range(64):
                        gi = g * 64 + i1
                        s.wait_ge(sev.h, f_ev_t0(gi))
                        if gi >= 2:
                            s.wait_ge(sgp.h, f_gp(gi - 2))
                        t0slot = scrC[:, 512 + (gi % 2) * 768:512 + (gi % 2) * 768 + 512]
                        oslot = scr[:, (gi % 2) * 2048:(gi % 2) * 2048 + 512]
                        ssc.inc(s.activation(oslot, t0slot, AF.Silu))
                        if gi % 8 == 7:
                            b = gi // 8
                            s.wait_ge(sgp.h, f_gp(gi))
                            if b >= 1:
                                s.wait_ge(sst.h, A_st + 128 * b)
                            for uloc in range(8):
                                u = g * 8 + uloc
                                fsrc = f32P1[:, 0:4096].rearrange(
                                    "p (sl c) -> p sl c", c=512)[:, :, uloc * 64:(uloc + 1) * 64]
                                sst.incd(s.dma_start(out=eprime[u, :, (b % 8) * 8:(b % 8) * 8 + 8, :],
                                                     in_=fsrc))

            @block.gpsimd
            def _(g_):
                for gi in range(128):
                    g_.wait_ge(ssc.h, f_sc_silu(gi))
                    g_.wait_ge(sev.h, f_ev_u1(gi))
                    if gi >= 8:
                        g_.wait_ge(sst.h, A_st + 128 * (gi // 8))
                    oslot = scr[:, (gi % 2) * 2048:(gi % 2) * 2048 + 512]
                    u1slot = scr[:, (gi % 2) * 2048 + 512:(gi % 2) * 2048 + 1024]
                    slot = f32P1[:, (gi % 8) * 512:(gi % 8) * 512 + 512]
                    sgp.inc(g_.tensor_tensor(slot, oslot, u1slot, ALU.mult))

            @block.vector
            def _(v):
                for g in range(2):
                    for ts_ in range(4):
                        v.wait_ge(smm.h, f_mm_deq(g, ts_))
                        sev.inc(v.tensor_copy(scr2[:, ts_ * 128:(ts_ + 1) * 128],
                                              pb6[:, ts_ * 128:(ts_ + 1) * 128]))
                        v.wait_ge(sev.h, sev.n)
                    for i1 in range(64):
                        gi = g * 64 + i1
                        v.wait_ge(smm.h, f_mm_gi(gi))
                        gB = banks[gi % 2]
                        uB = banks[2 + (gi % 2)]
                        t0slot = scrC[:, 512 + (gi % 2) * 768:512 + (gi % 2) * 768 + 512]
                        oslot = scr[:, (gi % 2) * 2048:(gi % 2) * 2048 + 512]
                        u1slot = scr[:, (gi % 2) * 2048 + 512:(gi % 2) * 2048 + 1024]
                        if gi >= 2:
                            v.wait_ge(ssc.h, f_sc_silu(gi - 2))
                        sev.inc(v.scalar_tensor_tensor(t0slot, gB[:, :], colp[:, 44:45],
                                                       scr2[:, 0:512], ALU.mult, ALU.mult))
                        if gi >= 2:
                            v.wait_ge(sgp.h, f_gp(gi - 2))
                        sev.inc(v.scalar_tensor_tensor(u1slot, uB[:, :], colp[:, 45:46],
                                                       scr2[:, 0:512], ALU.mult, ALU.mult))

            @block.tensor
            def _(tn):
                tn.wait_ge(scc.h, P1_CC_GU)
                for g in range(2):
                    for ts_ in range(4):
                        tn.wait_ge(ssc.h, f_sc_b(g, ts_))
                        if g == 1:
                            tn.wait_ge(sev.h, f_ev_deq(0, ts_))
                        smm.inc(tn.matmul(pb6[:, ts_ * 128:(ts_ + 1) * 128], scrC[:, 0:128],
                                          i128f[:, :], start=True, stop=True))
                    for i1 in range(64):
                        gi = g * 64 + i1
                        tn.wait_ge(sw.h, A_w + 32 * (gi + 1))
                        if gi >= 2:
                            tn.wait_ge(sev.h, f_ev_u1(gi - 2))
                        wslot = bfW[:, (gi % 2) * 4096:(gi % 2) * 4096 + 4096]
                        gB = banks[gi % 2]
                        uB = banks[2 + (gi % 2)]
                        for h in range(16):
                            tn.matmul(gB[:, :], wslot[:, h * 128:(h + 1) * 128],
                                      xq[:, h, g * 512:(g + 1) * 512],
                                      start=(h == 0), stop=(h == 15))
                        for h in range(16):
                            mm = tn.matmul(uB[:, :], wslot[:, 2048 + h * 128:2048 + (h + 1) * 128],
                                           xq[:, h, g * 512:(g + 1) * 512],
                                           start=(h == 0), stop=(h == 15))
                        smm.inc(mm)


            A_st_end = A_st + 128 * 16

            B_fb, B_sc, B_ev, B_mm, B_gp, B_w2 = sfb.n, ssc.n, sev.n, smm.n, sgp.n, sw.n
            M, _, _, _, _ = _sched_B(B_sc, B_ev, B_mm, B_gp)
            eq = f32P1[:, 0:4096]
            ewin = f32P1[:, 4096:8192]
            zr = scrC

            @block.sync
            def _(sync):
                sync.wait_ge(sst.h, A_st_end)
                for u in range(N_HU):
                    if u >= 1:
                        sync.wait_ge(sev.h, M["vdone", u - 1])
                        sync.wait_ge(sfb.h, B_fb + 16 * u)
                    src = eprime[u].rearrange("p a b -> p (a b)")
                    sfb.incd(sync.dma_start(out=eq, in_=src))


            @block.scalar
            def _(s):
                for u in range(N_HU):
                    u4 = u % 4
                    s.wait_ge(sfb.h, B_fb + 16 * (u + 1))
                    if u >= 1:
                        s.wait_ge(smm.h, M["r1", u - 1, 7])
                        s.wait_ge(sev.h, M["q", u - 1, 1])
                    ssc.inc(s.copy(
                        ewin.rearrange("p (t i) -> p t i", i=64),
                        eq.rearrange("p (i t) -> p i t", t=64).transpose([0, 2, 1])))
                    for b in range(8):
                        s.wait_ge(smm.h, M["r2", u, b])
                        yB = banks[2 + (b % 2)]
                        ssc.inc(s.activation(eq[:, b * 512:(b + 1) * 512], yB[:, :], AF.Identity))
                    for b in range(8):
                        s.wait_ge(smm.h, M["r3", u, b])
                        if u // 4 >= 1 and b == 0:
                            s.wait_ge(smm.h, M["dn", u // 4 - 1, 15])
                        bank = banks[4 + (b % 2)]
                        dst = bfBIG.rearrange("p (uu i b2 ths thi) -> p uu b2 ths thi i",
                                              i=64, b2=8, ths=4, thi=2)[:, u4, b, :, :, :]
                        srcv = bank.rearrange("q (ths thi i) -> q ths thi i", thi=2, i=64)
                        ssc.inc(s.activation(dst, srcv, AF.Identity))
                    if u % 4 == 3:
                        g4 = u // 4
                        if g4 == 0:
                            s.wait_ge(smm.h, f_mm_gi(127))
                        for o in range(16):
                            if not (g4 == 0 and o == 0):
                                po = (g4, o - 1) if o >= 1 else (g4 - 1, 15)
                                s.wait_ge(smm.h, M["dn", po[0], po[1]])
                                s.wait_ge(sw.h, B_w2 + 128 * (16 * g4 + o))
                            if g4 == 0 and o == 0:
                                s.wait_ge(scc.h, P1_CC2)
                            for r in range(NCORES):
                                off = d_page(o, r)
                                sw.incd(s.dma_start(
                                    out=bfW[:, r * 1024:(r + 1) * 1024],
                                    in_=ag_d_out.ap()[off:off + PG_D].rearrange("(p c) -> p c", c=1024)))
                            if o >= 1:
                                s.wait_ge(smm.h, M["dn", g4, o - 1])
                                if o >= 3:
                                    s.wait_ge(smm.h, M["dntm", g4, o - 3, 1])
                                dB = banks[6 + ((o - 1) % 2)]
                                ssc.inc(s.activation(scr2[:, ((o - 1) % 2) * 256:((o - 1) % 2) * 256 + 256],
                                                     dB[:, 0:256], AF.Identity))
                        s.wait_ge(smm.h, M["dn", g4, 15])
                        s.wait_ge(smm.h, M["dntm", g4, 13, 1])
                        dB = banks[6 + (15 % 2)]
                        ssc.inc(s.activation(scr2[:, (15 % 2) * 256:(15 % 2) * 256 + 256],
                                             dB[:, 0:256], AF.Identity))

            @block.tensor
            def _(tn):
                for u in range(N_HU):
                    u4 = u % 4
                    tn.wait_ge(ssc.h, M["reord", u])
                    if u % 4 == 0 and u >= 4:
                        tn.wait_ge(sev.h, M["resid", u // 4 - 1, 15, 1])
                    def emit_r2(b):
                        tn.wait_ge(sev.h, M["r1ev", u, b])
                        if b >= 2:
                            tn.wait_ge(ssc.h, M["yev", u, b - 2])
                        yB = banks[2 + (b % 2)]
                        smm.inc(tn.matmul(yB[:, :], h64b[:, :], zr[:, (b % 4) * 512:(b % 4) * 512 + 512],
                                          start=True, stop=True))

                    for b in range(8):
                        if b >= 2:
                            tn.wait_ge(sev.h, M["r1ev", u, b - 2])
                        bank = banks[b % 2]
                        for wsub in range(4):
                            w = 4 * b + wsub
                            mm = tn.matmul(bank[0:128, wsub * 128:(wsub + 1) * 128],
                                           ewin[:, w * 128:(w + 1) * 128], h128[:, :],
                                           start=True, stop=True)
                        smm.inc(mm)
                        if b >= 2:
                            emit_r2(b - 2)
                    for b in [6, 7]:
                        emit_r2(b)
                    tn.wait_ge(sev.h, M["mx1", u])
                    smm.inc(tn.matmul(pb6[0:32, 0:128], colp[:, 0:32], i128f[:, :], start=True, stop=True))
                    tn.wait_ge(sev.h, M["chain", u])
                    smm.inc(tn.matmul(pb6[0:2, 128:160], scr2[0:32, 646:648], i128f[0:32, 0:32],
                                      start=True, stop=True))
                    tn.wait_ge(sev.h, M["sTev", u])
                    smm.inc(tn.matmul(pb6[0:128, 160:192], sel2[:, :], srow[:, 0:32],
                                      start=True, stop=True))
                    for b in range(8):
                        tn.wait_ge(sev.h, M["q", u, 1 if b >= 4 else 0])
                        if b >= 2:
                            tn.wait_ge(ssc.h, M["r3ev", u, b - 2])
                        bank = banks[4 + (b % 2)]
                        for ths in range(4):
                            th = 4 * b + ths
                            mm = tn.matmul(bank[0:128, ths * 128:(ths + 1) * 128],
                                           bfSM[:, th * 128:(th + 1) * 128], i128[:, :],
                                           start=True, stop=True)
                        smm.inc(mm)
                    if u % 4 == 3:
                        g4 = u // 4
                        tn.wait_ge(sev.h, M["chain", u])
                        tn.wait_ge(ssc.h, M["r3ev", u, 7])
                        for tb in range(2):
                            for k in range(4):
                                usub, tlo = k // 2, k % 2
                                uu = 4 * g4 + 2 * tb + usub
                                mm = tn.matmul(pb4[0:128, 500 + tb:501 + tb],
                                               p4[:, (2 * usub + tlo) * 128:(2 * usub + tlo) * 128 + 128],
                                               scr2[0:32, 700 + 2 * uu + tlo:701 + 2 * uu + tlo],
                                               start=(k == 0), stop=(k == 3))
                            smm.inc(mm)
                        for o in range(16):
                            tn.wait_ge(sw.h, B_w2 + 128 * (16 * g4 + o + 1))
                            dB = banks[6 + (o % 2)]
                            if o >= 2:
                                tn.wait_ge(ssc.h, M["dnev", g4, o - 2])
                            for i1p in range(64):
                                r, iloc = i1p >> 3, i1p & 7
                                lt = bfW[:, r * 1024 + iloc * 128:r * 1024 + iloc * 128 + 128]
                                rhs = bfBIG.rearrange("p (uu i tl) -> p uu i tl",
                                                      i=64, tl=64)[:, :, i1p, :]
                                mm = tn.matmul(dB[:, 0:256], lt, rhs,
                                               start=(i1p == 0), stop=(i1p == 63))
                            smm.inc(mm)
                            tn.wait_ge(ssc.h, M["dnev", g4, o])
                            if (g4, o) > (0, 1):
                                po = (g4, o - 2) if o >= 2 else (g4 - 1, 14 + o)
                                tn.wait_ge(sev.h, M["resid", po[0], po[1], 1])
                            for tb in range(2):
                                mm = tn.matmul(banks[2 * tb + (o % 2)][:, 0:128],
                                               scr2[:, (o % 2) * 256 + tb * 128:(o % 2) * 256 + tb * 128 + 128],
                                               i128f[:, :], start=True, stop=True)
                                smm.inc(mm)

            @block.vector
            def _(v):
                for u in range(N_HU):
                    for b in range(8):
                        v.wait_ge(smm.h, M["r1", u, b])
                        if b >= 4:
                            v.wait_ge(smm.h, M["r2", u, b - 4])
                        bank = banks[b % 2]
                        sev.inc(v.tensor_copy(zr[:, (b % 4) * 512:(b % 4) * 512 + 512], bank[:, :]))
                    v.wait_ge(ssc.h, M["yev", u, 7])
                    sev.inc(v.tensor_reduce(colp[:, 0:32].rearrange("p (a o) -> p a o", o=1),
                                            eq.rearrange("p (a c) -> p a c", c=128), AX,
                                            ALU.max, apply_absolute_value=True))
                    v.wait_ge(smm.h, M["mx1T", u])
                    sev.inc(v.tensor_copy(scr2[0:32, 512:640], pb6[0:32, 0:128]))
                    v.wait_ge(sev.h, sev.n)
                    VS(v, v.tensor_reduce(scr2[0:32, 640:642].rearrange("p (a o) -> p a o", o=1),
                                          scr2[0:32, 512:640].rearrange("p (a c) -> p a c", c=64), AX,
                                          ALU.max, apply_absolute_value=False))
                    VS(v, v.tensor_scalar_max(scr2[0:32, 642:644], scr2[0:32, 640:642], 1e-5))
                    VS(v, v.reciprocal(scr2[0:32, 644:646], scr2[0:32, 642:644]))
                    VS(v, v.tensor_scalar_mul(scr2[0:32, 646:648], scr2[0:32, 644:646], 128.0))
                    VS(v, v.reciprocal(scr2[0:32, 648:650], scr2[0:32, 646:648]))
                    u4 = u % 4
                    sev.inc(v.tensor_copy(scr2[0:32, 700 + 2 * u:702 + 2 * u], scr2[0:32, 648:650]))
                    v.wait_ge(sev.h, sev.n)
                    v.wait_ge(smm.h, M["sT", u])
                    sev.inc(v.tensor_copy(srow[0:2, 0:32], pb6[0:2, 128:160]))
                    v.wait_ge(sev.h, sev.n)
                    v.wait_ge(smm.h, M["ssm", u])
                    sev.inc(v.tensor_copy(scr2[:, 1024:1056], pb6[:, 160:192]))
                    v.wait_ge(sev.h, sev.n)
                    for th in range(32):
                        sl = ewin[:, th * 128:(th + 1) * 128]
                        mmv = v.tensor_scalar_mul(sl, eq[:, th * 128:(th + 1) * 128],
                                                  scr2[:, 1024 + th:1025 + th])
                    sev.inc(mmv)
                    for half in range(2):
                        hs = slice(half * 2048, (half + 1) * 2048)
                        sev.inc(v.tensor_scalar_add(scr[:, 0:2048], ewin[:, hs], 256.0))
                        sev.inc(v.tensor_copy(iscr[:, 0:2048], scr[:, 0:2048]))
                        sev.inc(v.tensor_tensor(scr[:, 2048:4096], scr[:, 0:2048],
                                                iscr[:, 0:2048], ALU.subtract))
                        sev.inc(v.tensor_single_scalar(ewin[:, hs], scr[:, 2048:4096], 0.5,
                                                       ALU.is_ge))
                        sev.inc(v.tensor_tensor(ewin[:, hs], ewin[:, hs], iscr[:, 0:2048],
                                                ALU.add))
                        sev.inc(v.tensor_scalar(bfSM[:, hs], ewin[:, hs], -256.0, 127.0,
                                                ALU.add, ALU.min))
                    if u % 4 == 3:
                        g4 = u // 4
                        for tb in range(2):
                            v.wait_ge(smm.h, M["dqc", g4, tb])
                            sev.inc(v.tensor_copy(colp[:, 24 + tb:25 + tb], pb4[:, 500 + tb:501 + tb]))
                            v.wait_ge(sev.h, sev.n)
                        for o in range(16):
                            for tb in range(2):
                                v.wait_ge(smm.h, M["dntm", g4, o, tb])
                                tmp = scr2[:, 1088:1216]
                                v.tensor_scalar(tmp, banks[2 * tb + (o % 2)][:, 0:128],
                                                colp[:, 24 + tb:25 + tb],
                                                colp[:, 46:47], ALU.mult, ALU.mult)
                                xslice = xres[:, (2 * g4 + tb) * D + o * 128:(2 * g4 + tb) * D + o * 128 + 128]
                                sev.inc(v.scalar_tensor_tensor(xslice, tmp, 0.5, xslice,
                                                               ALU.mult, ALU.add))

            M_end, B_sc_end, B_ev_end, B_mm_end, B_gp_end = _sched_B(B_sc, B_ev, B_mm, B_gp)
            assert B_sc_end == ssc.n and B_ev_end == sev.n and B_mm_end == smm.n and B_gp_end == sgp.n, (
                "stage B schedule mismatch",
                (B_sc_end, ssc.n), (B_ev_end, sev.n), (B_mm_end, smm.n), (B_gp_end, sgp.n))
            return M["resid", 3, 15, 1]

        def attn_call(wait_ev):
            CALL_mm0 = smm.n
            llv = _ll()
            lamC_f = [float(np.float32(np.exp(llv[h] * CH))) for h in range(HEADS)]
            sScl_f = [float(np.float32(np.exp(llv[h] * 64.0))) for h in range(HEADS)]
            grp = bfBIG.rearrange("p (a c) -> p a c", c=512)
            wsv = bfW.rearrange("p (hh c) -> p hh c", c=512)
            wfv = bfW.rearrange("p (hh c) -> p hh c", c=128)
            xlv = bfBIG.rearrange("p (hh c) -> p hh c", c=512)
            smv = bfSM.rearrange("p (a c) -> p a c", c=512)
            wsrc = {"k": wk_d, "v": wv_d, "q": wq_d}
            cach = {"k": kcache, "v": vcache, "q": qcache}
            B = {}

            def bar(tag):
                B[tag] = (smm.n, sev.n, ssc.n, sgp.n, sw.n, sld.n, sst.n, sfb.n)

            _ckn = [0]

            def ck():
                _ckn[0] += 1
                if ATTN_STAGES is not None and _ckn[0] >= ATTN_STAGES:
                    raise _StopEmit(sev.n)

            @block.sync
            def _(sync):
                sync.wait_ge(sld.h, sld.n)
                sld.incd(sync.dma_start(out=nwrep[:, :], in_=tdkrep[:, :]))
            A0_ld = sld.n

            for g in range(2):
                L_ev, L_sc, L_mm = sev.n, ssc.n, smm.n

                @block.scalar
                def _(s, g=g, L_ev=L_ev, L_mm=L_mm):
                    for t4 in range(4):
                        t = g * 4 + t4
                        xt = xres[:, t * D:(t + 1) * D]
                        if g == 0 and t4 == 0:
                            s.wait_ge(sev.h, wait_ev)
                        if g == 1 and t4 == 0:
                            s.wait_ge(smm.h, B["gate", 0][0])
                            s.wait_ge(sev.h, B[("state", 0)][1])
                        ssc.inc(s.activation(scrC[:, 0:2048], xt, AF.Square))
                        s.wait_ge(sev.h, L_ev + 2 * t4 + 1)
                        ssc.inc(s.activation(colp[:, 8:9], colp[:, 6:7], AF.Sqrt))
                        s.wait_ge(ssc.h, ssc.n)
                        s.wait_ge(sev.h, L_ev + 2 * t4 + 2)
                        s.activation(scrC[:, 0:2048], xt, AF.Identity, bias=colp[:, 7:8])
                        ssc.inc(s.activation(bfSM[:, 0:2048], scrC[:, 0:2048], AF.Identity,
                                             scale=colp[:, 15:16]))
                        for b in range(4):
                            gb = 4 * t4 + b
                            s.wait_ge(smm.h, L_mm + gb + 1)
                            bank = banks[4 + (gb % 2)]
                            dst = xlv[:, 4 * b:4 * b + 4, t4 * 128:t4 * 128 + 128]
                            ssc.inc(s.activation(dst, bank.rearrange("p (h c) -> p h c", c=128),
                                                 AF.Identity))

                @block.vector
                def _(v, g=g, L_sc=L_sc):
                    for t4 in range(4):
                        t = g * 4 + t4
                        xt = xres[:, t * D:(t + 1) * D]
                        v.wait_ge(ssc.h, L_sc + 7 * t4 + 1)
                        VS(v, v.tensor_reduce(colp[:, 0:1], xt, AX, ALU.add))
                        VS(v, v.tensor_reduce(colp[:, 1:2], scrC[:, 0:2048], AX, ALU.add))
                        VS(v, v.tensor_scalar_mul(colp[:, 2:3], colp[:, 0:1], 1.0 / D))
                        VS(v, v.tensor_scalar_mul(colp[:, 3:4], colp[:, 1:2], 1.0 / D))
                        VS(v, v.tensor_tensor(colp[:, 4:5], colp[:, 2:3], colp[:, 2:3], ALU.mult))
                        VS(v, v.scalar_tensor_tensor(colp[:, 5:6], colp[:, 3:4], 1e-5, colp[:, 4:5],
                                                     ALU.add, ALU.subtract))
                        VS(v, v.reciprocal(colp[:, 6:7], colp[:, 5:6]))
                        sev.inc(v.tensor_scalar_mul(colp[:, 7:8], colp[:, 2:3], -1.0))
                        v.wait_ge(sev.h, sev.n)
                        v.wait_ge(ssc.h, L_sc + 7 * t4 + 2)
                        VS(v, v.tensor_tensor(colp[:, 14:15], colp[:, 8:9], colp[:, 8:9], ALU.mult))
                        VS(v, v.tensor_tensor(colp[:, 14:15], colp[:, 14:15], colp[:, 5:6], ALU.mult))
                        VS(v, v.tensor_scalar(colp[:, 14:15], colp[:, 14:15], -0.5, 1.5, ALU.mult, ALU.add))
                        sev.inc(v.tensor_tensor(colp[:, 15:16], colp[:, 8:9], colp[:, 14:15], ALU.mult))
                        v.wait_ge(sev.h, sev.n)

                @block.tensor
                def _(tn, g=g, L_sc=L_sc):
                    for t4 in range(4):
                        tn.wait_ge(ssc.h, L_sc + 7 * t4 + 3)
                        for h in range(16):
                            gb = 4 * t4 + h // 4
                            bank = banks[4 + (gb % 2)]
                            if h % 4 == 0 and gb >= 2:
                                pgb = gb - 2
                                tn.wait_ge(ssc.h, L_sc + 7 * (pgb // 4) + 3 + (pgb % 4) + 1)
                            mm = tn.matmul(bank[:, (h % 4) * 128:(h % 4) * 128 + 128],
                                           bfSM[:, h * 128:(h + 1) * 128], i128[:, :],
                                           start=True, stop=True)
                            if h % 4 == 3:
                                smm.inc(mm)
                bar(("ln", g))
                ck()

                def proj(pj, act, extra_st_wait=None, g=g):
                    P_g, P_mm, P_sc = sgst.n, smm.n, ssc.n

                    @block.gpsimd
                    def _(g_, pj=pj, P_mm=P_mm):
                        g_.wait_ge(smm.h, P_mm)
                        for ob in range(4):
                            if ob >= 1:
                                g_.wait_ge(smm.h, P_mm + 4 * ob)
                            src = wsrc[pj].rearrange("(hh p) o -> p hh o", p=128)[:, :, ob * 512:(ob + 1) * 512]
                            sgst.incd(g_.dma_start(out=wsv[:, 0:16, :], in_=src))

                    @block.tensor
                    def _(tn, P_g=P_g, P_sc=P_sc, extra_st_wait=extra_st_wait):
                        tn.wait_ge(ssc.h, P_sc)
                        if extra_st_wait is not None:
                            tn.wait_ge(sst.h, extra_st_wait)
                        for ob in range(4):
                            tn.wait_ge(sgst.h, P_g + 16 * (ob + 1))
                            for t4 in range(4):
                                idx = 4 * ob + t4
                                bank = banks[4 + (idx % 2)]
                                if idx >= 2:
                                    tn.wait_ge(ssc.h, P_sc + idx - 1)
                                for h in range(16):
                                    mm = tn.matmul(bank[:, :], xlv[:, h, t4 * 128:t4 * 128 + 128],
                                                   wsv[:, h, :], start=(h == 0), stop=(h == 15))
                                smm.inc(mm)

                    @block.scalar
                    def _(s, P_mm=P_mm, act=act):
                        for ob in range(4):
                            for t4 in range(4):
                                idx = 4 * ob + t4
                                s.wait_ge(smm.h, P_mm + idx + 1)
                                bank = banks[4 + (idx % 2)]
                                ssc.inc(s.activation(grp[:, 16 + t4 * 4 + ob, :], bank[:, :], act))
                    bar(("proj", g, pj))

                def rot(pj, write_kd, g=g):
                    R_ld, R_ev, R_st = sld.n, sev.n, sst.n

                    def stores(sync, t4, pj, write_kd, g):
                        t = g * 4 + t4
                        kt = bfBIG[:, 8192 + t4 * 2048:8192 + (t4 + 1) * 2048]
                        sst.incd(sync.dma_start(
                            out=cach[pj].rearrange("(a p) d -> a p d", p=128)[t], in_=kt))
                        if write_kd:
                            kd = bfSM[:, (t4 % 2) * 2048:(t4 % 2) * 2048 + 2048]
                            sst.incd(sync.dma_start(
                                out=kdcache.rearrange("(a p) d -> a p d", p=128)[t], in_=kd))

                    @block.sync
                    def _(sync, pj=pj, R_ev=R_ev, g=g, write_kd=write_kd):
                        sync.wait_ge(ssc.h, B[("proj", g, pj)][2])
                        for t4 in range(4):
                            t = g * 4 + t4
                            if t4 >= 1:
                                sync.wait_ge(sev.h, R_ev + 4 * t4)
                                stores(sync, t4 - 1, pj, write_kd, g)
                            sld.incd(sync.dma_start(out=scr[:, 0:1024],
                                                    in_=tcos.rearrange("(a p) j -> a p j", p=128)[t]))
                            sld.incd(sync.dma_start(out=scr[:, 1024:2048],
                                                    in_=tsin.rearrange("(a p) j -> a p j", p=128)[t]))
                        sync.wait_ge(sev.h, R_ev + 16)
                        stores(sync, 3, pj, write_kd, g)

                    @block.vector
                    def _(v, R_ld=R_ld, R_st=R_st, g=g, write_kd=write_kd):
                        v.wait_ge(smm.h, B[("ln", g)][0])
                        for t4 in range(4):
                            v.wait_ge(sld.h, R_ld + 32 * (t4 + 1))
                            kt = bfBIG[:, 8192 + t4 * 2048:8192 + (t4 + 1) * 2048]
                            ktv = kt.rearrange("p (h two j) -> p h two j", two=2, j=64)
                            x1, x2 = ktv[:, :, 0, :], ktv[:, :, 1, :]
                            cosr = scr[:, 0:1024].rearrange("p (h j) -> p h j", j=64)
                            sinr = scr[:, 1024:2048].rearrange("p (h j) -> p h j", j=64)
                            ra = scr2[:, 0:1024].rearrange("p (h j) -> p h j", j=64)
                            rb = scr2[:, 1024:2048].rearrange("p (h j) -> p h j", j=64)
                            rc = scrC[:, 0:1024].rearrange("p (h j) -> p h j", j=64)
                            rd = scrC[:, 1024:2048].rearrange("p (h j) -> p h j", j=64)
                            v.tensor_tensor(ra, x1, cosr, ALU.mult)
                            v.tensor_tensor(rb, x2, sinr, ALU.mult)
                            v.tensor_tensor(rc, x1, sinr, ALU.mult)
                            sev.inc(v.tensor_tensor(rd, x2, cosr, ALU.mult))
                            v.wait_ge(sev.h, sev.n)
                            sev.inc(v.tensor_tensor(x1, ra, rb, ALU.subtract))
                            v.wait_ge(sev.h, sev.n)
                            sev.inc(v.tensor_tensor(x2, rc, rd, ALU.add))
                            v.wait_ge(sev.h, sev.n)
                            if write_kd:
                                if t4 >= 2:
                                    v.wait_ge(sst.h, R_st + 2 * (t4 - 1))
                                kdd = bfSM[:, (t4 % 2) * 2048:(t4 % 2) * 2048 + 2048]
                                sev.inc(v.tensor_tensor(kdd, kt, nwrep[:, :], ALU.mult))
                                v.wait_ge(sev.h, sev.n)
                            else:
                                sev.inc(v.memset(colp[:, 33:34], 0.0))
                    bar(("rot", g, pj))

                proj("k", AF.Silu)
                ck()
                rot("k", True)
                ck()
                proj("v", AF.Identity, extra_st_wait=B[("rot", g, "k")][6])
                ck()

                F_fb, S_mm, S_ev = sfb.n, smm.n, sev.n

                @block.sync
                def _(sync, g=g, F_fb=F_fb, S_mm=S_mm):
                    sync.wait_ge(sst.h, B[("rot", g, "k")][6])
                    for t4 in range(4):
                        t = g * 4 + t4
                        if t4 >= 2:
                            sync.wait_ge(smm.h, S_mm + 16 * (t4 - 1))
                        dst = bfSM[:, (t4 % 2) * 2048:(t4 % 2) * 2048 + 2048]
                        sfb.incd(sync.dma_start(
                            out=dst, in_=kdcache.rearrange("(a p) d -> a p d", p=128)[t]))

                S_sc = ssc.n

                @block.tensor
                def _(tn, g=g, F_fb=F_fb, S_sc=S_sc):
                    for t4 in range(4):
                        tn.wait_ge(sfb.h, F_fb + 16 * (t4 + 1))
                        for h in range(16):
                            s_ = 16 * t4 + h
                            if s_ >= 4:
                                tn.wait_ge(ssc.h, S_sc + s_ - 3)
                            kdd = bfSM[:, (t4 % 2) * 2048 + h * 128:(t4 % 2) * 2048 + h * 128 + 128]
                            vt = bfBIG[:, 8192 + t4 * 2048 + h * 128:8192 + t4 * 2048 + h * 128 + 128]
                            smm.inc(tn.matmul(banks[h % 4][:, (h // 4) * 128:(h // 4) * 128 + 128],
                                              kdd, vt, start=True, stop=True))

                @block.scalar
                def _(s, g=g, S_mm=S_mm, S_ev=S_ev):
                    for t4 in range(4):
                        for h in range(16):
                            s.wait_ge(smm.h, S_mm + 16 * t4 + h + 1)
                            if t4 >= 1:
                                s.wait_ge(sev.h, S_ev + 16 * (t4 - 1) + h + 1)
                            bk = banks[h % 4][:, (h // 4) * 128:(h // 4) * 128 + 128]
                            ssc.inc(s.activation(scrC[:, h * 128:(h + 1) * 128], bk, AF.Identity))

                @block.vector
                def _(v, g=g, S_sc=S_sc):
                    v.wait_ge(ssc.h, B[("proj", g, "v")][2])
                    for t4 in range(4):
                        c = g * 4 + t4
                        for h in range(16):
                            v.wait_ge(ssc.h, S_sc + 16 * t4 + h + 1)
                            dst = f32P1[:, 2048 + h * 128:2048 + (h + 1) * 128]
                            sa = scrC[:, h * 128:(h + 1) * 128]
                            if c == 0:
                                sev.inc(v.tensor_copy(dst, sa))
                            else:
                                sev.inc(v.scalar_tensor_tensor(dst, dst, lamC_f[h], sa,
                                                               ALU.mult, ALU.add))
                            v.wait_ge(sev.h, sev.n)
                bar(("state", g))
                ck()

                @block.sync
                def _(sync, g=g):
                    sync.wait_ge(smm.h, B[("state", g)][0])
                    for t4 in range(4):
                        t = g * 4 + t4
                        vt = bfBIG[:, 8192 + t4 * 2048:8192 + (t4 + 1) * 2048]
                        sst.incd(sync.dma_start(
                            out=vcache.rearrange("(a p) d -> a p d", p=128)[t], in_=vt))
                bar(("vst", g))
                ck()

                proj("q", AF.Silu, extra_st_wait=B[("vst", g)][6])
                ck()
                rot("q", False)
                ck()

                G_g, G_mm, G_sc = sgst.n, smm.n, ssc.n

                @block.gpsimd
                def _(g_, g=g):
                    g_.wait_ge(smm.h, B[("proj", g, "q")][0])
                    sgst.incd(g_.dma_start(out=wfv[:, 0:16, :],
                                           in_=wg1_d.rearrange("(hh p) o -> p hh o", p=128)))
                    sgst.incd(g_.dma_start(out=bfSM.rearrange("p (a c) -> p a c", c=2048)[:, 1, :],
                                           in_=wg2_d[:, :]))

                @block.tensor
                def _(tn, G_g=G_g, G_sc=G_sc):
                    tn.wait_ge(sgst.h, G_g + 16)
                    tn.wait_ge(ssc.h, G_sc)
                    for h in range(16):
                        mm = tn.matmul(pb6[:, 0:512], wfv[:, h, :],
                                       xlv[:, h, 0:512], start=(h == 0), stop=(h == 15))
                    smm.inc(mm)
                    tn.wait_ge(sgst.h, G_g + 32)
                    tn.wait_ge(ssc.h, G_sc + 1)
                    for t4 in range(4):
                        for ob in range(4):
                            idx = 4 * t4 + ob
                            bank = banks[4 + (idx % 2)]
                            if idx >= 2:
                                tn.wait_ge(ssc.h, G_sc + idx)
                            smm.inc(tn.matmul(bank[:, :], bfSM[:, t4 * 128:t4 * 128 + 128],
                                              smv[:, 4 + ob, :], start=True, stop=True))

                @block.scalar
                def _(s, g=g, G_mm=G_mm):
                    s.wait_ge(sev.h, B[("rot", g, "q")][1])
                    s.wait_ge(smm.h, G_mm + 1)
                    ssc.inc(s.activation(bfSM[:, 0:512], pb6[:, 0:512], AF.Identity))
                    for t4 in range(4):
                        t = g * 4 + t4
                        for ob in range(4):
                            idx = 4 * t4 + ob
                            s.wait_ge(smm.h, G_mm + 1 + idx + 1)
                            bank = banks[4 + (idx % 2)]
                            ssc.inc(s.activation(scr2[:, ob * 512:(ob + 1) * 512], bank[:, :], AF.Sigmoid))
                        s.wait_ge(sst.h, sst.n)
                        sst.incd(s.dma_start(out=gcache.rearrange("(a p) d -> a p d", p=128)[t],
                                             in_=scr2[:, 0:2048]))
                        s.wait_ge(sst.h, sst.n)
                bar(("gate", g))
                ck()

            @block.sync
            def _(sync):
                sync.wait_ge(sev.h, B[("state", 1)][1])
                sync.wait_ge(sst.h, sst.n)
                for b in range(4):
                    dst = ag_st_in.rearrange("(hh pp) c -> pp hh c", pp=128)[:, 4 * b:4 * b + 4, :]
                    srcv = f32P1.rearrange("p (a c) -> p a c", c=128)[:, 16 + 4 * b:16 + 4 * b + 4, :]
                    sst.incd(sync.dma_start(out=dst, in_=srcv))
            E_st2 = sst.n

            @block.gpsimd
            def _(g_):
                g_.wait_ge(sst.h, E_st2)
                scc.inc(g_.collective_compute("AllGather", ALU.bypass, replica_groups=replica,
                                              ins=[ag_st_in.ap().opt()], outs=[ag_st_out.ap().opt()]))
            AT_CC = scc.n

            C_ld, C_ev = sld.n, sev.n

            @block.sync
            def _(sync):
                sync.wait_ge(scc.h, AT_CC)
                for cp in range(NCORES):
                    if cp >= 1:
                        sync.wait_ge(sev.h, C_ev + 1 + 16 * cp)
                    src = ag_st_out.rearrange("(cc hh pp) c -> cc pp hh c", pp=128, hh=HEADS)[cp]
                    sld.incd(sync.dma_start(out=scrC.rearrange("p (hh c) -> p hh c", c=128), in_=src))

            @block.vector
            def _(v):
                v.wait_ge(scc.h, AT_CC)
                sev.inc(v.memset(f32P1[:, 2048:4096], 0.0))
                for cp in range(NCORES):
                    v.wait_ge(sld.h, C_ld + 16 * (cp + 1))
                    for h in range(16):
                        sev.inc(v.scalar_tensor_tensor(
                            f32P1[:, 2048 + h * 128:2048 + h * 128 + 128],
                            scrC[:, h * 128:(h + 1) * 128], lamt[:, cp * 16 + h:cp * 16 + h + 1],
                            f32P1[:, 2048 + h * 128:2048 + h * 128 + 128], ALU.mult, ALU.add))
                    v.wait_ge(sev.h, sev.n)
            bar("sinit")
            ck()

            @block.sync
            def _(sync):
                sync.wait_ge(sld.h, sld.n)
                sld.incd(sync.dma_start(
                    out=f32P1.rearrange("p (a c) -> p a c", c=128)[:, 32:48, :],
                    in_=tdmask.rearrange("(hh s) t -> s hh t", s=128)))
                sync.wait_ge(sld.h, sld.n)
                sld.incd(sync.dma_start(out=nwrep[:, :], in_=tnwrep[:, :]))
            P2_ld0 = sld.n

            prev_ev = [None]
            for c in range(8):
                CL_ld = sfb.n

                @block.sync
                def _(sync, c=c, CL_ld=CL_ld):
                    if c == 0:
                        sync.wait_ge(smm.h, B[("gate", 1)][0])
                        sync.wait_ge(sld.h, P2_ld0)
                    else:
                        sync.wait_ge(smm.h, B[("scan", c - 1)][0])
                    for j, nm in enumerate(["q", "k", "kd", "v"]):
                        srcd = kdcache if nm == "kd" else cach[nm]
                        sfb.incd(sync.dma_start(out=bfBIG[:, j * 2048:(j + 1) * 2048],
                                                in_=srcd.rearrange("(a p) d -> a p d", p=128)[c]))

                H_mm, H_sc, H_ev = smm.n, ssc.n, sev.n
                P_EV = prev_ev[0]

                @block.tensor
                def _(tn, c=c, CL_ld=CL_ld, H_mm=H_mm, H_sc=H_sc, H_ev=H_ev):
                    tn.wait_ge(sfb.h, CL_ld + 64)
                    if c != 0:
                        tn.wait_ge(sev.h, B[("scan", c - 1)][1])
                        tn.wait_ge(ssc.h, B[("scan", c - 1)][2])
                    for h in range(16):
                        pq = pb4 if h % 2 == 0 else pb5
                        po = pb2 if h % 2 == 0 else pb3
                        scb = pb6 if h % 2 == 0 else pb7
                        if h >= 2:
                            tn.wait_ge(ssc.h, H_sc + 3 * (h - 2) + 1)
                        qs = bfBIG[:, h * 128:h * 128 + 128]
                        ks = bfBIG[:, 2048 + h * 128:2048 + h * 128 + 128]
                        tn.matmul(pq[:, 0:128], qs, i128[:, :], start=True, stop=True)
                        smm.inc(tn.matmul(pq[:, 128:256], ks, i128[:, :], start=True, stop=True))
                        tn.wait_ge(ssc.h, H_sc + 3 * h + 1)
                        if h >= 2:
                            tn.wait_ge(sev.h, H_ev + 2 * (h - 2) + 1)
                        smm.inc(tn.matmul(scb[:, 0:128],
                                          bfSM[:, 256:384], bfSM[:, 128:256], start=True, stop=True))
                        tn.wait_ge(ssc.h, H_sc + 3 * h + 2)
                        if h >= 2:
                            tn.wait_ge(ssc.h, H_sc + 3 * (h - 2) + 3)
                        tn.matmul(po[:, 0:128],
                                  bfSM[:, 128:256], bfSM[:, 384:512], start=True, stop=False)
                        tn.wait_ge(sev.h, H_ev + 2 * h + 1)
                        vt = bfBIG[:, 6144 + h * 128:6144 + h * 128 + 128]
                        smm.inc(tn.matmul(po[:, 0:128],
                                          bfSM[:, 0:128], vt, start=False, stop=True))
                        kdd = bfBIG[:, 4096 + h * 128:4096 + h * 128 + 128]
                        if h >= 2:
                            tn.wait_ge(sev.h, H_ev + 2 * (h - 2) + 2)
                        smm.inc(tn.matmul(banks[h % 2][:, 0:128], kdd, vt, start=True, stop=True))

                @block.scalar
                def _(s, c=c, H_mm=H_mm, P_EV=P_EV):
                    for h in range(16):
                        pq = pb4 if h % 2 == 0 else pb5
                        po = pb2 if h % 2 == 0 else pb3
                        s.wait_ge(smm.h, H_mm + 4 * h + 1)
                        if h >= 1:
                            s.wait_ge(smm.h, H_mm + 4 * (h - 1) + 3)
                        s.activation(bfSM[:, 128:256], pq[:, 0:128], AF.Identity)
                        ssc.inc(s.activation(bfSM[:, 256:384], pq[:, 128:256], AF.Identity))
                        if c == 0:
                            s.wait_ge(sev.h, B["sinit"][1])
                        else:
                            s.wait_ge(sev.h, P_EV + 2 * h + 2)
                        ssc.inc(s.activation(bfSM[:, 384:512],
                                             f32P1[:, 2048 + h * 128:2048 + (h + 1) * 128],
                                             AF.Identity, scale=sScl_f[h]))
                        s.wait_ge(smm.h, H_mm + 4 * h + 3)
                        ssc.inc(s.activation(f32P1[:, h * 128:(h + 1) * 128],
                                             po[:, 0:128],
                                             AF.Identity, scale=dqc[:, h:h + 1]))

                @block.vector
                def _(v, c=c, H_mm=H_mm, H_sc=H_sc):
                    if c == 0:
                        v.wait_ge(sld.h, P2_ld0)
                    for h in range(16):
                        v.wait_ge(smm.h, H_mm + 4 * h + 2)
                        dm = f32P1[:, 4096 + h * 128:4096 + h * 128 + 128]
                        scb = pb6 if h % 2 == 0 else pb7
                        sev.inc(v.tensor_tensor(bfSM[:, 0:128], scb[:, 0:128], dm, ALU.mult))
                        v.wait_ge(sev.h, sev.n)
                        v.wait_ge(smm.h, H_mm + 4 * h + 4)
                        v.wait_ge(ssc.h, H_sc + 3 * h + 2)
                        dst = f32P1[:, 2048 + h * 128:2048 + (h + 1) * 128]
                        sev.inc(v.scalar_tensor_tensor(
                            dst, dst, lamC_f[h], banks[h % 2][:, 0:128],
                            ALU.mult, ALU.add))
                        v.wait_ge(sev.h, sev.n)
                bar(("scan", c))
                prev_ev[0] = H_ev

                GA_ld, GA_ev, GA_sc = sld.n, sev.n, ssc.n

                @block.sync
                def _(sync, c=c):
                    sync.wait_ge(sld.h, sld.n)
                    sld.incd(sync.dma_start(out=scr2[:, 0:2048],
                                            in_=gcache.rearrange("(a p) d -> a p d", p=128)[c]))

                @block.vector
                def _(v, c=c, GA_ld=GA_ld, GA_sc=GA_sc):
                    v.wait_ge(sld.h, GA_ld + 16)
                    v.wait_ge(smm.h, B[("scan", c)][0])
                    v.wait_ge(ssc.h, B[("scan", c)][2])
                    sev.inc(v.tensor_tensor(scr[:, 0:2048], f32P1[:, 0:2048], scr2[:, 0:2048], ALU.mult))
                    v.wait_ge(sev.h, sev.n)
                    v.wait_ge(ssc.h, GA_sc + 1)
                    VS(v, v.tensor_reduce(colp[:, 0:16].rearrange("p (a o) -> p a o", o=1),
                                          scrC.rearrange("p (a c) -> p a c", c=128), AX, ALU.add))
                    VS(v, v.tensor_scalar(colp[:, 16:32], colp[:, 0:16], 1.0 / 128.0, 1e-6,
                                          ALU.mult, ALU.add))
                    sev.inc(v.reciprocal(colp[:, 48:64], colp[:, 16:32]))
                    v.wait_ge(sev.h, sev.n)
                    v.wait_ge(ssc.h, GA_sc + 2)
                    VS(v, v.tensor_tensor(colp[:, 0:16], colp[:, 48:64], colp[:, 48:64], ALU.mult))
                    VS(v, v.tensor_tensor(colp[:, 0:16], colp[:, 0:16], colp[:, 16:32], ALU.mult))
                    VS(v, v.tensor_scalar(colp[:, 0:16], colp[:, 0:16], -0.5, 1.5, ALU.mult, ALU.add))
                    VS(v, v.tensor_tensor(colp[:, 48:64], colp[:, 48:64], colp[:, 0:16], ALU.mult))
                    for h in range(16):
                        v.tensor_scalar_mul(scr.rearrange("p (a cc) -> p a cc", cc=128)[:, h, :],
                                            scr.rearrange("p (a cc) -> p a cc", cc=128)[:, h, :],
                                            colp[:, 48 + h:49 + h])
                    v.wait_ge(svs.h, svs.n)
                    sev.inc(v.tensor_tensor(bfSM[:, 0:2048], scr[:, 0:2048], nwrep[:, :], ALU.mult))
                    v.wait_ge(sev.h, sev.n)

                @block.scalar
                def _(s, GA_ev=GA_ev):
                    s.wait_ge(sev.h, GA_ev + 1)
                    ssc.inc(s.activation(scrC[:, 0:2048], scr[:, 0:2048], AF.Square))
                    s.wait_ge(sev.h, GA_ev + 2)
                    ssc.inc(s.activation(colp[:, 48:64], colp[:, 48:64], AF.Sqrt))
                    s.wait_ge(ssc.h, ssc.n)
                bar(("ga", c))

                @block.sync
                def _(sync, c=c):
                    sync.wait_ge(sev.h, B[("ga", c)][1])
                    sync.wait_ge(sst.h, sst.n)
                    sst.incd(sync.dma_start(out=oncache.rearrange("(a p) d -> a p d", p=128)[c],
                                            in_=bfSM[:, 0:2048]))
                bar(("sc2", c))
                ck()

            A11_mm0, A11_sc0, A11_ev0, A11_g0, A11_fb0 = smm.n, ssc.n, sev.n, sgst.n, sfb.n

            @block.gpsimd
            def _(g_):
                g_.wait_ge(smm.h, A11_mm0)
                for ob in range(4):
                    if ob >= 1:
                        g_.wait_ge(smm.h, A11_mm0 + 40 * ob)
                    src = wo_d.rearrange("(hh p) o -> p hh o", p=128)[:, :, ob * 512:(ob + 1) * 512]
                    sgst.incd(g_.dma_start(out=wsv[:, 0:16, :], in_=src))

            @block.sync
            def _(sync):
                sync.wait_ge(sst.h, B[("sc2", 7)][6])
                for ob in range(4):
                    for t in range(8):
                        it = 8 * ob + t
                        if it >= 2:
                            sync.wait_ge(smm.h, A11_mm0 + 5 * (it - 1))
                        dst = bfSM[:, (t % 2) * 2048:(t % 2) * 2048 + 2048]
                        sfb.incd(sync.dma_start(out=dst,
                                                in_=oncache.rearrange("(a p) d -> a p d", p=128)[t]))

            @block.tensor
            def _(tn):
                tn.wait_ge(ssc.h, A11_sc0)
                tn.wait_ge(sev.h, B[("sc2", 7)][1])
                for ob in range(4):
                    tn.wait_ge(sgst.h, A11_g0 + 16 * (ob + 1))
                    for t in range(8):
                        it = 8 * ob + t
                        tn.wait_ge(sfb.h, A11_fb0 + 16 * (it + 1))
                        ot = bfSM[:, (t % 2) * 2048:(t % 2) * 2048 + 2048]
                        if it >= 1:
                            tn.wait_ge(ssc.h, A11_sc0 + 4 * it)
                        for ec in range(16):
                            gb = ec // 4
                            bank = banks[4 + (gb % 2)]
                            if ec % 4 == 0 and gb >= 2:
                                tn.wait_ge(ssc.h, A11_sc0 + 4 * it + gb - 1)
                            mm = tn.matmul(bank[:, (ec % 4) * 128:(ec % 4) * 128 + 128],
                                           ot[:, ec * 128:ec * 128 + 128], i128[:, :],
                                           start=True, stop=True)
                            if ec % 4 == 3:
                                smm.inc(mm)
                        tn.wait_ge(ssc.h, A11_sc0 + 4 * it + 4)
                        if it >= 2:
                            tn.wait_ge(sev.h, A11_ev0 + it - 1)
                        accb = banks[6 + (it % 2)]
                        for ec in range(16):
                            mm = tn.matmul(accb[:, 0:512], bfBIG[:, ec * 128:ec * 128 + 128],
                                           wsv[:, ec, :], start=(ec == 0), stop=(ec == 15))
                        smm.inc(mm)

            @block.scalar
            def _(s):
                for ob in range(4):
                    for t in range(8):
                        it = 8 * ob + t
                        for gb in range(4):
                            s.wait_ge(smm.h, A11_mm0 + 5 * it + gb + 1)
                            if it >= 1 and gb == 0:
                                s.wait_ge(smm.h, A11_mm0 + 5 * it)
                            bank = banks[4 + (gb % 2)]
                            ssc.inc(s.activation(bfBIG[:, gb * 512:(gb + 1) * 512], bank[:, :],
                                                 AF.Identity))

            @block.vector
            def _(v):
                for ob in range(4):
                    for t in range(8):
                        it = 8 * ob + t
                        v.wait_ge(smm.h, A11_mm0 + 5 * it + 5)
                        accb = banks[6 + (it % 2)]
                        xsl = xres[:, t * D + ob * 512:t * D + ob * 512 + 512]
                        sev.inc(v.tensor_tensor(xsl, accb[:, 0:512], xsl, ALU.add))
            return A11_ev0 + 32

        last_ev = P0_EV
        cid = 0
        for p in parts:
            if p in ("ffn1", "ffn2"):
                last_ev = ffn_call(cid, last_ev)
                cid += 1
            elif p == "attn":
                try:
                    last_ev = attn_call(last_ev)
                except _StopEmit as e:
                    last_ev = e.ev
            else:
                raise ValueError(p)

        @block.sync
        def _(sync):
            sync.wait_ge(sev.h, last_ev)
            for t in range(NT):
                sst.incd(sync.dma_start(out=y_out.rearrange("(a p) d -> a p d", p=128)[t, :, :],
                                        in_=xres[:, t * D:(t + 1) * D]))
            if DEBUG_EPRIME:
                dbv = dbg_out.rearrange("(u p) c -> u p c", p=128)
                for u in range(N_HU):
                    sst.incd(sync.dma_start(out=dbv[u], in_=eprime[u].rearrange("p a b -> p (a b)")))
                sst.incd(sync.dma_start(out=dbg_sb[:, 0:8192], in_=f32P1[:, :]))
                sst.incd(sync.dma_start(out=dbg_sb[:, 8192:10240], in_=scr2[:, :]))
                sst.incd(sync.dma_start(out=dbg_sb[:, 10240:10304], in_=colp[:, :]))
                sst.incd(sync.dma_start(out=dbg_sb[:, 10304:10312], in_=adeq[:, :]))
                sst.incd(sync.dma_start(out=dbg_bf[:, :], in_=bfBIG[:, :]))
            sync.wait_ge(sst.h, sst.n)

    return nc


def kernel(**inputs):
    xs = np.split(np.ascontiguousarray(inputs["L0_attn_x"][0]), NCORES, axis=0)
    gs = np.split(np.ascontiguousarray(inputs["L0_ffn_w_gate"]), NCORES, axis=0)
    us = np.split(np.ascontiguousarray(inputs["L0_ffn_w_up"]), NCORES, axis=0)
    ds = np.split(np.ascontiguousarray(inputs["L0_ffn_w_down"]), NCORES, axis=1)
    nc = build_program(PARTS)
    in_maps = []
    for c in range(NCORES):
        tb = host_tables(c)
        in_maps.append({
            "x_in": xs[c],
            "wq": np.ascontiguousarray(inputs["L0_attn_Wq"]),
            "wk": np.ascontiguousarray(inputs["L0_attn_Wk"]),
            "wv": np.ascontiguousarray(inputs["L0_attn_Wv"]),
            "wo": np.ascontiguousarray(inputs["L0_attn_Wo"]),
            "wg1": np.ascontiguousarray(inputs["L0_attn_Wg1"]),
            "wg2": np.ascontiguousarray(inputs["L0_attn_Wg2"]),
            "nrmw": np.ascontiguousarray(inputs["L0_attn_norm_weight"]).reshape(1, D),
            "nrmw_rep": np.tile(np.ascontiguousarray(inputs["L0_attn_norm_weight"]).reshape(1, D), (128, 1)),
            "wg_sl": gs[c], "wu_sl": us[c], "wd_sl": np.ascontiguousarray(ds[c]),
            **tb,
        })
    res = run_bass_kernel_spmd(nc, in_maps, core_ids=list(range(NCORES)))
    y = np.concatenate([res.results[c]["y_out"] for c in range(NCORES)], axis=0)
    return y.reshape(1, T * NCORES, D).astype(np.float32)


# version 06
# speedup vs baseline: 1.4507x; 1.0349x over previous
import contextlib
import numpy as np
import ml_dtypes
import concourse.bass as bass
import concourse.mybir as mybir
from concourse.bass_utils import run_bass_kernel_spmd

F32 = mybir.dt.float32
BF16 = mybir.dt.bfloat16
I32 = mybir.dt.int32
ALU = mybir.AluOpType
AF = mybir.ActivationFunctionType
AX = mybir.AxisListType.X

NCORES = 8
T = 1024
D = 2048
HEADS = 16
HD = 128
CH = 128
NT = T // 128
HU = 64
N_HU = T // HU
NW_ELEMS = float(8192 * 2048)
PG_GU = 128 * 2048
PG_D = 128 * 1024
SL_ELEMS = 8 * PG_GU * 2 + 16 * PG_D

PARTS = ("ffn1", "attn", "ffn2")
ATTN_STAGES = None


class _StopEmit(Exception):
    def __init__(self, ev):
        self.ev = ev
DEBUG_EPRIME = False


def _sylvester(n):
    h = np.array([[1.0]])
    while h.shape[0] < n:
        h = np.block([[h, h], [h, -h]])
    return h.astype(np.float32)


def _ll():
    return -(2.0 ** (-8.0 * (np.arange(HEADS, dtype=np.float64) + 1.0) / HEADS))


def _lamtab(core):
    ll = _ll()
    t = np.zeros((128, 128), np.float32)
    for cp in range(NCORES):
        for h in range(HEADS):
            if cp < core:
                t[:, cp * 16 + h] = np.float32(np.exp(ll[h] * CH * 8.0 * (core - 1 - cp)))
    return t


def host_tables(core):
    H64B = np.kron(np.eye(2, dtype=np.float32), _sylvester(64)).astype(np.float32)
    tg = np.arange(core * T, (core + 1) * T, dtype=np.float64)
    j = np.arange(64, dtype=np.float64)
    theta = 10000.0 ** (-2.0 * j / 128.0)
    ang = tg[:, None] * theta[None, :]
    ll = _ll()
    s = np.arange(CH, dtype=np.float64)
    dmask = np.zeros((HEADS * CH, CH), dtype=np.float32)
    for h in range(HEADS):
        col = np.exp(-ll[h] * (s + 1.0 - 64.0))
        m = (np.arange(CH)[None, :] >= s[:, None])
        dmask[h * CH:(h + 1) * CH, :] = (col[:, None] * m).astype(np.float32)
    dqcol = np.exp(ll[None, :] * (s[:, None] + 1.0 - 64.0)).astype(np.float32)
    dkcol = np.exp(ll[None, :] * (CH - 1.0 - s[:, None])).astype(np.float32)
    P4 = np.zeros((32, 512), dtype=np.float32)
    for usub in range(2):
        for tlo in range(2):
            for th in range(32):
                P4[th, (2 * usub + tlo) * 128 + usub * 64 + 2 * th + tlo] = 1.0
    cos_rep = np.tile(np.cos(ang).astype(np.float32), (1, 16))
    sin_rep = np.tile(np.sin(ang).astype(np.float32), (1, 16))
    dkrep = np.zeros((128, D), np.float32)
    for h in range(HEADS):
        dkrep[:, h * 128:(h + 1) * 128] = dkcol[:, h:h + 1]
    return {
        "H128": _sylvester(128) * np.float32(1.0 / np.sqrt(8192)), "H64B": H64B,
        "I128": np.asarray(np.eye(128, dtype=np.float32).astype(ml_dtypes.bfloat16)),
        "cos_rep": cos_rep, "sin_rep": sin_rep, "dkrep": dkrep,
        "dmask": dmask, "dqcol": dqcol, "dkcol": dkcol, "P4": P4,
        "lamtab": _lamtab(core),
        "SEL2": np.concatenate([np.kron(np.eye(2, dtype=np.float32), np.ones((1, 64), np.float32)), np.zeros((30, 128), np.float32)], 0),
    }


class Sem:
    def __init__(self, h):
        self.h, self.n = h, 0

    def inc(self, instr, k=1):
        instr.then_inc(self.h, k)
        self.n += k
        return instr

    def incd(self, instr):
        return self.inc(instr, 16)


def _sched_B(base_sc, base_ev, base_mm, base_gp):
    M = {}
    sc, ev, mm, gp = base_sc, base_ev, base_mm, base_gp
    for u in range(N_HU):
        sc += 1; M["reord", u] = sc
        for b in range(8):
            mm += 1; M["r1", u, b] = mm
            if b >= 2:
                mm += 1; M["r2", u, b - 2] = mm
        for b in [6, 7]:
            mm += 1; M["r2", u, b] = mm
        for b in range(8):
            ev += 1; M["r1ev", u, b] = ev
        for b in range(8):
            sc += 1; M["yev", u, b] = sc
        ev += 1; M["mx1", u] = ev
        mm += 1; M["mx1T", u] = mm
        ev += 1; M["mx1Tev", u] = ev
        ev += 1; M["chain", u] = ev
        mm += 1; M["sT", u] = mm
        ev += 1; M["sTev", u] = ev
        mm += 1; M["ssm", u] = mm
        ev += 1; M["ssmev", u] = ev
        ev += 1; M["vmult", u] = ev
        for half in range(2):
            ev += 1; M["vp", u, half] = ev
            ev += 1; M["c", u, half] = ev
            ev += 1; M["dd", u, half] = ev
            ev += 1; M["ge", u, half] = ev
            ev += 1; M["r", u, half] = ev
            ev += 1; M["q", u, half] = ev
        M["vdone", u] = ev
        for b in range(8):
            mm += 1; M["r3", u, b] = mm
        for b in range(8):
            sc += 1; M["r3ev", u, b] = sc
        if u % 4 == 3:
            g4 = u // 4
            mm += 1; M["dqc", g4, 0] = mm
            mm += 1; M["dqc", g4, 1] = mm
            ev += 2; M["dqcev", g4] = ev
            for o in range(16):
                mm += 1; M["dn", g4, o] = mm
                sc += 1; M["dnev", g4, o] = sc
                for tb in range(2):
                    mm += 1; M["dntm", g4, o, tb] = mm
                    ev += 1; M["resid", g4, o, tb] = ev
    return M, sc, ev, mm, gp


def build_program(parts=PARTS):
    nc = bass.Bass("TRN2", target_bir_lowering=False, debug=False)
    ll = _ll()
    lamC = [float(np.float32(np.exp(ll[h] * CH))) for h in range(HEADS)]
    sScl = [float(np.float32(np.exp(ll[h] * 64.0))) for h in range(HEADS)]

    x_in = nc.dram_tensor("x_in", [T, D], F32, kind="ExternalInput").ap()
    wq_d = nc.dram_tensor("wq", [D, D], F32, kind="ExternalInput").ap()
    wk_d = nc.dram_tensor("wk", [D, D], F32, kind="ExternalInput").ap()
    wv_d = nc.dram_tensor("wv", [D, D], F32, kind="ExternalInput").ap()
    wo_d = nc.dram_tensor("wo", [D, D], F32, kind="ExternalInput").ap()
    wg1_d = nc.dram_tensor("wg1", [D, HD], F32, kind="ExternalInput").ap()
    wg2_d = nc.dram_tensor("wg2", [HD, D], F32, kind="ExternalInput").ap()
    nrmw_d = nc.dram_tensor("nrmw", [1, D], F32, kind="ExternalInput").ap()
    wg_sl = nc.dram_tensor("wg_sl", [T, D], F32, kind="ExternalInput").ap()
    wu_sl = nc.dram_tensor("wu_sl", [T, D], F32, kind="ExternalInput").ap()
    wd_sl = nc.dram_tensor("wd_sl", [D, T], F32, kind="ExternalInput").ap()
    y_out = nc.dram_tensor("y_out", [T, D], F32, kind="ExternalOutput").ap()
    dbg_out = (nc.dram_tensor("dbg_out", [N_HU * 128, 64 * HU], F32, kind="ExternalOutput").ap()
               if DEBUG_EPRIME else None)
    dbg_sb = (nc.dram_tensor("dbg_sb", [128, 10312], F32, kind="ExternalOutput").ap()
              if DEBUG_EPRIME else None)
    dbg_bf = (nc.dram_tensor("dbg_bf", [128, 16384], BF16, kind="ExternalOutput").ap()
              if DEBUG_EPRIME else None)

    tH128 = nc.dram_tensor("H128", [128, 128], F32, kind="ExternalInput").ap()
    tH64B = nc.dram_tensor("H64B", [128, 128], F32, kind="ExternalInput").ap()
    tI128 = nc.dram_tensor("I128", [128, 128], BF16, kind="ExternalInput").ap()
    tcos = nc.dram_tensor("cos_rep", [T, 1024], F32, kind="ExternalInput").ap()
    tsin = nc.dram_tensor("sin_rep", [T, 1024], F32, kind="ExternalInput").ap()
    tdkrep = nc.dram_tensor("dkrep", [128, D], F32, kind="ExternalInput").ap()
    tnwrep = nc.dram_tensor("nrmw_rep", [128, D], F32, kind="ExternalInput").ap()
    tdmask = nc.dram_tensor("dmask", [HEADS * CH, CH], F32, kind="ExternalInput").ap()
    tdqcol = nc.dram_tensor("dqcol", [CH, HEADS], F32, kind="ExternalInput").ap()
    tdkcol = nc.dram_tensor("dkcol", [CH, HEADS], F32, kind="ExternalInput").ap()
    tP4 = nc.dram_tensor("P4", [32, 512], F32, kind="ExternalInput").ap()
    tlam = nc.dram_tensor("lamtab", [128, 128], F32, kind="ExternalInput").ap()
    tSEL2 = nc.dram_tensor("SEL2", [32, 128], F32, kind="ExternalInput").ap()

    ag_w_in = nc.dram_tensor("ag_w_in", [SL_ELEMS], BF16)
    GU_SL = 16 * PG_GU
    D_SL = 16 * PG_D
    ag_gu_out = nc.dram_tensor("ag_gu_out", [NCORES * GU_SL], BF16)
    ag_d_out = nc.dram_tensor("ag_d_out", [NCORES * D_SL], BF16)
    ar_s_in = nc.dram_tensor("ar_s_in", [128, 4], F32)
    ar_s_out = nc.dram_tensor("ar_s_out", [128, 4], F32)
    ag_st_in = nc.dram_tensor("ag_st_in", [HEADS * HD, HD], F32)
    ag_st_out = nc.dram_tensor("ag_st_out", [NCORES * HEADS * HD, HD], F32)
    eprime = nc.dram_tensor("eprime", [N_HU, 128, 64, HU], F32)
    kcache = nc.dram_tensor("kcache", [T, D], BF16)
    kdcache = nc.dram_tensor("kdcache", [T, D], BF16)
    vcache = nc.dram_tensor("vcache", [T, D], BF16)
    qcache = nc.dram_tensor("qcache", [T, D], BF16)
    gcache = nc.dram_tensor("gcache", [T, D], F32)
    oncache = nc.dram_tensor("oncache", [T, D], BF16)
    replica = [list(range(NCORES))]

    with contextlib.ExitStack() as _stack:
        _e = _stack.enter_context
        xres = _e(nc.sbuf_tensor("xres", [128, NT * D], F32))
        h128 = _e(nc.sbuf_tensor("h128", [128, 128], F32))
        h64b = _e(nc.sbuf_tensor("h64b", [128, 128], F32))
        i128 = _e(nc.sbuf_tensor("i128", [128, 128], BF16))
        i128f = _e(nc.sbuf_tensor("i128f", [128, 128], F32))
        ones = _e(nc.sbuf_tensor("ones", [128, 128], F32))
        sel2 = _e(nc.sbuf_tensor("sel2", [32, 128], F32))
        srow = _e(nc.sbuf_tensor("srow", [32, 64], F32))
        p4 = _e(nc.sbuf_tensor("p4", [32, 512], F32))
        colp = _e(nc.sbuf_tensor("colp", [128, 64], F32))
        adeq = _e(nc.sbuf_tensor("adeq", [128, NT], F32))
        dqc = _e(nc.sbuf_tensor("dqc", [128, HEADS], F32))
        dkc = _e(nc.sbuf_tensor("dkc", [128, HEADS], F32))
        nwrep = _e(nc.sbuf_tensor("nwrep", [128, D], F32))
        lamt = _e(nc.sbuf_tensor("lamt", [128, 128], F32))
        bfBIG = _e(nc.sbuf_tensor("bfBIG", [128, 16384], BF16))
        bfSM = _e(nc.sbuf_tensor("bfSM", [128, 4096], BF16))
        f32P1 = _e(nc.sbuf_tensor("f32P1", [128, 8192], F32))
        scr = _e(nc.sbuf_tensor("scr", [128, 4096], F32))
        scr2 = _e(nc.sbuf_tensor("scr2", [128, 2048], F32))
        scrC = _e(nc.sbuf_tensor("scrC", [128, 2048], F32))
        bfW = _e(nc.sbuf_tensor("bfW", [128, 8192], BF16))
        iscr = _e(nc.sbuf_tensor("iscr", [128, 2048], I32))
        _s1 = _e(nc.semaphore("s_ld"))
        _s2 = _e(nc.semaphore("s_w"))
        _s3 = _e(nc.semaphore("s_mm"))
        _s4 = _e(nc.semaphore("s_ev"))
        _s5 = _e(nc.semaphore("s_sc"))
        _s6 = _e(nc.semaphore("s_gp"))
        _s7 = _e(nc.semaphore("s_st"))
        _s8 = _e(nc.semaphore("s_cc"))
        _s9 = _e(nc.semaphore("s_fb"))
        _s10 = _e(nc.semaphore("s_gst"))
        _s11 = _e(nc.semaphore("s_vs"))
        _s12 = _e(nc.semaphore("s_ss"))
        pb0 = _e(nc.psum_tensor("pb0", [128, 512], F32))
        pb1 = _e(nc.psum_tensor("pb1", [128, 512], F32))
        pb2 = _e(nc.psum_tensor("pb2", [128, 512], F32))
        pb3 = _e(nc.psum_tensor("pb3", [128, 512], F32))
        pb4 = _e(nc.psum_tensor("pb4", [128, 512], F32))
        pb5 = _e(nc.psum_tensor("pb5", [128, 512], F32))
        pb6 = _e(nc.psum_tensor("pb6", [128, 512], F32))
        pb7 = _e(nc.psum_tensor("pb7", [128, 512], F32))
        block = _e(nc.Block())
        sld, sw, smm, sev, ssc, sgp, sst, scc, sfb = (
            Sem(_s1), Sem(_s2), Sem(_s3), Sem(_s4), Sem(_s5), Sem(_s6), Sem(_s7), Sem(_s8), Sem(_s9))
        sgst = Sem(_s10)
        svs, sss = Sem(_s11), Sem(_s12)

        def VS(v, instr):
            svs.inc(instr)
            v.wait_ge(svs.h, svs.n)
            return instr

        def SS(s, instr):
            sss.inc(instr)
            s.wait_ge(sss.h, sss.n)
            return instr
        banks = [pb0, pb1, pb2, pb3, pb4, pb5, pb6, pb7]

        @block.sync
        def _(sync):
            for t in range(NT):
                sld.incd(sync.dma_start(out=xres[:, t * D:(t + 1) * D],
                                        in_=x_in.rearrange("(a p) d -> a p d", p=128)[t, :, :]))
            sld.incd(sync.dma_start(out=h128[:], in_=tH128[:, :]))
            sld.incd(sync.dma_start(out=h64b[:], in_=tH64B[:, :]))
            sld.incd(sync.dma_start(out=i128[:], in_=tI128[:, :]))
            sld.incd(sync.dma_start(out=p4[:, :], in_=tP4[:, :]))
            sld.incd(sync.dma_start(out=dqc[:, :], in_=tdqcol[:, :]))
            sld.incd(sync.dma_start(out=dkc[:, :], in_=tdkcol[:, :]))
            sld.incd(sync.dma_start(out=sel2[:, :], in_=tSEL2[:, :]))
            sld.incd(sync.dma_start(out=lamt[:, :], in_=tlam[:, :]))
        P0_LD = sld.n

        @block.vector
        def _(v):
            v.wait_ge(sld.h, P0_LD)
            v.tensor_copy(i128f[:], i128[:])
            v.memset(srow[:, :], 0.0)
            sev.inc(v.memset(ones[:], 1.0))
        P0_EV = sev.n

        tiles = [(0, wg_sl, i, 2048) for i in range(8)] + \
                [(1, wu_sl, i, 2048) for i in range(8)] + \
                [(2, wd_sl, i, 1024) for i in range(16)]
        B1_ev = sev.n

        @block.sync
        def _(sync):
            sync.wait_ge(sld.h, P0_LD)
            for idx, (m, ap_, i, w_) in enumerate(tiles):
                if idx >= 2:
                    sync.wait_ge(sev.h, B1_ev + idx - 1)
                if idx >= 1:
                    sync.wait_ge(sld.h, P0_LD + 16 * idx)
                buf = scr if idx % 2 == 0 else scr2
                src = ap_.rearrange("(a p) d -> a p d", p=128)[i, :, :]
                sld.incd(sync.dma_start(out=buf[:, 0:w_], in_=src))

        @block.vector
        def _(v):
            v.wait_ge(sev.h, P0_EV)
            v.memset(colp[:, 0:4], 0.0)
            for idx, (m, ap_, i, w_) in enumerate(tiles):
                v.wait_ge(sld.h, P0_LD + 16 * (idx + 1))
                buf = scr if idx % 2 == 0 else scr2
                VS(v, v.tensor_reduce(colp[:, 8:9], buf[:, 0:w_], AX, ALU.add, apply_absolute_value=True))
                sev.inc(v.tensor_tensor(colp[:, m:m + 1], colp[:, m:m + 1], colp[:, 8:9], ALU.add))
                v.wait_ge(sev.h, sev.n)
        P1A_EV = sev.n

        @block.tensor
        def _(tn):
            tn.wait_ge(sev.h, P1A_EV)
            smm.inc(tn.matmul(pb0[0:128, 0:4], ones[:, 0:128], colp[:, 0:4], start=True, stop=True))
        P1A_MM = smm.n

        @block.vector
        def _(v):
            v.wait_ge(smm.h, P1A_MM)
            sev.inc(v.tensor_copy(colp[:, 16:20], pb0[:, 0:4]))
        P1B_EV = sev.n

        @block.sync
        def _(sync):
            sync.wait_ge(sev.h, P1B_EV)
            sst.incd(sync.dma_start(out=ar_s_in[:, :], in_=colp[:, 16:20]))
        P1_ST1 = sst.n

        @block.gpsimd
        def _(g):
            g.wait_ge(sst.h, P1_ST1)
            scc.inc(g.collective_compute("AllReduce", ALU.add, replica_groups=replica,
                                         ins=[ar_s_in.ap().opt()], outs=[ar_s_out.ap().opt()]))
        P1_CC1 = scc.n

        @block.sync
        def _(sync):
            sync.wait_ge(scc.h, P1_CC1)
            sync.wait_ge(sld.h, sld.n)
            sld.incd(sync.dma_start(out=colp[:, 20:24], in_=ar_s_out[:, :]))
        P1_LD2 = sld.n

        @block.vector
        def _(v):
            v.wait_ge(sld.h, P1_LD2)
            VS(v, v.tensor_scalar(colp[:, 24:28], colp[:, 20:24], 1.0 / NW_ELEMS, 1e-5, ALU.mult, ALU.max))
            VS(v, v.reciprocal(colp[:, 40:44], colp[:, 24:28]))
            sev.inc(v.reciprocal(colp[:, 44:48], colp[:, 40:44]))
            v.wait_ge(sev.h, sev.n)
        P1D_EV = sev.n

        PAGE_OFF = [0, 8 * PG_GU, 16 * PG_GU]
        B2_ld, B2_ev, B2_mm, B2_sc, B2_st = sw.n, sev.n, smm.n, ssc.n, sst.n
        nbat = [w_ // 512 for (_, _, _, w_) in tiles]
        cumb = np.cumsum([0] + nbat).tolist()

        @block.sync
        def _(sync):
            for idx, (m, ap_, i, w_) in enumerate(tiles):
                if idx >= 2:
                    sync.wait_ge(sev.h, B2_ev + idx - 1)
                if idx >= 1:
                    sync.wait_ge(sw.h, B2_ld + 16 * idx)
                buf = scr if idx % 2 == 0 else scr2
                src = ap_.rearrange("(a p) d -> a p d", p=128)[i, :, :]
                sw.incd(sync.dma_start(out=buf[:, 0:w_], in_=src))

        @block.vector
        def _(v):
            v.wait_ge(sev.h, P1D_EV)
            for idx, (m, ap_, i, w_) in enumerate(tiles):
                v.wait_ge(sw.h, B2_ld + 16 * (idx + 1))
                if idx >= 2:
                    v.wait_ge(smm.h, B2_mm + cumb[idx - 1])
                buf = scr if idx % 2 == 0 else scr2
                tb = bfSM[:, 0:2048] if idx % 2 == 0 else bfSM[:, 2048:4096]
                sc_ = colp[:, 40 + m:41 + m]
                v.tensor_scalar(scrC[:, 0:w_], buf[:, 0:w_], sc_, 0.5, ALU.mult, ALU.is_ge)
                v.tensor_scalar(buf[:, 0:w_], buf[:, 0:w_], sc_, -0.5, ALU.mult, ALU.is_le)
                sev.inc(v.tensor_tensor(tb[:, 0:w_], scrC[:, 0:w_], buf[:, 0:w_], ALU.subtract))

        @block.tensor
        def _(tn):
            gb = 0
            for idx, (m, ap_, i, w_) in enumerate(tiles):
                tn.wait_ge(sev.h, B2_ev + idx + 1)
                tb = bfSM[:, 0:2048] if idx % 2 == 0 else bfSM[:, 2048:4096]
                for h in range(w_ // 128):
                    bank = banks[4 + (gb % 2)]
                    if h % 4 == 0 and gb >= 2:
                        tn.wait_ge(ssc.h, B2_sc + gb - 1)
                    mm = tn.matmul(bank[:, (h % 4) * 128:(h % 4) * 128 + 128],
                                   tb[:, h * 128:(h + 1) * 128], i128[:, :],
                                   start=True, stop=True)
                    if h % 4 == 3:
                        smm.inc(mm)
                        gb += 1

        @block.scalar
        def _(s):
            gb = 0
            for idx, (m, ap_, i, w_) in enumerate(tiles):
                slot = bfW[:, 0:4096] if idx % 2 == 0 else bfW[:, 4096:8192]
                if idx >= 2:
                    s.wait_ge(sgst.h, 16 * (idx - 1))
                for b in range(w_ // 512):
                    s.wait_ge(smm.h, B2_mm + gb + 1)
                    bank = banks[4 + (gb % 2)]
                    ssc.inc(s.activation(slot[:, b * 512:(b + 1) * 512], bank[:, :], AF.Identity))
                    gb += 1

        @block.gpsimd
        def _(g):
            for idx, (m, ap_, i, w_) in enumerate(tiles):
                g.wait_ge(ssc.h, B2_sc + cumb[idx + 1])
                if idx >= 1:
                    g.wait_ge(sgst.h, 16 * idx)
                slot = bfW[:, 0:4096] if idx % 2 == 0 else bfW[:, 4096:8192]
                off = PAGE_OFF[m] + i * (PG_GU if m < 2 else PG_D)
                dst = ag_w_in.ap()[off:off + 128 * w_].rearrange("(p c) -> p c", c=w_)
                sgst.incd(g.dma_start(out=dst, in_=slot[:, 0:w_]))
                if idx == 15:
                    g.wait_ge(sgst.h, 16 * 16)
                    scc.inc(g.collective_compute(
                        "AllGather", ALU.bypass, replica_groups=replica,
                        ins=[ag_w_in.ap()[0:GU_SL].opt()], outs=[ag_gu_out.ap().opt()]))
            g.wait_ge(sgst.h, sgst.n)
            scc.inc(g.collective_compute(
                "AllGather", ALU.bypass, replica_groups=replica,
                ins=[ag_w_in.ap()[GU_SL:SL_ELEMS].opt()], outs=[ag_d_out.ap().opt()]))
        P1E_ST = sst.n
        P1_CC_GU = scc.n - 1
        P1_CC2 = scc.n

        def gu_page(m, i1):
            r, iloc = i1 >> 3, i1 & 7
            return r * GU_SL + PAGE_OFF[m] + iloc * PG_GU

        def d_page(o, r):
            return r * D_SL + o * PG_D

        xq = bfBIG.rearrange("p (hh c) -> p hh c", c=T)

        def ffn_call(cid, wait_ev):
            CALL_mm0 = smm.n
            L_ev, L_sc, L_mm = sev.n, ssc.n, smm.n

            @block.scalar
            def _(s):
                s.wait_ge(sev.h, wait_ev)
                for t in range(NT):
                    xt = xres[:, t * D:(t + 1) * D]
                    ssc.inc(s.activation(scrC[:, 0:2048], xt, AF.Square))
                    s.wait_ge(sev.h, L_ev + 3 * t + 1)
                    ssc.inc(s.activation(colp[:, 8:9], colp[:, 6:7], AF.Sqrt))
                    s.wait_ge(ssc.h, ssc.n)
                    s.wait_ge(sev.h, L_ev + 3 * t + 2)
                    s.activation(scrC[:, 0:2048], xt, AF.Identity, bias=colp[:, 7:8])
                    ssc.inc(s.activation(scr[:, 0:2048], scrC[:, 0:2048], AF.Identity, scale=colp[:, 15:16]))
                    for b in range(4):
                        gb = 4 * t + b
                        s.wait_ge(smm.h, L_mm + gb + 1)
                        bank = banks[4 + (gb % 2)]
                        dst = xq[:, 4 * b:4 * b + 4, t * 128:t * 128 + 128]
                        ssc.inc(s.activation(dst, bank.rearrange("p (h c) -> p h c", c=128), AF.Identity))

            @block.vector
            def _(v):
                for t in range(NT):
                    xt = xres[:, t * D:(t + 1) * D]
                    v.wait_ge(ssc.h, L_sc + 7 * t + 1)
                    VS(v, v.tensor_reduce(colp[:, 0:1], xt, AX, ALU.add))
                    VS(v, v.tensor_reduce(colp[:, 1:2], scrC[:, 0:2048], AX, ALU.add))
                    VS(v, v.tensor_scalar_mul(colp[:, 2:3], colp[:, 0:1], 1.0 / D))
                    VS(v, v.tensor_scalar_mul(colp[:, 3:4], colp[:, 1:2], 1.0 / D))
                    VS(v, v.tensor_tensor(colp[:, 4:5], colp[:, 2:3], colp[:, 2:3], ALU.mult))
                    VS(v, v.scalar_tensor_tensor(colp[:, 5:6], colp[:, 3:4], 1e-5, colp[:, 4:5],
                                                 ALU.add, ALU.subtract))
                    VS(v, v.reciprocal(colp[:, 6:7], colp[:, 5:6]))
                    sev.inc(v.tensor_scalar_mul(colp[:, 7:8], colp[:, 2:3], -1.0))
                    v.wait_ge(sev.h, sev.n)
                    v.wait_ge(ssc.h, L_sc + 7 * t + 2)
                    VS(v, v.tensor_tensor(colp[:, 14:15], colp[:, 8:9], colp[:, 8:9], ALU.mult))
                    VS(v, v.tensor_tensor(colp[:, 14:15], colp[:, 14:15], colp[:, 5:6], ALU.mult))
                    VS(v, v.tensor_scalar(colp[:, 14:15], colp[:, 14:15], -0.5, 1.5, ALU.mult, ALU.add))
                    sev.inc(v.tensor_tensor(colp[:, 15:16], colp[:, 8:9], colp[:, 14:15], ALU.mult))
                    v.wait_ge(sev.h, sev.n)
                    v.wait_ge(ssc.h, L_sc + 7 * t + 3)
                    if t >= 2:
                        v.wait_ge(smm.h, L_mm + 4 * (t - 1))
                    VS(v, v.tensor_reduce(colp[:, 10:11], scr[:, 0:2048], AX, ALU.max, apply_absolute_value=True))
                    VS(v, v.tensor_scalar_max(colp[:, 11:12], colp[:, 10:11], 1e-5))
                    VS(v, v.reciprocal(colp[:, 12:13], colp[:, 11:12]))
                    VS(v, v.tensor_scalar_mul(colp[:, 13:14], colp[:, 12:13], 128.0))
                    VS(v, v.reciprocal(adeq[:, t:t + 1], colp[:, 13:14]))
                    sl = bfSM[:, (t % 2) * 2048:(t % 2) * 2048 + 2048]
                    v.tensor_scalar_mul(scr2[:, 0:2048], scr[:, 0:2048], colp[:, 13:14])
                    v.tensor_scalar_add(iscr[:, 0:2048], scr2[:, 0:2048], 256.0)
                    v.scalar_tensor_tensor(scr[:, 0:2048], scr2[:, 0:2048], 256.0, iscr[:, 0:2048],
                                           ALU.add, ALU.subtract)
                    v.scalar_tensor_tensor(scr2[:, 0:2048], scr[:, 0:2048], 0.5, iscr[:, 0:2048],
                                           ALU.is_ge, ALU.add)
                    sev.inc(v.tensor_scalar(sl, scr2[:, 0:2048], -256.0, 127.0, ALU.add, ALU.min))

            @block.tensor
            def _(tn):
                for t in range(NT):
                    tn.wait_ge(sev.h, L_ev + 3 * t + 3)
                    sl = bfSM[:, (t % 2) * 2048:(t % 2) * 2048 + 2048]
                    for h in range(16):
                        gb = 4 * t + h // 4
                        bank = banks[4 + (gb % 2)]
                        if h % 4 == 0 and gb >= 2:
                            pgb = gb - 2
                            tn.wait_ge(ssc.h, L_sc + 7 * (pgb // 4) + 3 + (pgb % 4) + 1)
                        mm = tn.matmul(bank[:, (h % 4) * 128:(h % 4) * 128 + 128],
                                       sl[:, h * 128:(h + 1) * 128], i128[:, :],
                                       start=True, stop=True)
                        if h % 4 == 3:
                            smm.inc(mm)
            L_done_ev = sev.n

            A_w, A_mm, A_ev, A_sc, A_gp, A_st = sw.n, smm.n, sev.n, ssc.n, sgp.n, sst.n

            def f_mm_deq(g, ts_):
                return A_mm + 68 * g + ts_ + 1

            def f_mm_gi(gi):
                return A_mm + 68 * (gi // 64) + 4 + (gi % 64) + 1

            def f_sc_b(g, ts_):
                return A_sc + 68 * g + ts_ + 1

            def f_sc_silu(gi):
                return A_sc + 68 * (gi // 64) + 4 + (gi % 64) + 1

            def f_ev_deq(g, ts_):
                return A_ev + 132 * g + ts_ + 1

            def f_ev_t0(gi):
                return A_ev + 132 * (gi // 64) + 4 + 2 * (gi % 64) + 1

            def f_ev_u1(gi):
                return A_ev + 132 * (gi // 64) + 4 + 2 * (gi % 64) + 2

            def f_gp(gi):
                return A_gp + gi + 1

            @block.sync
            def _(sync):
                sync.wait_ge(scc.h, P1_CC_GU)
                sync.wait_ge(smm.h, CALL_mm0)
                for gi in range(128):
                    i1 = gi % 64
                    if gi >= 2:
                        sync.wait_ge(smm.h, f_mm_gi(gi - 2))
                    if gi >= 1:
                        sync.wait_ge(sw.h, A_w + 32 * gi)
                    wslot = bfW[:, (gi % 2) * 4096:(gi % 2) * 4096 + 4096]
                    go, uo = gu_page(0, i1), gu_page(1, i1)
                    sw.incd(sync.dma_start(out=wslot[:, 0:2048],
                                           in_=ag_gu_out.ap()[go:go + PG_GU].rearrange("(p c) -> p c", c=2048)))
                    sw.incd(sync.dma_start(out=wslot[:, 2048:4096],
                                           in_=ag_gu_out.ap()[uo:uo + PG_GU].rearrange("(p c) -> p c", c=2048)))

            @block.scalar
            def _(s):
                for g in range(2):
                    for ts_ in range(4):
                        tt = 4 * g + ts_
                        if g == 0 and ts_ == 0:
                            s.wait_ge(sev.h, L_done_ev)
                        if g == 1 and ts_ == 0:
                            s.wait_ge(smm.h, f_mm_deq(0, 3))
                        ssc.inc(s.activation(scrC[:, 0:128], ones[:, 0:128], AF.Identity,
                                             scale=adeq[:, tt:tt + 1]))
                        if ts_ < 3:
                            s.wait_ge(smm.h, f_mm_deq(g, ts_))
                    for i1 in range(64):
                        gi = g * 64 + i1
                        s.wait_ge(sev.h, f_ev_t0(gi))
                        if gi >= 2:
                            s.wait_ge(sgp.h, f_gp(gi - 2))
                        t0slot = scrC[:, 512 + (gi % 2) * 768:512 + (gi % 2) * 768 + 512]
                        oslot = scr[:, (gi % 2) * 2048:(gi % 2) * 2048 + 512]
                        ssc.inc(s.activation(oslot, t0slot, AF.Silu))
                        if gi % 8 == 7:
                            b = gi // 8
                            s.wait_ge(sgp.h, f_gp(gi))
                            if b >= 1:
                                s.wait_ge(sst.h, A_st + 128 * b)
                            for uloc in range(8):
                                u = g * 8 + uloc
                                fsrc = f32P1[:, 0:4096].rearrange(
                                    "p (sl c) -> p sl c", c=512)[:, :, uloc * 64:(uloc + 1) * 64]
                                sst.incd(s.dma_start(out=eprime[u, :, (b % 8) * 8:(b % 8) * 8 + 8, :],
                                                     in_=fsrc))

            @block.gpsimd
            def _(g_):
                for gi in range(128):
                    g_.wait_ge(ssc.h, f_sc_silu(gi))
                    g_.wait_ge(sev.h, f_ev_u1(gi))
                    if gi >= 8:
                        g_.wait_ge(sst.h, A_st + 128 * (gi // 8))
                    oslot = scr[:, (gi % 2) * 2048:(gi % 2) * 2048 + 512]
                    u1slot = scr[:, (gi % 2) * 2048 + 512:(gi % 2) * 2048 + 1024]
                    slot = f32P1[:, (gi % 8) * 512:(gi % 8) * 512 + 512]
                    sgp.inc(g_.tensor_tensor(slot, oslot, u1slot, ALU.mult))

            @block.vector
            def _(v):
                for g in range(2):
                    for ts_ in range(4):
                        v.wait_ge(smm.h, f_mm_deq(g, ts_))
                        sev.inc(v.tensor_copy(scr2[:, ts_ * 128:(ts_ + 1) * 128],
                                              pb6[:, ts_ * 128:(ts_ + 1) * 128]))
                        v.wait_ge(sev.h, sev.n)
                    for i1 in range(64):
                        gi = g * 64 + i1
                        v.wait_ge(smm.h, f_mm_gi(gi))
                        gB = banks[gi % 2]
                        uB = banks[2 + (gi % 2)]
                        t0slot = scrC[:, 512 + (gi % 2) * 768:512 + (gi % 2) * 768 + 512]
                        oslot = scr[:, (gi % 2) * 2048:(gi % 2) * 2048 + 512]
                        u1slot = scr[:, (gi % 2) * 2048 + 512:(gi % 2) * 2048 + 1024]
                        if gi >= 2:
                            v.wait_ge(ssc.h, f_sc_silu(gi - 2))
                        sev.inc(v.scalar_tensor_tensor(t0slot, gB[:, :], colp[:, 44:45],
                                                       scr2[:, 0:512], ALU.mult, ALU.mult))
                        if gi >= 2:
                            v.wait_ge(sgp.h, f_gp(gi - 2))
                        sev.inc(v.scalar_tensor_tensor(u1slot, uB[:, :], colp[:, 45:46],
                                                       scr2[:, 0:512], ALU.mult, ALU.mult))

            @block.tensor
            def _(tn):
                tn.wait_ge(scc.h, P1_CC_GU)
                for g in range(2):
                    for ts_ in range(4):
                        tn.wait_ge(ssc.h, f_sc_b(g, ts_))
                        if g == 1:
                            tn.wait_ge(sev.h, f_ev_deq(0, ts_))
                        smm.inc(tn.matmul(pb6[:, ts_ * 128:(ts_ + 1) * 128], scrC[:, 0:128],
                                          i128f[:, :], start=True, stop=True))
                    for i1 in range(64):
                        gi = g * 64 + i1
                        tn.wait_ge(sw.h, A_w + 32 * (gi + 1))
                        if gi >= 2:
                            tn.wait_ge(sev.h, f_ev_u1(gi - 2))
                        wslot = bfW[:, (gi % 2) * 4096:(gi % 2) * 4096 + 4096]
                        gB = banks[gi % 2]
                        uB = banks[2 + (gi % 2)]
                        for h in range(16):
                            tn.matmul(gB[:, :], wslot[:, h * 128:(h + 1) * 128],
                                      xq[:, h, g * 512:(g + 1) * 512],
                                      start=(h == 0), stop=(h == 15))
                        for h in range(16):
                            mm = tn.matmul(uB[:, :], wslot[:, 2048 + h * 128:2048 + (h + 1) * 128],
                                           xq[:, h, g * 512:(g + 1) * 512],
                                           start=(h == 0), stop=(h == 15))
                        smm.inc(mm)


            A_st_end = A_st + 128 * 16

            B_fb, B_sc, B_ev, B_mm, B_gp, B_w2 = sfb.n, ssc.n, sev.n, smm.n, sgp.n, sw.n
            M, _, _, _, _ = _sched_B(B_sc, B_ev, B_mm, B_gp)
            eq = f32P1[:, 0:4096]
            ewin = f32P1[:, 4096:8192]
            zr = scrC

            @block.sync
            def _(sync):
                sync.wait_ge(sst.h, A_st_end)
                for u in range(N_HU):
                    if u >= 1:
                        sync.wait_ge(sev.h, M["vdone", u - 1])
                        sync.wait_ge(sfb.h, B_fb + 16 * u)
                    src = eprime[u].rearrange("p a b -> p (a b)")
                    sfb.incd(sync.dma_start(out=eq, in_=src))


            @block.scalar
            def _(s):
                for u in range(N_HU):
                    u4 = u % 4
                    s.wait_ge(sfb.h, B_fb + 16 * (u + 1))
                    if u >= 1:
                        s.wait_ge(smm.h, M["r1", u - 1, 7])
                        s.wait_ge(sev.h, M["q", u - 1, 1])
                    ssc.inc(s.copy(
                        ewin.rearrange("p (t i) -> p t i", i=64),
                        eq.rearrange("p (i t) -> p i t", t=64).transpose([0, 2, 1])))
                    for b in range(8):
                        s.wait_ge(smm.h, M["r2", u, b])
                        yB = banks[2 + (b % 2)]
                        ssc.inc(s.activation(eq[:, b * 512:(b + 1) * 512], yB[:, :], AF.Identity))
                    for b in range(8):
                        s.wait_ge(smm.h, M["r3", u, b])
                        if u // 4 >= 1 and b == 0:
                            s.wait_ge(smm.h, M["dn", u // 4 - 1, 15])
                        bank = banks[4 + (b % 2)]
                        dst = bfBIG.rearrange("p (uu i b2 ths thi) -> p uu b2 ths thi i",
                                              i=64, b2=8, ths=4, thi=2)[:, u4, b, :, :, :]
                        srcv = bank.rearrange("q (ths thi i) -> q ths thi i", thi=2, i=64)
                        ssc.inc(s.activation(dst, srcv, AF.Identity))
                    if u % 4 == 3:
                        g4 = u // 4
                        if g4 == 0:
                            s.wait_ge(smm.h, f_mm_gi(127))
                        for o in range(16):
                            if not (g4 == 0 and o == 0):
                                po = (g4, o - 1) if o >= 1 else (g4 - 1, 15)
                                s.wait_ge(smm.h, M["dn", po[0], po[1]])
                                s.wait_ge(sw.h, B_w2 + 128 * (16 * g4 + o))
                            if g4 == 0 and o == 0:
                                s.wait_ge(scc.h, P1_CC2)
                            for r in range(NCORES):
                                off = d_page(o, r)
                                sw.incd(s.dma_start(
                                    out=bfW[:, r * 1024:(r + 1) * 1024],
                                    in_=ag_d_out.ap()[off:off + PG_D].rearrange("(p c) -> p c", c=1024)))
                            if o >= 1:
                                s.wait_ge(smm.h, M["dn", g4, o - 1])
                                if o >= 3:
                                    s.wait_ge(smm.h, M["dntm", g4, o - 3, 1])
                                dB = banks[6 + ((o - 1) % 2)]
                                ssc.inc(s.activation(scr2[:, ((o - 1) % 2) * 256:((o - 1) % 2) * 256 + 256],
                                                     dB[:, 0:256], AF.Identity))
                        s.wait_ge(smm.h, M["dn", g4, 15])
                        s.wait_ge(smm.h, M["dntm", g4, 13, 1])
                        dB = banks[6 + (15 % 2)]
                        ssc.inc(s.activation(scr2[:, (15 % 2) * 256:(15 % 2) * 256 + 256],
                                             dB[:, 0:256], AF.Identity))

            @block.tensor
            def _(tn):
                for u in range(N_HU):
                    u4 = u % 4
                    tn.wait_ge(ssc.h, M["reord", u])
                    if u % 4 == 0 and u >= 4:
                        tn.wait_ge(sev.h, M["resid", u // 4 - 1, 15, 1])
                    def emit_r2(b):
                        tn.wait_ge(sev.h, M["r1ev", u, b])
                        if b >= 2:
                            tn.wait_ge(ssc.h, M["yev", u, b - 2])
                        yB = banks[2 + (b % 2)]
                        smm.inc(tn.matmul(yB[:, :], h64b[:, :], zr[:, (b % 4) * 512:(b % 4) * 512 + 512],
                                          start=True, stop=True))

                    for b in range(8):
                        if b >= 2:
                            tn.wait_ge(sev.h, M["r1ev", u, b - 2])
                        bank = banks[b % 2]
                        for wsub in range(4):
                            w = 4 * b + wsub
                            mm = tn.matmul(bank[0:128, wsub * 128:(wsub + 1) * 128],
                                           ewin[:, w * 128:(w + 1) * 128], h128[:, :],
                                           start=True, stop=True)
                        smm.inc(mm)
                        if b >= 2:
                            emit_r2(b - 2)
                    for b in [6, 7]:
                        emit_r2(b)
                    tn.wait_ge(sev.h, M["mx1", u])
                    smm.inc(tn.matmul(pb6[0:32, 0:128], colp[:, 0:32], i128f[:, :], start=True, stop=True))
                    tn.wait_ge(sev.h, M["chain", u])
                    smm.inc(tn.matmul(pb6[0:2, 128:160], scr2[0:32, 646:648], i128f[0:32, 0:32],
                                      start=True, stop=True))
                    tn.wait_ge(sev.h, M["sTev", u])
                    smm.inc(tn.matmul(pb6[0:128, 160:192], sel2[:, :], srow[:, 0:32],
                                      start=True, stop=True))
                    for b in range(8):
                        tn.wait_ge(sev.h, M["q", u, 1 if b >= 4 else 0])
                        if b >= 2:
                            tn.wait_ge(ssc.h, M["r3ev", u, b - 2])
                        bank = banks[4 + (b % 2)]
                        for ths in range(4):
                            th = 4 * b + ths
                            mm = tn.matmul(bank[0:128, ths * 128:(ths + 1) * 128],
                                           bfSM[:, th * 128:(th + 1) * 128], i128[:, :],
                                           start=True, stop=True)
                        smm.inc(mm)
                    if u % 4 == 3:
                        g4 = u // 4
                        tn.wait_ge(sev.h, M["chain", u])
                        tn.wait_ge(ssc.h, M["r3ev", u, 7])
                        for tb in range(2):
                            for k in range(4):
                                usub, tlo = k // 2, k % 2
                                uu = 4 * g4 + 2 * tb + usub
                                mm = tn.matmul(pb4[0:128, 500 + tb:501 + tb],
                                               p4[:, (2 * usub + tlo) * 128:(2 * usub + tlo) * 128 + 128],
                                               scr2[0:32, 700 + 2 * uu + tlo:701 + 2 * uu + tlo],
                                               start=(k == 0), stop=(k == 3))
                            smm.inc(mm)
                        for o in range(16):
                            tn.wait_ge(sw.h, B_w2 + 128 * (16 * g4 + o + 1))
                            dB = banks[6 + (o % 2)]
                            if o >= 2:
                                tn.wait_ge(ssc.h, M["dnev", g4, o - 2])
                            for i1p in range(64):
                                r, iloc = i1p >> 3, i1p & 7
                                lt = bfW[:, r * 1024 + iloc * 128:r * 1024 + iloc * 128 + 128]
                                rhs = bfBIG.rearrange("p (uu i tl) -> p uu i tl",
                                                      i=64, tl=64)[:, :, i1p, :]
                                mm = tn.matmul(dB[:, 0:256], lt, rhs,
                                               start=(i1p == 0), stop=(i1p == 63))
                            smm.inc(mm)
                            tn.wait_ge(ssc.h, M["dnev", g4, o])
                            if (g4, o) > (0, 1):
                                po = (g4, o - 2) if o >= 2 else (g4 - 1, 14 + o)
                                tn.wait_ge(sev.h, M["resid", po[0], po[1], 1])
                            for tb in range(2):
                                mm = tn.matmul(banks[2 * tb + (o % 2)][:, 0:128],
                                               scr2[:, (o % 2) * 256 + tb * 128:(o % 2) * 256 + tb * 128 + 128],
                                               i128f[:, :], start=True, stop=True)
                                smm.inc(mm)

            @block.vector
            def _(v):
                for u in range(N_HU):
                    for b in range(8):
                        v.wait_ge(smm.h, M["r1", u, b])
                        if b >= 4:
                            v.wait_ge(smm.h, M["r2", u, b - 4])
                        bank = banks[b % 2]
                        sev.inc(v.tensor_copy(zr[:, (b % 4) * 512:(b % 4) * 512 + 512], bank[:, :]))
                    v.wait_ge(ssc.h, M["yev", u, 7])
                    sev.inc(v.tensor_reduce(colp[:, 0:32].rearrange("p (a o) -> p a o", o=1),
                                            eq.rearrange("p (a c) -> p a c", c=128), AX,
                                            ALU.max, apply_absolute_value=True))
                    v.wait_ge(smm.h, M["mx1T", u])
                    sev.inc(v.tensor_copy(scr2[0:32, 512:640], pb6[0:32, 0:128]))
                    v.wait_ge(sev.h, sev.n)
                    VS(v, v.tensor_reduce(scr2[0:32, 640:642].rearrange("p (a o) -> p a o", o=1),
                                          scr2[0:32, 512:640].rearrange("p (a c) -> p a c", c=64), AX,
                                          ALU.max, apply_absolute_value=False))
                    VS(v, v.tensor_scalar_max(scr2[0:32, 642:644], scr2[0:32, 640:642], 1e-5))
                    VS(v, v.reciprocal(scr2[0:32, 644:646], scr2[0:32, 642:644]))
                    VS(v, v.tensor_scalar_mul(scr2[0:32, 646:648], scr2[0:32, 644:646], 128.0))
                    VS(v, v.reciprocal(scr2[0:32, 648:650], scr2[0:32, 646:648]))
                    u4 = u % 4
                    sev.inc(v.tensor_copy(scr2[0:32, 700 + 2 * u:702 + 2 * u], scr2[0:32, 648:650]))
                    v.wait_ge(sev.h, sev.n)
                    v.wait_ge(smm.h, M["sT", u])
                    sev.inc(v.tensor_copy(srow[0:2, 0:32], pb6[0:2, 128:160]))
                    v.wait_ge(sev.h, sev.n)
                    v.wait_ge(smm.h, M["ssm", u])
                    sev.inc(v.tensor_copy(scr2[:, 1024:1056], pb6[:, 160:192]))
                    v.wait_ge(sev.h, sev.n)
                    for th in range(32):
                        sl = ewin[:, th * 128:(th + 1) * 128]
                        mmv = v.tensor_scalar_mul(sl, eq[:, th * 128:(th + 1) * 128],
                                                  scr2[:, 1024 + th:1025 + th])
                    sev.inc(mmv)
                    for half in range(2):
                        hs = slice(half * 2048, (half + 1) * 2048)
                        sev.inc(v.tensor_scalar_add(scr[:, 0:2048], ewin[:, hs], 256.0))
                        sev.inc(v.tensor_copy(iscr[:, 0:2048], scr[:, 0:2048]))
                        sev.inc(v.tensor_tensor(scr[:, 2048:4096], scr[:, 0:2048],
                                                iscr[:, 0:2048], ALU.subtract))
                        sev.inc(v.tensor_single_scalar(ewin[:, hs], scr[:, 2048:4096], 0.5,
                                                       ALU.is_ge))
                        sev.inc(v.tensor_tensor(ewin[:, hs], ewin[:, hs], iscr[:, 0:2048],
                                                ALU.add))
                        sev.inc(v.tensor_scalar(bfSM[:, hs], ewin[:, hs], -256.0, 127.0,
                                                ALU.add, ALU.min))
                    if u % 4 == 3:
                        g4 = u // 4
                        for tb in range(2):
                            v.wait_ge(smm.h, M["dqc", g4, tb])
                            sev.inc(v.tensor_copy(colp[:, 24 + tb:25 + tb], pb4[:, 500 + tb:501 + tb]))
                            v.wait_ge(sev.h, sev.n)
                        for o in range(16):
                            for tb in range(2):
                                v.wait_ge(smm.h, M["dntm", g4, o, tb])
                                tmp = scr2[:, 1088:1216]
                                v.tensor_scalar(tmp, banks[2 * tb + (o % 2)][:, 0:128],
                                                colp[:, 24 + tb:25 + tb],
                                                colp[:, 46:47], ALU.mult, ALU.mult)
                                xslice = xres[:, (2 * g4 + tb) * D + o * 128:(2 * g4 + tb) * D + o * 128 + 128]
                                sev.inc(v.scalar_tensor_tensor(xslice, tmp, 0.5, xslice,
                                                               ALU.mult, ALU.add))

            M_end, B_sc_end, B_ev_end, B_mm_end, B_gp_end = _sched_B(B_sc, B_ev, B_mm, B_gp)
            assert B_sc_end == ssc.n and B_ev_end == sev.n and B_mm_end == smm.n and B_gp_end == sgp.n, (
                "stage B schedule mismatch",
                (B_sc_end, ssc.n), (B_ev_end, sev.n), (B_mm_end, smm.n), (B_gp_end, sgp.n))
            return M["resid", 3, 15, 1]

        def attn_call(wait_ev):
            CALL_mm0 = smm.n
            llv = _ll()
            lamC_f = [float(np.float32(np.exp(llv[h] * CH))) for h in range(HEADS)]
            sScl_f = [float(np.float32(np.exp(llv[h] * 64.0))) for h in range(HEADS)]
            grp = bfBIG.rearrange("p (a c) -> p a c", c=512)
            wsv = bfW.rearrange("p (hh c) -> p hh c", c=512)
            wfv = bfW.rearrange("p (hh c) -> p hh c", c=128)
            xlv = bfBIG.rearrange("p (hh c) -> p hh c", c=512)
            smv = bfSM.rearrange("p (a c) -> p a c", c=512)
            wsrc = {"k": wk_d, "v": wv_d, "q": wq_d}
            cach = {"k": kcache, "v": vcache, "q": qcache}
            B = {}

            def bar(tag):
                B[tag] = (smm.n, sev.n, ssc.n, sgp.n, sw.n, sld.n, sst.n, sfb.n)

            _ckn = [0]

            def ck():
                _ckn[0] += 1
                if ATTN_STAGES is not None and _ckn[0] >= ATTN_STAGES:
                    raise _StopEmit(sev.n)

            @block.sync
            def _(sync):
                sync.wait_ge(sld.h, sld.n)
                sld.incd(sync.dma_start(out=nwrep[:, :], in_=tdkrep[:, :]))
            A0_ld = sld.n

            for g in range(2):
                L_ev, L_sc, L_mm = sev.n, ssc.n, smm.n

                @block.scalar
                def _(s, g=g, L_ev=L_ev, L_mm=L_mm):
                    for t4 in range(4):
                        t = g * 4 + t4
                        xt = xres[:, t * D:(t + 1) * D]
                        if g == 0 and t4 == 0:
                            s.wait_ge(sev.h, wait_ev)
                        if g == 1 and t4 == 0:
                            s.wait_ge(smm.h, B["gate", 0][0])
                            s.wait_ge(sev.h, B[("state", 0)][1])
                        ssc.inc(s.activation(scrC[:, 0:2048], xt, AF.Square))
                        s.wait_ge(sev.h, L_ev + 2 * t4 + 1)
                        ssc.inc(s.activation(colp[:, 8:9], colp[:, 6:7], AF.Sqrt))
                        s.wait_ge(ssc.h, ssc.n)
                        s.wait_ge(sev.h, L_ev + 2 * t4 + 2)
                        s.activation(scrC[:, 0:2048], xt, AF.Identity, bias=colp[:, 7:8])
                        ssc.inc(s.activation(bfSM[:, 0:2048], scrC[:, 0:2048], AF.Identity,
                                             scale=colp[:, 15:16]))
                        for b in range(4):
                            gb = 4 * t4 + b
                            s.wait_ge(smm.h, L_mm + gb + 1)
                            bank = banks[4 + (gb % 2)]
                            dst = xlv[:, 4 * b:4 * b + 4, t4 * 128:t4 * 128 + 128]
                            ssc.inc(s.activation(dst, bank.rearrange("p (h c) -> p h c", c=128),
                                                 AF.Identity))

                @block.vector
                def _(v, g=g, L_sc=L_sc):
                    for t4 in range(4):
                        t = g * 4 + t4
                        xt = xres[:, t * D:(t + 1) * D]
                        v.wait_ge(ssc.h, L_sc + 7 * t4 + 1)
                        VS(v, v.tensor_reduce(colp[:, 0:1], xt, AX, ALU.add))
                        VS(v, v.tensor_reduce(colp[:, 1:2], scrC[:, 0:2048], AX, ALU.add))
                        VS(v, v.tensor_scalar_mul(colp[:, 2:3], colp[:, 0:1], 1.0 / D))
                        VS(v, v.tensor_scalar_mul(colp[:, 3:4], colp[:, 1:2], 1.0 / D))
                        VS(v, v.tensor_tensor(colp[:, 4:5], colp[:, 2:3], colp[:, 2:3], ALU.mult))
                        VS(v, v.scalar_tensor_tensor(colp[:, 5:6], colp[:, 3:4], 1e-5, colp[:, 4:5],
                                                     ALU.add, ALU.subtract))
                        VS(v, v.reciprocal(colp[:, 6:7], colp[:, 5:6]))
                        sev.inc(v.tensor_scalar_mul(colp[:, 7:8], colp[:, 2:3], -1.0))
                        v.wait_ge(sev.h, sev.n)
                        v.wait_ge(ssc.h, L_sc + 7 * t4 + 2)
                        VS(v, v.tensor_tensor(colp[:, 14:15], colp[:, 8:9], colp[:, 8:9], ALU.mult))
                        VS(v, v.tensor_tensor(colp[:, 14:15], colp[:, 14:15], colp[:, 5:6], ALU.mult))
                        VS(v, v.tensor_scalar(colp[:, 14:15], colp[:, 14:15], -0.5, 1.5, ALU.mult, ALU.add))
                        sev.inc(v.tensor_tensor(colp[:, 15:16], colp[:, 8:9], colp[:, 14:15], ALU.mult))
                        v.wait_ge(sev.h, sev.n)

                @block.tensor
                def _(tn, g=g, L_sc=L_sc):
                    for t4 in range(4):
                        tn.wait_ge(ssc.h, L_sc + 7 * t4 + 3)
                        for h in range(16):
                            gb = 4 * t4 + h // 4
                            bank = banks[4 + (gb % 2)]
                            if h % 4 == 0 and gb >= 2:
                                pgb = gb - 2
                                tn.wait_ge(ssc.h, L_sc + 7 * (pgb // 4) + 3 + (pgb % 4) + 1)
                            mm = tn.matmul(bank[:, (h % 4) * 128:(h % 4) * 128 + 128],
                                           bfSM[:, h * 128:(h + 1) * 128], i128[:, :],
                                           start=True, stop=True)
                            if h % 4 == 3:
                                smm.inc(mm)
                bar(("ln", g))
                ck()

                def proj(pj, act, extra_st_wait=None, g=g):
                    P_g, P_mm, P_sc = sgst.n, smm.n, ssc.n

                    @block.gpsimd
                    def _(g_, pj=pj, P_mm=P_mm):
                        g_.wait_ge(smm.h, P_mm)
                        for ob in range(4):
                            if ob >= 1:
                                g_.wait_ge(smm.h, P_mm + 4 * ob)
                            src = wsrc[pj].rearrange("(hh p) o -> p hh o", p=128)[:, :, ob * 512:(ob + 1) * 512]
                            sgst.incd(g_.dma_start(out=wsv[:, 0:16, :], in_=src))

                    @block.tensor
                    def _(tn, P_g=P_g, P_sc=P_sc, extra_st_wait=extra_st_wait):
                        tn.wait_ge(ssc.h, P_sc)
                        if extra_st_wait is not None:
                            tn.wait_ge(sst.h, extra_st_wait)
                        for ob in range(4):
                            tn.wait_ge(sgst.h, P_g + 16 * (ob + 1))
                            for t4 in range(4):
                                idx = 4 * ob + t4
                                bank = banks[4 + (idx % 2)]
                                if idx >= 2:
                                    tn.wait_ge(ssc.h, P_sc + idx - 1)
                                for h in range(16):
                                    mm = tn.matmul(bank[:, :], xlv[:, h, t4 * 128:t4 * 128 + 128],
                                                   wsv[:, h, :], start=(h == 0), stop=(h == 15))
                                smm.inc(mm)

                    @block.scalar
                    def _(s, P_mm=P_mm, act=act):
                        for ob in range(4):
                            for t4 in range(4):
                                idx = 4 * ob + t4
                                s.wait_ge(smm.h, P_mm + idx + 1)
                                bank = banks[4 + (idx % 2)]
                                ssc.inc(s.activation(grp[:, 16 + t4 * 4 + ob, :], bank[:, :], act))
                    bar(("proj", g, pj))

                def rot(pj, write_kd, g=g):
                    R_ld, R_ev, R_st = sld.n, sev.n, sst.n

                    def stores(sync, t4, pj, write_kd, g):
                        t = g * 4 + t4
                        kt = bfBIG[:, 8192 + t4 * 2048:8192 + (t4 + 1) * 2048]
                        sst.incd(sync.dma_start(
                            out=cach[pj].rearrange("(a p) d -> a p d", p=128)[t], in_=kt))
                        if write_kd:
                            kd = bfSM[:, (t4 % 2) * 2048:(t4 % 2) * 2048 + 2048]
                            sst.incd(sync.dma_start(
                                out=kdcache.rearrange("(a p) d -> a p d", p=128)[t], in_=kd))

                    @block.sync
                    def _(sync, pj=pj, R_ev=R_ev, g=g, write_kd=write_kd):
                        sync.wait_ge(ssc.h, B[("proj", g, pj)][2])
                        for t4 in range(4):
                            t = g * 4 + t4
                            if t4 >= 1:
                                sync.wait_ge(sev.h, R_ev + 4 * t4)
                                stores(sync, t4 - 1, pj, write_kd, g)
                            sld.incd(sync.dma_start(out=scr[:, 0:1024],
                                                    in_=tcos.rearrange("(a p) j -> a p j", p=128)[t]))
                            sld.incd(sync.dma_start(out=scr[:, 1024:2048],
                                                    in_=tsin.rearrange("(a p) j -> a p j", p=128)[t]))
                        sync.wait_ge(sev.h, R_ev + 16)
                        stores(sync, 3, pj, write_kd, g)

                    @block.vector
                    def _(v, R_ld=R_ld, R_st=R_st, g=g, write_kd=write_kd):
                        v.wait_ge(smm.h, B[("ln", g)][0])
                        for t4 in range(4):
                            v.wait_ge(sld.h, R_ld + 32 * (t4 + 1))
                            kt = bfBIG[:, 8192 + t4 * 2048:8192 + (t4 + 1) * 2048]
                            ktv = kt.rearrange("p (h two j) -> p h two j", two=2, j=64)
                            x1, x2 = ktv[:, :, 0, :], ktv[:, :, 1, :]
                            cosr = scr[:, 0:1024].rearrange("p (h j) -> p h j", j=64)
                            sinr = scr[:, 1024:2048].rearrange("p (h j) -> p h j", j=64)
                            ra = scr2[:, 0:1024].rearrange("p (h j) -> p h j", j=64)
                            rb = scr2[:, 1024:2048].rearrange("p (h j) -> p h j", j=64)
                            rc = scrC[:, 0:1024].rearrange("p (h j) -> p h j", j=64)
                            rd = scrC[:, 1024:2048].rearrange("p (h j) -> p h j", j=64)
                            v.tensor_tensor(ra, x1, cosr, ALU.mult)
                            v.tensor_tensor(rb, x2, sinr, ALU.mult)
                            v.tensor_tensor(rc, x1, sinr, ALU.mult)
                            sev.inc(v.tensor_tensor(rd, x2, cosr, ALU.mult))
                            v.wait_ge(sev.h, sev.n)
                            sev.inc(v.tensor_tensor(x1, ra, rb, ALU.subtract))
                            v.wait_ge(sev.h, sev.n)
                            sev.inc(v.tensor_tensor(x2, rc, rd, ALU.add))
                            v.wait_ge(sev.h, sev.n)
                            if write_kd:
                                if t4 >= 2:
                                    v.wait_ge(sst.h, R_st + 2 * (t4 - 1))
                                kdd = bfSM[:, (t4 % 2) * 2048:(t4 % 2) * 2048 + 2048]
                                sev.inc(v.tensor_tensor(kdd, kt, nwrep[:, :], ALU.mult))
                                v.wait_ge(sev.h, sev.n)
                            else:
                                sev.inc(v.memset(colp[:, 33:34], 0.0))
                    bar(("rot", g, pj))

                proj("k", AF.Silu)
                ck()
                rot("k", True)
                ck()
                proj("v", AF.Identity, extra_st_wait=B[("rot", g, "k")][6])
                ck()

                F_fb, S_mm, S_ev = sfb.n, smm.n, sev.n

                @block.sync
                def _(sync, g=g, F_fb=F_fb, S_mm=S_mm):
                    sync.wait_ge(sst.h, B[("rot", g, "k")][6])
                    for t4 in range(4):
                        t = g * 4 + t4
                        if t4 >= 2:
                            sync.wait_ge(smm.h, S_mm + 16 * (t4 - 1))
                        dst = bfSM[:, (t4 % 2) * 2048:(t4 % 2) * 2048 + 2048]
                        sfb.incd(sync.dma_start(
                            out=dst, in_=kdcache.rearrange("(a p) d -> a p d", p=128)[t]))

                S_sc = ssc.n

                @block.tensor
                def _(tn, g=g, F_fb=F_fb, S_sc=S_sc):
                    for t4 in range(4):
                        tn.wait_ge(sfb.h, F_fb + 16 * (t4 + 1))
                        for h in range(16):
                            s_ = 16 * t4 + h
                            if s_ >= 4:
                                tn.wait_ge(ssc.h, S_sc + s_ - 3)
                            kdd = bfSM[:, (t4 % 2) * 2048 + h * 128:(t4 % 2) * 2048 + h * 128 + 128]
                            vt = bfBIG[:, 8192 + t4 * 2048 + h * 128:8192 + t4 * 2048 + h * 128 + 128]
                            smm.inc(tn.matmul(banks[h % 4][:, (h // 4) * 128:(h // 4) * 128 + 128],
                                              kdd, vt, start=True, stop=True))

                @block.scalar
                def _(s, g=g, S_mm=S_mm, S_ev=S_ev):
                    for t4 in range(4):
                        for h in range(16):
                            s.wait_ge(smm.h, S_mm + 16 * t4 + h + 1)
                            if t4 >= 1:
                                s.wait_ge(sev.h, S_ev + 16 * (t4 - 1) + h + 1)
                            bk = banks[h % 4][:, (h // 4) * 128:(h // 4) * 128 + 128]
                            ssc.inc(s.activation(scrC[:, h * 128:(h + 1) * 128], bk, AF.Identity))

                @block.vector
                def _(v, g=g, S_sc=S_sc):
                    v.wait_ge(ssc.h, B[("proj", g, "v")][2])
                    for t4 in range(4):
                        c = g * 4 + t4
                        for h in range(16):
                            v.wait_ge(ssc.h, S_sc + 16 * t4 + h + 1)
                            dst = f32P1[:, 2048 + h * 128:2048 + (h + 1) * 128]
                            sa = scrC[:, h * 128:(h + 1) * 128]
                            if c == 0:
                                sev.inc(v.tensor_copy(dst, sa))
                            else:
                                sev.inc(v.scalar_tensor_tensor(dst, dst, lamC_f[h], sa,
                                                               ALU.mult, ALU.add))
                            v.wait_ge(sev.h, sev.n)
                bar(("state", g))
                ck()

                @block.sync
                def _(sync, g=g):
                    sync.wait_ge(smm.h, B[("state", g)][0])
                    for t4 in range(4):
                        t = g * 4 + t4
                        vt = bfBIG[:, 8192 + t4 * 2048:8192 + (t4 + 1) * 2048]
                        sst.incd(sync.dma_start(
                            out=vcache.rearrange("(a p) d -> a p d", p=128)[t], in_=vt))
                bar(("vst", g))
                ck()

                proj("q", AF.Silu, extra_st_wait=B[("vst", g)][6])
                ck()
                rot("q", False)
                ck()

                G_g, G_mm, G_sc = sgst.n, smm.n, ssc.n

                @block.gpsimd
                def _(g_, g=g):
                    g_.wait_ge(smm.h, B[("proj", g, "q")][0])
                    sgst.incd(g_.dma_start(out=wfv[:, 0:16, :],
                                           in_=wg1_d.rearrange("(hh p) o -> p hh o", p=128)))
                    sgst.incd(g_.dma_start(out=bfSM.rearrange("p (a c) -> p a c", c=2048)[:, 1, :],
                                           in_=wg2_d[:, :]))

                @block.tensor
                def _(tn, G_g=G_g, G_sc=G_sc):
                    tn.wait_ge(sgst.h, G_g + 16)
                    tn.wait_ge(ssc.h, G_sc)
                    for h in range(16):
                        mm = tn.matmul(pb6[:, 0:512], wfv[:, h, :],
                                       xlv[:, h, 0:512], start=(h == 0), stop=(h == 15))
                    smm.inc(mm)
                    tn.wait_ge(sgst.h, G_g + 32)
                    tn.wait_ge(ssc.h, G_sc + 1)
                    for t4 in range(4):
                        for ob in range(4):
                            idx = 4 * t4 + ob
                            bank = banks[4 + (idx % 2)]
                            if idx >= 2:
                                tn.wait_ge(ssc.h, G_sc + idx)
                            smm.inc(tn.matmul(bank[:, :], bfSM[:, t4 * 128:t4 * 128 + 128],
                                              smv[:, 4 + ob, :], start=True, stop=True))

                @block.scalar
                def _(s, g=g, G_mm=G_mm):
                    s.wait_ge(sev.h, B[("rot", g, "q")][1])
                    s.wait_ge(smm.h, G_mm + 1)
                    ssc.inc(s.activation(bfSM[:, 0:512], pb6[:, 0:512], AF.Identity))
                    for t4 in range(4):
                        t = g * 4 + t4
                        for ob in range(4):
                            idx = 4 * t4 + ob
                            s.wait_ge(smm.h, G_mm + 1 + idx + 1)
                            bank = banks[4 + (idx % 2)]
                            ssc.inc(s.activation(scr2[:, ob * 512:(ob + 1) * 512], bank[:, :], AF.Sigmoid))
                        s.wait_ge(sst.h, sst.n)
                        sst.incd(s.dma_start(out=gcache.rearrange("(a p) d -> a p d", p=128)[t],
                                             in_=scr2[:, 0:2048]))
                        s.wait_ge(sst.h, sst.n)
                bar(("gate", g))
                ck()

            @block.sync
            def _(sync):
                sync.wait_ge(sev.h, B[("state", 1)][1])
                sync.wait_ge(sst.h, sst.n)
                for b in range(4):
                    dst = ag_st_in.rearrange("(hh pp) c -> pp hh c", pp=128)[:, 4 * b:4 * b + 4, :]
                    srcv = f32P1.rearrange("p (a c) -> p a c", c=128)[:, 16 + 4 * b:16 + 4 * b + 4, :]
                    sst.incd(sync.dma_start(out=dst, in_=srcv))
            E_st2 = sst.n

            @block.gpsimd
            def _(g_):
                g_.wait_ge(sst.h, E_st2)
                scc.inc(g_.collective_compute("AllGather", ALU.bypass, replica_groups=replica,
                                              ins=[ag_st_in.ap().opt()], outs=[ag_st_out.ap().opt()]))
            AT_CC = scc.n

            C_ld, C_ev = sld.n, sev.n

            @block.sync
            def _(sync):
                sync.wait_ge(scc.h, AT_CC)
                for cp in range(NCORES):
                    if cp >= 1:
                        sync.wait_ge(sev.h, C_ev + 1 + 16 * cp)
                    src = ag_st_out.rearrange("(cc hh pp) c -> cc pp hh c", pp=128, hh=HEADS)[cp]
                    sld.incd(sync.dma_start(out=scrC.rearrange("p (hh c) -> p hh c", c=128), in_=src))

            @block.vector
            def _(v):
                v.wait_ge(scc.h, AT_CC)
                sev.inc(v.memset(f32P1[:, 2048:4096], 0.0))
                for cp in range(NCORES):
                    v.wait_ge(sld.h, C_ld + 16 * (cp + 1))
                    for h in range(16):
                        sev.inc(v.scalar_tensor_tensor(
                            f32P1[:, 2048 + h * 128:2048 + h * 128 + 128],
                            scrC[:, h * 128:(h + 1) * 128], lamt[:, cp * 16 + h:cp * 16 + h + 1],
                            f32P1[:, 2048 + h * 128:2048 + h * 128 + 128], ALU.mult, ALU.add))
                    v.wait_ge(sev.h, sev.n)
            bar("sinit")
            ck()

            @block.sync
            def _(sync):
                sync.wait_ge(sld.h, sld.n)
                sld.incd(sync.dma_start(
                    out=f32P1.rearrange("p (a c) -> p a c", c=128)[:, 32:48, :],
                    in_=tdmask.rearrange("(hh s) t -> s hh t", s=128)))
                sync.wait_ge(sld.h, sld.n)
                sld.incd(sync.dma_start(out=nwrep[:, :], in_=tnwrep[:, :]))
            P2_ld0 = sld.n

            prev_ev = [None]
            for c in range(8):
                CL_ld = sfb.n

                @block.sync
                def _(sync, c=c, CL_ld=CL_ld):
                    if c == 0:
                        sync.wait_ge(smm.h, B[("gate", 1)][0])
                        sync.wait_ge(sld.h, P2_ld0)
                    else:
                        sync.wait_ge(smm.h, B[("scan", c - 1)][0])
                    for j, nm in enumerate(["q", "k", "kd", "v"]):
                        srcd = kdcache if nm == "kd" else cach[nm]
                        sfb.incd(sync.dma_start(out=bfBIG[:, j * 2048:(j + 1) * 2048],
                                                in_=srcd.rearrange("(a p) d -> a p d", p=128)[c]))

                H_mm, H_sc, H_ev = smm.n, ssc.n, sev.n
                P_EV = prev_ev[0]

                @block.tensor
                def _(tn, c=c, CL_ld=CL_ld, H_mm=H_mm, H_sc=H_sc, H_ev=H_ev):
                    tn.wait_ge(sfb.h, CL_ld + 64)
                    if c != 0:
                        tn.wait_ge(sev.h, B[("scan", c - 1)][1])
                        tn.wait_ge(ssc.h, B[("scan", c - 1)][2])
                    for h in range(16):
                        sb = (h % 2) * 512
                        pq = pb4 if h % 2 == 0 else pb5
                        po = pb2 if h % 2 == 0 else pb3
                        scb = pb6 if h % 2 == 0 else pb7
                        if h >= 2:
                            tn.wait_ge(ssc.h, H_sc + 3 * (h - 2) + 1)
                        qs = bfBIG[:, h * 128:h * 128 + 128]
                        ks = bfBIG[:, 2048 + h * 128:2048 + h * 128 + 128]
                        tn.matmul(pq[:, 0:128], qs, i128[:, :], start=True, stop=True)
                        smm.inc(tn.matmul(pq[:, 128:256], ks, i128[:, :], start=True, stop=True))
                        tn.wait_ge(ssc.h, H_sc + 3 * h + 1)
                        if h >= 2:
                            tn.wait_ge(sev.h, H_ev + 2 * (h - 2) + 1)
                        smm.inc(tn.matmul(scb[:, 0:128],
                                          bfSM[:, sb + 256:sb + 384], bfSM[:, sb + 128:sb + 256],
                                          start=True, stop=True))
                        tn.wait_ge(ssc.h, H_sc + 3 * h + 2)
                        if h >= 2:
                            tn.wait_ge(ssc.h, H_sc + 3 * (h - 2) + 3)
                        tn.matmul(po[:, 0:128],
                                  bfSM[:, sb + 128:sb + 256], bfSM[:, sb + 384:sb + 512],
                                  start=True, stop=False)
                        tn.wait_ge(sev.h, H_ev + 2 * h + 1)
                        vt = bfBIG[:, 6144 + h * 128:6144 + h * 128 + 128]
                        smm.inc(tn.matmul(po[:, 0:128],
                                          bfSM[:, sb:sb + 128], vt, start=False, stop=True))
                        kdd = bfBIG[:, 4096 + h * 128:4096 + h * 128 + 128]
                        if h >= 2:
                            tn.wait_ge(sev.h, H_ev + 2 * (h - 2) + 2)
                        smm.inc(tn.matmul(banks[h % 2][:, 0:128], kdd, vt, start=True, stop=True))

                @block.scalar
                def _(s, c=c, H_mm=H_mm, P_EV=P_EV):
                    for h in range(16):
                        sb = (h % 2) * 512
                        pq = pb4 if h % 2 == 0 else pb5
                        po = pb2 if h % 2 == 0 else pb3
                        if h == 0 and c >= 1:
                            s.wait_ge(sst.h, B[("sc2", c - 1)][6])
                        s.wait_ge(smm.h, H_mm + 4 * h + 1)
                        if h >= 2:
                            s.wait_ge(smm.h, H_mm + 4 * (h - 2) + 3)
                        s.activation(bfSM[:, sb + 128:sb + 256], pq[:, 0:128], AF.Identity)
                        ssc.inc(s.activation(bfSM[:, sb + 256:sb + 384], pq[:, 128:256], AF.Identity))
                        if c == 0:
                            s.wait_ge(sev.h, B["sinit"][1])
                        else:
                            s.wait_ge(sev.h, P_EV + 2 * h + 2)
                        ssc.inc(s.activation(bfSM[:, sb + 384:sb + 512],
                                             f32P1[:, 2048 + h * 128:2048 + (h + 1) * 128],
                                             AF.Identity, scale=sScl_f[h]))
                        s.wait_ge(smm.h, H_mm + 4 * h + 3)
                        ssc.inc(s.activation(f32P1[:, h * 128:(h + 1) * 128],
                                             po[:, 0:128],
                                             AF.Identity, scale=dqc[:, h:h + 1]))

                @block.vector
                def _(v, c=c, H_mm=H_mm, H_sc=H_sc):
                    if c == 0:
                        v.wait_ge(sld.h, P2_ld0)
                    for h in range(16):
                        sb = (h % 2) * 512
                        v.wait_ge(smm.h, H_mm + 4 * h + 2)
                        dm = f32P1[:, 4096 + h * 128:4096 + h * 128 + 128]
                        scb = pb6 if h % 2 == 0 else pb7
                        sev.inc(v.tensor_tensor(bfSM[:, sb:sb + 128], scb[:, 0:128], dm, ALU.mult))
                        v.wait_ge(sev.h, sev.n)
                        v.wait_ge(smm.h, H_mm + 4 * h + 4)
                        v.wait_ge(ssc.h, H_sc + 3 * h + 2)
                        dst = f32P1[:, 2048 + h * 128:2048 + (h + 1) * 128]
                        sev.inc(v.scalar_tensor_tensor(
                            dst, dst, lamC_f[h], banks[h % 2][:, 0:128],
                            ALU.mult, ALU.add))
                        v.wait_ge(sev.h, sev.n)
                bar(("scan", c))
                prev_ev[0] = H_ev

                GA_ld, GA_ev, GA_sc = sld.n, sev.n, ssc.n

                @block.sync
                def _(sync, c=c):
                    sync.wait_ge(sld.h, sld.n)
                    sld.incd(sync.dma_start(out=scr2[:, 0:2048],
                                            in_=gcache.rearrange("(a p) d -> a p d", p=128)[c]))

                @block.vector
                def _(v, c=c, GA_ld=GA_ld, GA_sc=GA_sc):
                    v.wait_ge(sld.h, GA_ld + 16)
                    v.wait_ge(smm.h, B[("scan", c)][0])
                    v.wait_ge(ssc.h, B[("scan", c)][2])
                    sev.inc(v.tensor_tensor(scr[:, 0:2048], f32P1[:, 0:2048], scr2[:, 0:2048], ALU.mult))
                    v.wait_ge(sev.h, sev.n)
                    v.wait_ge(ssc.h, GA_sc + 1)
                    VS(v, v.tensor_reduce(colp[:, 0:16].rearrange("p (a o) -> p a o", o=1),
                                          scrC.rearrange("p (a c) -> p a c", c=128), AX, ALU.add))
                    VS(v, v.tensor_scalar(colp[:, 16:32], colp[:, 0:16], 1.0 / 128.0, 1e-6,
                                          ALU.mult, ALU.add))
                    sev.inc(v.reciprocal(colp[:, 48:64], colp[:, 16:32]))
                    v.wait_ge(sev.h, sev.n)
                    v.wait_ge(ssc.h, GA_sc + 2)
                    VS(v, v.tensor_tensor(colp[:, 0:16], colp[:, 48:64], colp[:, 48:64], ALU.mult))
                    VS(v, v.tensor_tensor(colp[:, 0:16], colp[:, 0:16], colp[:, 16:32], ALU.mult))
                    VS(v, v.tensor_scalar(colp[:, 0:16], colp[:, 0:16], -0.5, 1.5, ALU.mult, ALU.add))
                    VS(v, v.tensor_tensor(colp[:, 48:64], colp[:, 48:64], colp[:, 0:16], ALU.mult))
                    for h in range(16):
                        v.tensor_scalar_mul(scr.rearrange("p (a cc) -> p a cc", cc=128)[:, h, :],
                                            scr.rearrange("p (a cc) -> p a cc", cc=128)[:, h, :],
                                            colp[:, 48 + h:49 + h])
                    v.wait_ge(svs.h, svs.n)
                    sev.inc(v.tensor_tensor(bfSM[:, 0:2048], scr[:, 0:2048], nwrep[:, :], ALU.mult))
                    v.wait_ge(sev.h, sev.n)

                @block.scalar
                def _(s, GA_ev=GA_ev):
                    s.wait_ge(sev.h, GA_ev + 1)
                    ssc.inc(s.activation(scrC[:, 0:2048], scr[:, 0:2048], AF.Square))
                    s.wait_ge(sev.h, GA_ev + 2)
                    ssc.inc(s.activation(colp[:, 48:64], colp[:, 48:64], AF.Sqrt))
                    s.wait_ge(ssc.h, ssc.n)
                bar(("ga", c))

                @block.sync
                def _(sync, c=c):
                    sync.wait_ge(sev.h, B[("ga", c)][1])
                    sync.wait_ge(sst.h, sst.n)
                    sst.incd(sync.dma_start(out=oncache.rearrange("(a p) d -> a p d", p=128)[c],
                                            in_=bfSM[:, 0:2048]))
                bar(("sc2", c))
                ck()

            A11_mm0, A11_sc0, A11_ev0, A11_g0, A11_fb0 = smm.n, ssc.n, sev.n, sgst.n, sfb.n

            @block.gpsimd
            def _(g_):
                g_.wait_ge(smm.h, A11_mm0)
                for ob in range(4):
                    if ob >= 1:
                        g_.wait_ge(smm.h, A11_mm0 + 40 * ob)
                    src = wo_d.rearrange("(hh p) o -> p hh o", p=128)[:, :, ob * 512:(ob + 1) * 512]
                    sgst.incd(g_.dma_start(out=wsv[:, 0:16, :], in_=src))

            @block.sync
            def _(sync):
                sync.wait_ge(sst.h, B[("sc2", 7)][6])
                for ob in range(4):
                    for t in range(8):
                        it = 8 * ob + t
                        if it >= 2:
                            sync.wait_ge(smm.h, A11_mm0 + 5 * (it - 1))
                        dst = bfSM[:, (t % 2) * 2048:(t % 2) * 2048 + 2048]
                        sfb.incd(sync.dma_start(out=dst,
                                                in_=oncache.rearrange("(a p) d -> a p d", p=128)[t]))

            @block.tensor
            def _(tn):
                tn.wait_ge(ssc.h, A11_sc0)
                tn.wait_ge(sev.h, B[("sc2", 7)][1])
                for ob in range(4):
                    tn.wait_ge(sgst.h, A11_g0 + 16 * (ob + 1))
                    for t in range(8):
                        it = 8 * ob + t
                        tn.wait_ge(sfb.h, A11_fb0 + 16 * (it + 1))
                        ot = bfSM[:, (t % 2) * 2048:(t % 2) * 2048 + 2048]
                        if it >= 1:
                            tn.wait_ge(ssc.h, A11_sc0 + 4 * it)
                        for ec in range(16):
                            gb = ec // 4
                            bank = banks[4 + (gb % 2)]
                            if ec % 4 == 0 and gb >= 2:
                                tn.wait_ge(ssc.h, A11_sc0 + 4 * it + gb - 1)
                            mm = tn.matmul(bank[:, (ec % 4) * 128:(ec % 4) * 128 + 128],
                                           ot[:, ec * 128:ec * 128 + 128], i128[:, :],
                                           start=True, stop=True)
                            if ec % 4 == 3:
                                smm.inc(mm)
                        tn.wait_ge(ssc.h, A11_sc0 + 4 * it + 4)
                        if it >= 2:
                            tn.wait_ge(sev.h, A11_ev0 + it - 1)
                        accb = banks[6 + (it % 2)]
                        for ec in range(16):
                            mm = tn.matmul(accb[:, 0:512], bfBIG[:, ec * 128:ec * 128 + 128],
                                           wsv[:, ec, :], start=(ec == 0), stop=(ec == 15))
                        smm.inc(mm)

            @block.scalar
            def _(s):
                for ob in range(4):
                    for t in range(8):
                        it = 8 * ob + t
                        for gb in range(4):
                            s.wait_ge(smm.h, A11_mm0 + 5 * it + gb + 1)
                            if it >= 1 and gb == 0:
                                s.wait_ge(smm.h, A11_mm0 + 5 * it)
                            bank = banks[4 + (gb % 2)]
                            ssc.inc(s.activation(bfBIG[:, gb * 512:(gb + 1) * 512], bank[:, :],
                                                 AF.Identity))

            @block.vector
            def _(v):
                for ob in range(4):
                    for t in range(8):
                        it = 8 * ob + t
                        v.wait_ge(smm.h, A11_mm0 + 5 * it + 5)
                        accb = banks[6 + (it % 2)]
                        xsl = xres[:, t * D + ob * 512:t * D + ob * 512 + 512]
                        sev.inc(v.tensor_tensor(xsl, accb[:, 0:512], xsl, ALU.add))
            return A11_ev0 + 32

        last_ev = P0_EV
        cid = 0
        for p in parts:
            if p in ("ffn1", "ffn2"):
                last_ev = ffn_call(cid, last_ev)
                cid += 1
            elif p == "attn":
                try:
                    last_ev = attn_call(last_ev)
                except _StopEmit as e:
                    last_ev = e.ev
            else:
                raise ValueError(p)

        @block.sync
        def _(sync):
            sync.wait_ge(sev.h, last_ev)
            for t in range(NT):
                sst.incd(sync.dma_start(out=y_out.rearrange("(a p) d -> a p d", p=128)[t, :, :],
                                        in_=xres[:, t * D:(t + 1) * D]))
            if DEBUG_EPRIME:
                dbv = dbg_out.rearrange("(u p) c -> u p c", p=128)
                for u in range(N_HU):
                    sst.incd(sync.dma_start(out=dbv[u], in_=eprime[u].rearrange("p a b -> p (a b)")))
                sst.incd(sync.dma_start(out=dbg_sb[:, 0:8192], in_=f32P1[:, :]))
                sst.incd(sync.dma_start(out=dbg_sb[:, 8192:10240], in_=scr2[:, :]))
                sst.incd(sync.dma_start(out=dbg_sb[:, 10240:10304], in_=colp[:, :]))
                sst.incd(sync.dma_start(out=dbg_sb[:, 10304:10312], in_=adeq[:, :]))
                sst.incd(sync.dma_start(out=dbg_bf[:, :], in_=bfBIG[:, :]))
            sync.wait_ge(sst.h, sst.n)

    return nc


def kernel(**inputs):
    xs = np.split(np.ascontiguousarray(inputs["L0_attn_x"][0]), NCORES, axis=0)
    gs = np.split(np.ascontiguousarray(inputs["L0_ffn_w_gate"]), NCORES, axis=0)
    us = np.split(np.ascontiguousarray(inputs["L0_ffn_w_up"]), NCORES, axis=0)
    ds = np.split(np.ascontiguousarray(inputs["L0_ffn_w_down"]), NCORES, axis=1)
    nc = build_program(PARTS)
    in_maps = []
    for c in range(NCORES):
        tb = host_tables(c)
        in_maps.append({
            "x_in": xs[c],
            "wq": np.ascontiguousarray(inputs["L0_attn_Wq"]),
            "wk": np.ascontiguousarray(inputs["L0_attn_Wk"]),
            "wv": np.ascontiguousarray(inputs["L0_attn_Wv"]),
            "wo": np.ascontiguousarray(inputs["L0_attn_Wo"]),
            "wg1": np.ascontiguousarray(inputs["L0_attn_Wg1"]),
            "wg2": np.ascontiguousarray(inputs["L0_attn_Wg2"]),
            "nrmw": np.ascontiguousarray(inputs["L0_attn_norm_weight"]).reshape(1, D),
            "nrmw_rep": np.tile(np.ascontiguousarray(inputs["L0_attn_norm_weight"]).reshape(1, D), (128, 1)),
            "wg_sl": gs[c], "wu_sl": us[c], "wd_sl": np.ascontiguousarray(ds[c]),
            **tb,
        })
    res = run_bass_kernel_spmd(nc, in_maps, core_ids=list(range(NCORES)))
    y = np.concatenate([res.results[c]["y_out"] for c in range(NCORES)], axis=0)
    return y.reshape(1, T * NCORES, D).astype(np.float32)


# version 07
# speedup vs baseline: 1.4528x; 1.0014x over previous
import contextlib
import numpy as np
import ml_dtypes
import concourse.bass as bass
import concourse.mybir as mybir
from concourse.bass_utils import run_bass_kernel_spmd

F32 = mybir.dt.float32
BF16 = mybir.dt.bfloat16
I32 = mybir.dt.int32
ALU = mybir.AluOpType
AF = mybir.ActivationFunctionType
AX = mybir.AxisListType.X

NCORES = 8
T = 1024
D = 2048
HEADS = 16
HD = 128
CH = 128
NT = T // 128
HU = 64
N_HU = T // HU
NW_ELEMS = float(8192 * 2048)
PG_GU = 128 * 2048
PG_D = 128 * 1024
SL_ELEMS = 8 * PG_GU * 2 + 16 * PG_D

PARTS = ("ffn1", "attn", "ffn2")
ATTN_STAGES = None


class _StopEmit(Exception):
    def __init__(self, ev):
        self.ev = ev
DEBUG_EPRIME = False


def _sylvester(n):
    h = np.array([[1.0]])
    while h.shape[0] < n:
        h = np.block([[h, h], [h, -h]])
    return h.astype(np.float32)


def _ll():
    return -(2.0 ** (-8.0 * (np.arange(HEADS, dtype=np.float64) + 1.0) / HEADS))


def _lamtab(core):
    ll = _ll()
    t = np.zeros((128, 128), np.float32)
    for cp in range(NCORES):
        for h in range(HEADS):
            if cp < core:
                t[:, cp * 16 + h] = np.float32(np.exp(ll[h] * CH * 8.0 * (core - 1 - cp)))
    return t


def host_tables(core):
    H64B = np.kron(np.eye(2, dtype=np.float32), _sylvester(64)).astype(np.float32)
    tg = np.arange(core * T, (core + 1) * T, dtype=np.float64)
    j = np.arange(64, dtype=np.float64)
    theta = 10000.0 ** (-2.0 * j / 128.0)
    ang = tg[:, None] * theta[None, :]
    ll = _ll()
    s = np.arange(CH, dtype=np.float64)
    dmask = np.zeros((HEADS * CH, CH), dtype=np.float32)
    for h in range(HEADS):
        col = np.exp(-ll[h] * (s + 1.0 - 64.0))
        m = (np.arange(CH)[None, :] >= s[:, None])
        dmask[h * CH:(h + 1) * CH, :] = (col[:, None] * m).astype(np.float32)
    dqcol = np.exp(ll[None, :] * (s[:, None] + 1.0 - 64.0)).astype(np.float32)
    dkcol = np.exp(ll[None, :] * (CH - 1.0 - s[:, None])).astype(np.float32)
    P4 = np.zeros((32, 512), dtype=np.float32)
    for usub in range(2):
        for tlo in range(2):
            for th in range(32):
                P4[th, (2 * usub + tlo) * 128 + usub * 64 + 2 * th + tlo] = 1.0
    cos_rep = np.tile(np.cos(ang).astype(np.float32), (1, 16))
    sin_rep = np.tile(np.sin(ang).astype(np.float32), (1, 16))
    dkrep = np.zeros((128, D), np.float32)
    for h in range(HEADS):
        dkrep[:, h * 128:(h + 1) * 128] = dkcol[:, h:h + 1]
    return {
        "H128": _sylvester(128) * np.float32(1.0 / np.sqrt(8192)), "H64B": H64B,
        "I128": np.asarray(np.eye(128, dtype=np.float32).astype(ml_dtypes.bfloat16)),
        "cos_rep": cos_rep, "sin_rep": sin_rep, "dkrep": dkrep,
        "dmask": dmask, "dqcol": dqcol, "dkcol": dkcol, "P4": P4,
        "lamtab": _lamtab(core),
        "SEL2": np.concatenate([np.kron(np.eye(2, dtype=np.float32), np.ones((1, 64), np.float32)), np.zeros((30, 128), np.float32)], 0),
    }


class Sem:
    def __init__(self, h):
        self.h, self.n = h, 0

    def inc(self, instr, k=1):
        instr.then_inc(self.h, k)
        self.n += k
        return instr

    def incd(self, instr):
        return self.inc(instr, 16)


def _sched_B(base_sc, base_ev, base_mm, base_gp):
    M = {}
    sc, ev, mm, gp = base_sc, base_ev, base_mm, base_gp
    for u in range(N_HU):
        sc += 1; M["reord", u] = sc
        for b in range(8):
            mm += 1; M["r1", u, b] = mm
            if b >= 2:
                mm += 1; M["r2", u, b - 2] = mm
        for b in [6, 7]:
            mm += 1; M["r2", u, b] = mm
        for b in range(8):
            ev += 1; M["r1ev", u, b] = ev
        for b in range(8):
            sc += 1; M["yev", u, b] = sc
        ev += 1; M["mx1", u] = ev
        mm += 1; M["mx1T", u] = mm
        ev += 1; M["mx1Tev", u] = ev
        ev += 1; M["chain", u] = ev
        mm += 1; M["sT", u] = mm
        ev += 1; M["sTev", u] = ev
        mm += 1; M["ssm", u] = mm
        ev += 1; M["ssmev", u] = ev
        ev += 1; M["vmult", u] = ev
        for half in range(2):
            ev += 1; M["vp", u, half] = ev
            ev += 1; M["c", u, half] = ev
            ev += 1; M["dd", u, half] = ev
            ev += 1; M["ge", u, half] = ev
            ev += 1; M["r", u, half] = ev
            ev += 1; M["q", u, half] = ev
        M["vdone", u] = ev
        for b in range(8):
            mm += 1; M["r3", u, b] = mm
        for b in range(8):
            sc += 1; M["r3ev", u, b] = sc
        if u % 4 == 3:
            g4 = u // 4
            mm += 1; M["dqc", g4, 0] = mm
            mm += 1; M["dqc", g4, 1] = mm
            ev += 2; M["dqcev", g4] = ev
            for o in range(16):
                mm += 1; M["dn", g4, o] = mm
                sc += 1; M["dnev", g4, o] = sc
                for tb in range(2):
                    mm += 1; M["dntm", g4, o, tb] = mm
                    ev += 1; M["resid", g4, o, tb] = ev
    return M, sc, ev, mm, gp


def build_program(parts=PARTS):
    nc = bass.Bass("TRN2", target_bir_lowering=False, debug=False)
    ll = _ll()
    lamC = [float(np.float32(np.exp(ll[h] * CH))) for h in range(HEADS)]
    sScl = [float(np.float32(np.exp(ll[h] * 64.0))) for h in range(HEADS)]

    x_in = nc.dram_tensor("x_in", [T, D], F32, kind="ExternalInput").ap()
    wq_d = nc.dram_tensor("wq", [D, D], F32, kind="ExternalInput").ap()
    wk_d = nc.dram_tensor("wk", [D, D], F32, kind="ExternalInput").ap()
    wv_d = nc.dram_tensor("wv", [D, D], F32, kind="ExternalInput").ap()
    wo_d = nc.dram_tensor("wo", [D, D], F32, kind="ExternalInput").ap()
    wg1_d = nc.dram_tensor("wg1", [D, HD], F32, kind="ExternalInput").ap()
    wg2_d = nc.dram_tensor("wg2", [HD, D], F32, kind="ExternalInput").ap()
    nrmw_d = nc.dram_tensor("nrmw", [1, D], F32, kind="ExternalInput").ap()
    wg_sl = nc.dram_tensor("wg_sl", [T, D], F32, kind="ExternalInput").ap()
    wu_sl = nc.dram_tensor("wu_sl", [T, D], F32, kind="ExternalInput").ap()
    wd_sl = nc.dram_tensor("wd_sl", [D, T], F32, kind="ExternalInput").ap()
    y_out = nc.dram_tensor("y_out", [T, D], F32, kind="ExternalOutput").ap()
    dbg_out = (nc.dram_tensor("dbg_out", [N_HU * 128, 64 * HU], F32, kind="ExternalOutput").ap()
               if DEBUG_EPRIME else None)
    dbg_sb = (nc.dram_tensor("dbg_sb", [128, 10312], F32, kind="ExternalOutput").ap()
              if DEBUG_EPRIME else None)
    dbg_bf = (nc.dram_tensor("dbg_bf", [128, 16384], BF16, kind="ExternalOutput").ap()
              if DEBUG_EPRIME else None)

    tH128 = nc.dram_tensor("H128", [128, 128], F32, kind="ExternalInput").ap()
    tH64B = nc.dram_tensor("H64B", [128, 128], F32, kind="ExternalInput").ap()
    tI128 = nc.dram_tensor("I128", [128, 128], BF16, kind="ExternalInput").ap()
    tcos = nc.dram_tensor("cos_rep", [T, 1024], F32, kind="ExternalInput").ap()
    tsin = nc.dram_tensor("sin_rep", [T, 1024], F32, kind="ExternalInput").ap()
    tdkrep = nc.dram_tensor("dkrep", [128, D], F32, kind="ExternalInput").ap()
    tnwrep = nc.dram_tensor("nrmw_rep", [128, D], F32, kind="ExternalInput").ap()
    tdmask = nc.dram_tensor("dmask", [HEADS * CH, CH], F32, kind="ExternalInput").ap()
    tdqcol = nc.dram_tensor("dqcol", [CH, HEADS], F32, kind="ExternalInput").ap()
    tdkcol = nc.dram_tensor("dkcol", [CH, HEADS], F32, kind="ExternalInput").ap()
    tP4 = nc.dram_tensor("P4", [32, 512], F32, kind="ExternalInput").ap()
    tlam = nc.dram_tensor("lamtab", [128, 128], F32, kind="ExternalInput").ap()
    tSEL2 = nc.dram_tensor("SEL2", [32, 128], F32, kind="ExternalInput").ap()

    ag_w_in = nc.dram_tensor("ag_w_in", [SL_ELEMS], BF16)
    GU_SL = 16 * PG_GU
    D_SL = 16 * PG_D
    ag_gu_out = nc.dram_tensor("ag_gu_out", [NCORES * GU_SL], BF16)
    ag_d_out = nc.dram_tensor("ag_d_out", [NCORES * D_SL], BF16)
    ar_s_in = nc.dram_tensor("ar_s_in", [128, 4], F32)
    ar_s_out = nc.dram_tensor("ar_s_out", [128, 4], F32)
    ag_st_in = nc.dram_tensor("ag_st_in", [HEADS * HD, HD], F32)
    ag_st_out = nc.dram_tensor("ag_st_out", [NCORES * HEADS * HD, HD], F32)
    eprime = nc.dram_tensor("eprime", [N_HU, 128, 64, HU], F32)
    kcache = nc.dram_tensor("kcache", [T, D], BF16)
    kdcache = nc.dram_tensor("kdcache", [T, D], BF16)
    vcache = nc.dram_tensor("vcache", [T, D], BF16)
    qcache = nc.dram_tensor("qcache", [T, D], BF16)
    gcache = nc.dram_tensor("gcache", [T, D], F32)
    oncache = nc.dram_tensor("oncache", [T, D], BF16)
    replica = [list(range(NCORES))]

    with contextlib.ExitStack() as _stack:
        _e = _stack.enter_context
        xres = _e(nc.sbuf_tensor("xres", [128, NT * D], F32))
        h128 = _e(nc.sbuf_tensor("h128", [128, 128], F32))
        h64b = _e(nc.sbuf_tensor("h64b", [128, 128], F32))
        i128 = _e(nc.sbuf_tensor("i128", [128, 128], BF16))
        i128f = _e(nc.sbuf_tensor("i128f", [128, 128], F32))
        ones = _e(nc.sbuf_tensor("ones", [128, 128], F32))
        sel2 = _e(nc.sbuf_tensor("sel2", [32, 128], F32))
        srow = _e(nc.sbuf_tensor("srow", [32, 64], F32))
        p4 = _e(nc.sbuf_tensor("p4", [32, 512], F32))
        colp = _e(nc.sbuf_tensor("colp", [128, 64], F32))
        adeq = _e(nc.sbuf_tensor("adeq", [128, NT], F32))
        dqc = _e(nc.sbuf_tensor("dqc", [128, HEADS], F32))
        dkc = _e(nc.sbuf_tensor("dkc", [128, HEADS], F32))
        nwrep = _e(nc.sbuf_tensor("nwrep", [128, D], F32))
        lamt = _e(nc.sbuf_tensor("lamt", [128, 128], F32))
        bfBIG = _e(nc.sbuf_tensor("bfBIG", [128, 16384], BF16))
        bfSM = _e(nc.sbuf_tensor("bfSM", [128, 4096], BF16))
        f32P1 = _e(nc.sbuf_tensor("f32P1", [128, 8192], F32))
        scr = _e(nc.sbuf_tensor("scr", [128, 4096], F32))
        scr2 = _e(nc.sbuf_tensor("scr2", [128, 2048], F32))
        scrC = _e(nc.sbuf_tensor("scrC", [128, 2048], F32))
        bfW = _e(nc.sbuf_tensor("bfW", [128, 8192], BF16))
        iscr = _e(nc.sbuf_tensor("iscr", [128, 2048], I32))
        _s1 = _e(nc.semaphore("s_ld"))
        _s2 = _e(nc.semaphore("s_w"))
        _s3 = _e(nc.semaphore("s_mm"))
        _s4 = _e(nc.semaphore("s_ev"))
        _s5 = _e(nc.semaphore("s_sc"))
        _s6 = _e(nc.semaphore("s_gp"))
        _s7 = _e(nc.semaphore("s_st"))
        _s8 = _e(nc.semaphore("s_cc"))
        _s9 = _e(nc.semaphore("s_fb"))
        _s10 = _e(nc.semaphore("s_gst"))
        _s11 = _e(nc.semaphore("s_vs"))
        _s12 = _e(nc.semaphore("s_ss"))
        pb0 = _e(nc.psum_tensor("pb0", [128, 512], F32))
        pb1 = _e(nc.psum_tensor("pb1", [128, 512], F32))
        pb2 = _e(nc.psum_tensor("pb2", [128, 512], F32))
        pb3 = _e(nc.psum_tensor("pb3", [128, 512], F32))
        pb4 = _e(nc.psum_tensor("pb4", [128, 512], F32))
        pb5 = _e(nc.psum_tensor("pb5", [128, 512], F32))
        pb6 = _e(nc.psum_tensor("pb6", [128, 512], F32))
        pb7 = _e(nc.psum_tensor("pb7", [128, 512], F32))
        block = _e(nc.Block())
        sld, sw, smm, sev, ssc, sgp, sst, scc, sfb = (
            Sem(_s1), Sem(_s2), Sem(_s3), Sem(_s4), Sem(_s5), Sem(_s6), Sem(_s7), Sem(_s8), Sem(_s9))
        sgst = Sem(_s10)
        svs, sss = Sem(_s11), Sem(_s12)

        def VS(v, instr):
            svs.inc(instr)
            v.wait_ge(svs.h, svs.n)
            return instr

        def SS(s, instr):
            sss.inc(instr)
            s.wait_ge(sss.h, sss.n)
            return instr
        banks = [pb0, pb1, pb2, pb3, pb4, pb5, pb6, pb7]

        @block.sync
        def _(sync):
            for t in range(NT):
                sld.incd(sync.dma_start(out=xres[:, t * D:(t + 1) * D],
                                        in_=x_in.rearrange("(a p) d -> a p d", p=128)[t, :, :]))
            sld.incd(sync.dma_start(out=h128[:], in_=tH128[:, :]))
            sld.incd(sync.dma_start(out=h64b[:], in_=tH64B[:, :]))
            sld.incd(sync.dma_start(out=i128[:], in_=tI128[:, :]))
            sld.incd(sync.dma_start(out=p4[:, :], in_=tP4[:, :]))
            sld.incd(sync.dma_start(out=dqc[:, :], in_=tdqcol[:, :]))
            sld.incd(sync.dma_start(out=dkc[:, :], in_=tdkcol[:, :]))
            sld.incd(sync.dma_start(out=sel2[:, :], in_=tSEL2[:, :]))
            sld.incd(sync.dma_start(out=lamt[:, :], in_=tlam[:, :]))
        P0_LD = sld.n

        @block.vector
        def _(v):
            v.wait_ge(sld.h, P0_LD)
            v.tensor_copy(i128f[:], i128[:])
            v.memset(srow[:, :], 0.0)
            sev.inc(v.memset(ones[:], 1.0))
        P0_EV = sev.n

        tiles = [(0, wg_sl, i, 2048) for i in range(8)] + \
                [(1, wu_sl, i, 2048) for i in range(8)] + \
                [(2, wd_sl, i, 1024) for i in range(16)]
        B1_ev = sev.n

        @block.sync
        def _(sync):
            sync.wait_ge(sld.h, P0_LD)
            for idx, (m, ap_, i, w_) in enumerate(tiles):
                if idx >= 2:
                    sync.wait_ge(sev.h, B1_ev + idx - 1)
                if idx >= 1:
                    sync.wait_ge(sld.h, P0_LD + 16 * idx)
                buf = scr if idx % 2 == 0 else scr2
                src = ap_.rearrange("(a p) d -> a p d", p=128)[i, :, :]
                sld.incd(sync.dma_start(out=buf[:, 0:w_], in_=src))

        @block.vector
        def _(v):
            v.wait_ge(sev.h, P0_EV)
            v.memset(colp[:, 0:4], 0.0)
            for idx, (m, ap_, i, w_) in enumerate(tiles):
                v.wait_ge(sld.h, P0_LD + 16 * (idx + 1))
                buf = scr if idx % 2 == 0 else scr2
                VS(v, v.tensor_reduce(colp[:, 8:9], buf[:, 0:w_], AX, ALU.add, apply_absolute_value=True))
                sev.inc(v.tensor_tensor(colp[:, m:m + 1], colp[:, m:m + 1], colp[:, 8:9], ALU.add))
                v.wait_ge(sev.h, sev.n)
        P1A_EV = sev.n

        @block.tensor
        def _(tn):
            tn.wait_ge(sev.h, P1A_EV)
            smm.inc(tn.matmul(pb0[0:128, 0:4], ones[:, 0:128], colp[:, 0:4], start=True, stop=True))
        P1A_MM = smm.n

        @block.vector
        def _(v):
            v.wait_ge(smm.h, P1A_MM)
            sev.inc(v.tensor_copy(colp[:, 16:20], pb0[:, 0:4]))
        P1B_EV = sev.n

        @block.sync
        def _(sync):
            sync.wait_ge(sev.h, P1B_EV)
            sst.incd(sync.dma_start(out=ar_s_in[:, :], in_=colp[:, 16:20]))
        P1_ST1 = sst.n

        @block.gpsimd
        def _(g):
            g.wait_ge(sst.h, P1_ST1)
            scc.inc(g.collective_compute("AllReduce", ALU.add, replica_groups=replica,
                                         ins=[ar_s_in.ap().opt()], outs=[ar_s_out.ap().opt()]))
        P1_CC1 = scc.n

        @block.sync
        def _(sync):
            sync.wait_ge(scc.h, P1_CC1)
            sync.wait_ge(sld.h, sld.n)
            sld.incd(sync.dma_start(out=colp[:, 20:24], in_=ar_s_out[:, :]))
        P1_LD2 = sld.n

        @block.vector
        def _(v):
            v.wait_ge(sld.h, P1_LD2)
            VS(v, v.tensor_scalar(colp[:, 24:28], colp[:, 20:24], 1.0 / NW_ELEMS, 1e-5, ALU.mult, ALU.max))
            VS(v, v.reciprocal(colp[:, 40:44], colp[:, 24:28]))
            sev.inc(v.reciprocal(colp[:, 44:48], colp[:, 40:44]))
            v.wait_ge(sev.h, sev.n)
        P1D_EV = sev.n

        PAGE_OFF = [0, 8 * PG_GU, 16 * PG_GU]
        B2_ld, B2_ev, B2_mm, B2_sc, B2_st = sw.n, sev.n, smm.n, ssc.n, sst.n
        nbat = [w_ // 512 for (_, _, _, w_) in tiles]
        cumb = np.cumsum([0] + nbat).tolist()

        @block.sync
        def _(sync):
            for idx, (m, ap_, i, w_) in enumerate(tiles):
                if idx >= 2:
                    sync.wait_ge(sev.h, B2_ev + idx - 1)
                if idx >= 1:
                    sync.wait_ge(sw.h, B2_ld + 16 * idx)
                buf = scr if idx % 2 == 0 else scr2
                src = ap_.rearrange("(a p) d -> a p d", p=128)[i, :, :]
                sw.incd(sync.dma_start(out=buf[:, 0:w_], in_=src))

        @block.vector
        def _(v):
            v.wait_ge(sev.h, P1D_EV)
            for idx, (m, ap_, i, w_) in enumerate(tiles):
                v.wait_ge(sw.h, B2_ld + 16 * (idx + 1))
                if idx >= 2:
                    v.wait_ge(smm.h, B2_mm + cumb[idx - 1])
                buf = scr if idx % 2 == 0 else scr2
                tb = bfSM[:, 0:2048] if idx % 2 == 0 else bfSM[:, 2048:4096]
                sc_ = colp[:, 40 + m:41 + m]
                v.tensor_scalar(scrC[:, 0:w_], buf[:, 0:w_], sc_, 0.5, ALU.mult, ALU.is_ge)
                v.tensor_scalar(buf[:, 0:w_], buf[:, 0:w_], sc_, -0.5, ALU.mult, ALU.is_le)
                sev.inc(v.tensor_tensor(tb[:, 0:w_], scrC[:, 0:w_], buf[:, 0:w_], ALU.subtract))

        @block.tensor
        def _(tn):
            gb = 0
            for idx, (m, ap_, i, w_) in enumerate(tiles):
                tn.wait_ge(sev.h, B2_ev + idx + 1)
                tb = bfSM[:, 0:2048] if idx % 2 == 0 else bfSM[:, 2048:4096]
                for h in range(w_ // 128):
                    bank = banks[4 + (gb % 2)]
                    if h % 4 == 0 and gb >= 2:
                        tn.wait_ge(ssc.h, B2_sc + gb - 1)
                    mm = tn.matmul(bank[:, (h % 4) * 128:(h % 4) * 128 + 128],
                                   tb[:, h * 128:(h + 1) * 128], i128[:, :],
                                   start=True, stop=True)
                    if h % 4 == 3:
                        smm.inc(mm)
                        gb += 1

        @block.scalar
        def _(s):
            gb = 0
            for idx, (m, ap_, i, w_) in enumerate(tiles):
                slot = bfW[:, 0:4096] if idx % 2 == 0 else bfW[:, 4096:8192]
                if idx >= 2:
                    s.wait_ge(sgst.h, 16 * (idx - 1))
                for b in range(w_ // 512):
                    s.wait_ge(smm.h, B2_mm + gb + 1)
                    bank = banks[4 + (gb % 2)]
                    ssc.inc(s.activation(slot[:, b * 512:(b + 1) * 512], bank[:, :], AF.Identity))
                    gb += 1

        @block.gpsimd
        def _(g):
            for idx, (m, ap_, i, w_) in enumerate(tiles):
                g.wait_ge(ssc.h, B2_sc + cumb[idx + 1])
                if idx >= 1:
                    g.wait_ge(sgst.h, 16 * idx)
                slot = bfW[:, 0:4096] if idx % 2 == 0 else bfW[:, 4096:8192]
                off = PAGE_OFF[m] + i * (PG_GU if m < 2 else PG_D)
                dst = ag_w_in.ap()[off:off + 128 * w_].rearrange("(p c) -> p c", c=w_)
                sgst.incd(g.dma_start(out=dst, in_=slot[:, 0:w_]))
                if idx == 15:
                    g.wait_ge(sgst.h, 16 * 16)
                    scc.inc(g.collective_compute(
                        "AllGather", ALU.bypass, replica_groups=replica,
                        ins=[ag_w_in.ap()[0:GU_SL].opt()], outs=[ag_gu_out.ap().opt()]))
            g.wait_ge(sgst.h, sgst.n)
            scc.inc(g.collective_compute(
                "AllGather", ALU.bypass, replica_groups=replica,
                ins=[ag_w_in.ap()[GU_SL:SL_ELEMS].opt()], outs=[ag_d_out.ap().opt()]))
        P1E_ST = sst.n
        P1_CC_GU = scc.n - 1
        P1_CC2 = scc.n

        def gu_page(m, i1):
            r, iloc = i1 >> 3, i1 & 7
            return r * GU_SL + PAGE_OFF[m] + iloc * PG_GU

        def d_page(o, r):
            return r * D_SL + o * PG_D

        xq = bfBIG.rearrange("p (hh c) -> p hh c", c=T)

        def ffn_call(cid, wait_ev):
            CALL_mm0 = smm.n
            L_ev, L_sc, L_mm = sev.n, ssc.n, smm.n

            @block.scalar
            def _(s):
                s.wait_ge(sev.h, wait_ev)
                for t in range(NT):
                    xt = xres[:, t * D:(t + 1) * D]
                    ssc.inc(s.activation(scrC[:, 0:2048], xt, AF.Square))
                    s.wait_ge(sev.h, L_ev + 3 * t + 1)
                    ssc.inc(s.activation(colp[:, 8:9], colp[:, 6:7], AF.Sqrt))
                    s.wait_ge(ssc.h, ssc.n)
                    s.wait_ge(sev.h, L_ev + 3 * t + 2)
                    s.activation(scrC[:, 0:2048], xt, AF.Identity, bias=colp[:, 7:8])
                    ssc.inc(s.activation(scr[:, 0:2048], scrC[:, 0:2048], AF.Identity, scale=colp[:, 15:16]))
                    for b in range(4):
                        gb = 4 * t + b
                        s.wait_ge(smm.h, L_mm + gb + 1)
                        bank = banks[4 + (gb % 2)]
                        dst = xq[:, 4 * b:4 * b + 4, t * 128:t * 128 + 128]
                        ssc.inc(s.activation(dst, bank.rearrange("p (h c) -> p h c", c=128), AF.Identity))

            @block.vector
            def _(v):
                for t in range(NT):
                    xt = xres[:, t * D:(t + 1) * D]
                    v.wait_ge(ssc.h, L_sc + 7 * t + 1)
                    VS(v, v.tensor_reduce(colp[:, 0:1], xt, AX, ALU.add))
                    VS(v, v.tensor_reduce(colp[:, 1:2], scrC[:, 0:2048], AX, ALU.add))
                    VS(v, v.tensor_scalar_mul(colp[:, 2:3], colp[:, 0:1], 1.0 / D))
                    VS(v, v.tensor_scalar_mul(colp[:, 3:4], colp[:, 1:2], 1.0 / D))
                    VS(v, v.tensor_tensor(colp[:, 4:5], colp[:, 2:3], colp[:, 2:3], ALU.mult))
                    VS(v, v.scalar_tensor_tensor(colp[:, 5:6], colp[:, 3:4], 1e-5, colp[:, 4:5],
                                                 ALU.add, ALU.subtract))
                    VS(v, v.reciprocal(colp[:, 6:7], colp[:, 5:6]))
                    sev.inc(v.tensor_scalar_mul(colp[:, 7:8], colp[:, 2:3], -1.0))
                    v.wait_ge(sev.h, sev.n)
                    v.wait_ge(ssc.h, L_sc + 7 * t + 2)
                    VS(v, v.tensor_tensor(colp[:, 14:15], colp[:, 8:9], colp[:, 8:9], ALU.mult))
                    VS(v, v.tensor_tensor(colp[:, 14:15], colp[:, 14:15], colp[:, 5:6], ALU.mult))
                    VS(v, v.tensor_scalar(colp[:, 14:15], colp[:, 14:15], -0.5, 1.5, ALU.mult, ALU.add))
                    sev.inc(v.tensor_tensor(colp[:, 15:16], colp[:, 8:9], colp[:, 14:15], ALU.mult))
                    v.wait_ge(sev.h, sev.n)
                    v.wait_ge(ssc.h, L_sc + 7 * t + 3)
                    if t >= 2:
                        v.wait_ge(smm.h, L_mm + 4 * (t - 1))
                    VS(v, v.tensor_reduce(colp[:, 10:11], scr[:, 0:2048], AX, ALU.max, apply_absolute_value=True))
                    VS(v, v.tensor_scalar_max(colp[:, 11:12], colp[:, 10:11], 1e-5))
                    VS(v, v.reciprocal(colp[:, 12:13], colp[:, 11:12]))
                    VS(v, v.tensor_scalar_mul(colp[:, 13:14], colp[:, 12:13], 128.0))
                    VS(v, v.reciprocal(adeq[:, t:t + 1], colp[:, 13:14]))
                    sl = bfSM[:, (t % 2) * 2048:(t % 2) * 2048 + 2048]
                    v.tensor_scalar_mul(scr2[:, 0:2048], scr[:, 0:2048], colp[:, 13:14])
                    v.tensor_scalar_add(iscr[:, 0:2048], scr2[:, 0:2048], 256.0)
                    v.scalar_tensor_tensor(scr[:, 0:2048], scr2[:, 0:2048], 256.0, iscr[:, 0:2048],
                                           ALU.add, ALU.subtract)
                    v.scalar_tensor_tensor(scr2[:, 0:2048], scr[:, 0:2048], 0.5, iscr[:, 0:2048],
                                           ALU.is_ge, ALU.add)
                    sev.inc(v.tensor_scalar(sl, scr2[:, 0:2048], -256.0, 127.0, ALU.add, ALU.min))

            @block.tensor
            def _(tn):
                for t in range(NT):
                    tn.wait_ge(sev.h, L_ev + 3 * t + 3)
                    sl = bfSM[:, (t % 2) * 2048:(t % 2) * 2048 + 2048]
                    for h in range(16):
                        gb = 4 * t + h // 4
                        bank = banks[4 + (gb % 2)]
                        if h % 4 == 0 and gb >= 2:
                            pgb = gb - 2
                            tn.wait_ge(ssc.h, L_sc + 7 * (pgb // 4) + 3 + (pgb % 4) + 1)
                        mm = tn.matmul(bank[:, (h % 4) * 128:(h % 4) * 128 + 128],
                                       sl[:, h * 128:(h + 1) * 128], i128[:, :],
                                       start=True, stop=True)
                        if h % 4 == 3:
                            smm.inc(mm)
            L_done_ev = sev.n

            A_w, A_mm, A_ev, A_sc, A_gp, A_st = sw.n, smm.n, sev.n, ssc.n, sgp.n, sst.n

            def f_mm_deq(g, ts_):
                return A_mm + 68 * g + ts_ + 1

            def f_mm_gi(gi):
                return A_mm + 68 * (gi // 64) + 4 + (gi % 64) + 1

            def f_sc_b(g, ts_):
                return A_sc + 68 * g + ts_ + 1

            def f_sc_silu(gi):
                return A_sc + 68 * (gi // 64) + 4 + (gi % 64) + 1

            def f_ev_deq(g, ts_):
                return A_ev + 132 * g + ts_ + 1

            def f_ev_t0(gi):
                return A_ev + 132 * (gi // 64) + 4 + 2 * (gi % 64) + 1

            def f_ev_u1(gi):
                return A_ev + 132 * (gi // 64) + 4 + 2 * (gi % 64) + 2

            def f_gp(gi):
                return A_gp + gi + 1

            @block.sync
            def _(sync):
                sync.wait_ge(scc.h, P1_CC_GU)
                sync.wait_ge(smm.h, CALL_mm0)
                for gi in range(128):
                    i1 = gi % 64
                    if gi >= 2:
                        sync.wait_ge(smm.h, f_mm_gi(gi - 2))
                    if gi >= 1:
                        sync.wait_ge(sw.h, A_w + 32 * gi)
                    wslot = bfW[:, (gi % 2) * 4096:(gi % 2) * 4096 + 4096]
                    go, uo = gu_page(0, i1), gu_page(1, i1)
                    sw.incd(sync.dma_start(out=wslot[:, 0:2048],
                                           in_=ag_gu_out.ap()[go:go + PG_GU].rearrange("(p c) -> p c", c=2048)))
                    sw.incd(sync.dma_start(out=wslot[:, 2048:4096],
                                           in_=ag_gu_out.ap()[uo:uo + PG_GU].rearrange("(p c) -> p c", c=2048)))

            @block.scalar
            def _(s):
                for g in range(2):
                    for ts_ in range(4):
                        tt = 4 * g + ts_
                        if g == 0 and ts_ == 0:
                            s.wait_ge(sev.h, L_done_ev)
                        if g == 1 and ts_ == 0:
                            s.wait_ge(smm.h, f_mm_deq(0, 3))
                        ssc.inc(s.activation(scrC[:, 0:128], ones[:, 0:128], AF.Identity,
                                             scale=adeq[:, tt:tt + 1]))
                        if ts_ < 3:
                            s.wait_ge(smm.h, f_mm_deq(g, ts_))
                    for i1 in range(64):
                        gi = g * 64 + i1
                        s.wait_ge(sev.h, f_ev_t0(gi))
                        if gi >= 2:
                            s.wait_ge(sgp.h, f_gp(gi - 2))
                        t0slot = scrC[:, 512 + (gi % 2) * 768:512 + (gi % 2) * 768 + 512]
                        oslot = scr[:, (gi % 2) * 2048:(gi % 2) * 2048 + 512]
                        ssc.inc(s.activation(oslot, t0slot, AF.Silu))
                        if gi % 8 == 7:
                            b = gi // 8
                            s.wait_ge(sgp.h, f_gp(gi))
                            if b >= 1:
                                s.wait_ge(sst.h, A_st + 128 * b)
                            for uloc in range(8):
                                u = g * 8 + uloc
                                fsrc = f32P1[:, 0:4096].rearrange(
                                    "p (sl c) -> p sl c", c=512)[:, :, uloc * 64:(uloc + 1) * 64]
                                sst.incd(s.dma_start(out=eprime[u, :, (b % 8) * 8:(b % 8) * 8 + 8, :],
                                                     in_=fsrc))

            @block.gpsimd
            def _(g_):
                for gi in range(128):
                    g_.wait_ge(ssc.h, f_sc_silu(gi))
                    g_.wait_ge(sev.h, f_ev_u1(gi))
                    if gi >= 8:
                        g_.wait_ge(sst.h, A_st + 128 * (gi // 8))
                    oslot = scr[:, (gi % 2) * 2048:(gi % 2) * 2048 + 512]
                    u1slot = scr[:, (gi % 2) * 2048 + 512:(gi % 2) * 2048 + 1024]
                    slot = f32P1[:, (gi % 8) * 512:(gi % 8) * 512 + 512]
                    sgp.inc(g_.tensor_tensor(slot, oslot, u1slot, ALU.mult))

            @block.vector
            def _(v):
                for g in range(2):
                    for ts_ in range(4):
                        v.wait_ge(smm.h, f_mm_deq(g, ts_))
                        sev.inc(v.tensor_copy(scr2[:, ts_ * 128:(ts_ + 1) * 128],
                                              pb6[:, ts_ * 128:(ts_ + 1) * 128]))
                        v.wait_ge(sev.h, sev.n)
                    for i1 in range(64):
                        gi = g * 64 + i1
                        v.wait_ge(smm.h, f_mm_gi(gi))
                        gB = banks[gi % 2]
                        uB = banks[2 + (gi % 2)]
                        t0slot = scrC[:, 512 + (gi % 2) * 768:512 + (gi % 2) * 768 + 512]
                        oslot = scr[:, (gi % 2) * 2048:(gi % 2) * 2048 + 512]
                        u1slot = scr[:, (gi % 2) * 2048 + 512:(gi % 2) * 2048 + 1024]
                        if gi >= 2:
                            v.wait_ge(ssc.h, f_sc_silu(gi - 2))
                        sev.inc(v.scalar_tensor_tensor(t0slot, gB[:, :], colp[:, 44:45],
                                                       scr2[:, 0:512], ALU.mult, ALU.mult))
                        if gi >= 2:
                            v.wait_ge(sgp.h, f_gp(gi - 2))
                        sev.inc(v.scalar_tensor_tensor(u1slot, uB[:, :], colp[:, 45:46],
                                                       scr2[:, 0:512], ALU.mult, ALU.mult))

            @block.tensor
            def _(tn):
                tn.wait_ge(scc.h, P1_CC_GU)
                for g in range(2):
                    for ts_ in range(4):
                        tn.wait_ge(ssc.h, f_sc_b(g, ts_))
                        if g == 1:
                            tn.wait_ge(sev.h, f_ev_deq(0, ts_))
                        smm.inc(tn.matmul(pb6[:, ts_ * 128:(ts_ + 1) * 128], scrC[:, 0:128],
                                          i128f[:, :], start=True, stop=True))
                    for i1 in range(64):
                        gi = g * 64 + i1
                        tn.wait_ge(sw.h, A_w + 32 * (gi + 1))
                        if gi >= 2:
                            tn.wait_ge(sev.h, f_ev_u1(gi - 2))
                        wslot = bfW[:, (gi % 2) * 4096:(gi % 2) * 4096 + 4096]
                        gB = banks[gi % 2]
                        uB = banks[2 + (gi % 2)]
                        for h in range(16):
                            tn.matmul(gB[:, :], wslot[:, h * 128:(h + 1) * 128],
                                      xq[:, h, g * 512:(g + 1) * 512],
                                      start=(h == 0), stop=(h == 15))
                        for h in range(16):
                            mm = tn.matmul(uB[:, :], wslot[:, 2048 + h * 128:2048 + (h + 1) * 128],
                                           xq[:, h, g * 512:(g + 1) * 512],
                                           start=(h == 0), stop=(h == 15))
                        smm.inc(mm)


            A_st_end = A_st + 128 * 16

            B_fb, B_sc, B_ev, B_mm, B_gp, B_w2 = sfb.n, ssc.n, sev.n, smm.n, sgp.n, sw.n
            M, _, _, _, _ = _sched_B(B_sc, B_ev, B_mm, B_gp)
            eq = f32P1[:, 0:4096]
            ewin = f32P1[:, 4096:8192]
            zr = scrC

            @block.sync
            def _(sync):
                sync.wait_ge(sst.h, A_st_end)
                for u in range(N_HU):
                    if u >= 1:
                        sync.wait_ge(sev.h, M["vdone", u - 1])
                        sync.wait_ge(sfb.h, B_fb + 16 * u)
                    src = eprime[u].rearrange("p a b -> p (a b)")
                    sfb.incd(sync.dma_start(out=eq, in_=src))


            @block.scalar
            def _(s):
                for u in range(N_HU):
                    u4 = u % 4
                    s.wait_ge(sfb.h, B_fb + 16 * (u + 1))
                    if u >= 1:
                        s.wait_ge(smm.h, M["r1", u - 1, 7])
                        s.wait_ge(sev.h, M["q", u - 1, 1])
                    ssc.inc(s.copy(
                        ewin.rearrange("p (t i) -> p t i", i=64),
                        eq.rearrange("p (i t) -> p i t", t=64).transpose([0, 2, 1])))
                    for b in range(8):
                        s.wait_ge(smm.h, M["r2", u, b])
                        yB = banks[2 + (b % 2)]
                        ssc.inc(s.activation(eq[:, b * 512:(b + 1) * 512], yB[:, :], AF.Identity))
                    for b in range(8):
                        s.wait_ge(smm.h, M["r3", u, b])
                        if u // 4 >= 1 and b == 0:
                            s.wait_ge(smm.h, M["dn", u // 4 - 1, 15])
                        bank = banks[4 + (b % 2)]
                        dst = bfBIG.rearrange("p (uu i b2 ths thi) -> p uu b2 ths thi i",
                                              i=64, b2=8, ths=4, thi=2)[:, u4, b, :, :, :]
                        srcv = bank.rearrange("q (ths thi i) -> q ths thi i", thi=2, i=64)
                        ssc.inc(s.activation(dst, srcv, AF.Identity))
                    if u % 4 == 3:
                        g4 = u // 4
                        if g4 == 0:
                            s.wait_ge(smm.h, f_mm_gi(127))
                        for o in range(16):
                            if not (g4 == 0 and o == 0):
                                po = (g4, o - 1) if o >= 1 else (g4 - 1, 15)
                                s.wait_ge(smm.h, M["dn", po[0], po[1]])
                                s.wait_ge(sw.h, B_w2 + 128 * (16 * g4 + o))
                            if g4 == 0 and o == 0:
                                s.wait_ge(scc.h, P1_CC2)
                            for r in range(NCORES):
                                off = d_page(o, r)
                                sw.incd(s.dma_start(
                                    out=bfW[:, r * 1024:(r + 1) * 1024],
                                    in_=ag_d_out.ap()[off:off + PG_D].rearrange("(p c) -> p c", c=1024)))
                            if o >= 1:
                                s.wait_ge(smm.h, M["dn", g4, o - 1])
                                if o >= 3:
                                    s.wait_ge(smm.h, M["dntm", g4, o - 3, 1])
                                dB = banks[6 + ((o - 1) % 2)]
                                ssc.inc(s.activation(scr2[:, ((o - 1) % 2) * 256:((o - 1) % 2) * 256 + 256],
                                                     dB[:, 0:256], AF.Identity))
                        s.wait_ge(smm.h, M["dn", g4, 15])
                        s.wait_ge(smm.h, M["dntm", g4, 13, 1])
                        dB = banks[6 + (15 % 2)]
                        ssc.inc(s.activation(scr2[:, (15 % 2) * 256:(15 % 2) * 256 + 256],
                                             dB[:, 0:256], AF.Identity))

            @block.tensor
            def _(tn):
                for u in range(N_HU):
                    u4 = u % 4
                    tn.wait_ge(ssc.h, M["reord", u])
                    if u % 4 == 0 and u >= 4:
                        tn.wait_ge(sev.h, M["resid", u // 4 - 1, 15, 1])
                    def emit_r2(b):
                        tn.wait_ge(sev.h, M["r1ev", u, b])
                        if b >= 2:
                            tn.wait_ge(ssc.h, M["yev", u, b - 2])
                        yB = banks[2 + (b % 2)]
                        smm.inc(tn.matmul(yB[:, :], h64b[:, :], zr[:, (b % 4) * 512:(b % 4) * 512 + 512],
                                          start=True, stop=True))

                    for b in range(8):
                        if b >= 2:
                            tn.wait_ge(sev.h, M["r1ev", u, b - 2])
                        bank = banks[b % 2]
                        for wsub in range(4):
                            w = 4 * b + wsub
                            mm = tn.matmul(bank[0:128, wsub * 128:(wsub + 1) * 128],
                                           ewin[:, w * 128:(w + 1) * 128], h128[:, :],
                                           start=True, stop=True)
                        smm.inc(mm)
                        if b >= 2:
                            emit_r2(b - 2)
                    for b in [6, 7]:
                        emit_r2(b)
                    tn.wait_ge(sev.h, M["mx1", u])
                    smm.inc(tn.matmul(pb6[0:32, 0:128], colp[:, 0:32], i128f[:, :], start=True, stop=True))
                    tn.wait_ge(sev.h, M["chain", u])
                    smm.inc(tn.matmul(pb6[0:2, 128:160], scr2[0:32, 646:648], i128f[0:32, 0:32],
                                      start=True, stop=True))
                    tn.wait_ge(sev.h, M["sTev", u])
                    smm.inc(tn.matmul(pb6[0:128, 160:192], sel2[:, :], srow[:, 0:32],
                                      start=True, stop=True))
                    for b in range(8):
                        tn.wait_ge(sev.h, M["q", u, 1 if b >= 4 else 0])
                        if b >= 2:
                            tn.wait_ge(ssc.h, M["r3ev", u, b - 2])
                        bank = banks[4 + (b % 2)]
                        for ths in range(4):
                            th = 4 * b + ths
                            mm = tn.matmul(bank[0:128, ths * 128:(ths + 1) * 128],
                                           bfSM[:, th * 128:(th + 1) * 128], i128[:, :],
                                           start=True, stop=True)
                        smm.inc(mm)
                    if u % 4 == 3:
                        g4 = u // 4
                        tn.wait_ge(sev.h, M["chain", u])
                        tn.wait_ge(ssc.h, M["r3ev", u, 7])
                        for tb in range(2):
                            for k in range(4):
                                usub, tlo = k // 2, k % 2
                                uu = 4 * g4 + 2 * tb + usub
                                mm = tn.matmul(pb4[0:128, 500 + tb:501 + tb],
                                               p4[:, (2 * usub + tlo) * 128:(2 * usub + tlo) * 128 + 128],
                                               scr2[0:32, 700 + 2 * uu + tlo:701 + 2 * uu + tlo],
                                               start=(k == 0), stop=(k == 3))
                            smm.inc(mm)
                        for o in range(16):
                            tn.wait_ge(sw.h, B_w2 + 128 * (16 * g4 + o + 1))
                            dB = banks[6 + (o % 2)]
                            if o >= 2:
                                tn.wait_ge(ssc.h, M["dnev", g4, o - 2])
                            for i1p in range(64):
                                r, iloc = i1p >> 3, i1p & 7
                                lt = bfW[:, r * 1024 + iloc * 128:r * 1024 + iloc * 128 + 128]
                                rhs = bfBIG.rearrange("p (uu i tl) -> p uu i tl",
                                                      i=64, tl=64)[:, :, i1p, :]
                                mm = tn.matmul(dB[:, 0:256], lt, rhs,
                                               start=(i1p == 0), stop=(i1p == 63))
                            smm.inc(mm)
                            tn.wait_ge(ssc.h, M["dnev", g4, o])
                            if (g4, o) > (0, 1):
                                po = (g4, o - 2) if o >= 2 else (g4 - 1, 14 + o)
                                tn.wait_ge(sev.h, M["resid", po[0], po[1], 1])
                            for tb in range(2):
                                mm = tn.matmul(banks[2 * tb + (o % 2)][:, 0:128],
                                               scr2[:, (o % 2) * 256 + tb * 128:(o % 2) * 256 + tb * 128 + 128],
                                               i128f[:, :], start=True, stop=True)
                                smm.inc(mm)

            @block.vector
            def _(v):
                for u in range(N_HU):
                    for b in range(8):
                        v.wait_ge(smm.h, M["r1", u, b])
                        if b >= 4:
                            v.wait_ge(smm.h, M["r2", u, b - 4])
                        bank = banks[b % 2]
                        sev.inc(v.tensor_copy(zr[:, (b % 4) * 512:(b % 4) * 512 + 512], bank[:, :]))
                    v.wait_ge(ssc.h, M["yev", u, 7])
                    sev.inc(v.tensor_reduce(colp[:, 0:32].rearrange("p (a o) -> p a o", o=1),
                                            eq.rearrange("p (a c) -> p a c", c=128), AX,
                                            ALU.max, apply_absolute_value=True))
                    v.wait_ge(smm.h, M["mx1T", u])
                    sev.inc(v.tensor_copy(scr2[0:32, 512:640], pb6[0:32, 0:128]))
                    v.wait_ge(sev.h, sev.n)
                    VS(v, v.tensor_reduce(scr2[0:32, 640:642].rearrange("p (a o) -> p a o", o=1),
                                          scr2[0:32, 512:640].rearrange("p (a c) -> p a c", c=64), AX,
                                          ALU.max, apply_absolute_value=False))
                    VS(v, v.tensor_scalar_max(scr2[0:32, 642:644], scr2[0:32, 640:642], 1e-5))
                    VS(v, v.reciprocal(scr2[0:32, 644:646], scr2[0:32, 642:644]))
                    VS(v, v.tensor_scalar_mul(scr2[0:32, 646:648], scr2[0:32, 644:646], 128.0))
                    VS(v, v.reciprocal(scr2[0:32, 648:650], scr2[0:32, 646:648]))
                    u4 = u % 4
                    sev.inc(v.tensor_copy(scr2[0:32, 700 + 2 * u:702 + 2 * u], scr2[0:32, 648:650]))
                    v.wait_ge(sev.h, sev.n)
                    v.wait_ge(smm.h, M["sT", u])
                    sev.inc(v.tensor_copy(srow[0:2, 0:32], pb6[0:2, 128:160]))
                    v.wait_ge(sev.h, sev.n)
                    v.wait_ge(smm.h, M["ssm", u])
                    sev.inc(v.tensor_copy(scr2[:, 1024:1056], pb6[:, 160:192]))
                    v.wait_ge(sev.h, sev.n)
                    for th in range(32):
                        sl = ewin[:, th * 128:(th + 1) * 128]
                        mmv = v.tensor_scalar_mul(sl, eq[:, th * 128:(th + 1) * 128],
                                                  scr2[:, 1024 + th:1025 + th])
                    sev.inc(mmv)
                    for half in range(2):
                        hs = slice(half * 2048, (half + 1) * 2048)
                        sev.inc(v.tensor_scalar_add(scr[:, 0:2048], ewin[:, hs], 256.0))
                        sev.inc(v.tensor_copy(iscr[:, 0:2048], scr[:, 0:2048]))
                        sev.inc(v.tensor_tensor(scr[:, 2048:4096], scr[:, 0:2048],
                                                iscr[:, 0:2048], ALU.subtract))
                        sev.inc(v.tensor_single_scalar(ewin[:, hs], scr[:, 2048:4096], 0.5,
                                                       ALU.is_ge))
                        sev.inc(v.tensor_tensor(ewin[:, hs], ewin[:, hs], iscr[:, 0:2048],
                                                ALU.add))
                        sev.inc(v.tensor_scalar(bfSM[:, hs], ewin[:, hs], -256.0, 127.0,
                                                ALU.add, ALU.min))
                    if u % 4 == 3:
                        g4 = u // 4
                        for tb in range(2):
                            v.wait_ge(smm.h, M["dqc", g4, tb])
                            sev.inc(v.tensor_copy(colp[:, 24 + tb:25 + tb], pb4[:, 500 + tb:501 + tb]))
                            v.wait_ge(sev.h, sev.n)
                        for o in range(16):
                            for tb in range(2):
                                v.wait_ge(smm.h, M["dntm", g4, o, tb])
                                tmp = scr2[:, 1088:1216]
                                v.tensor_scalar(tmp, banks[2 * tb + (o % 2)][:, 0:128],
                                                colp[:, 24 + tb:25 + tb],
                                                colp[:, 46:47], ALU.mult, ALU.mult)
                                xslice = xres[:, (2 * g4 + tb) * D + o * 128:(2 * g4 + tb) * D + o * 128 + 128]
                                sev.inc(v.scalar_tensor_tensor(xslice, tmp, 0.5, xslice,
                                                               ALU.mult, ALU.add))

            M_end, B_sc_end, B_ev_end, B_mm_end, B_gp_end = _sched_B(B_sc, B_ev, B_mm, B_gp)
            assert B_sc_end == ssc.n and B_ev_end == sev.n and B_mm_end == smm.n and B_gp_end == sgp.n, (
                "stage B schedule mismatch",
                (B_sc_end, ssc.n), (B_ev_end, sev.n), (B_mm_end, smm.n), (B_gp_end, sgp.n))
            return M["resid", 3, 15, 1]

        def attn_call(wait_ev):
            CALL_mm0 = smm.n
            llv = _ll()
            lamC_f = [float(np.float32(np.exp(llv[h] * CH))) for h in range(HEADS)]
            sScl_f = [float(np.float32(np.exp(llv[h] * 64.0))) for h in range(HEADS)]
            grp = bfBIG.rearrange("p (a c) -> p a c", c=512)
            wsv = bfW.rearrange("p (hh c) -> p hh c", c=512)
            wfv = bfW.rearrange("p (hh c) -> p hh c", c=128)
            xlv = bfBIG.rearrange("p (hh c) -> p hh c", c=512)
            smv = bfSM.rearrange("p (a c) -> p a c", c=512)
            wsrc = {"k": wk_d, "v": wv_d, "q": wq_d}
            cach = {"k": kcache, "v": vcache, "q": qcache}
            B = {}

            def bar(tag):
                B[tag] = (smm.n, sev.n, ssc.n, sgp.n, sw.n, sld.n, sst.n, sfb.n)

            _ckn = [0]

            def ck():
                _ckn[0] += 1
                if ATTN_STAGES is not None and _ckn[0] >= ATTN_STAGES:
                    raise _StopEmit(sev.n)

            @block.sync
            def _(sync):
                sync.wait_ge(sld.h, sld.n)
                sld.incd(sync.dma_start(out=nwrep[:, :], in_=tdkrep[:, :]))
            A0_ld = sld.n

            for g in range(2):
                L_ev, L_sc, L_mm = sev.n, ssc.n, smm.n

                @block.scalar
                def _(s, g=g, L_ev=L_ev, L_mm=L_mm):
                    for t4 in range(4):
                        t = g * 4 + t4
                        xt = xres[:, t * D:(t + 1) * D]
                        if g == 0 and t4 == 0:
                            s.wait_ge(sev.h, wait_ev)
                        if g == 1 and t4 == 0:
                            s.wait_ge(smm.h, B["gate", 0][0])
                            s.wait_ge(sev.h, B[("state", 0)][1])
                        ssc.inc(s.activation(scrC[:, 0:2048], xt, AF.Square))
                        s.wait_ge(sev.h, L_ev + 2 * t4 + 1)
                        ssc.inc(s.activation(colp[:, 8:9], colp[:, 6:7], AF.Sqrt))
                        s.wait_ge(ssc.h, ssc.n)
                        s.wait_ge(sev.h, L_ev + 2 * t4 + 2)
                        s.activation(scrC[:, 0:2048], xt, AF.Identity, bias=colp[:, 7:8])
                        ssc.inc(s.activation(bfSM[:, 0:2048], scrC[:, 0:2048], AF.Identity,
                                             scale=colp[:, 15:16]))
                        for b in range(4):
                            gb = 4 * t4 + b
                            s.wait_ge(smm.h, L_mm + gb + 1)
                            bank = banks[4 + (gb % 2)]
                            dst = xlv[:, 4 * b:4 * b + 4, t4 * 128:t4 * 128 + 128]
                            ssc.inc(s.activation(dst, bank.rearrange("p (h c) -> p h c", c=128),
                                                 AF.Identity))

                @block.vector
                def _(v, g=g, L_sc=L_sc):
                    for t4 in range(4):
                        t = g * 4 + t4
                        xt = xres[:, t * D:(t + 1) * D]
                        v.wait_ge(ssc.h, L_sc + 7 * t4 + 1)
                        VS(v, v.tensor_reduce(colp[:, 0:1], xt, AX, ALU.add))
                        VS(v, v.tensor_reduce(colp[:, 1:2], scrC[:, 0:2048], AX, ALU.add))
                        VS(v, v.tensor_scalar_mul(colp[:, 2:3], colp[:, 0:1], 1.0 / D))
                        VS(v, v.tensor_scalar_mul(colp[:, 3:4], colp[:, 1:2], 1.0 / D))
                        VS(v, v.tensor_tensor(colp[:, 4:5], colp[:, 2:3], colp[:, 2:3], ALU.mult))
                        VS(v, v.scalar_tensor_tensor(colp[:, 5:6], colp[:, 3:4], 1e-5, colp[:, 4:5],
                                                     ALU.add, ALU.subtract))
                        VS(v, v.reciprocal(colp[:, 6:7], colp[:, 5:6]))
                        sev.inc(v.tensor_scalar_mul(colp[:, 7:8], colp[:, 2:3], -1.0))
                        v.wait_ge(sev.h, sev.n)
                        v.wait_ge(ssc.h, L_sc + 7 * t4 + 2)
                        VS(v, v.tensor_tensor(colp[:, 14:15], colp[:, 8:9], colp[:, 8:9], ALU.mult))
                        VS(v, v.tensor_tensor(colp[:, 14:15], colp[:, 14:15], colp[:, 5:6], ALU.mult))
                        VS(v, v.tensor_scalar(colp[:, 14:15], colp[:, 14:15], -0.5, 1.5, ALU.mult, ALU.add))
                        sev.inc(v.tensor_tensor(colp[:, 15:16], colp[:, 8:9], colp[:, 14:15], ALU.mult))
                        v.wait_ge(sev.h, sev.n)

                @block.tensor
                def _(tn, g=g, L_sc=L_sc):
                    for t4 in range(4):
                        tn.wait_ge(ssc.h, L_sc + 7 * t4 + 3)
                        for h in range(16):
                            gb = 4 * t4 + h // 4
                            bank = banks[4 + (gb % 2)]
                            if h % 4 == 0 and gb >= 2:
                                pgb = gb - 2
                                tn.wait_ge(ssc.h, L_sc + 7 * (pgb // 4) + 3 + (pgb % 4) + 1)
                            mm = tn.matmul(bank[:, (h % 4) * 128:(h % 4) * 128 + 128],
                                           bfSM[:, h * 128:(h + 1) * 128], i128[:, :],
                                           start=True, stop=True)
                            if h % 4 == 3:
                                smm.inc(mm)
                bar(("ln", g))
                ck()

                def proj(pj, act, extra_st_wait=None, g=g):
                    P_g, P_mm, P_sc = sgst.n, smm.n, ssc.n

                    @block.gpsimd
                    def _(g_, pj=pj, P_mm=P_mm):
                        g_.wait_ge(smm.h, P_mm)
                        for ob in range(4):
                            if ob >= 1:
                                g_.wait_ge(smm.h, P_mm + 4 * ob)
                            src = wsrc[pj].rearrange("(hh p) o -> p hh o", p=128)[:, :, ob * 512:(ob + 1) * 512]
                            sgst.incd(g_.dma_start(out=wsv[:, 0:16, :], in_=src))

                    @block.tensor
                    def _(tn, P_g=P_g, P_sc=P_sc, extra_st_wait=extra_st_wait):
                        tn.wait_ge(ssc.h, P_sc)
                        if extra_st_wait is not None:
                            tn.wait_ge(sst.h, extra_st_wait)
                        for ob in range(4):
                            tn.wait_ge(sgst.h, P_g + 16 * (ob + 1))
                            for t4 in range(4):
                                idx = 4 * ob + t4
                                bank = banks[4 + (idx % 2)]
                                if idx >= 2:
                                    tn.wait_ge(ssc.h, P_sc + idx - 1)
                                for h in range(16):
                                    mm = tn.matmul(bank[:, :], xlv[:, h, t4 * 128:t4 * 128 + 128],
                                                   wsv[:, h, :], start=(h == 0), stop=(h == 15))
                                smm.inc(mm)

                    @block.scalar
                    def _(s, P_mm=P_mm, act=act):
                        for ob in range(4):
                            for t4 in range(4):
                                idx = 4 * ob + t4
                                s.wait_ge(smm.h, P_mm + idx + 1)
                                bank = banks[4 + (idx % 2)]
                                ssc.inc(s.activation(grp[:, 16 + t4 * 4 + ob, :], bank[:, :], act))
                    bar(("proj", g, pj))

                def rot(pj, write_kd, g=g):
                    R_ld, R_ev, R_st = sld.n, sev.n, sst.n

                    def stores(sync, t4, pj, write_kd, g):
                        t = g * 4 + t4
                        kt = bfBIG[:, 8192 + t4 * 2048:8192 + (t4 + 1) * 2048]
                        sst.incd(sync.dma_start(
                            out=cach[pj].rearrange("(a p) d -> a p d", p=128)[t], in_=kt))
                        if write_kd:
                            kd = bfSM[:, (t4 % 2) * 2048:(t4 % 2) * 2048 + 2048]
                            sst.incd(sync.dma_start(
                                out=kdcache.rearrange("(a p) d -> a p d", p=128)[t], in_=kd))

                    @block.sync
                    def _(sync, pj=pj, R_ev=R_ev, g=g, write_kd=write_kd):
                        sync.wait_ge(ssc.h, B[("proj", g, pj)][2])
                        for t4 in range(4):
                            t = g * 4 + t4
                            if t4 >= 1:
                                sync.wait_ge(sev.h, R_ev + 4 * t4)
                                stores(sync, t4 - 1, pj, write_kd, g)
                            sld.incd(sync.dma_start(out=scr[:, 0:1024],
                                                    in_=tcos.rearrange("(a p) j -> a p j", p=128)[t]))
                            sld.incd(sync.dma_start(out=scr[:, 1024:2048],
                                                    in_=tsin.rearrange("(a p) j -> a p j", p=128)[t]))
                        sync.wait_ge(sev.h, R_ev + 16)
                        stores(sync, 3, pj, write_kd, g)

                    @block.vector
                    def _(v, R_ld=R_ld, R_st=R_st, g=g, write_kd=write_kd):
                        v.wait_ge(smm.h, B[("ln", g)][0])
                        for t4 in range(4):
                            v.wait_ge(sld.h, R_ld + 32 * (t4 + 1))
                            kt = bfBIG[:, 8192 + t4 * 2048:8192 + (t4 + 1) * 2048]
                            ktv = kt.rearrange("p (h two j) -> p h two j", two=2, j=64)
                            x1, x2 = ktv[:, :, 0, :], ktv[:, :, 1, :]
                            cosr = scr[:, 0:1024].rearrange("p (h j) -> p h j", j=64)
                            sinr = scr[:, 1024:2048].rearrange("p (h j) -> p h j", j=64)
                            ra = scr2[:, 0:1024].rearrange("p (h j) -> p h j", j=64)
                            rb = scr2[:, 1024:2048].rearrange("p (h j) -> p h j", j=64)
                            rc = scrC[:, 0:1024].rearrange("p (h j) -> p h j", j=64)
                            rd = scrC[:, 1024:2048].rearrange("p (h j) -> p h j", j=64)
                            v.tensor_tensor(ra, x1, cosr, ALU.mult)
                            v.tensor_tensor(rb, x2, sinr, ALU.mult)
                            v.tensor_tensor(rc, x1, sinr, ALU.mult)
                            sev.inc(v.tensor_tensor(rd, x2, cosr, ALU.mult))
                            v.wait_ge(sev.h, sev.n)
                            sev.inc(v.tensor_tensor(x1, ra, rb, ALU.subtract))
                            v.wait_ge(sev.h, sev.n)
                            sev.inc(v.tensor_tensor(x2, rc, rd, ALU.add))
                            v.wait_ge(sev.h, sev.n)
                            if write_kd:
                                if t4 >= 2:
                                    v.wait_ge(sst.h, R_st + 2 * (t4 - 1))
                                kdd = bfSM[:, (t4 % 2) * 2048:(t4 % 2) * 2048 + 2048]
                                sev.inc(v.tensor_tensor(kdd, kt, nwrep[:, :], ALU.mult))
                                v.wait_ge(sev.h, sev.n)
                            else:
                                sev.inc(v.memset(colp[:, 33:34], 0.0))
                    bar(("rot", g, pj))

                proj("k", AF.Silu)
                ck()
                rot("k", True)
                ck()
                proj("v", AF.Identity, extra_st_wait=B[("rot", g, "k")][6])
                ck()

                F_fb, S_mm, S_ev = sfb.n, smm.n, sev.n

                @block.sync
                def _(sync, g=g, F_fb=F_fb, S_mm=S_mm):
                    sync.wait_ge(sst.h, B[("rot", g, "k")][6])
                    for t4 in range(4):
                        t = g * 4 + t4
                        if t4 >= 2:
                            sync.wait_ge(smm.h, S_mm + 16 * (t4 - 1))
                        dst = bfSM[:, (t4 % 2) * 2048:(t4 % 2) * 2048 + 2048]
                        sfb.incd(sync.dma_start(
                            out=dst, in_=kdcache.rearrange("(a p) d -> a p d", p=128)[t]))

                S_sc = ssc.n

                @block.tensor
                def _(tn, g=g, F_fb=F_fb, S_sc=S_sc):
                    for t4 in range(4):
                        tn.wait_ge(sfb.h, F_fb + 16 * (t4 + 1))
                        for h in range(16):
                            s_ = 16 * t4 + h
                            if s_ >= 4:
                                tn.wait_ge(ssc.h, S_sc + s_ - 3)
                            kdd = bfSM[:, (t4 % 2) * 2048 + h * 128:(t4 % 2) * 2048 + h * 128 + 128]
                            vt = bfBIG[:, 8192 + t4 * 2048 + h * 128:8192 + t4 * 2048 + h * 128 + 128]
                            smm.inc(tn.matmul(banks[h % 4][:, (h // 4) * 128:(h // 4) * 128 + 128],
                                              kdd, vt, start=True, stop=True))

                @block.scalar
                def _(s, g=g, S_mm=S_mm, S_ev=S_ev):
                    for t4 in range(4):
                        for h in range(16):
                            s.wait_ge(smm.h, S_mm + 16 * t4 + h + 1)
                            if t4 >= 1:
                                s.wait_ge(sev.h, S_ev + 16 * (t4 - 1) + h + 1)
                            bk = banks[h % 4][:, (h // 4) * 128:(h // 4) * 128 + 128]
                            ssc.inc(s.activation(scrC[:, h * 128:(h + 1) * 128], bk, AF.Identity))

                @block.vector
                def _(v, g=g, S_sc=S_sc):
                    v.wait_ge(ssc.h, B[("proj", g, "v")][2])
                    for t4 in range(4):
                        c = g * 4 + t4
                        for h in range(16):
                            v.wait_ge(ssc.h, S_sc + 16 * t4 + h + 1)
                            dst = f32P1[:, 2048 + h * 128:2048 + (h + 1) * 128]
                            sa = scrC[:, h * 128:(h + 1) * 128]
                            if c == 0:
                                sev.inc(v.tensor_copy(dst, sa))
                            else:
                                sev.inc(v.scalar_tensor_tensor(dst, dst, lamC_f[h], sa,
                                                               ALU.mult, ALU.add))
                            v.wait_ge(sev.h, sev.n)
                bar(("state", g))
                ck()

                @block.sync
                def _(sync, g=g):
                    sync.wait_ge(smm.h, B[("state", g)][0])
                    for t4 in range(4):
                        t = g * 4 + t4
                        vt = bfBIG[:, 8192 + t4 * 2048:8192 + (t4 + 1) * 2048]
                        sst.incd(sync.dma_start(
                            out=vcache.rearrange("(a p) d -> a p d", p=128)[t], in_=vt))
                bar(("vst", g))
                ck()

                proj("q", AF.Silu, extra_st_wait=B[("vst", g)][6])
                ck()
                rot("q", False)
                ck()

                G_g, G_mm, G_sc = sgst.n, smm.n, ssc.n

                @block.gpsimd
                def _(g_, g=g):
                    g_.wait_ge(smm.h, B[("proj", g, "q")][0])
                    sgst.incd(g_.dma_start(out=wfv[:, 0:16, :],
                                           in_=wg1_d.rearrange("(hh p) o -> p hh o", p=128)))
                    sgst.incd(g_.dma_start(out=bfSM.rearrange("p (a c) -> p a c", c=2048)[:, 1, :],
                                           in_=wg2_d[:, :]))

                @block.tensor
                def _(tn, G_g=G_g, G_sc=G_sc):
                    tn.wait_ge(sgst.h, G_g + 16)
                    tn.wait_ge(ssc.h, G_sc)
                    for h in range(16):
                        mm = tn.matmul(pb6[:, 0:512], wfv[:, h, :],
                                       xlv[:, h, 0:512], start=(h == 0), stop=(h == 15))
                    smm.inc(mm)
                    tn.wait_ge(sgst.h, G_g + 32)
                    tn.wait_ge(ssc.h, G_sc + 1)
                    for t4 in range(4):
                        for ob in range(4):
                            idx = 4 * t4 + ob
                            bank = banks[4 + (idx % 2)]
                            if idx >= 2:
                                tn.wait_ge(ssc.h, G_sc + idx)
                            smm.inc(tn.matmul(bank[:, :], bfSM[:, t4 * 128:t4 * 128 + 128],
                                              smv[:, 4 + ob, :], start=True, stop=True))

                @block.scalar
                def _(s, g=g, G_mm=G_mm):
                    s.wait_ge(sev.h, B[("rot", g, "q")][1])
                    s.wait_ge(smm.h, G_mm + 1)
                    ssc.inc(s.activation(bfSM[:, 0:512], pb6[:, 0:512], AF.Identity))
                    for t4 in range(4):
                        t = g * 4 + t4
                        for ob in range(4):
                            idx = 4 * t4 + ob
                            s.wait_ge(smm.h, G_mm + 1 + idx + 1)
                            bank = banks[4 + (idx % 2)]
                            ssc.inc(s.activation(scr2[:, ob * 512:(ob + 1) * 512], bank[:, :], AF.Sigmoid))
                        s.wait_ge(sst.h, sst.n)
                        sst.incd(s.dma_start(out=gcache.rearrange("(a p) d -> a p d", p=128)[t],
                                             in_=scr2[:, 0:2048]))
                        s.wait_ge(sst.h, sst.n)
                bar(("gate", g))
                ck()

            @block.sync
            def _(sync):
                sync.wait_ge(sev.h, B[("state", 1)][1])
                sync.wait_ge(sst.h, sst.n)
                for b in range(4):
                    dst = ag_st_in.rearrange("(hh pp) c -> pp hh c", pp=128)[:, 4 * b:4 * b + 4, :]
                    srcv = f32P1.rearrange("p (a c) -> p a c", c=128)[:, 16 + 4 * b:16 + 4 * b + 4, :]
                    sst.incd(sync.dma_start(out=dst, in_=srcv))
            E_st2 = sst.n

            @block.gpsimd
            def _(g_):
                g_.wait_ge(sst.h, E_st2)
                scc.inc(g_.collective_compute("AllGather", ALU.bypass, replica_groups=replica,
                                              ins=[ag_st_in.ap().opt()], outs=[ag_st_out.ap().opt()]))
            AT_CC = scc.n

            C_ld, C_ev = sld.n, sev.n

            @block.sync
            def _(sync):
                sync.wait_ge(scc.h, AT_CC)
                for cp in range(NCORES):
                    if cp >= 1:
                        sync.wait_ge(sev.h, C_ev + 1 + 16 * cp)
                    src = ag_st_out.rearrange("(cc hh pp) c -> cc pp hh c", pp=128, hh=HEADS)[cp]
                    sld.incd(sync.dma_start(out=scrC.rearrange("p (hh c) -> p hh c", c=128), in_=src))

            @block.vector
            def _(v):
                v.wait_ge(scc.h, AT_CC)
                sev.inc(v.memset(f32P1[:, 2048:4096], 0.0))
                for cp in range(NCORES):
                    v.wait_ge(sld.h, C_ld + 16 * (cp + 1))
                    for h in range(16):
                        sev.inc(v.scalar_tensor_tensor(
                            f32P1[:, 2048 + h * 128:2048 + h * 128 + 128],
                            scrC[:, h * 128:(h + 1) * 128], lamt[:, cp * 16 + h:cp * 16 + h + 1],
                            f32P1[:, 2048 + h * 128:2048 + h * 128 + 128], ALU.mult, ALU.add))
                    v.wait_ge(sev.h, sev.n)
            bar("sinit")
            ck()

            @block.sync
            def _(sync):
                sync.wait_ge(sld.h, sld.n)
                sld.incd(sync.dma_start(
                    out=f32P1.rearrange("p (a c) -> p a c", c=128)[:, 32:48, :],
                    in_=tdmask.rearrange("(hh s) t -> s hh t", s=128)))
                sync.wait_ge(sld.h, sld.n)
                sld.incd(sync.dma_start(out=nwrep[:, :], in_=tnwrep[:, :]))
            P2_ld0 = sld.n

            prev_ev = [None]
            for c in range(8):
                CL_ld = sfb.n

                @block.sync
                def _(sync, c=c, CL_ld=CL_ld):
                    if c == 0:
                        sync.wait_ge(smm.h, B[("gate", 1)][0])
                        sync.wait_ge(sld.h, P2_ld0)
                    else:
                        sync.wait_ge(smm.h, B[("scan", c - 1)][0])
                    for j, nm in enumerate(["q", "k", "kd", "v"]):
                        srcd = kdcache if nm == "kd" else cach[nm]
                        sfb.incd(sync.dma_start(out=bfBIG[:, j * 2048:(j + 1) * 2048],
                                                in_=srcd.rearrange("(a p) d -> a p d", p=128)[c]))

                H_mm, H_sc, H_ev = smm.n, ssc.n, sev.n
                P_EV = prev_ev[0]

                @block.tensor
                def _(tn, c=c, CL_ld=CL_ld, H_mm=H_mm, H_sc=H_sc, H_ev=H_ev):
                    tn.wait_ge(sfb.h, CL_ld + 64)
                    if c != 0:
                        tn.wait_ge(sev.h, B[("scan", c - 1)][1])
                        tn.wait_ge(ssc.h, B[("scan", c - 1)][2])
                    for h in range(16):
                        pq = pb4 if h % 2 == 0 else pb5
                        po = pb2 if h % 2 == 0 else pb3
                        scb = pb6 if h % 2 == 0 else pb7
                        if h >= 2:
                            tn.wait_ge(ssc.h, H_sc + 3 * (h - 2) + 1)
                        qs = bfBIG[:, h * 128:h * 128 + 128]
                        ks = bfBIG[:, 2048 + h * 128:2048 + h * 128 + 128]
                        tn.matmul(pq[:, 0:128], qs, i128[:, :], start=True, stop=True)
                        smm.inc(tn.matmul(pq[:, 128:256], ks, i128[:, :], start=True, stop=True))
                        tn.wait_ge(ssc.h, H_sc + 3 * h + 1)
                        if h >= 2:
                            tn.wait_ge(sev.h, H_ev + 2 * (h - 2) + 1)
                        smm.inc(tn.matmul(scb[:, 0:128],
                                          bfSM[:, 256:384], bfSM[:, 128:256], start=True, stop=True))
                        tn.wait_ge(ssc.h, H_sc + 3 * h + 2)
                        if h >= 2:
                            tn.wait_ge(ssc.h, H_sc + 3 * (h - 2) + 3)
                        tn.matmul(po[:, 0:128],
                                  bfSM[:, 128:256], bfSM[:, 384:512], start=True, stop=False)
                        tn.wait_ge(sev.h, H_ev + 2 * h + 1)
                        vt = bfBIG[:, 6144 + h * 128:6144 + h * 128 + 128]
                        smm.inc(tn.matmul(po[:, 0:128],
                                          bfSM[:, 0:128], vt, start=False, stop=True))
                        kdd = bfBIG[:, 4096 + h * 128:4096 + h * 128 + 128]
                        if h >= 2:
                            tn.wait_ge(sev.h, H_ev + 2 * (h - 2) + 2)
                        smm.inc(tn.matmul(banks[h % 2][:, 0:128], kdd, vt, start=True, stop=True))

                @block.scalar
                def _(s, c=c, H_mm=H_mm, P_EV=P_EV):
                    for h in range(16):
                        pq = pb4 if h % 2 == 0 else pb5
                        po = pb2 if h % 2 == 0 else pb3
                        if h == 0 and c >= 1:
                            s.wait_ge(sst.h, B[("sc2", c - 1)][6])
                        s.wait_ge(smm.h, H_mm + 4 * h + 1)
                        if h >= 1:
                            s.wait_ge(smm.h, H_mm + 4 * (h - 1) + 3)
                        s.activation(bfSM[:, 128:256], pq[:, 0:128], AF.Identity)
                        ssc.inc(s.activation(bfSM[:, 256:384], pq[:, 128:256], AF.Identity))
                        if c == 0:
                            s.wait_ge(sev.h, B["sinit"][1])
                        else:
                            s.wait_ge(sev.h, P_EV + 2 * h + 2)
                        ssc.inc(s.activation(bfSM[:, 384:512],
                                             f32P1[:, 2048 + h * 128:2048 + (h + 1) * 128],
                                             AF.Identity, scale=sScl_f[h]))
                        s.wait_ge(smm.h, H_mm + 4 * h + 3)
                        ssc.inc(s.activation(f32P1[:, h * 128:(h + 1) * 128],
                                             po[:, 0:128],
                                             AF.Identity, scale=dqc[:, h:h + 1]))

                @block.vector
                def _(v, c=c, H_mm=H_mm, H_sc=H_sc):
                    if c == 0:
                        v.wait_ge(sld.h, P2_ld0)
                    for h in range(16):
                        v.wait_ge(smm.h, H_mm + 4 * h + 2)
                        dm = f32P1[:, 4096 + h * 128:4096 + h * 128 + 128]
                        scb = pb6 if h % 2 == 0 else pb7
                        sev.inc(v.tensor_tensor(bfSM[:, 0:128], scb[:, 0:128], dm, ALU.mult))
                        v.wait_ge(sev.h, sev.n)
                        v.wait_ge(smm.h, H_mm + 4 * h + 4)
                        v.wait_ge(ssc.h, H_sc + 3 * h + 2)
                        dst = f32P1[:, 2048 + h * 128:2048 + (h + 1) * 128]
                        sev.inc(v.scalar_tensor_tensor(
                            dst, dst, lamC_f[h], banks[h % 2][:, 0:128],
                            ALU.mult, ALU.add))
                        v.wait_ge(sev.h, sev.n)
                bar(("scan", c))
                prev_ev[0] = H_ev

                GA_ld, GA_ev, GA_sc = sld.n, sev.n, ssc.n

                @block.sync
                def _(sync, c=c):
                    sync.wait_ge(sld.h, sld.n)
                    sld.incd(sync.dma_start(out=scr2[:, 0:2048],
                                            in_=gcache.rearrange("(a p) d -> a p d", p=128)[c]))

                @block.vector
                def _(v, c=c, GA_ld=GA_ld, GA_sc=GA_sc):
                    v.wait_ge(sld.h, GA_ld + 16)
                    v.wait_ge(smm.h, B[("scan", c)][0])
                    v.wait_ge(ssc.h, B[("scan", c)][2])
                    sev.inc(v.tensor_tensor(scr[:, 0:2048], f32P1[:, 0:2048], scr2[:, 0:2048], ALU.mult))
                    v.wait_ge(sev.h, sev.n)
                    v.wait_ge(ssc.h, GA_sc + 1)
                    VS(v, v.tensor_reduce(colp[:, 0:16].rearrange("p (a o) -> p a o", o=1),
                                          scrC.rearrange("p (a c) -> p a c", c=128), AX, ALU.add))
                    VS(v, v.tensor_scalar(colp[:, 16:32], colp[:, 0:16], 1.0 / 128.0, 1e-6,
                                          ALU.mult, ALU.add))
                    sev.inc(v.reciprocal(colp[:, 48:64], colp[:, 16:32]))
                    v.wait_ge(sev.h, sev.n)
                    v.wait_ge(ssc.h, GA_sc + 2)
                    VS(v, v.tensor_tensor(colp[:, 0:16], colp[:, 48:64], colp[:, 48:64], ALU.mult))
                    VS(v, v.tensor_tensor(colp[:, 0:16], colp[:, 0:16], colp[:, 16:32], ALU.mult))
                    VS(v, v.tensor_scalar(colp[:, 0:16], colp[:, 0:16], -0.5, 1.5, ALU.mult, ALU.add))
                    VS(v, v.tensor_tensor(colp[:, 48:64], colp[:, 48:64], colp[:, 0:16], ALU.mult))
                    for h in range(16):
                        v.tensor_scalar_mul(scr.rearrange("p (a cc) -> p a cc", cc=128)[:, h, :],
                                            scr.rearrange("p (a cc) -> p a cc", cc=128)[:, h, :],
                                            colp[:, 48 + h:49 + h])
                    v.wait_ge(svs.h, svs.n)
                    sev.inc(v.tensor_tensor(bfSM[:, 0:2048], scr[:, 0:2048], nwrep[:, :], ALU.mult))
                    v.wait_ge(sev.h, sev.n)

                @block.scalar
                def _(s, GA_ev=GA_ev):
                    s.wait_ge(sev.h, GA_ev + 1)
                    ssc.inc(s.activation(scrC[:, 0:2048], scr[:, 0:2048], AF.Square))
                    s.wait_ge(sev.h, GA_ev + 2)
                    ssc.inc(s.activation(colp[:, 48:64], colp[:, 48:64], AF.Sqrt))
                    s.wait_ge(ssc.h, ssc.n)
                bar(("ga", c))

                @block.sync
                def _(sync, c=c):
                    sync.wait_ge(sev.h, B[("ga", c)][1])
                    sync.wait_ge(sst.h, sst.n)
                    sst.incd(sync.dma_start(out=oncache.rearrange("(a p) d -> a p d", p=128)[c],
                                            in_=bfSM[:, 0:2048]))
                bar(("sc2", c))
                ck()

            A11_mm0, A11_sc0, A11_ev0, A11_g0, A11_fb0 = smm.n, ssc.n, sev.n, sgst.n, sfb.n

            @block.gpsimd
            def _(g_):
                g_.wait_ge(smm.h, A11_mm0)
                for ob in range(4):
                    if ob >= 1:
                        g_.wait_ge(smm.h, A11_mm0 + 40 * ob)
                    src = wo_d.rearrange("(hh p) o -> p hh o", p=128)[:, :, ob * 512:(ob + 1) * 512]
                    sgst.incd(g_.dma_start(out=wsv[:, 0:16, :], in_=src))

            @block.sync
            def _(sync):
                sync.wait_ge(sst.h, B[("sc2", 7)][6])
                for ob in range(4):
                    for t in range(8):
                        it = 8 * ob + t
                        if it >= 2:
                            sync.wait_ge(smm.h, A11_mm0 + 5 * (it - 1))
                        dst = bfSM[:, (t % 2) * 2048:(t % 2) * 2048 + 2048]
                        sfb.incd(sync.dma_start(out=dst,
                                                in_=oncache.rearrange("(a p) d -> a p d", p=128)[t]))

            @block.tensor
            def _(tn):
                tn.wait_ge(ssc.h, A11_sc0)
                tn.wait_ge(sev.h, B[("sc2", 7)][1])
                for ob in range(4):
                    tn.wait_ge(sgst.h, A11_g0 + 16 * (ob + 1))
                    for t in range(8):
                        it = 8 * ob + t
                        tn.wait_ge(sfb.h, A11_fb0 + 16 * (it + 1))
                        ot = bfSM[:, (t % 2) * 2048:(t % 2) * 2048 + 2048]
                        if it >= 1:
                            tn.wait_ge(ssc.h, A11_sc0 + 4 * it)
                        for ec in range(16):
                            gb = ec // 4
                            bank = banks[4 + (gb % 2)]
                            if ec % 4 == 0 and gb >= 2:
                                tn.wait_ge(ssc.h, A11_sc0 + 4 * it + gb - 1)
                            mm = tn.matmul(bank[:, (ec % 4) * 128:(ec % 4) * 128 + 128],
                                           ot[:, ec * 128:ec * 128 + 128], i128[:, :],
                                           start=True, stop=True)
                            if ec % 4 == 3:
                                smm.inc(mm)
                        tn.wait_ge(ssc.h, A11_sc0 + 4 * it + 4)
                        if it >= 2:
                            tn.wait_ge(sev.h, A11_ev0 + it - 1)
                        accb = banks[6 + (it % 2)]
                        for ec in range(16):
                            mm = tn.matmul(accb[:, 0:512], bfBIG[:, ec * 128:ec * 128 + 128],
                                           wsv[:, ec, :], start=(ec == 0), stop=(ec == 15))
                        smm.inc(mm)

            @block.scalar
            def _(s):
                for ob in range(4):
                    for t in range(8):
                        it = 8 * ob + t
                        for gb in range(4):
                            s.wait_ge(smm.h, A11_mm0 + 5 * it + gb + 1)
                            if it >= 1 and gb == 0:
                                s.wait_ge(smm.h, A11_mm0 + 5 * it)
                            bank = banks[4 + (gb % 2)]
                            ssc.inc(s.activation(bfBIG[:, gb * 512:(gb + 1) * 512], bank[:, :],
                                                 AF.Identity))

            @block.vector
            def _(v):
                for ob in range(4):
                    for t in range(8):
                        it = 8 * ob + t
                        v.wait_ge(smm.h, A11_mm0 + 5 * it + 5)
                        accb = banks[6 + (it % 2)]
                        xsl = xres[:, t * D + ob * 512:t * D + ob * 512 + 512]
                        sev.inc(v.tensor_tensor(xsl, accb[:, 0:512], xsl, ALU.add))
            return A11_ev0 + 32

        last_ev = P0_EV
        cid = 0
        for p in parts:
            if p in ("ffn1", "ffn2"):
                last_ev = ffn_call(cid, last_ev)
                cid += 1
            elif p == "attn":
                try:
                    last_ev = attn_call(last_ev)
                except _StopEmit as e:
                    last_ev = e.ev
            else:
                raise ValueError(p)

        @block.sync
        def _(sync):
            sync.wait_ge(sev.h, last_ev)
            for t in range(NT):
                sst.incd(sync.dma_start(out=y_out.rearrange("(a p) d -> a p d", p=128)[t, :, :],
                                        in_=xres[:, t * D:(t + 1) * D]))
            if DEBUG_EPRIME:
                dbv = dbg_out.rearrange("(u p) c -> u p c", p=128)
                for u in range(N_HU):
                    sst.incd(sync.dma_start(out=dbv[u], in_=eprime[u].rearrange("p a b -> p (a b)")))
                sst.incd(sync.dma_start(out=dbg_sb[:, 0:8192], in_=f32P1[:, :]))
                sst.incd(sync.dma_start(out=dbg_sb[:, 8192:10240], in_=scr2[:, :]))
                sst.incd(sync.dma_start(out=dbg_sb[:, 10240:10304], in_=colp[:, :]))
                sst.incd(sync.dma_start(out=dbg_sb[:, 10304:10312], in_=adeq[:, :]))
                sst.incd(sync.dma_start(out=dbg_bf[:, :], in_=bfBIG[:, :]))
            sync.wait_ge(sst.h, sst.n)

    return nc


def kernel(**inputs):
    xs = np.split(np.ascontiguousarray(inputs["L0_attn_x"][0]), NCORES, axis=0)
    gs = np.split(np.ascontiguousarray(inputs["L0_ffn_w_gate"]), NCORES, axis=0)
    us = np.split(np.ascontiguousarray(inputs["L0_ffn_w_up"]), NCORES, axis=0)
    ds = np.split(np.ascontiguousarray(inputs["L0_ffn_w_down"]), NCORES, axis=1)
    nc = build_program(PARTS)
    in_maps = []
    for c in range(NCORES):
        tb = host_tables(c)
        in_maps.append({
            "x_in": xs[c],
            "wq": np.ascontiguousarray(inputs["L0_attn_Wq"]),
            "wk": np.ascontiguousarray(inputs["L0_attn_Wk"]),
            "wv": np.ascontiguousarray(inputs["L0_attn_Wv"]),
            "wo": np.ascontiguousarray(inputs["L0_attn_Wo"]),
            "wg1": np.ascontiguousarray(inputs["L0_attn_Wg1"]),
            "wg2": np.ascontiguousarray(inputs["L0_attn_Wg2"]),
            "nrmw": np.ascontiguousarray(inputs["L0_attn_norm_weight"]).reshape(1, D),
            "nrmw_rep": np.tile(np.ascontiguousarray(inputs["L0_attn_norm_weight"]).reshape(1, D), (128, 1)),
            "wg_sl": gs[c], "wu_sl": us[c], "wd_sl": np.ascontiguousarray(ds[c]),
            **tb,
        })
    res = run_bass_kernel_spmd(nc, in_maps, core_ids=list(range(NCORES)))
    y = np.concatenate([res.results[c]["y_out"] for c in range(NCORES)], axis=0)
    return y.reshape(1, T * NCORES, D).astype(np.float32)
